# version 05
# baseline (speedup 1.0000x reference)
import numpy as np
import ml_dtypes
from contextlib import ExitStack

import concourse.bass as bass
import concourse.mybir as mybir
from concourse.bass_utils import run_bass_kernel_spmd

F32 = mybir.dt.float32
BF16 = mybir.dt.bfloat16
AF = mybir.ActivationFunctionType
ALU = mybir.AluOpType
BF = ml_dtypes.bfloat16

NCORES = 8
DEBUG_A3C = False
TOWN = 1024
HALO = 8
TLOC = TOWN + HALO
ENGS = ("tensor", "vector", "scalar", "gpsimd", "sync")


class Buf:
    __slots__ = ("ap", "name", "ws", "rs", "children")

    def __init__(self, ap, name):
        self.ap = ap
        self.name = name
        self.ws = []
        self.rs = []
        self.children = []

    def __getitem__(self, sl):
        return self.ap[sl]


class P:
    def __init__(self, name):
        self.nc = bass.Bass("TRN2", target_bir_lowering=False)
        self.name = name
        self.ctx = ExitStack()
        self.streams = {e: [] for e in ENGS}
        self.sems = {}
        self.cnt = {}
        self.out_names = []
        self.store_sems = set()
        self._open_sbufs = []
        self._freed_markers = []

    def sem(self, name):
        if name not in self.sems:
            self.sems[name] = self.ctx.enter_context(self.nc.semaphore(name))
            self.cnt[name] = 0
        return self.sems[name]

    def sbuf(self, name, shape, dtype, side=None):
        cm = self.nc.sbuf_tensor(name, shape, dtype, side=side)
        b = Buf(cm.__enter__(), name)
        b.rs = list(self._freed_markers)
        self._open_sbufs.append((b, cm))
        return b

    def free(self, *bufs):
        group = set(id(b) for b in bufs)
        for b in bufs:
            for bb in [b] + b.children:
                self._freed_markers.extend(bb.ws)
                self._freed_markers.extend(bb.rs)
        while self._open_sbufs and id(self._open_sbufs[-1][0]) in group:
            b, cm = self._open_sbufs.pop()
            cm.__exit__(None, None, None)
            group.discard(id(b))
        assert not group, f"free(): non-LIFO free requested: {[b.name for b in bufs if id(b) in group]}"

    def psum(self, name, shape, dtype=F32):
        return Buf(self.ctx.enter_context(self.nc.psum_tensor(name, shape, dtype)), name)

    def subdiv(self, buf, n, w):
        cs = []
        for i in range(n):
            c = Buf(buf.ap[:, i * w:(i + 1) * w], f"{buf.name}.{i}")
            c.ws = list(buf.ws)
            c.rs = list(buf.rs)
            cs.append(c)
        buf.children.extend(cs)
        return cs

    def merge(self, buf):
        ws, rs = list(buf.ws), list(buf.rs)
        for c in buf.children:
            ws.extend(c.ws)
            rs.extend(c.rs)
        for b in [buf] + buf.children:
            b.ws = list(ws)
            b.rs = list(rs)

    def din(self, name, shape, dtype):
        return self.nc.dram_tensor(name, shape, dtype, kind="ExternalInput").ap()

    def dout(self, name, shape, dtype):
        self.out_names.append(name)
        return self.nc.dram_tensor(name, shape, dtype, kind="ExternalOutput").ap()

    def _resolve(self, eng, reads, writes, extra):
        waits = {}
        drain = False
        me = "m_" + eng
        def add(mk):
            nonlocal drain
            if mk is None:
                return
            sn, v = mk
            if sn == me:
                drain = True
            elif sn.startswith("m_"):
                waits[sn] = max(waits.get(sn, 0), v)
            else:
                waits[sn] = max(waits.get(sn, 0), self.cnt[sn])
        for b in reads:
            for w in b.ws:
                add(w)
        for b in writes:
            for w in b.ws:
                add(w)
            for r in b.rs:
                add(r)
        for mk in extra:
            add(mk)
        return sorted(waits.items()), drain

    def _update(self, marker, reads, writes):
        for b in reads:
            b.rs.append(marker)
        for b in writes:
            b.ws = [marker]
            b.rs = []

    def step(self, eng, fn, reads=(), writes=(), extra=(), n=1):
        waits, drain = self._resolve(eng, reads, writes, extra)
        m = "m_" + eng
        self.sem(m)
        self.cnt[m] += 1
        val = self.cnt[m]
        def cl(engine, fn=fn, waits=waits, drain=drain, m=m):
            for sn, v in waits:
                engine.wait_ge(self.sems[sn], v)
            if drain and eng != "tensor":
                engine.drain()
            ins = fn(engine)
            ins.then_inc(self.sems[m], 1)
        self.streams[eng].append(cl)
        marker = (m, val)
        self._update(marker, reads, writes)
        return marker

    def step1(self, eng, meth, *args, reads=(), writes=(), extra=(), **kwargs):
        def fn(engine, meth=meth, args=args, kwargs=kwargs):
            return getattr(engine, meth)(*args, **kwargs)
        return self.step(eng, fn, reads=reads, writes=writes, extra=extra)

    def dma(self, eng, out, in_, sem=None, reads=(), writes=(), extra=()):
        if sem is None:
            if writes:
                sem = "d_" + writes[0].name
            else:
                sem = "st_" + reads[0].name
                self.store_sems.add(sem)
        waits, _ = self._resolve("__dma__", reads, writes, extra)
        self.sem(sem)
        self.cnt[sem] += 16
        val = self.cnt[sem]
        def cl(engine, out=out, in_=in_, waits=waits, sem=sem):
            for sn, v in waits:
                engine.wait_ge(self.sems[sn], v)
            engine.dma_start(out=out, in_=in_).then_inc(self.sems[sem], 16)
        self.streams[eng].append(cl)
        marker = (sem, val)
        self._update(marker, reads, writes)
        return marker

    def finalize(self, final_waits):
        def cl(engine, wl=tuple(final_waits)):
            for sn, v in wl:
                engine.wait_ge(self.sems[sn], v)
        self.streams["sync"].append(cl)
        block = self.ctx.enter_context(self.nc.Block())
        for e in ENGS:
            fns = self.streams[e]
            if not fns:
                continue
            getattr(block, e)(lambda engine, fns=fns: [fn(engine) for fn in fns])
        while self._open_sbufs:
            _, cm = self._open_sbufs.pop()
            cm.__exit__(None, None, None)
        self.ctx.close()
        return self.nc



def mm_group(p, ps, lhsT_slices, rhs_slices, reads, writes, extra=()):
    nk = len(lhsT_slices)
    def fn(t):
        ins = None
        for k in range(nk):
            ins = t.matmul(ps, lhsT_slices[k], rhs_slices[k], start=(k == 0), stop=(k == nk - 1))
        return ins
    return p.step("tensor", fn, reads=reads, writes=writes, extra=extra, n=nk)


def part_reduce(p, ps_row, ones_bf, src_bf_ap, reads, writes):
    return p.step("tensor", lambda t: t.matmul(ps_row, ones_bf, src_bf_ap, start=True, stop=True),
                  reads=reads, writes=writes)


def bcast_row(p, ps_dst, ones_row_f, row_ap, reads, writes):
    return p.step("tensor", lambda t: t.matmul(ps_dst, ones_row_f, row_ap, start=True, stop=True),
                  reads=reads, writes=writes)



def _c(a):
    return np.ascontiguousarray(a)


def _pcol(vec):
    v = np.asarray(vec)
    return _c(v.reshape(-1, 128).T)


def stage_common(inputs):
    g = {}
    x = np.asarray(inputs["L0_attn_x"])[0]
    g["xT"] = _c(x.T)
    g["xT_pad"] = np.pad(g["xT"], ((0, 0), (HALO, 0)))
    for nm in ("Wq", "Wk", "Wv", "Wo"):
        g["l0_" + nm] = _c(np.asarray(inputs[f"L0_attn_{nm}"]).T)
    g["l0_ig"] = _c(np.asarray(inputs["L0_attn_ig_w"]).T)
    g["l0_og"] = _c(np.asarray(inputs["L0_attn_og_w"]).T)
    g["l0_gam"] = _c(np.asarray(inputs["L0_attn_gamma_w"]).T)
    cw = np.asarray(inputs["L0_attn_conv_w"])[:, 0, :]
    g["l0_convw"] = _c(cw.reshape(16, 128, 4).transpose(1, 0, 2).reshape(128, 64))
    for nm, key in (("convb", "L0_attn_conv_b"), ("igb", "L0_attn_ig_b"), ("ogb", "L0_attn_og_b"),
                    ("gng", "L0_attn_gn_g"), ("gnb", "L0_attn_gn_b")):
        g["l0_" + nm] = _pcol(np.asarray(inputs[key]))
    g["l0_gamb"] = _c(np.asarray(inputs["L0_attn_gamma_b"]).reshape(16, 1))
    for nm, key in (("vng", "L0_attn_vn_g"), ("vnb", "L0_attn_vn_b"),
                    ("mng", "L0_attn_mn_g"), ("mnb", "L0_attn_mn_b")):
        g["l0_" + nm] = _c(np.asarray(inputs[key]).reshape(128, 1))
    g["up_l"] = _c(np.asarray(inputs["L0_ffn_W_up"]).T)
    g["gt_l"] = _c(np.asarray(inputs["L0_ffn_W_gate"]).T)
    g["dn_l"] = _c(np.asarray(inputs["L0_ffn_W_down"]).T)
    g["wsug"] = _pcol(np.asarray(inputs["L0_ffn_Wscale_ug"]))
    g["wsdp"] = _pcol(np.asarray(inputs["L0_ffn_Wscale_dp"]))
    g["suug"] = _pcol(np.asarray(inputs["L0_ffn_SU_ug"]))
    g["sudp"] = _pcol(np.asarray(inputs["L0_ffn_SU_dp"]))
    g["h128b"] = _hadamard(128, BF)
    g["l1_Wq"] = _c(np.asarray(inputs["L1_attn_Wq"]))
    for i in range(2):
        g[f"l1_Wk{i}"] = _c(np.asarray(inputs["L1_attn_Wk"])[i])
        g[f"l1_Wv{i}"] = _c(np.asarray(inputs["L1_attn_Wv"])[i])
        g[f"l1_Wb{i}"] = _c(np.asarray(inputs["L1_attn_Wb"])[i])
    g["l1_Wa"] = _c(np.asarray(inputs["L1_attn_Wa"]))
    g["l1_Wb0"] = g["l1_Wb0"]; g["l1_Wb1"] = g["l1_Wb1"]
    def _cpack(cwa):
        C = cwa.shape[0]
        return _c(cwa.reshape(C // 128, 128, 4).transpose(1, 0, 2).reshape(128, -1))
    g["l1_cqw"] = _cpack(np.asarray(inputs["L1_attn_conv_q_w"]))
    g["l1_ckw0"] = _cpack(np.asarray(inputs["L1_attn_conv_k_w"])[0])
    g["l1_ckw1"] = _cpack(np.asarray(inputs["L1_attn_conv_k_w"])[1])
    g["l1_cvw0"] = _cpack(np.asarray(inputs["L1_attn_conv_v_w"])[0])
    g["l1_cvw1"] = _cpack(np.asarray(inputs["L1_attn_conv_v_w"])[1])
    g["l1_alog"] = _c(np.asarray(inputs["L1_attn_A_log"]).reshape(6, 1))
    g["l1_dtb"] = _c(np.asarray(inputs["L1_attn_dt_bias"]).reshape(6, 1))
    g["l1_Wg"] = _c(np.asarray(inputs["L1_attn_Wg"]))
    g["l1_Wo"] = _c(np.asarray(inputs["L1_attn_Wo"]))
    g["l1_onw"] = _c(np.asarray(inputs["L1_attn_o_norm_w"]).reshape(1, 512))
    g["f1_gate"] = _c(np.asarray(inputs["L1_ffn_gate_w"]).T)
    g["f1_w1l"] = _c(np.asarray(inputs["L1_ffn_w1"]).transpose(0, 2, 1))
    g["f1_w3l"] = _c(np.asarray(inputs["L1_ffn_w3"]).transpose(0, 2, 1))
    g["f1_w2l"] = _c(np.asarray(inputs["L1_ffn_w2"]).transpose(0, 2, 1))
    g["f1_ws1l"] = _c(np.asarray(inputs["L1_ffn_ws1"]).T)
    g["f1_ws3l"] = _c(np.asarray(inputs["L1_ffn_ws3"]).T)
    g["f1_ws2l"] = _c(np.asarray(inputs["L1_ffn_ws2"]).T)
    r_ = np.arange(128)
    g["utri_inc"] = (r_[:, None] <= r_[None, :]).astype(np.float32)
    g["utri_strict"] = (r_[:, None] < r_[None, :]).astype(np.float32)
    g["ltri_strict"] = (r_[:, None] > r_[None, :]).astype(np.float32)
    g["ones_col_bf"] = np.ones((128, 1), BF)
    g["ones_row_f"] = np.ones((1, 128), np.float32)
    g["id_bf"] = np.eye(128, dtype=BF)
    g["id_f"] = np.eye(128, dtype=np.float32)
    return g



def emit_ln_c(p, ctxb, src_f32, dst_bf, T, nts, src_is_bf=False):
    xbfB = src_f32 if src_is_bf else dst_bf
    sqs = ctxb["sqs"]
    onesb = ctxb["onesb"]; onesrf = ctxb["onesrf"]
    ps_r1 = ctxb["ps_r1"]; ps_r2 = ctxb["ps_r2"]; ps_m = ctxb["ps_m"]; ps_n = ctxb["ps_n"]
    srow = ctxb["srow"]; srow2 = ctxb["srow2"]; srow3 = ctxb["srow3"]; rrep = ctxb["rrep"]; scr = ctxb["scr"]
    if not src_is_bf:
        for k in range(16):
            p.step1("scalar", "activation", xbfB[:, k * T:(k + 1) * T], src_f32[:, k * T:(k + 1) * T],
                    AF.Copy, reads=[src_f32], writes=[xbfB])
    for (o0, n) in nts:
        mm_group(p, ps_r1.ap[0:1, 0:n], [onesb[:, :]] * 16,
                 [xbfB[:, k * T + o0:k * T + o0 + n] for k in range(16)],
                 reads=[xbfB, onesb], writes=[ps_r1])
        for k in range(16):
            sb = sqs[k % 2]
            p.step1("scalar", "activation", sb[:, 0:n], src_f32[:, k * T + o0:k * T + o0 + n],
                    AF.Square, reads=[src_f32], writes=[sb])
            p.step1("tensor", "matmul", ps_r2[0:1, 0:n], onesb[:, :], sb[:, 0:n],
                    start=(k == 0), stop=(k == 15), reads=[sb, onesb], writes=[ps_r2])
        p.step1("vector", "tensor_scalar_mul", srow[0:1, 0:n], ps_r1[0:1, 0:n], 1.0 / 2048,
                reads=[ps_r1], writes=[srow])
        p.step1("vector", "tensor_scalar_mul", srow2[0:1, 0:n], ps_r2[0:1, 0:n], 1.0 / 2048,
                reads=[ps_r2], writes=[srow2])
        p.step1("vector", "tensor_mul", srow3[0:1, 0:n], srow[0:1, 0:n], srow[0:1, 0:n],
                reads=[srow], writes=[srow3])
        p.step1("vector", "tensor_sub", srow2[0:1, 0:n], srow2[0:1, 0:n], srow3[0:1, 0:n],
                reads=[srow2, srow3], writes=[srow2])
        p.step1("vector", "tensor_scalar_add", srow2[0:1, 0:n], srow2[0:1, 0:n], 1e-5,
                reads=[srow2], writes=[srow2])
        p.step1("scalar", "activation", srow2[0:1, 0:n], srow2[0:1, 0:n], AF.Sqrt,
                reads=[srow2], writes=[srow2])
        p.step1("vector", "reciprocal", srow2[0:1, 0:n], srow2[0:1, 0:n],
                reads=[srow2], writes=[srow2])
        bcast_row(p, ps_m.ap[:, 0:n], onesrf[:, :], srow[0:1, 0:n], reads=[srow, onesrf], writes=[ps_m])
        bcast_row(p, ps_n.ap[:, 0:n], onesrf[:, :], srow2[0:1, 0:n], reads=[srow2, onesrf], writes=[ps_n])
        p.step1("vector", "tensor_copy", rrep[:, 0:n], ps_n[:, 0:n], reads=[ps_n], writes=[rrep])
        scr2 = ctxb["scr2"]
        for k in range(16):
            if src_is_bf:
                p.step1("scalar", "activation", scr2[:, 0:n], src_f32[:, k * T + o0:k * T + o0 + n],
                        AF.Copy, reads=[src_f32], writes=[scr2])
                p.step1("vector", "tensor_sub", scr[:, 0:n], scr2[:, 0:n],
                        ps_m[:, 0:n], reads=[scr2, ps_m], writes=[scr])
            else:
                p.step1("vector", "tensor_sub", scr[:, 0:n], src_f32[:, k * T + o0:k * T + o0 + n],
                        ps_m[:, 0:n], reads=[src_f32, ps_m], writes=[scr])
            p.step1("vector", "tensor_mul", dst_bf[:, k * T + o0:k * T + o0 + n], scr[:, 0:n],
                    rrep[:, 0:n], reads=[scr, rrep], writes=[dst_bf])


def mk_ln_ctx(p, T):
    return {
        "sqs": [p.sbuf("lnsq0", [128, 512], BF16), p.sbuf("lnsq1", [128, 512], BF16)],
        "srow": p.sbuf("srow", [1, 512], F32),
        "srow2": p.sbuf("srow2", [1, 512], F32),
        "srow3": p.sbuf("srow3", [1, 512], F32),
        "rrep": p.sbuf("rrep", [128, 512], F32),
        "scr": p.sbuf("lnscr", [128, 512], F32),
        "scr2": p.sbuf("lnscr2", [128, 512], F32),
        "ps_r1": p.psum("ps_r1", [1, 512], F32),
        "ps_r2": p.psum("ps_r2", [1, 512], F32),
        "ps_m": p.psum("ps_m", [128, 512], F32),
        "ps_n": p.psum("ps_n", [128, 512], F32),
    }



def build_a1():
    p = P("a1")
    XT = p.din("xt", [2048, TLOC], F32)
    WQ = p.din("wq", [2048, 2048], F32)
    WK = p.din("wk", [2048, 2048], F32)
    WV = p.din("wv", [2048, 2048], F32)
    WI = p.din("wi", [2048, 2048], F32)
    WOG = p.din("wog", [2048, 2048], F32)
    WGAM = p.din("wgam", [2048, 16], F32)
    CONVW = p.din("convw", [128, 64], F32)
    CONVB = p.din("convb", [128, 16], F32)
    IGB = p.din("igb", [128, 16], F32)
    OGB = p.din("ogb", [128, 16], F32)
    GAMB = p.din("gamb", [16, 1], F32)
    VNG = p.din("vng", [128, 1], F32)
    VNB = p.din("vnb", [128, 1], F32)
    ONESB = p.din("ones_col_bf", [128, 1], BF16)
    ONESRF = p.din("ones_row_f", [1, 128], F32)

    QO = p.dout("qo", [2048, TOWN], BF16)
    BO = p.dout("bo", [2048, TOWN], BF16)
    OO = p.dout("oo", [2048, TOWN], BF16)
    GO = p.dout("go", [16, TOWN], F32)

    xt = p.sbuf("sb_xt", [128, 16 * TLOC], F32)
    xln = p.sbuf("xln", [128, 16 * TLOC], BF16)
    xc = p.sbuf("xc", [128, 16 * TOWN], BF16)
    lc = mk_ln_ctx(p, TLOC)
    wbuf = {nm: [p.sbuf(f"w_{nm}{s}", [128, 16 * 128], BF16) for s in range(2)]
            for nm in ("q", "k", "v", "i", "o")}
    wgam = p.sbuf("wgam_b", [128, 16 * 16], BF16)
    convw = p.sbuf("c_convw", [128, 64], F32)
    convb = p.sbuf("c_convb", [128, 16], F32)
    igb = p.sbuf("c_igb", [128, 16], F32)
    ogb = p.sbuf("c_ogb", [128, 16], F32)
    gamb = p.sbuf("c_gamb", [16, 1], F32)
    vng = p.sbuf("c_vng", [128, 1], F32)
    vnb = p.sbuf("c_vnb", [128, 1], F32)
    onesb = p.sbuf("c_onesb", [128, 1], BF16)
    onesrf = p.sbuf("c_onesrf", [1, 128], F32)
    lc["onesb"] = onesb
    lc["onesrf"] = onesrf
    srow3 = lc["srow3"]
    acc = [p.sbuf(f"acc{i}", [128, TOWN], F32) for i in range(2)]
    sqb = p.sbuf("sqb", [128, 512], BF16)
    vraw = p.sbuf("vraw", [128, 512], BF16)
    srow = lc["srow"]; srow2 = lc["srow2"]; rrep = lc["rrep"]; scr = lc["scr"]
    ps_r1 = lc["ps_r1"]; ps_r2 = lc["ps_r2"]; ps_m = lc["ps_m"]; ps_n = lc["ps_n"]
    kbf = p.sbuf("kbf", [128, TOWN], BF16)
    vbf = p.sbuf("vbf", [128, TOWN], BF16)
    ibf = p.sbuf("ibf", [128, 512], BF16)
    qst = p.sbuf("qst", [128, TOWN], BF16)
    bst = p.sbuf("bst", [128, TOWN], BF16)
    ost = p.sbuf("ost", [128, TOWN], BF16)
    gam = p.sbuf("gam", [16, 512], F32)
    ps_a = p.psum("ps_a", [128, 512], F32)
    ps_b = p.psum("ps_b", [128, 512], F32)
    ps_g = p.psum("ps_g", [16, 512], F32)

    for i in range(16):
        p.dma("sync", xt[:, i * TLOC:(i + 1) * TLOC], XT[i * 128:(i + 1) * 128, :], writes=[xt])
    for b, srcdr in ((convw, CONVW), (convb, CONVB), (igb, IGB), (ogb, OGB), (gamb, GAMB),
                     (vng, VNG), (vnb, VNB), (onesb, ONESB), (onesrf, ONESRF)):
        p.dma("sync", b[:, :], srcdr, writes=[b])
    p.dma("gpsimd", wgam.ap.rearrange("p (k m) -> p k m", k=16),
          WGAM.rearrange("(k p) m -> p k m", p=128), writes=[wgam])

    emit_ln_c(p, lc, xt, xln, TLOC, [(0, 344), (344, 344), (688, 344)])

    for kk in range(16):
        eng = "vector"
        a0, a1 = acc[0], acc[1]
        p.step1(eng, "tensor_scalar", a0[:, :], xln[:, kk * TLOC + 5:kk * TLOC + 5 + TOWN],
                convw[:, 4 * kk:4 * kk + 1], None, ALU.mult, reads=[xln, convw], writes=[a0])
        for j in range(3):
            srcacc, dst = (a0, a1) if j % 2 == 0 else (a1, a0)
            p.step1(eng, "scalar_tensor_tensor", dst[:, :],
                    xln[:, kk * TLOC + 6 + j:kk * TLOC + 6 + j + TOWN],
                    convw[:, 4 * kk + j + 1:4 * kk + j + 2], srcacc[:, :], ALU.mult, ALU.add,
                    reads=[xln, convw, srcacc], writes=[dst])
        p.step1("scalar", "activation", xc[:, kk * TOWN:(kk + 1) * TOWN], a1[:, :], AF.Silu,
                bias=convb[:, kk:kk + 1], reads=[a1, convb], writes=[xc])

    def pstats_mu_rstd(eps):
        p.step1("vector", "tensor_scalar_mul", srow[0:1, 0:512], ps_r1[0:1, 0:512], 1.0 / 128,
                reads=[ps_r1], writes=[srow])
        p.step1("vector", "tensor_scalar_mul", srow2[0:1, 0:512], ps_r2[0:1, 0:512], 1.0 / 128,
                reads=[ps_r2], writes=[srow2])
        p.step1("vector", "tensor_mul", srow3[0:1, 0:512], srow[0:1, 0:512], srow[0:1, 0:512],
                reads=[srow], writes=[srow3])
        p.step1("vector", "tensor_sub", srow2[0:1, 0:512], srow2[0:1, 0:512], srow3[0:1, 0:512],
                reads=[srow2, srow3], writes=[srow2])
        p.step1("vector", "tensor_scalar_add", srow2[0:1, 0:512], srow2[0:1, 0:512], eps,
                reads=[srow2], writes=[srow2])
        p.step1("scalar", "activation", srow2[0:1, 0:512], srow2[0:1, 0:512], AF.Sqrt,
                reads=[srow2], writes=[srow2])
        p.step1("vector", "reciprocal", srow2[0:1, 0:512], srow2[0:1, 0:512],
                reads=[srow2], writes=[srow2])

    xln_own = [xln[:, k * TLOC + HALO:k * TLOC + HALO + TOWN] for k in range(16)]
    for h in range(16):
        for nm, W in (("q", WQ), ("k", WK), ("v", WV), ("i", WI), ("o", WOG)):
            p.dma("gpsimd", wbuf[nm][h % 2].ap.rearrange("p (k m) -> p k m", k=16),
                  W[:, h * 128:(h + 1) * 128].rearrange("(k p) m -> p k m", p=128),
                  writes=[wbuf[nm][h % 2]])
        for nt in range(2):
            o0 = nt * 512
            osl = slice(o0, o0 + 512)
            mm_group(p, ps_a.ap[:, 0:512], [wbuf["q"][h % 2][:, k * 128:(k + 1) * 128] for k in range(16)],
                     [c[:, osl] for c in xln_own], reads=[wbuf["q"][h % 2], xln], writes=[ps_a])
            p.step1("scalar", "activation", sqb[:, 0:512], ps_a[:, 0:512], AF.Square,
                    reads=[ps_a], writes=[sqb])
            p.step1("tensor", "matmul", ps_r1[0:1, 0:512], onesb[:, :], sqb[:, 0:512],
                    start=True, stop=True, reads=[sqb, onesb], writes=[ps_r1])
            p.step1("scalar", "activation", srow[0:1, 0:512], ps_r1[0:1, 0:512], AF.Sqrt,
                    reads=[ps_r1], writes=[srow])
            p.step1("vector", "tensor_scalar_max", srow[0:1, 0:512], srow[0:1, 0:512], 1e-12,
                    reads=[srow], writes=[srow])
            p.step1("vector", "reciprocal", srow[0:1, 0:512], srow[0:1, 0:512],
                    reads=[srow], writes=[srow])
            bcast_row(p, ps_m.ap[:, 0:512], onesrf[:, :], srow[0:1, 0:512], reads=[srow, onesrf], writes=[ps_m])
            p.step1("vector", "tensor_copy", rrep[:, 0:512], ps_m[:, 0:512], reads=[ps_m], writes=[rrep])
            p.step1("vector", "tensor_mul", qst[:, osl], ps_a[:, 0:512], rrep[:, 0:512],
                    reads=[ps_a, rrep], writes=[qst])
            p.dma("sync", QO[h * 128:(h + 1) * 128, osl], qst[:, osl], reads=[qst])
            mm_group(p, ps_b.ap[:, 0:512], [wbuf["k"][h % 2][:, k * 128:(k + 1) * 128] for k in range(16)],
                     [c[:, osl] for c in xln_own], reads=[wbuf["k"][h % 2], xln], writes=[ps_b])
            p.step1("scalar", "activation", sqb[:, 0:512], ps_b[:, 0:512], AF.Square,
                    reads=[ps_b], writes=[sqb])
            p.step1("tensor", "matmul", ps_r1[0:1, 0:512], onesb[:, :], sqb[:, 0:512],
                    start=True, stop=True, reads=[sqb, onesb], writes=[ps_r1])
            p.step1("scalar", "activation", srow[0:1, 0:512], ps_r1[0:1, 0:512], AF.Sqrt,
                    reads=[ps_r1], writes=[srow])
            p.step1("vector", "tensor_scalar_max", srow[0:1, 0:512], srow[0:1, 0:512], 1e-12,
                    reads=[srow], writes=[srow])
            p.step1("vector", "reciprocal", srow[0:1, 0:512], srow[0:1, 0:512],
                    reads=[srow], writes=[srow])
            bcast_row(p, ps_m.ap[:, 0:512], onesrf[:, :], srow[0:1, 0:512], reads=[srow, onesrf], writes=[ps_m])
            p.step1("vector", "tensor_copy", rrep[:, 0:512], ps_m[:, 0:512], reads=[ps_m], writes=[rrep])
            p.step1("vector", "tensor_mul", kbf[:, osl], ps_b[:, 0:512], rrep[:, 0:512],
                    reads=[ps_b, rrep], writes=[kbf])
            mm_group(p, ps_a.ap[:, 0:512], [wbuf["v"][h % 2][:, k * 128:(k + 1) * 128] for k in range(16)],
                     [c[:, osl] for c in xln_own], reads=[wbuf["v"][h % 2], xln], writes=[ps_a])
            p.step1("scalar", "activation", vraw[:, 0:512], ps_a[:, 0:512], AF.Copy,
                    reads=[ps_a], writes=[vraw])
            p.step1("scalar", "activation", sqb[:, 0:512], ps_a[:, 0:512], AF.Square,
                    reads=[ps_a], writes=[sqb])
            p.step1("tensor", "matmul", ps_r1[0:1, 0:512], onesb[:, :], vraw[:, 0:512],
                    start=True, stop=True, reads=[vraw, onesb], writes=[ps_r1])
            p.step1("tensor", "matmul", ps_r2[0:1, 0:512], onesb[:, :], sqb[:, 0:512],
                    start=True, stop=True, reads=[sqb, onesb], writes=[ps_r2])
            pstats_mu_rstd(1e-5)
            bcast_row(p, ps_m.ap[:, 0:512], onesrf[:, :], srow[0:1, 0:512], reads=[srow, onesrf], writes=[ps_m])
            bcast_row(p, ps_n.ap[:, 0:512], onesrf[:, :], srow2[0:1, 0:512], reads=[srow2, onesrf], writes=[ps_n])
            p.step1("vector", "tensor_copy", rrep[:, 0:512], ps_n[:, 0:512], reads=[ps_n], writes=[rrep])
            p.step1("vector", "tensor_copy", lc["scr2"].ap[:, 0:512], ps_m[:, 0:512],
                    reads=[ps_m], writes=[lc["scr2"]])
            p.step1("vector", "tensor_sub", scr[:, 0:512], ps_a[:, 0:512], lc["scr2"][:, 0:512],
                    reads=[ps_a, lc["scr2"]], writes=[scr])
            p.step1("vector", "tensor_mul", scr[:, 0:512], scr[:, 0:512], rrep[:, 0:512],
                    reads=[scr, rrep], writes=[scr])
            p.step1("vector", "tensor_scalar", vbf[:, osl], scr[:, 0:512], vng[:, :], vnb[:, :],
                    ALU.mult, ALU.add, reads=[scr, vng, vnb], writes=[vbf])
            mm_group(p, ps_b.ap[:, 0:512], [wbuf["i"][h % 2][:, k * 128:(k + 1) * 128] for k in range(16)],
                     [xc[:, k * TOWN + o0:k * TOWN + o0 + 512] for k in range(16)],
                     reads=[wbuf["i"][h % 2], xc], writes=[ps_b])
            p.step1("scalar", "activation", ibf[:, 0:512], ps_b[:, 0:512], AF.Sigmoid,
                    bias=igb[:, h:h + 1], reads=[ps_b, igb], writes=[ibf])
            p.step1("vector", "tensor_mul", scr[:, 0:512], kbf[:, osl], vbf[:, osl],
                    reads=[kbf, vbf], writes=[scr])
            p.step1("vector", "tensor_mul", bst[:, osl], scr[:, 0:512], ibf[:, 0:512],
                    reads=[scr, ibf], writes=[bst])
            p.dma("sync", BO[h * 128:(h + 1) * 128, osl], bst[:, osl], reads=[bst])
            mm_group(p, ps_a.ap[:, 0:512], [wbuf["o"][h % 2][:, k * 128:(k + 1) * 128] for k in range(16)],
                     [xc[:, k * TOWN + o0:k * TOWN + o0 + 512] for k in range(16)],
                     reads=[wbuf["o"][h % 2], xc], writes=[ps_a])
            p.step1("scalar", "activation", ost[:, osl], ps_a[:, 0:512], AF.Sigmoid,
                    bias=ogb[:, h:h + 1], reads=[ps_a, ogb], writes=[ost])
            p.dma("sync", OO[h * 128:(h + 1) * 128, osl], ost[:, osl], reads=[ost])

    for nt in range(2):
        o0 = nt * 512
        mm_group(p, ps_g.ap[0:16, 0:512], [wgam[:, k * 16:(k + 1) * 16] for k in range(16)],
                 [xc[:, k * TOWN + o0:k * TOWN + o0 + 512] for k in range(16)],
                 reads=[wgam, xc], writes=[ps_g])
        p.step1("scalar", "activation", gam[0:16, 0:512], ps_g[0:16, 0:512], AF.Sigmoid,
                bias=gamb[0:16, :], reads=[ps_g, gamb], writes=[gam])
        p.dma("sync", GO[:, o0:o0 + 512], gam[0:16, 0:512], reads=[gam])

    return p.finalize([(s, p.cnt[s]) for s in sorted(p.store_sems)])



def build_a2():
    p = P("a2")
    BO2 = p.din("bo2", [256, 8192], BF16)
    GO2 = p.din("go2", [2, 8192], F32)
    ONESRF = p.din("ones_row_f", [1, 128], F32)
    MO = p.dout("mo", [256, 8192], BF16)

    onesrf = p.sbuf("c_onesrf", [1, 128], F32)
    grow = p.sbuf("grow", [1, 8192], F32)
    bb = p.sbuf("bb", [128, 8192], BF16)
    bf32 = p.sbuf("bf32", [128, 8192], F32)
    grep = p.sbuf("grep", [128, 8192], F32)
    mem = p.sbuf("mem", [128, 8192], BF16)
    ps_bc = p.psum("ps_bc", [128, 512], F32)

    p.dma("sync", onesrf[:, :], ONESRF, writes=[onesrf])
    for j in range(2):
        p.dma("sync", bb[:, :], BO2[j * 128:(j + 1) * 128, :], writes=[bb])
        p.dma("sync", grow[0:1, :], GO2[j:j + 1, :], writes=[grow])
        p.step1("scalar", "activation", bf32[:, :], bb[:, :], AF.Copy, reads=[bb], writes=[bf32])
        for nt in range(16):
            o0 = nt * 512
            bcast_row(p, ps_bc.ap[:, 0:512], onesrf[:, :], grow[0:1, o0:o0 + 512],
                      reads=[grow, onesrf], writes=[ps_bc])
            p.step1("vector", "tensor_copy", grep[:, o0:o0 + 512], ps_bc[:, 0:512],
                    reads=[ps_bc], writes=[grep])
        p.step1("vector", "tensor_tensor_scan", mem[:, :], grep[:, :], bf32[:, :], 0.0,
                ALU.mult, ALU.add, reads=[grep, bf32], writes=[mem])
        p.dma("sync", MO[j * 128:(j + 1) * 128, :], mem[:, :], reads=[mem])
    return p.finalize([(s, p.cnt[s]) for s in sorted(p.store_sems)])



def build_a3c():
    p = P("a3c")
    MO3 = p.din("mo3", [2048, TOWN], BF16)
    QO3 = p.din("qo3", [2048, TOWN], BF16)
    OO3 = p.din("oo3", [2048, TOWN], BF16)
    XT3 = p.din("xt3", [2048, TOWN], F32)
    WOL = p.din("wol", [2048, 2048], F32)
    WUPL = p.din("wupl", [2048, 8192], F32)
    WGTL = p.din("wgtl", [2048, 8192], F32)
    WDNL = p.din("wdnl", [8192, 2048], F32)
    WSUG = p.din("wsug", [128, 128], F32)
    WSDP = p.din("wsdp", [128, 16], F32)
    SUUG = p.din("suug", [128, 16], F32)
    SUDP = p.din("sudp", [128, 64], F32)
    H128B = p.din("h128b", [128, 128], BF16)
    MNG = p.din("mng", [128, 1], F32)
    MNB = p.din("mnb", [128, 1], F32)
    GNG = p.din("gng", [128, 16], F32)
    GNB = p.din("gnb", [128, 16], F32)
    ONESB = p.din("ones_col_bf", [128, 1], BF16)
    ONESRF = p.din("ones_row_f", [1, 128], F32)
    X2O = p.dout("x2o", [2048, TOWN], F32)
    X2LNO = p.dout("x2lno", [2048, TOWN], BF16)
    X1D = p.nc.dram_tensor("x1spill", [2048, TOWN], F32, kind="Internal").ap()

    NT2 = [(0, 512), (512, 512)]
    lc = mk_ln_ctx(p, TOWN)
    onesb = p.sbuf("c_onesb", [128, 1], BF16)
    onesrf = p.sbuf("c_onesrf", [1, 128], F32)
    lc["onesb"] = onesb; lc["onesrf"] = onesrf
    wsug = p.sbuf("c_wsug", [128, 128], F32)
    wsdp = p.sbuf("c_wsdp", [128, 16], F32)
    suug = p.sbuf("c_suug", [128, 16], F32)
    sudp = p.sbuf("c_sudp", [128, 64], F32)
    h128b = p.sbuf("c_h128", [128, 128], BF16)
    mng = p.sbuf("c_mng", [128, 1], F32)
    mnb = p.sbuf("c_mnb", [128, 1], F32)
    gng = p.sbuf("c_gng", [128, 16], F32)
    gnb = p.sbuf("c_gnb", [128, 16], F32)
    x1su = p.sbuf("x1su", [128, 16 * TOWN], BF16)
    x1su_c = p.subdiv(x1su, 16, TOWN)
    srow = lc["srow"]; srow2 = lc["srow2"]; rrep = lc["rrep"]; scr = lc["scr"]; scr2 = lc["scr2"]
    ps_r1 = lc["ps_r1"]; ps_r2 = lc["ps_r2"]; ps_m = lc["ps_m"]; ps_n = lc["ps_n"]
    ps_a = p.psum("ps_a", [128, 512], F32)
    ps_b = p.psum("ps_b", [128, 512], F32)
    sqb = p.sbuf("sqb", [128, 512], BF16)
    t2b = p.sbuf("t2b", [128, 512], BF16)

    for b, srcdr in ((onesb, ONESB), (onesrf, ONESRF), (wsug, WSUG), (wsdp, WSDP),
                     (suug, SUUG), (sudp, SUDP), (h128b, H128B), (mng, MNG), (mnb, MNB),
                     (gng, GNG), (gnb, GNB)):
        p.dma("sync", b.ap[:, :], srcdr, writes=[b])
    p.step1("vector", "tensor_scalar_mul", wsug[:, :], wsug[:, :], 64.0, reads=[wsug], writes=[wsug])
    p.step1("vector", "tensor_scalar_mul", wsdp[:, :], wsdp[:, :], 64.0, reads=[wsdp], writes=[wsdp])
    C2 = 1.0 / (64.0 * float(np.sqrt(8192.0)))
    p.step1("vector", "tensor_scalar_mul", sudp[:, :], sudp[:, :], C2, reads=[sudp], writes=[sudp])

    rscr = p.sbuf("rscr", [128, TOWN], F32, side="right")
    ofin = p.sbuf("ofin", [128, 16 * TOWN], BF16, side="right")
    mo3 = p.sbuf("mo3b", [128, 16 * TOWN], BF16, side="right")
    qo3 = p.sbuf("qo3b", [128, 16 * TOWN], BF16, side="right")
    oo3 = p.sbuf("oo3b", [128, 16 * TOWN], BF16, side="right")
    for b, srcdr in ((mo3, MO3), (qo3, QO3), (oo3, OO3)):
        p.dma("sync", b.ap.rearrange("p (k t) -> p k t", k=16),
              srcdr.rearrange("(k p) t -> p k t", p=128), writes=[b])

    for h in range(16):
        hs = slice(h * TOWN, (h + 1) * TOWN)
        for (o0, n) in NT2:
            sl = slice(h * TOWN + o0, h * TOWN + o0 + n)
            p.step1("scalar", "activation", sqb[:, 0:n], mo3[:, sl], AF.Square, reads=[mo3], writes=[sqb])
            p.step1("tensor", "matmul", ps_r1[0:1, 0:n], onesb[:, :], mo3[:, sl],
                    start=True, stop=True, reads=[mo3, onesb], writes=[ps_r1])
            p.step1("tensor", "matmul", ps_r2[0:1, 0:n], onesb[:, :], sqb[:, 0:n],
                    start=True, stop=True, reads=[sqb, onesb], writes=[ps_r2])
            _pstats(p, lc, 1e-5)
            bcast_row(p, ps_m.ap[:, 0:n], onesrf[:, :], srow[0:1, 0:n], reads=[srow, onesrf], writes=[ps_m])
            bcast_row(p, ps_n.ap[:, 0:n], onesrf[:, :], srow2[0:1, 0:n], reads=[srow2, onesrf], writes=[ps_n])
            p.step1("scalar", "activation", scr2[:, 0:n], mo3[:, sl], AF.Copy, reads=[mo3], writes=[scr2])
            p.step1("vector", "tensor_sub", scr[:, 0:n], scr2[:, 0:n], ps_m[:, 0:n],
                    reads=[scr2, ps_m], writes=[scr])
            p.step1("vector", "tensor_copy", rrep[:, 0:n], ps_n[:, 0:n], reads=[ps_n], writes=[rrep])
            p.step1("vector", "tensor_mul", scr[:, 0:n], scr[:, 0:n], rrep[:, 0:n],
                    reads=[scr, rrep], writes=[scr])
            p.step1("vector", "tensor_scalar", scr[:, 0:n], scr[:, 0:n], mng[:, :], mnb[:, :],
                    ALU.mult, ALU.add, reads=[scr, mng, mnb], writes=[scr])
            p.step1("scalar", "activation", scr2[:, 0:n], qo3[:, sl], AF.Copy, reads=[qo3], writes=[scr2])
            p.step1("vector", "tensor_mul", scr[:, 0:n], scr[:, 0:n], scr2[:, 0:n],
                    reads=[scr, scr2], writes=[scr])
            p.step1("scalar", "activation", t2b[:, 0:n], scr[:, 0:n], AF.Copy, reads=[scr], writes=[t2b])
            p.step1("scalar", "activation", sqb[:, 0:n], scr[:, 0:n], AF.Square, reads=[scr], writes=[sqb])
            p.step1("tensor", "matmul", ps_r1[0:1, 0:n], onesb[:, :], t2b[:, 0:n],
                    start=True, stop=True, reads=[t2b, onesb], writes=[ps_r1])
            p.step1("tensor", "matmul", ps_r2[0:1, 0:n], onesb[:, :], sqb[:, 0:n],
                    start=True, stop=True, reads=[sqb, onesb], writes=[ps_r2])
            _pstats(p, lc, 1e-5)
            bcast_row(p, ps_m.ap[:, 0:n], onesrf[:, :], srow[0:1, 0:n], reads=[srow, onesrf], writes=[ps_m])
            bcast_row(p, ps_n.ap[:, 0:n], onesrf[:, :], srow2[0:1, 0:n], reads=[srow2, onesrf], writes=[ps_n])
            p.step1("vector", "tensor_sub", scr[:, 0:n], scr[:, 0:n], ps_m[:, 0:n],
                    reads=[scr, ps_m], writes=[scr])
            p.step1("vector", "tensor_copy", rrep[:, 0:n], ps_n[:, 0:n], reads=[ps_n], writes=[rrep])
            p.step1("vector", "tensor_mul", scr[:, 0:n], scr[:, 0:n], rrep[:, 0:n],
                    reads=[scr, rrep], writes=[scr])
            p.step1("vector", "tensor_scalar", scr[:, 0:n], scr[:, 0:n], gng[:, h:h + 1], gnb[:, h:h + 1],
                    ALU.mult, ALU.add, reads=[scr, gng, gnb], writes=[scr])
            p.step1("scalar", "activation", scr2[:, 0:n], oo3[:, sl], AF.Copy, reads=[oo3], writes=[scr2])
            p.step1("vector", "tensor_mul", ofin[:, sl], scr[:, 0:n], scr2[:, 0:n],
                    reads=[scr, scr2], writes=[ofin])
    p.free(oo3, qo3, mo3)

    yb = p.sbuf("yb", [128, 16 * TOWN], BF16, side="right")
    wo0 = p.sbuf("wo0", [128, 16 * 128], BF16, side="right")
    wo1 = p.sbuf("wo1", [128, 16 * 128], BF16, side="right")
    ofin_c = p.subdiv(ofin, 16, TOWN)
    pool4 = [ps_a, ps_b, ps_m, ps_n]
    pi = 0
    for mc in range(16):
        wb = (wo0, wo1)[mc % 2]
        p.dma("gpsimd", wb.ap.rearrange("p (k m) -> p k m", k=16),
              WOL[:, mc * 128:(mc + 1) * 128].rearrange("(k p) m -> p k m", p=128), writes=[wb])
        for (o0, n) in NT2:
            psx = pool4[pi % 4]; pi += 1
            mm_group(p, psx.ap[:, 0:n], [wb[:, k * 128:(k + 1) * 128] for k in range(16)],
                     [c[:, o0:o0 + n] for c in ofin_c],
                     reads=[wb] + ofin_c, writes=[psx])
            p.step1("scalar", "activation", yb.ap[:, mc * TOWN + o0:mc * TOWN + o0 + n], psx[:, 0:n],
                    AF.Copy, reads=[psx], writes=[yb])

    if DEBUG_A3C:
        DBGY = p.dout("dbg_y", [2048, TOWN], BF16)
        for k in range(16):
            p.dma("sync", DBGY[k * 128:(k + 1) * 128, :], yb[:, k * TOWN:(k + 1) * TOWN], reads=[yb])
        DBGO = p.dout("dbg_o", [2048, TOWN], BF16)
        p.merge(ofin)
        for k in range(16):
            p.dma("sync", DBGO[k * 128:(k + 1) * 128, :], ofin[:, k * TOWN:(k + 1) * TOWN], reads=[ofin])

    xt3 = p.sbuf("xt3b", [128, 16 * TOWN], F32, side="right")
    p.dma("sync", xt3.ap.rearrange("p (k t) -> p k t", k=16),
          XT3.rearrange("(k p) t -> p k t", p=128), writes=[xt3])
    emit_ln_c(p, lc, yb, x1su, TOWN, NT2, src_is_bf=True)
    for k in range(16):
        ks = slice(k * TOWN, (k + 1) * TOWN)
        p.step1("scalar", "activation", rscr[:, :], x1su[:, ks], AF.Copy, reads=[x1su], writes=[rscr])
        p.step1("vector", "tensor_add", xt3[:, ks], xt3[:, ks], rscr[:, :],
                reads=[xt3, rscr], writes=[xt3])
        p.dma("sync", X1D[k * 128:(k + 1) * 128, :], xt3[:, ks], sem="sp_x1", reads=[xt3])
    emit_ln_c(p, lc, xt3, x1su, TOWN, NT2)
    p.free(xt3, wo1, wo0, yb)
    p.free(ofin)

    p.merge(x1su)

    prod = p.sbuf("prod", [128, 64 * TOWN], BF16, side="right")
    prod_c = p.subdiv(prod, 64, TOWN)
    wu0 = p.sbuf("wu0", [128, 16 * 128], BF16, side="right")
    wu1 = p.sbuf("wu1", [128, 16 * 128], BF16, side="right")
    wg0 = p.sbuf("wg0", [128, 16 * 128], BF16, side="right")
    wg1 = p.sbuf("wg1", [128, 16 * 128], BF16, side="right")
    gbuf = p.sbuf("gbuf", [128, 512], BF16, side="right")
    ubuf = p.sbuf("ubuf", [128, 512], BF16, side="right")
    btmp = p.sbuf("btmp", [128, TOWN], BF16, side="right")

    for k in range(16):
        p.step1("vector", "tensor_scalar_mul", x1su_c[k].ap[:, :], x1su_c[k][:, :], suug[:, k:k + 1],
                reads=[x1su_c[k], suug], writes=[x1su_c[k]])
    C1 = 1.0 / (64.0 * float(np.sqrt(2048.0)))
    for k in range(16):
        for (o0, n) in NT2:
            psx = pool4[pi % 4]; pi += 1
            p.step1("tensor", "matmul", psx[:, 0:n], h128b[:, :], x1su_c[k][:, o0:o0 + n],
                    start=True, stop=True, reads=[h128b, x1su_c[k]], writes=[psx])
            p.step1("scalar", "activation", x1su_c[k].ap[:, o0:o0 + n], psx[:, 0:n], AF.Copy,
                    scale=C1, reads=[psx], writes=[x1su_c[k]])
    for st_i, d in enumerate((8, 4, 2, 1)):
        pairs = [(gg, gg + d) for gg in range(16) if (gg // d) % 2 == 0]
        for (ga, gb_) in pairs:
            eng = "vector" if (ga // d) % 4 == 0 else "gpsimd"
            A, Bc = x1su_c[ga], x1su_c[gb_]
            p.step1(eng, "tensor_copy", btmp[:, :], A[:, :], reads=[A], writes=[btmp])
            p.step1(eng, "tensor_add", A.ap[:, :], btmp[:, :], Bc[:, :], reads=[btmp, Bc], writes=[A])
            p.step1(eng, "tensor_sub", Bc.ap[:, :], btmp[:, :], Bc[:, :], reads=[btmp, Bc], writes=[Bc])
    for mc in range(64):
        wu = (wu0, wu1)[mc % 2]
        wg = (wg0, wg1)[mc % 2]
        p.dma("gpsimd", wu.ap.rearrange("p (k m) -> p k m", k=16),
              WUPL[:, mc * 128:(mc + 1) * 128].rearrange("(k p) m -> p k m", p=128), writes=[wu])
        p.dma("gpsimd", wg.ap.rearrange("p (k m) -> p k m", k=16),
              WGTL[:, mc * 128:(mc + 1) * 128].rearrange("(k p) m -> p k m", p=128), writes=[wg])
        scol = mc // 64, mc % 64
        for (o0, n) in NT2:
            psx = pool4[pi % 4]; pi += 1
            mm_group(p, psx.ap[:, 0:n], [wg[:, k * 128:(k + 1) * 128] for k in range(16)],
                     [c[:, o0:o0 + n] for c in x1su_c], reads=[wg] + x1su_c, writes=[psx])
            p.step1("scalar", "activation", gbuf[:, 0:n], psx[:, 0:n], AF.Silu,
                    scale=wsug[:, 64 + mc:64 + mc + 1], reads=[psx, wsug], writes=[gbuf])
            psy = pool4[pi % 4]; pi += 1
            mm_group(p, psy.ap[:, 0:n], [wu[:, k * 128:(k + 1) * 128] for k in range(16)],
                     [c[:, o0:o0 + n] for c in x1su_c], reads=[wu] + x1su_c, writes=[psy])
            p.step1("vector", "tensor_scalar_mul", ubuf[:, 0:n], psy[:, 0:n], wsug[:, mc:mc + 1],
                    reads=[psy, wsug], writes=[ubuf])
            p.step1("vector", "scalar_tensor_tensor", prod_c[mc].ap[:, o0:o0 + n], gbuf[:, 0:n],
                    sudp[:, mc:mc + 1], ubuf[:, 0:n], ALU.mult, ALU.mult,
                    reads=[gbuf, sudp, ubuf], writes=[prod_c[mc]])
    for mc in range(64):
        for (o0, n) in NT2:
            psx = pool4[pi % 4]; pi += 1
            p.step1("tensor", "matmul", psx[:, 0:n], h128b[:, :], prod_c[mc][:, o0:o0 + n],
                    start=True, stop=True, reads=[h128b, prod_c[mc]], writes=[psx])
            p.step1("scalar", "activation", prod_c[mc].ap[:, o0:o0 + n], psx[:, 0:n], AF.Copy,
                    reads=[psx], writes=[prod_c[mc]])
    for st_i, d in enumerate((32, 16, 8, 4, 2, 1)):
        pairs = [(gg, gg + d) for gg in range(64) if (gg // d) % 2 == 0]
        for pi, (ga, gb_) in enumerate(pairs):
            eng = "vector" if pi % 2 == 0 else "gpsimd"
            A, Bc = prod_c[ga], prod_c[gb_]
            p.step1(eng, "tensor_copy", btmp[:, :], A[:, :], reads=[A], writes=[btmp])
            p.step1(eng, "tensor_add", A.ap[:, :], btmp[:, :], Bc[:, :], reads=[btmp, Bc], writes=[A])
            p.step1(eng, "tensor_sub", Bc.ap[:, :], btmp[:, :], Bc[:, :], reads=[btmp, Bc], writes=[Bc])
    p.free(btmp, ubuf, gbuf, wg1, wg0, wu1, wu0)

    wd0 = p.sbuf("wd0", [128, 64 * 128], BF16, side="right")
    for mc in range(16):
        wd = wd0
        p.dma("gpsimd", wd.ap.rearrange("p (k m) -> p k m", k=64),
              WDNL[:, mc * 128:(mc + 1) * 128].rearrange("(k p) m -> p k m", p=128), writes=[wd])
        for (o0, n) in NT2:
            psx = pool4[pi % 4]; pi += 1
            mm_group(p, psx.ap[:, 0:n], [wd[:, k * 128:(k + 1) * 128] for k in range(64)],
                     [c[:, o0:o0 + n] for c in prod_c], reads=[wd] + prod_c, writes=[psx])
            p.step1("scalar", "activation", x1su_c[mc].ap[:, o0:o0 + n], psx[:, 0:n], AF.Copy,
                    scale=wsdp[:, mc:mc + 1], reads=[psx, wsdp], writes=[x1su_c[mc]])
    p.free(wd0, prod)

    p.merge(x1su)
    x1f = p.sbuf("x1f", [128, 16 * TOWN], F32, side="right")
    p.dma("sync", x1f.ap.rearrange("p (k t) -> p k t", k=16),
          X1D.rearrange("(k p) t -> p k t", p=128), writes=[x1f],
          extra=[("sp_x1", p.cnt.get("sp_x1", 0))])
    zln = p.sbuf("zln", [128, 16 * TOWN], BF16, side="right")
    emit_ln_c(p, lc, x1su, zln, TOWN, NT2, src_is_bf=True)
    for k in range(16):
        ks = slice(k * TOWN, (k + 1) * TOWN)
        p.step1("scalar", "activation", rscr[:, :], zln[:, ks], AF.Copy, reads=[zln], writes=[rscr])
        p.step1("vector", "tensor_add", x1f[:, ks], x1f[:, ks], rscr[:, :], reads=[x1f, rscr], writes=[x1f])
        p.dma("sync", X2O[k * 128:(k + 1) * 128, :], x1f[:, ks], reads=[x1f])
    emit_ln_c(p, lc, x1f, zln, TOWN, NT2)
    for k in range(16):
        p.dma("sync", X2LNO[k * 128:(k + 1) * 128, :], zln[:, k * TOWN:(k + 1) * TOWN], reads=[zln])
    return p.finalize([(s, p.cnt[s]) for s in sorted(p.store_sems)])


def _pstats(p, lc, eps):
    srow = lc["srow"]; srow2 = lc["srow2"]; srow3 = lc["srow3"]
    ps_r1 = lc["ps_r1"]; ps_r2 = lc["ps_r2"]
    p.step1("vector", "tensor_scalar_mul", srow[0:1, 0:512], ps_r1[0:1, 0:512], 1.0 / 128,
            reads=[ps_r1], writes=[srow])
    p.step1("vector", "tensor_scalar_mul", srow2[0:1, 0:512], ps_r2[0:1, 0:512], 1.0 / 128,
            reads=[ps_r2], writes=[srow2])
    p.step1("vector", "tensor_mul", srow3[0:1, 0:512], srow[0:1, 0:512], srow[0:1, 0:512],
            reads=[srow], writes=[srow3])
    p.step1("vector", "tensor_sub", srow2[0:1, 0:512], srow2[0:1, 0:512], srow3[0:1, 0:512],
            reads=[srow2, srow3], writes=[srow2])
    p.step1("vector", "tensor_scalar_add", srow2[0:1, 0:512], srow2[0:1, 0:512], eps,
            reads=[srow2], writes=[srow2])
    p.step1("scalar", "activation", srow2[0:1, 0:512], srow2[0:1, 0:512], AF.Sqrt,
            reads=[srow2], writes=[srow2])
    p.step1("vector", "reciprocal", srow2[0:1, 0:512], srow2[0:1, 0:512],
            reads=[srow2], writes=[srow2])


def _hadamard(n, dtype):
    H = np.array([[1.0]], np.float32)
    while H.shape[0] < n:
        H = np.block([[H, H], [H, -H]])
    return H.astype(dtype)


def stage_a3c(g, M_full, Q_full, O_full, core):
    s = slice(1024 * core, 1024 * (core + 1))
    return {
        "mo3": _c(M_full[:, s]), "qo3": _c(Q_full[:, s]), "oo3": _c(O_full[:, s]),
        "xt3": _c(g["xT"][:, s]),
        "wol": g["l0_Wo"], "wupl": g["up_l"], "wgtl": g["gt_l"], "wdnl": g["dn_l"],
        "wsug": g["wsug"], "wsdp": g["wsdp"], "suug": g["suug"], "sudp": g["sudp"],
        "h128b": g["h128b"], "mng": g["l0_mng"], "mnb": g["l0_mnb"],
        "gng": g["l0_gng"], "gnb": g["l0_gnb"],
        "ones_col_bf": g["ones_col_bf"], "ones_row_f": g["ones_row_f"],
    }



def build_d1():
    p = P("d1")
    HT = p.din("ht", [2048, TLOC], BF16)
    WQ1 = p.din("wq1", [2048, 1536], F32)
    WK10 = p.din("wk10", [2048, 1536], F32)
    WK11 = p.din("wk11", [2048, 1536], F32)
    WV10 = p.din("wv10", [2048, 3072], F32)
    WV11 = p.din("wv11", [2048, 3072], F32)
    WA1 = p.din("wa1", [2048, 6], F32)
    WB10 = p.din("wb10", [2048, 6], F32)
    WB11 = p.din("wb11", [2048, 6], F32)
    CQW = p.din("cqw", [128, 12 * 4], F32)
    CKW0 = p.din("ckw0", [128, 12 * 4], F32)
    CKW1 = p.din("ckw1", [128, 12 * 4], F32)
    CVW0 = p.din("cvw0", [128, 24 * 4], F32)
    CVW1 = p.din("cvw1", [128, 24 * 4], F32)
    ALOG = p.din("alog", [6, 1], F32)
    DTB = p.din("dtb", [6, 1], F32)
    ONESB = p.din("ones_col_bf", [128, 1], BF16)
    ONESRF = p.din("ones_row_f", [1, 128], F32)
    QHO = p.dout("qho", [1536, 2 * TOWN], BF16)
    KHO = p.dout("kho", [1536, 2 * TOWN], BF16)
    VHO = p.dout("vho", [3072, 2 * TOWN], BF16)
    GBO = p.dout("gbo", [12, 2 * TOWN], F32)

    ht = p.sbuf("htb", [128, 16 * TLOC], BF16)
    lc = mk_ln_ctx(p, TLOC)
    onesb = p.sbuf("c_onesb", [128, 1], BF16)
    onesrf = p.sbuf("c_onesrf", [1, 128], F32)
    lc["onesb"] = onesb; lc["onesrf"] = onesrf
    cw = {"q": p.sbuf("c_cqw", [128, 48], F32), "k0": p.sbuf("c_ckw0", [128, 48], F32),
          "k1": p.sbuf("c_ckw1", [128, 48], F32), "v0": p.sbuf("c_cvw0", [128, 96], F32),
          "v1": p.sbuf("c_cvw1", [128, 96], F32)}
    alog = p.sbuf("c_alog", [6, 1], F32)
    dtb = p.sbuf("c_dtb", [6, 1], F32)
    negea = p.sbuf("negea", [6, 1], F32)
    raw = [p.sbuf(f"raw{i}", [128, TLOC], BF16) for i in range(4)]
    cvt = [p.sbuf(f"cvt{i}", [128, TOWN], BF16) for i in range(4)]
    acc = [p.sbuf(f"acc{i}", [128, TOWN], F32) for i in range(4)]
    w0 = p.sbuf("w0b", [128, 16 * 128], BF16)
    w1 = p.sbuf("w1b", [128, 16 * 128], BF16)
    ilv = p.sbuf("ilv", [128, 2 * TOWN], BF16)
    srow = lc["srow"]; srow2 = lc["srow2"]; rrep = lc["rrep"]
    ps_a = p.psum("ps_a", [128, 512], F32)
    ps_r1 = lc["ps_r1"]; ps_m = lc["ps_m"]
    pr6 = p.sbuf("pr6", [6, TOWN], F32)
    sb6 = [p.sbuf(f"sb6_{i}", [6, TOWN], F32) for i in range(3)]
    gb = p.sbuf("gb", [6, 2 * TOWN], F32)
    ps_s = p.psum("ps_s", [6, 512], F32)

    p.dma("sync", ht.ap.rearrange("p (k t) -> p k t", k=16),
          HT.rearrange("(k p) t -> p k t", p=128), writes=[ht])
    for b, srcdr in ((onesb, ONESB), (onesrf, ONESRF), (cw["q"], CQW), (cw["k0"], CKW0),
                     (cw["k1"], CKW1), (cw["v0"], CVW0), (cw["v1"], CVW1),
                     (alog, ALOG), (dtb, DTB)):
        p.dma("sync", b.ap[:, :], srcdr, writes=[b])
    p.step1("scalar", "activation", negea[0:6, :], alog[0:6, :], AF.Exp, reads=[alog], writes=[negea])
    p.step1("vector", "tensor_scalar_mul", negea[0:6, :], negea[0:6, :], -1.0, reads=[negea], writes=[negea])

    NT3 = [(0, 344), (344, 344), (688, 344)]
    NT2 = [(0, 512), (512, 512)]
    ht_c = [ht[:, k * TLOC:(k + 1) * TLOC] for k in range(16)]

    def raw_project(W, mc, dst, wslot):
        wb = (w0, w1)[wslot % 2]
        p.dma("gpsimd", wb.ap.rearrange("p (k m) -> p k m", k=16),
              W[:, mc * 128:(mc + 1) * 128].rearrange("(k p) m -> p k m", p=128), writes=[wb])
        for (o0, n) in NT3:
            mm_group(p, ps_a.ap[:, 0:n], [wb[:, k * 128:(k + 1) * 128] for k in range(16)],
                     [c[:, o0:o0 + n] for c in ht_c], reads=[wb, ht], writes=[ps_a])
            p.step1("scalar", "activation", dst.ap[:, o0:o0 + n], ps_a[:, 0:n],
                    AF.Copy, reads=[ps_a], writes=[dst])

    def conv_silu(wcol, mc, srcb, dst, lane):
        a0, a1 = (acc[0], acc[1]) if lane == 0 else (acc[2], acc[3])
        eng = "vector"
        p.step1(eng, "tensor_scalar", a0[:, :], srcb[:, 5:5 + TOWN],
                wcol[:, 4 * mc:4 * mc + 1], None, ALU.mult, reads=[srcb, wcol], writes=[a0])
        for j in range(3):
            sa, da = (a0, a1) if j % 2 == 0 else (a1, a0)
            p.step1(eng, "scalar_tensor_tensor", da[:, :], srcb[:, 6 + j:6 + j + TOWN],
                    wcol[:, 4 * mc + j + 1:4 * mc + j + 2], sa[:, :], ALU.mult, ALU.add,
                    reads=[srcb, wcol, sa], writes=[da])
        p.step1("scalar", "activation", dst.ap[:, :], a1[:, :], AF.Silu, reads=[a1], writes=[dst])

    def l2n_pair(c0, c1, scale):
        for (o0, n) in NT2:
            for ci, cb in enumerate((c0, c1)):
                p.step1("scalar", "activation", lc["sqs"][ci].ap[:, 0:n], cb[:, o0:o0 + n],
                        AF.Square, reads=[cb], writes=[lc["sqs"][ci]])
                p.step1("tensor", "matmul", ps_r1[0:1, 0:n], onesb[:, :], lc["sqs"][ci][:, 0:n],
                        start=(ci == 0), stop=(ci == 1), reads=[lc["sqs"][ci], onesb], writes=[ps_r1])
            p.step1("vector", "tensor_scalar_add", srow[0:1, 0:n], ps_r1[0:1, 0:n], 1e-6,
                    reads=[ps_r1], writes=[srow])
            p.step1("scalar", "activation", srow[0:1, 0:n], srow[0:1, 0:n], AF.Sqrt,
                    reads=[srow], writes=[srow])
            p.step1("vector", "reciprocal", srow[0:1, 0:n], srow[0:1, 0:n], reads=[srow], writes=[srow])
            if scale != 1.0:
                p.step1("vector", "tensor_scalar_mul", srow[0:1, 0:n], srow[0:1, 0:n], scale,
                        reads=[srow], writes=[srow])
            bcast_row(p, ps_m.ap[:, 0:n], onesrf[:, :], srow[0:1, 0:n],
                      reads=[srow, onesrf], writes=[ps_m])
            p.step1("vector", "tensor_copy", rrep[:, 0:n], ps_m[:, 0:n], reads=[ps_m], writes=[rrep])
            for cb in (c0, c1):
                p.step1("vector", "tensor_mul", cb.ap[:, o0:o0 + n], cb[:, o0:o0 + n], rrep[:, 0:n],
                        reads=[cb, rrep], writes=[cb])

    for h in range(6):
        for ci, mc in enumerate((2 * h, 2 * h + 1)):
            raw_project(WQ1, mc, raw[ci], ci)
            conv_silu(cw["q"], mc, raw[ci], cvt[ci], ci % 2)
        l2n_pair(cvt[0], cvt[1], float(256.0 ** -0.5))
        for ci, mc in enumerate((2 * h, 2 * h + 1)):
            p.step1("vector", "memset", ilv.ap[:, :], 0.0, reads=[], writes=[ilv])
            p.step1("vector", "tensor_copy", ilv.ap[:, 1::2], cvt[ci][:, :], reads=[cvt[ci]], writes=[ilv])
            p.dma("sync", QHO[mc * 128:(mc + 1) * 128, :], ilv[:, :], reads=[ilv])
    for h in range(6):
        for ci, mc in enumerate((2 * h, 2 * h + 1)):
            raw_project(WK10, mc, raw[ci], ci)
            conv_silu(cw["k0"], mc, raw[ci], cvt[ci], ci % 2)
            raw_project(WK11, mc, raw[2 + ci], ci)
            conv_silu(cw["k1"], mc, raw[2 + ci], cvt[2 + ci], ci % 2)
        l2n_pair(cvt[0], cvt[1], 1.0)
        l2n_pair(cvt[2], cvt[3], 1.0)
        for ci, mc in enumerate((2 * h, 2 * h + 1)):
            p.step1("vector", "tensor_copy", ilv.ap[:, 0::2], cvt[ci][:, :], reads=[cvt[ci]], writes=[ilv])
            p.step1("gpsimd", "tensor_copy", ilv.ap[:, 1::2], cvt[2 + ci][:, :], reads=[cvt[2 + ci]], writes=[ilv])
            p.dma("sync", KHO[mc * 128:(mc + 1) * 128, :], ilv[:, :], reads=[ilv])
    for mc in range(24):
        raw_project(WV10, mc, raw[0], 0)
        conv_silu(cw["v0"], mc, raw[0], cvt[0], 0)
        raw_project(WV11, mc, raw[1], 1)
        conv_silu(cw["v1"], mc, raw[1], cvt[1], 1)
        p.step1("vector", "tensor_copy", ilv.ap[:, 0::2], cvt[0][:, :], reads=[cvt[0]], writes=[ilv])
        p.step1("gpsimd", "tensor_copy", ilv.ap[:, 1::2], cvt[1][:, :], reads=[cvt[1]], writes=[ilv])
        p.dma("sync", VHO[mc * 128:(mc + 1) * 128, :], ilv[:, :], reads=[ilv])

    ht_own = [ht[:, k * TLOC + HALO:k * TLOC + HALO + TOWN] for k in range(16)]
    for wi, (W, dst, bias) in enumerate(((WA1, pr6, dtb), (WB10, sb6[0], None), (WB11, sb6[1], None))):
        p.dma("gpsimd", w0.ap.rearrange("p (k m) -> p k m", k=16)[:, :, 0:6],
              W.rearrange("(k p) m -> p k m", p=128), writes=[w0])
        for (o0, n) in NT2:
            mm_group(p, ps_s.ap[0:6, 0:n], [w0[:, k * 128:k * 128 + 6] for k in range(16)],
                     [c[:, o0:o0 + n] for c in ht_own], reads=[w0, ht], writes=[ps_s])
            if wi == 0:
                p.step1("vector", "tensor_scalar_add", dst.ap[0:6, o0:o0 + n], ps_s[0:6, 0:n],
                        dtb[0:6, :], reads=[ps_s, dtb], writes=[dst])
            else:
                p.step1("scalar", "activation", dst.ap[0:6, o0:o0 + n], ps_s[0:6, 0:n], AF.Sigmoid,
                        reads=[ps_s], writes=[dst])
    p.step1("scalar", "activation", sb6[2].ap[0:6, :], pr6[0:6, :], AF.Abs, reads=[pr6], writes=[sb6[2]])
    p.step1("scalar", "activation", sb6[2].ap[0:6, :], sb6[2][0:6, :], AF.Exp, scale=-1.0,
            reads=[sb6[2]], writes=[sb6[2]])
    p.step1("scalar", "activation", sb6[2].ap[0:6, :], sb6[2][0:6, :], AF.Ln, bias=1.0,
            reads=[sb6[2]], writes=[sb6[2]])
    p.step1("vector", "tensor_scalar_max", pr6.ap[0:6, :], pr6[0:6, :], 0.0, reads=[pr6], writes=[pr6])
    p.step1("vector", "tensor_add", pr6.ap[0:6, :], pr6[0:6, :], sb6[2][0:6, :],
            reads=[pr6, sb6[2]], writes=[pr6])
    p.step1("vector", "tensor_scalar_mul", pr6.ap[0:6, :], pr6[0:6, :], negea[0:6, :],
            reads=[pr6, negea], writes=[pr6])
    p.step1("vector", "memset", gb.ap[0:6, :], 0.0, reads=[], writes=[gb])
    p.step1("vector", "tensor_copy", gb.ap[0:6, 0::2], pr6[0:6, :], reads=[pr6], writes=[gb])
    p.dma("sync", GBO[0:6, :], gb[0:6, :], reads=[gb])
    p.step1("vector", "tensor_copy", gb.ap[0:6, 0::2], sb6[0][0:6, :], reads=[sb6[0], gb], writes=[gb])
    p.step1("vector", "tensor_copy", gb.ap[0:6, 1::2], sb6[1][0:6, :], reads=[sb6[1]], writes=[gb])
    p.dma("sync", GBO[6:12, :], gb[0:6, :], reads=[gb])
    return p.finalize([(s, p.cnt[s]) for s in sorted(p.store_sems)])


def stage_d1(g, X2LN_pad, core):
    s = slice(1024 * core, 1024 * core + TLOC)
    return {
        "ht": _c(X2LN_pad[:, s]),
        "wq1": g["l1_Wq"], "wk10": g["l1_Wk0"], "wk11": g["l1_Wk1"],
        "wv10": g["l1_Wv0"], "wv11": g["l1_Wv1"],
        "wa1": g["l1_Wa"], "wb10": g["l1_Wb0"], "wb11": g["l1_Wb1"],
        "cqw": g["l1_cqw"], "ckw0": g["l1_ckw0"], "ckw1": g["l1_ckw1"],
        "cvw0": g["l1_cvw0"], "cvw1": g["l1_cvw1"],
        "alog": g["l1_alog"], "dtb": g["l1_dtb"],
        "ones_col_bf": g["ones_col_bf"], "ones_row_f": g["ones_row_f"],
    }



def build_d2(nseg=16):
    p = P("d2")
    QH2 = p.din("qh2", [256, 16384], BF16)
    KH2 = p.din("kh2", [256, 16384], BF16)
    VH2 = p.din("vh2", [512, 16384], BF16)
    GR = p.din("gr", [1, 16384], F32)
    BR = p.din("br", [1, 16384], F32)
    IDB = p.din("id_bf", [128, 128], BF16)
    IDF = p.din("id_f", [128, 128], F32)
    ID64B = p.din("id128b", [128, 128], BF16)
    UTI = p.din("utri_inc", [128, 128], F32)
    UTS = p.din("utri_strict", [128, 128], F32)
    LTS = p.din("ltri_strict", [128, 128], F32)
    ONESRF = p.din("ones_row_f", [1, 128], F32)
    OO2 = p.dout("oo2", [16384, 512], BF16)
    SCR_D = p.nc.dram_tensor("rowscr", [16, 128], F32, kind="Internal").ap()

    idb = p.sbuf("c_idb", [128, 128], BF16)
    idf = p.sbuf("c_idf", [128, 128], F32)
    id128 = p.sbuf("c_id128", [128, 128], BF16)
    uti = p.sbuf("c_uti", [128, 128], F32)
    uts = p.sbuf("c_uts", [128, 128], F32)
    lts = p.sbuf("c_lts", [128, 128], F32)
    onesrf = p.sbuf("c_onesrf", [1, 128], F32)
    ones8 = p.sbuf("ones8", [8, 128], F32)
    kseg = p.sbuf("kseg", [128, 2 * 1024], BF16)
    qseg = p.sbuf("qseg", [128, 2 * 1024], BF16)
    vseg = p.sbuf("vseg", [128, 4 * 1024], BF16)
    gseg = p.sbuf("gseg", [8, 128], F32)
    bseg = p.sbuf("bseg", [8, 128], F32)
    cumg = p.sbuf("cumg", [8, 128], F32)
    cumgt = p.sbuf("cumgt", [128, 8], F32)
    betat = p.sbuf("betat", [128, 8], F32)
    growc = p.sbuf("growc", [1, 1024], F32)
    brwc = p.sbuf("brwc", [1, 1024], F32)
    CB = []
    for par in range(2):
        CB.append({
            "lam": p.sbuf(f"lam{par}", [128, 1], F32),
            "lamend": p.sbuf(f"lamend{par}", [128, 1], F32),
            "lesc": p.sbuf(f"lesc{par}", [1, 1], F32),
            "e1": p.sbuf(f"e1_{par}", [128, 128], F32),
            "e2": p.sbuf(f"e2_{par}", [128, 128], F32),
            "s64a": p.sbuf(f"s64a{par}", [128, 128], F32),
            "s64b": p.sbuf(f"s64b{par}", [128, 128], F32),
            "wpb": p.sbuf(f"wpb{par}", [128, 128], BF16),
            "abf": [p.sbuf(f"abf{par}_{i}", [128, 128], BF16) for i in range(2)],
            "bbf": [p.sbuf(f"bbf{par}_{i}", [128, 128], BF16) for i in range(2)],
            "ptb": [p.sbuf(f"ptb{par}_{i}", [128, 128], BF16) for i in range(2)],
            "vtb": p.sbuf(f"vtb{par}", [128, 512], F32),
            "rbf": p.sbuf(f"rbf{par}", [128, 512], BF16),
            "ubf": p.sbuf(f"ubf{par}", [128, 512], BF16),
            "obf": p.sbuf(f"obf{par}", [128, 512], BF16),
            "ktb": p.sbuf(f"ktb{par}", [128, 256], BF16),
            "scro": p.sbuf(f"scro{par}", [128, 512], F32),
        })
    sf32 = [p.sbuf(f"sf32_{i}", [128, 512], F32) for i in range(2)]
    sbfb = [p.sbuf(f"sbf_{i}", [128, 512], BF16) for i in range(2)]

    ps_p1 = p.psum("ps_p1", [128, 512], F32)
    ps_p2 = p.psum("ps_p2", [128, 512], F32)
    ps_r = p.psum("ps_r", [128, 512], F32)
    ps_u = p.psum("ps_u", [128, 512], F32)
    ps_o = p.psum("ps_o", [128, 512], F32)
    ps_t = p.psum("ps_t", [128, 768], BF16)
    ps_s1 = p.psum("ps_s1", [128, 512], F32)
    ps_s2 = p.psum("ps_s2", [128, 512], F32)

    for b, srcdr in ((idb, IDB), (idf, IDF), (id128, ID64B), (uti, UTI), (uts, UTS),
                     (lts, LTS), (onesrf, ONESRF)):
        p.dma("sync", b.ap[:, :], srcdr, writes=[b])
    p.step1("vector", "memset", ones8.ap[:, :], 1.0, reads=[], writes=[ones8])
    for i in range(2):
        p.step1("vector", "memset", sf32[i].ap[:, :], 0.0, reads=[], writes=[sf32[i]])
        p.step1("vector", "memset", sbfb[i].ap[:, :], 0.0, reads=[], writes=[sbfb[i]])

    for seg in range(nseg):
        ssl = slice(seg * 1024, (seg + 1) * 1024)
        p.dma("sync", kseg.ap.rearrange("p (c t) -> p c t", c=2),
              KH2[:, ssl].rearrange("(c p) t -> p c t", p=128), writes=[kseg])
        p.dma("sync", qseg.ap.rearrange("p (c t) -> p c t", c=2),
              QH2[:, ssl].rearrange("(c p) t -> p c t", p=128), writes=[qseg])
        p.dma("sync", vseg.ap.rearrange("p (c t) -> p c t", c=4),
              VH2[:, ssl].rearrange("(c p) t -> p c t", p=128), writes=[vseg])
        p.dma("sync", gseg.ap[0:8, :], GR[0:1, ssl].rearrange("a (c t) -> (a c) t", c=8), writes=[gseg])
        p.dma("sync", bseg.ap[0:8, :], BR[0:1, ssl].rearrange("a (c t) -> (a c) t", c=8), writes=[bseg])
        p.step1("vector", "tensor_tensor_scan", cumg.ap[0:8, :], ones8[0:8, :], gseg[0:8, :],
                0.0, ALU.mult, ALU.add, reads=[ones8, gseg], writes=[cumg])
        p.step1("tensor", "transpose", ps_p1[:, 0:8], cumg[0:8, :], idf[0:8, 0:8],
                reads=[cumg, idf], writes=[ps_p1])
        p.step1("vector", "tensor_copy", cumgt.ap[:, 0:8], ps_p1[:, 0:8], reads=[ps_p1], writes=[cumgt])
        p.step1("tensor", "transpose", ps_p1[:, 8:16], bseg[0:8, :], idf[0:8, 0:8],
                reads=[bseg, idf], writes=[ps_p1])
        p.step1("vector", "tensor_copy", betat.ap[:, 0:8], ps_p1[:, 8:16], reads=[ps_p1], writes=[betat])
        p.dma("sync", SCR_D[0:8, :], cumg[0:8, :], sem="d_rows", reads=[cumg])
        p.dma("sync", SCR_D[8:16, :], bseg[0:8, :], sem="d_rows", reads=[bseg])
        p.dma("sync", growc.ap[0:1, :], SCR_D[0:8, :].rearrange("c t -> (c t)"),
              writes=[growc], extra=[("d_rows", 0)])
        p.dma("sync", brwc.ap[0:1, :], SCR_D[8:16, :].rearrange("c t -> (c t)"),
              writes=[brwc], extra=[("d_rows", 0)])

        for c8 in range(8):
            cb_ = CB[(seg * 8 + c8) % 2]
            lam = cb_["lam"]; lamend = cb_["lamend"]; lesc = cb_["lesc"]
            e1 = cb_["e1"]; e2 = cb_["e2"]; s64a = cb_["s64a"]; s64b = cb_["s64b"]
            wpb = cb_["wpb"]; abf = cb_["abf"]; bbf = cb_["bbf"]; ptb = cb_["ptb"]
            vtb = cb_["vtb"]; rbf = cb_["rbf"]; ubf = cb_["ubf"]; obf = cb_["obf"]
            ktb = cb_["ktb"]; scro = cb_["scro"]
            csl = slice(c8 * 128, (c8 + 1) * 128)
            kc = [kseg[:, 1024 * i + c8 * 128:1024 * i + (c8 + 1) * 128] for i in range(2)]
            qc = [qseg[:, 1024 * i + c8 * 128:1024 * i + (c8 + 1) * 128] for i in range(2)]
            gcol = cumgt[:, c8:c8 + 1]
            bcol = betat[:, c8:c8 + 1]
            for i in range(2):
                p.step1("tensor", "matmul", ps_p1[:, 0:128], kc[i], kc[i],
                        start=(i == 0), stop=(i == 1), reads=[kseg], writes=[ps_p1])
            for i in range(2):
                p.step1("tensor", "matmul", ps_p1[:, 128:256], kc[i], qc[i],
                        start=(i == 0), stop=(i == 1), reads=[kseg, qseg], writes=[ps_p1])
            p.step1("tensor", "matmul", ps_p1[:, 256:384], onesrf[0:1, :], growc[0:1, csl],
                    start=True, stop=True, reads=[onesrf, growc], writes=[ps_p1])
            p.step1("tensor", "matmul", ps_p1[:, 384:512], onesrf[0:1, :], brwc[0:1, csl],
                    start=True, stop=True, reads=[onesrf, brwc], writes=[ps_p1])
            p.step1("vector", "tensor_scalar", e1.ap[:, :], ps_p1[:, 256:384], gcol, None,
                    ALU.subtract, reads=[ps_p1, cumgt], writes=[e1])
            p.step1("vector", "tensor_scalar_min", e1.ap[:, :], e1[:, :], 0.0, reads=[e1], writes=[e1])
            p.step1("scalar", "activation", e1.ap[:, :], e1[:, :], AF.Exp, reads=[e1], writes=[e1])
            p.step1("vector", "tensor_scalar", e2.ap[:, :], ps_p1[:, 256:384], -1.0, gcol,
                    ALU.mult, ALU.add, reads=[ps_p1, cumgt], writes=[e2])
            p.step1("vector", "tensor_scalar_min", e2.ap[:, :], e2[:, :], 0.0, reads=[e2], writes=[e2])
            p.step1("scalar", "activation", e2.ap[:, :], e2[:, :], AF.Exp, reads=[e2], writes=[e2])
            p.step1("vector", "tensor_mul", s64a.ap[:, :], e1[:, :], uti[:, :],
                    reads=[e1, uti], writes=[s64a])
            p.step1("vector", "tensor_mul", wpb.ap[:, :], s64a[:, :], ps_p1[:, 128:256],
                    reads=[s64a, ps_p1], writes=[wpb])
            p.step1("vector", "tensor_mul", s64a.ap[:, :], e1[:, :], uts[:, :],
                    reads=[e1, uts], writes=[s64a])
            p.step1("vector", "tensor_mul", s64a.ap[:, :], s64a[:, :], ps_p1[:, 384:512],
                    reads=[s64a, ps_p1], writes=[s64a])
            p.step1("vector", "tensor_mul", bbf[0].ap[:, :], s64a[:, :], ps_p1[:, 0:128],
                    reads=[s64a, ps_p1], writes=[bbf[0]])
            p.step1("vector", "tensor_mul", s64b.ap[:, :], e2[:, :], lts[:, :],
                    reads=[e2, lts], writes=[s64b])
            p.step1("vector", "tensor_scalar_mul", s64b.ap[:, :], s64b[:, :], bcol,
                    reads=[s64b, betat], writes=[s64b])
            p.step1("vector", "tensor_mul", abf[0].ap[:, :], s64b[:, :], ps_p1[:, 0:128],
                    reads=[s64b, ps_p1], writes=[abf[0]])
            p.step1("vector", "tensor_sub", ptb[0].ap[:, :], id128[:, :], bbf[0][:, :],
                    reads=[id128, bbf[0]], writes=[ptb[0]])
            cura, curb, curpt = 0, 0, 0
            for st_i in range(6):
                na, nb, npt = 1 - cura, 1 - curb, 1 - curpt
                p.step1("tensor", "matmul", ps_p2[:, 0:128], bbf[curb][:, :], abf[cura][:, :],
                        start=True, stop=True, reads=[bbf[curb], abf[cura]], writes=[ps_p2])
                p.step1("scalar", "activation", abf[na].ap[:, :], ps_p2[:, 0:128], AF.Copy,
                        reads=[ps_p2], writes=[abf[na]])
                if st_i < 5:
                    p.step1("tensor", "matmul", ps_p2[:, 128:256], abf[cura][:, :], bbf[curb][:, :],
                            start=True, stop=True, reads=[abf[cura], bbf[curb]], writes=[ps_p2])
                    p.step1("scalar", "activation", bbf[nb].ap[:, :], ps_p2[:, 128:256], AF.Copy,
                            reads=[ps_p2], writes=[bbf[nb]])
                p.step1("tensor", "matmul", ps_p2[:, 256:384], abf[na][:, :], ptb[curpt][:, :],
                        start=True, stop=True, reads=[abf[na], ptb[curpt]], writes=[ps_p2])
                p.step1("vector", "tensor_add", ptb[npt].ap[:, :], ps_p2[:, 256:384], ptb[curpt][:, :],
                        reads=[ps_p2, ptb[curpt]], writes=[ptb[npt]])
                cura, curb, curpt = na, nb, npt
            PT = ptb[curpt]
            p.step1("scalar", "activation", lam.ap[:, 0:1], gcol, AF.Exp, reads=[cumgt], writes=[lam])
            for i in range(2):
                p.step1("tensor", "matmul", ps_r[:, 0:512], kc[i], sbfb[i][:, :],
                        start=(i == 0), stop=(i == 1), reads=[kseg, sbfb[i]], writes=[ps_r])
            for i in range(4):
                p.step1("tensor", "transpose", ps_t[:, i * 128:(i + 1) * 128],
                        vseg[:, 1024 * i + c8 * 128:1024 * i + (c8 + 1) * 128], idb[:, :],
                        reads=[vseg, idb], writes=[ps_t])
            p.step1("vector", "tensor_copy", vtb.ap[:, :], ps_t[:, 0:512], reads=[ps_t], writes=[vtb])
            p.step1("vector", "tensor_scalar", scro.ap[:, :], ps_r[:, 0:512], lam[:, 0:1], None,
                    ALU.mult, reads=[ps_r, lam], writes=[scro])
            p.step1("vector", "tensor_sub", scro.ap[:, :], vtb[:, :], scro[:, :],
                    reads=[vtb, scro], writes=[scro])
            p.step1("vector", "tensor_scalar", rbf.ap[:, :], scro[:, :], bcol, None,
                    ALU.mult, reads=[scro, betat], writes=[rbf])
            p.step1("tensor", "matmul", ps_u[:, 0:512], PT[:, :], rbf[:, :],
                    start=True, stop=True, reads=[PT, rbf], writes=[ps_u])
            p.step1("scalar", "activation", ubf.ap[:, :], ps_u[:, 0:512], AF.Copy,
                    reads=[ps_u], writes=[ubf])
            p.step1("tensor", "matmul", ps_o[:, 0:512], wpb[:, :], ubf[:, :],
                    start=True, stop=True, reads=[wpb, ubf], writes=[ps_o])
            for i in range(2):
                p.step1("tensor", "matmul", ps_r[:, 0:512], qc[i], sbfb[i][:, :],
                        start=(i == 0), stop=(i == 1), reads=[qseg, sbfb[i]], writes=[ps_r])
            p.step1("vector", "tensor_scalar", scro.ap[:, :], ps_r[:, 0:512], lam[:, 0:1], None,
                    ALU.mult, reads=[ps_r, lam], writes=[scro])
            p.step1("vector", "tensor_add", obf.ap[:, :], scro[:, :], ps_o[:, 0:512],
                    reads=[scro, ps_o], writes=[obf])
            cg = 8 * seg + c8
            p.dma("sync", OO2[128 * cg:128 * (cg + 1), :], obf[:, :], reads=[obf])
            for i in range(2):
                p.step1("tensor", "transpose", ps_t[:, 512 + i * 128:512 + (i + 1) * 128],
                        kc[i], idb[:, :], reads=[kseg, idb], writes=[ps_t])
            p.step1("vector", "tensor_copy", ktb.ap[:, :], ps_t[:, 512:768], reads=[ps_t], writes=[ktb])
            p.step1("vector", "tensor_scalar", ktb.ap[:, :], ktb[:, :], e1[:, 127:128], None,
                    ALU.mult, reads=[ktb, e1], writes=[ktb])
            p.step1("scalar", "activation", lesc.ap[0:1, 0:1], growc[0:1, 128 * c8 + 127:128 * c8 + 128],
                    AF.Exp, reads=[growc], writes=[lesc])
            p.step1("tensor", "matmul", ps_p2[:, 384:385], onesrf[0:1, :], lesc[0:1, 0:1],
                    start=True, stop=True, reads=[onesrf, lesc], writes=[ps_p2])
            p.step1("vector", "tensor_copy", lamend.ap[:, 0:1], ps_p2[:, 384:385],
                    reads=[ps_p2], writes=[lamend])
            for i in range(2):
                pss = (ps_s1, ps_s2)[i]
                p.step1("tensor", "matmul", pss[:, 0:512], ktb[:, i * 128:(i + 1) * 128], ubf[:, :],
                        start=True, stop=True, reads=[ktb, ubf], writes=[pss])
                p.step1("vector", "scalar_tensor_tensor", sf32[i].ap[:, :], sf32[i][:, :],
                        lamend[:, 0:1], pss[:, 0:512], ALU.mult, ALU.add,
                        reads=[sf32[i], lamend, pss], writes=[sf32[i]])
                p.step1("scalar", "activation", sbfb[i].ap[:, :], sf32[i][:, :], AF.Copy,
                        reads=[sf32[i]], writes=[sbfb[i]])
    return p.finalize([(s, p.cnt[s]) for s in sorted(p.store_sems)])


def stage_d2(g, QH, KH, VH, GB, core):
    h = core if core < 6 else 0
    return {
        "qh2": _c(QH[256 * h:256 * (h + 1), :]),
        "kh2": _c(KH[256 * h:256 * (h + 1), :]),
        "vh2": _c(VH[512 * h:512 * (h + 1), :]),
        "gr": _c(GB[h:h + 1, :]),
        "br": _c(GB[6 + h:7 + h, :]),
        "id_bf": g["id_bf"], "id_f": g["id_f"], "id128b": g["id_bf"],
        "utri_inc": g["utri_inc"], "utri_strict": g["utri_strict"], "ltri_strict": g["ltri_strict"],
        "ones_row_f": g["ones_row_f"],
    }



def build_d3e():
    p = P("d3e")
    OH = p.din("oh", [TOWN, 3072], BF16)
    X2T = p.din("x2t", [2048, TOWN], F32)
    X2LNT = p.din("x2lnt", [2048, TOWN], BF16)
    WG1 = p.din("wg1", [2048, 3072], F32)
    WO1 = p.din("wo1", [3072, 2048], F32)
    ONW = p.din("onw", [1, 512], F32)
    GATEW = p.din("gatew", [2048, 8], F32)
    W1L = p.din("w1l", [8, 2048, 1024], F32)
    W3L = p.din("w3l", [8, 2048, 1024], F32)
    W2L = p.din("w2l", [8, 1024, 2048], F32)
    WS1L = p.din("ws1l", [2048, 1024], F32)
    WS3L = p.din("ws3l", [2048, 1024], F32)
    WS2L = p.din("ws2l", [1024, 2048], F32)
    ONESB = p.din("ones_col_bf", [128, 1], BF16)
    ONESRF = p.din("ones_row_f", [1, 128], F32)
    IDB = p.din("id_bf", [128, 128], BF16)
    X4O = p.dout("x4o", [2048, TOWN], F32)
    CROW_D = p.nc.dram_tensor("crow_d", [8, TOWN], F32, kind="Internal").ap()

    NT2 = [(0, 512), (512, 512)]
    lc = mk_ln_ctx(p, TOWN)
    onesb = p.sbuf("c_onesb", [128, 1], BF16)
    onesrf = p.sbuf("c_onesrf", [1, 128], F32)
    lc["onesb"] = onesb; lc["onesrf"] = onesrf
    idb = p.sbuf("c_idb", [128, 128], BF16)
    onwrep = p.sbuf("onwrep", [128, 512], F32)
    onwrow = p.sbuf("onwrow", [1, 512], F32)

    yacc = p.sbuf("yacc", [128, 16 * TOWN], F32)
    srow = lc["srow"]; srow2 = lc["srow2"]; rrep = lc["rrep"]; scr = lc["scr"]; scr2 = lc["scr2"]
    ps_r1 = lc["ps_r1"]; ps_m = lc["ps_m"]; ps_n = lc["ps_n"]
    ps_a = p.psum("ps_a", [128, 512], F32)
    ps_b = p.psum("ps_b", [128, 512], F32)
    ps_t = p.psum("ps_t", [128, 1024], BF16)

    X3D = p.nc.dram_tensor("x3spill", [2048, TOWN], F32, kind="Internal").ap()
    for b, srcdr in ((onesb, ONESB), (onesrf, ONESRF), (idb, IDB), (onwrow, ONW)):
        p.dma("sync", b.ap[:, :], srcdr, writes=[b])
    x3ln = p.sbuf("x3ln", [128, 16 * TOWN], BF16, side="right")

    bcast_row(p, ps_m.ap[:, 0:512], onesrf[:, :], onwrow[0:1, :], reads=[onwrow, onesrf], writes=[ps_m])
    p.step1("vector", "tensor_copy", onwrep.ap[:, :], ps_m[:, 0:512], reads=[ps_m], writes=[onwrep])

    x3ln_r = x3ln
    x2lnf = p.sbuf("x2lnf", [128, 16 * TOWN], BF16, side="right")
    p.dma("sync", x2lnf.ap.rearrange("p (k t) -> p k t", k=16),
          X2LNT.rearrange("(k p) t -> p k t", p=128), writes=[x2lnf])
    ggall = p.sbuf("ggall", [128, 4 * 3072], BF16, side="right")
    wgb = p.sbuf("wgb", [128, 16 * 512], BF16, side="right")
    otile = p.sbuf("otile", [128, 3072], BF16, side="right")
    oft = p.sbuf("oft", [128, 24 * 128], BF16, side="right")
    wol = [p.sbuf("wol0", [128, 24 * 128], BF16, side="right")] * 2
    rsq = p.sbuf("rsq", [128, 1], F32, side="right")
    pool4 = [ps_a, ps_b, ps_m, ps_n]
    pi = 0
    for half in range(2):
        tts = range(4 * half, 4 * half + 4)
        for ns in range(6):
            p.dma("gpsimd", wgb.ap.rearrange("p (k m) -> p k m", k=16),
                  WG1[:, ns * 512:(ns + 1) * 512].rearrange("(k p) m -> p k m", p=128), writes=[wgb])
            for tt in tts:
                ps = pool4[pi % 4]; pi += 1
                mm_group(p, ps.ap[:, 0:512],
                         [x2lnf[:, k * TOWN + tt * 128:k * TOWN + (tt + 1) * 128] for k in range(16)],
                         [wgb[:, k * 512:(k + 1) * 512] for k in range(16)],
                         reads=[x2lnf, wgb], writes=[ps])
                p.step1("scalar", "activation",
                        ggall.ap[:, (tt - 4 * half) * 3072 + ns * 512:(tt - 4 * half) * 3072 + (ns + 1) * 512],
                        ps[:, 0:512], AF.Silu, reads=[ps], writes=[ggall])
        for tt in tts:
            tsl = slice(tt * 128, (tt + 1) * 128)
            g0 = (tt - 4 * half) * 3072
            p.dma("sync", otile.ap[:, :], OH[tsl, :], writes=[otile])
            for h in range(6):
                hsl = slice(h * 512, (h + 1) * 512)
                p.step1("scalar", "activation", lc["sqs"][0].ap[:, 0:512], otile[:, hsl], AF.Square,
                        reads=[otile], writes=[lc["sqs"][0]])
                p.step1("vector", "tensor_reduce", rsq.ap[:, 0:1], lc["sqs"][0][:, 0:512],
                        mybir.AxisListType.X, ALU.add, reads=[lc["sqs"][0]], writes=[rsq])
                p.step1("vector", "tensor_scalar", rsq.ap[:, 0:1], rsq[:, 0:1], 1.0 / 512, 1e-5,
                        ALU.mult, ALU.add, reads=[rsq], writes=[rsq])
                p.step1("scalar", "activation", rsq.ap[:, 0:1], rsq[:, 0:1], AF.Sqrt,
                        reads=[rsq], writes=[rsq])
                p.step1("vector", "reciprocal", rsq.ap[:, 0:1], rsq[:, 0:1], reads=[rsq], writes=[rsq])
                p.step1("vector", "tensor_scalar", scr.ap[:, 0:512], otile[:, hsl], rsq[:, 0:1], None,
                        ALU.mult, reads=[otile, rsq], writes=[scr])
                p.step1("vector", "tensor_mul", scr.ap[:, 0:512], scr[:, 0:512], onwrep[:, :],
                        reads=[scr, onwrep], writes=[scr])
                p.step1("vector", "tensor_mul", otile.ap[:, hsl], scr[:, 0:512],
                        ggall[:, g0 + h * 512:g0 + (h + 1) * 512],
                        reads=[scr, ggall], writes=[otile])
            for v in range(24):
                p.step1("tensor", "transpose", ps_t[:, (v % 8) * 128:(v % 8 + 1) * 128],
                        otile[:, v * 128:(v + 1) * 128], idb[:, :], reads=[otile, idb], writes=[ps_t])
                p.step1("scalar", "activation", oft.ap[:, v * 128:(v + 1) * 128],
                        ps_t[:, (v % 8) * 128:(v % 8 + 1) * 128], AF.Copy, reads=[ps_t], writes=[oft])
            for mc in range(16):
                wb = wol[mc % 2]
                p.dma("gpsimd", wb.ap.rearrange("p (k m) -> p k m", k=24)[:, 0:24, :],
                      WO1[:, mc * 128:(mc + 1) * 128].rearrange("(k p) m -> p k m", p=128), writes=[wb])
                ps = pool4[pi % 4]; pi += 1
                mm_group(p, ps.ap[:, 0:128], [wb[:, k * 128:(k + 1) * 128] for k in range(24)],
                         [oft[:, k * 128:(k + 1) * 128] for k in range(24)],
                         reads=[wb, oft], writes=[ps])
                p.step1("vector", "tensor_copy", yacc.ap[:, mc * TOWN + tt * 128:mc * TOWN + (tt + 1) * 128],
                        ps[:, 0:128], reads=[ps], writes=[yacc])
    p.free(rsq)
    p.free(wol[0], oft, otile, wgb, ggall, x2lnf)

    x2 = p.sbuf("x2b", [128, 16 * TOWN], F32, side="right")
    p.dma("sync", x2.ap.rearrange("p (k t) -> p k t", k=16),
          X2T.rearrange("(k p) t -> p k t", p=128), writes=[x2])
    emit_ln_c(p, lc, yacc, x3ln, TOWN, NT2)
    for k in range(16):
        for (o0, n) in NT2:
            p.step1("scalar", "activation", scr2.ap[:, 0:n], x3ln[:, k * TOWN + o0:k * TOWN + o0 + n],
                    AF.Copy, reads=[x3ln], writes=[scr2])
            p.step1("vector", "tensor_add", x2.ap[:, k * TOWN + o0:k * TOWN + o0 + n],
                    x2[:, k * TOWN + o0:k * TOWN + o0 + n], scr2[:, 0:n], reads=[x2, scr2], writes=[x2])
        p.dma("sync", X3D[k * 128:(k + 1) * 128, :], x2[:, k * TOWN:(k + 1) * TOWN], sem="sp_x3", reads=[x2])
    emit_ln_c(p, lc, x2, x3ln, TOWN, NT2)
    p.free(x2)

    gw = p.sbuf("gw", [128, 8], BF16, side="right")
    sc8 = p.sbuf("sc8", [128, 8], F32, side="right")
    m1c = p.sbuf("m1c", [128, 1], F32, side="right")
    combt = p.sbuf("combt", [8, TOWN], F32, side="right")
    x3ln_c = p.subdiv(x3ln, 16, TOWN)
    for tt in range(8):
        tsl = slice(tt * 128, (tt + 1) * 128)
        for k in range(16):
            if tt == 0 and k == 0:
                pass
            p.dma("gpsimd", gw.ap[:, :], GATEW[k * 128:(k + 1) * 128, :], writes=[gw])
            p.step1("tensor", "matmul", ps_a[:, 0:8], x3ln_c[k][:, tsl], gw[:, :],
                    start=(k == 0), stop=(k == 15), reads=[x3ln_c[k], gw], writes=[ps_a])
        p.step1("vector", "reduce_max", m1c.ap[:, 0:1], ps_a[:, 0:8], mybir.AxisListType.X,
                reads=[ps_a], writes=[m1c])
        p.step1("vector", "tensor_scalar", sc8.ap[:, :], ps_a[:, 0:8], m1c[:, 0:1], None,
                ALU.subtract, reads=[ps_a, m1c], writes=[sc8])
        p.step1("scalar", "activation", sc8.ap[:, :], sc8[:, :], AF.Exp, reads=[sc8], writes=[sc8])
        p.step1("vector", "tensor_reduce", m1c.ap[:, 0:1], sc8[:, :], mybir.AxisListType.X, ALU.add,
                reads=[sc8], writes=[m1c])
        p.step1("vector", "reciprocal", m1c.ap[:, 0:1], m1c[:, 0:1], reads=[m1c], writes=[m1c])
        p.step1("vector", "tensor_scalar", sc8.ap[:, :], sc8[:, :], m1c[:, 0:1], None, ALU.mult,
                reads=[sc8, m1c], writes=[sc8])
        p.step1("vector", "reduce_max", m1c.ap[:, 0:1], sc8[:, :], mybir.AxisListType.X,
                reads=[sc8], writes=[m1c])
        p.step1("vector", "tensor_scalar", scr.ap[:, 0:8], sc8[:, :], m1c[:, 0:1], None,
                ALU.is_ge, reads=[sc8, m1c], writes=[scr])
        p.step1("vector", "scalar_tensor_tensor", scr.ap[:, 8:16], scr[:, 0:8], -1e30,
                sc8[:, :], ALU.mult, ALU.add, reads=[scr, sc8], writes=[scr])
        p.step1("vector", "reduce_max", m1c.ap[:, 0:1], scr[:, 8:16], mybir.AxisListType.X,
                reads=[scr], writes=[m1c])
        p.step1("vector", "tensor_scalar", scr.ap[:, 0:8], sc8[:, :], m1c[:, 0:1], None,
                ALU.is_ge, reads=[sc8, m1c], writes=[scr])
        p.step1("vector", "tensor_mul", sc8.ap[:, :], sc8[:, :], scr[:, 0:8],
                reads=[sc8, scr], writes=[sc8])
        p.step1("scalar", "activation", lc["sqs"][0].ap[:, 0:8], sc8[:, :], AF.Copy,
                reads=[sc8], writes=[lc["sqs"][0]])
        p.step1("tensor", "transpose", ps_t[0:8, 0:128], lc["sqs"][0][:, 0:8], idb[:, :],
                reads=[lc["sqs"][0], idb], writes=[ps_t])
        p.step1("vector", "tensor_copy", combt.ap[0:8, tsl], ps_t[0:8, 0:128],
                reads=[ps_t], writes=[combt])
    p.dma("sync", CROW_D, combt[0:8, :], sem="d_crow", reads=[combt])
    p.free(combt, m1c, sc8, gw)
    crow = p.sbuf("crow", [1, TOWN], F32, side="right")

    prodb = p.sbuf("prodb", [128, 8 * TOWN], BF16, side="right")
    prod_c = p.subdiv(prodb, 8, TOWN)
    w1b = [p.sbuf(f"w1b{i}", [128, 16 * 128], BF16, side="right") for i in range(2)]
    w3b = [p.sbuf(f"w3b{i}", [128, 16 * 128], BF16, side="right") for i in range(2)]
    w2b = [p.sbuf(f"w2b{i}", [128, 8 * 128], BF16, side="right") for i in range(2)]
    gb1 = p.sbuf("gb1", [128, 512], BF16, side="right")
    crep = p.sbuf("crep", [128, TOWN], F32, side="right")
    p.step1("vector", "memset", yacc.ap[:, :], 0.0, reads=[yacc], writes=[yacc])
    for e in range(9):
        if e < 8:
            p.dma("sync", crow.ap[0:1, :], CROW_D[e:e + 1, :], writes=[crow], extra=[("d_crow", 0)])
            for (o0, n) in NT2:
                bcast_row(p, ps_m.ap[:, 0:n], onesrf[:, :], crow[0:1, o0:o0 + n],
                          reads=[crow, onesrf], writes=[ps_m])
                p.step1("vector", "tensor_copy", crep.ap[:, o0:o0 + n], ps_m[:, 0:n],
                        reads=[ps_m], writes=[crep])
        else:
            p.step1("vector", "memset", crep.ap[:, :], 1.0, reads=[], writes=[crep])
        for mc in range(8):
            wa = w1b[mc % 2]; wc = w3b[mc % 2]
            srcW1 = W1L[e] if e < 8 else WS1L
            srcW3 = W3L[e] if e < 8 else WS3L
            p.dma("gpsimd", wa.ap.rearrange("p (k m) -> p k m", k=16),
                  srcW1[:, mc * 128:(mc + 1) * 128].rearrange("(k p) m -> p k m", p=128), writes=[wa])
            p.dma("gpsimd", wc.ap.rearrange("p (k m) -> p k m", k=16),
                  srcW3[:, mc * 128:(mc + 1) * 128].rearrange("(k p) m -> p k m", p=128), writes=[wc])
            for (o0, n) in NT2:
                psx = pool4[pi % 4]; pi += 1
                mm_group(p, psx.ap[:, 0:n], [wa[:, k * 128:(k + 1) * 128] for k in range(16)],
                         [c[:, o0:o0 + n] for c in x3ln_c], reads=[wa] + x3ln_c, writes=[psx])
                p.step1("scalar", "activation", gb1.ap[:, 0:n], psx[:, 0:n], AF.Silu,
                        reads=[psx], writes=[gb1])
                psy = pool4[pi % 4]; pi += 1
                mm_group(p, psy.ap[:, 0:n], [wc[:, k * 128:(k + 1) * 128] for k in range(16)],
                         [c[:, o0:o0 + n] for c in x3ln_c], reads=[wc] + x3ln_c, writes=[psy])
                p.step1("vector", "tensor_mul", scr.ap[:, 0:n], gb1[:, 0:n], psy[:, 0:n],
                        reads=[gb1, psy], writes=[scr])
                p.step1("vector", "tensor_mul", prod_c[mc].ap[:, o0:o0 + n], scr[:, 0:n],
                        crep[:, o0:o0 + n], reads=[scr, crep], writes=[prod_c[mc]])
        for mc in range(16):
            wd = w2b[mc % 2]
            srcW2 = W2L[e] if e < 8 else WS2L
            p.dma("gpsimd", wd.ap.rearrange("p (k m) -> p k m", k=8),
                  srcW2[:, mc * 128:(mc + 1) * 128].rearrange("(k p) m -> p k m", p=128), writes=[wd])
            for (o0, n) in NT2:
                psx = pool4[pi % 4]; pi += 1
                mm_group(p, psx.ap[:, 0:n], [wd[:, k * 128:(k + 1) * 128] for k in range(8)],
                         [c[:, o0:o0 + n] for c in prod_c], reads=[wd] + prod_c, writes=[psx])
                p.step1("vector", "tensor_add", yacc.ap[:, mc * TOWN + o0:mc * TOWN + o0 + n],
                        yacc[:, mc * TOWN + o0:mc * TOWN + o0 + n], psx[:, 0:n],
                        reads=[yacc, psx], writes=[yacc])
    p.free(crep, gb1, w2b[1], w2b[0], w3b[1], w3b[0], w1b[1], w1b[0], prodb)

    x3f = p.sbuf("x3f", [128, 16 * TOWN], F32, side="right")
    rscr2 = p.sbuf("rscr2", [128, TOWN], F32, side="right")
    p.dma("sync", x3f.ap.rearrange("p (k t) -> p k t", k=16),
          X3D.rearrange("(k p) t -> p k t", p=128), writes=[x3f],
          extra=[("sp_x3", 0)])
    emit_ln_c(p, lc, yacc, x3ln, TOWN, NT2)
    for k in range(16):
        ks = slice(k * TOWN, (k + 1) * TOWN)
        p.step1("scalar", "activation", rscr2.ap[:, :], x3ln[:, ks], AF.Copy, reads=[x3ln], writes=[rscr2])
        p.step1("vector", "tensor_add", x3f.ap[:, ks], x3f[:, ks], rscr2[:, :], reads=[x3f, rscr2], writes=[x3f])
        p.dma("sync", X4O[k * 128:(k + 1) * 128, :], x3f[:, ks], reads=[x3f])
    return p.finalize([(s, p.cnt[s]) for s in sorted(p.store_sems)])


def stage_d3e(g, O_full, X2, X2LN, core):
    s = slice(1024 * core, 1024 * (core + 1))
    return {
        "oh": _c(O_full[s, :]), "x2t": _c(X2[:, s]), "x2lnt": _c(X2LN[:, s]),
        "wg1": g["l1_Wg"], "wo1": g["l1_Wo"], "onw": g["l1_onw"],
        "gatew": g["f1_gate"], "w1l": g["f1_w1l"], "w3l": g["f1_w3l"], "w2l": g["f1_w2l"],
        "ws1l": g["f1_ws1l"], "ws3l": g["f1_ws3l"], "ws2l": g["f1_ws2l"],
        "ones_col_bf": g["ones_col_bf"], "ones_row_f": g["ones_row_f"], "id_bf": g["id_bf"],
    }


def stage_a2(B_full, G_full, g, core):
    return {
        "bo2": _c(B_full[256 * core:256 * (core + 1), :]),
        "go2": _c(G_full[2 * core:2 * (core + 1), :]),
        "ones_row_f": g["ones_row_f"],
    }


def stage_a1(g, core):
    s = slice(1024 * core, 1024 * core + TLOC)
    return {
        "xt": _c(g["xT_pad"][:, s]),
        "wq": g["l0_Wq"], "wk": g["l0_Wk"], "wv": g["l0_Wv"], "wi": g["l0_ig"],
        "wog": g["l0_og"], "wgam": g["l0_gam"],
        "convw": g["l0_convw"], "convb": g["l0_convb"], "igb": g["l0_igb"],
        "ogb": g["l0_ogb"], "gamb": g["l0_gamb"], "vng": g["l0_vng"], "vnb": g["l0_vnb"],
        "ones_col_bf": g["ones_col_bf"], "ones_row_f": g["ones_row_f"],
    }


_BUILT = {}


def _get(name, builder):
    if name not in _BUILT:
        _BUILT[name] = builder()
    return _BUILT[name]


def kernel(**inputs):
    g = stage_common(inputs)
    cores = list(range(NCORES))

    nc = _get("a1", build_a1)
    res = run_bass_kernel_spmd(nc, [stage_a1(g, c) for c in cores], core_ids=cores)
    Q_full = np.concatenate([res.results[c]["qo"] for c in cores], axis=1)
    B_full = np.concatenate([res.results[c]["bo"] for c in cores], axis=1)
    O_full = np.concatenate([res.results[c]["oo"] for c in cores], axis=1)
    G_full = np.concatenate([res.results[c]["go"] for c in cores], axis=1)

    nc = _get("a2", build_a2)
    res = run_bass_kernel_spmd(nc, [stage_a2(B_full, G_full, g, c) for c in cores], core_ids=cores)
    M_full = np.concatenate([res.results[c]["mo"] for c in cores], axis=0)

    nc = _get("a3c", build_a3c)
    res = run_bass_kernel_spmd(nc, [stage_a3c(g, M_full, Q_full, O_full, c) for c in cores],
                               core_ids=cores)
    X2 = np.concatenate([res.results[c]["x2o"] for c in cores], axis=1)
    X2LN = np.concatenate([res.results[c]["x2lno"] for c in cores], axis=1)

    X2LN_pad = np.pad(X2LN, ((0, 0), (HALO, 0)))
    nc = _get("d1", build_d1)
    res = run_bass_kernel_spmd(nc, [stage_d1(g, X2LN_pad, c) for c in cores], core_ids=cores)
    QH = np.concatenate([res.results[c]["qho"] for c in cores], axis=1)
    KH = np.concatenate([res.results[c]["kho"] for c in cores], axis=1)
    VH = np.concatenate([res.results[c]["vho"] for c in cores], axis=1)
    GB = np.concatenate([res.results[c]["gbo"] for c in cores], axis=1)

    nc = _get("d2", lambda: build_d2(16))
    res = run_bass_kernel_spmd(nc, [stage_d2(g, QH, KH, VH, GB, c) for c in cores], core_ids=cores)
    O_tm = np.concatenate([np.asarray(res.results[h]["oo2"])[1::2] for h in range(6)], axis=1)

    nc = _get("d3e", build_d3e)
    res = run_bass_kernel_spmd(nc, [stage_d3e(g, O_tm, X2, X2LN, c) for c in cores], core_ids=cores)
    X4 = np.concatenate([res.results[c]["x4o"] for c in cores], axis=1)
    return np.ascontiguousarray(X4.T).reshape(1, 8192, 2048).astype(np.float32)


# version 06
# speedup vs baseline: 1.0160x; 1.0160x over previous
import numpy as np
import ml_dtypes
from contextlib import ExitStack

import concourse.bass as bass
import concourse.mybir as mybir
from concourse.bass_utils import run_bass_kernel_spmd

F32 = mybir.dt.float32
BF16 = mybir.dt.bfloat16
AF = mybir.ActivationFunctionType
ALU = mybir.AluOpType
BF = ml_dtypes.bfloat16

NCORES = 8
DEBUG_A3C = False
TOWN = 1024
HALO = 8
TLOC = TOWN + HALO
ENGS = ("tensor", "vector", "scalar", "gpsimd", "sync")


class Buf:
    __slots__ = ("ap", "name", "ws", "rs", "children")

    def __init__(self, ap, name):
        self.ap = ap
        self.name = name
        self.ws = []
        self.rs = []
        self.children = []

    def __getitem__(self, sl):
        return self.ap[sl]


class P:
    def __init__(self, name):
        self.nc = bass.Bass("TRN2", target_bir_lowering=False)
        self.name = name
        self.ctx = ExitStack()
        self.streams = {e: [] for e in ENGS}
        self.sems = {}
        self.cnt = {}
        self.out_names = []
        self.store_sems = set()
        self._open_sbufs = {}
        self._freed_markers = []

    def sem(self, name):
        if name not in self.sems:
            self.sems[name] = self.ctx.enter_context(self.nc.semaphore(name))
            self.cnt[name] = 0
        return self.sems[name]

    def sbuf(self, name, shape, dtype, side=None):
        cm = self.nc.sbuf_tensor(name, shape, dtype, side=side)
        b = Buf(cm.__enter__(), name)
        b.rs = list(self._freed_markers)
        self._open_sbufs.setdefault(side or "left", []).append((b, cm))
        return b

    def free(self, *bufs):
        group = set(id(b) for b in bufs)
        for b in bufs:
            for bb in [b] + b.children:
                self._freed_markers.extend(bb.ws)
                self._freed_markers.extend(bb.rs)
        for side in list(self._open_sbufs):
            st = self._open_sbufs[side]
            while st and id(st[-1][0]) in group:
                b, cm = st.pop()
                cm.__exit__(None, None, None)
                group.discard(id(b))
        assert not group, f"free(): non-LIFO free requested: {[b.name for b in bufs if id(b) in group]}"

    def psum(self, name, shape, dtype=F32):
        return Buf(self.ctx.enter_context(self.nc.psum_tensor(name, shape, dtype)), name)

    def subdiv(self, buf, n, w):
        cs = []
        for i in range(n):
            c = Buf(buf.ap[:, i * w:(i + 1) * w], f"{buf.name}.{i}")
            c.ws = list(buf.ws)
            c.rs = list(buf.rs)
            cs.append(c)
        buf.children.extend(cs)
        return cs

    def merge(self, buf):
        ws, rs = list(buf.ws), list(buf.rs)
        for c in buf.children:
            ws.extend(c.ws)
            rs.extend(c.rs)
        for b in [buf] + buf.children:
            b.ws = list(ws)
            b.rs = list(rs)

    def din(self, name, shape, dtype):
        return self.nc.dram_tensor(name, shape, dtype, kind="ExternalInput").ap()

    def dout(self, name, shape, dtype):
        self.out_names.append(name)
        return self.nc.dram_tensor(name, shape, dtype, kind="ExternalOutput").ap()

    def _resolve(self, eng, reads, writes, extra):
        waits = {}
        drain = False
        me = "m_" + eng
        def add(mk):
            nonlocal drain
            if mk is None:
                return
            sn, v = mk
            if sn == me:
                drain = True
            elif sn.startswith("m_"):
                waits[sn] = max(waits.get(sn, 0), v)
            else:
                waits[sn] = max(waits.get(sn, 0), self.cnt[sn])
        for b in reads:
            for w in b.ws:
                add(w)
        for b in writes:
            for w in b.ws:
                add(w)
            for r in b.rs:
                add(r)
        for mk in extra:
            add(mk)
        return sorted(waits.items()), drain

    def _update(self, marker, reads, writes):
        for b in reads:
            b.rs.append(marker)
        for b in writes:
            b.ws = [marker]
            b.rs = []

    def step(self, eng, fn, reads=(), writes=(), extra=(), n=1):
        waits, drain = self._resolve(eng, reads, writes, extra)
        m = "m_" + eng
        self.sem(m)
        self.cnt[m] += 1
        val = self.cnt[m]
        def cl(engine, fn=fn, waits=waits, drain=drain, m=m):
            for sn, v in waits:
                engine.wait_ge(self.sems[sn], v)
            if drain and eng != "tensor":
                engine.drain()
            ins = fn(engine)
            ins.then_inc(self.sems[m], 1)
        self.streams[eng].append(cl)
        marker = (m, val)
        self._update(marker, reads, writes)
        return marker

    def step1(self, eng, meth, *args, reads=(), writes=(), extra=(), **kwargs):
        def fn(engine, meth=meth, args=args, kwargs=kwargs):
            return getattr(engine, meth)(*args, **kwargs)
        return self.step(eng, fn, reads=reads, writes=writes, extra=extra)

    def dma(self, eng, out, in_, sem=None, reads=(), writes=(), extra=()):
        if sem is None:
            if writes:
                sem = "d_" + writes[0].name
            else:
                sem = "st_" + reads[0].name
                self.store_sems.add(sem)
        waits, _ = self._resolve("__dma__", reads, writes, extra)
        self.sem(sem)
        self.cnt[sem] += 16
        val = self.cnt[sem]
        def cl(engine, out=out, in_=in_, waits=waits, sem=sem):
            for sn, v in waits:
                engine.wait_ge(self.sems[sn], v)
            engine.dma_start(out=out, in_=in_).then_inc(self.sems[sem], 16)
        self.streams[eng].append(cl)
        marker = (sem, val)
        self._update(marker, reads, writes)
        return marker

    def finalize(self, final_waits):
        def cl(engine, wl=tuple(final_waits)):
            for sn, v in wl:
                engine.wait_ge(self.sems[sn], v)
        self.streams["sync"].append(cl)
        block = self.ctx.enter_context(self.nc.Block())
        for e in ENGS:
            fns = self.streams[e]
            if not fns:
                continue
            getattr(block, e)(lambda engine, fns=fns: [fn(engine) for fn in fns])
        for side in list(self._open_sbufs):
            st = self._open_sbufs[side]
            while st:
                _, cm = st.pop()
                cm.__exit__(None, None, None)
        self.ctx.close()
        return self.nc



def mm_group(p, ps, lhsT_slices, rhs_slices, reads, writes, extra=()):
    nk = len(lhsT_slices)
    def fn(t):
        ins = None
        for k in range(nk):
            ins = t.matmul(ps, lhsT_slices[k], rhs_slices[k], start=(k == 0), stop=(k == nk - 1))
        return ins
    return p.step("tensor", fn, reads=reads, writes=writes, extra=extra, n=nk)


def part_reduce(p, ps_row, ones_bf, src_bf_ap, reads, writes):
    return p.step("tensor", lambda t: t.matmul(ps_row, ones_bf, src_bf_ap, start=True, stop=True),
                  reads=reads, writes=writes)


def bcast_row(p, ps_dst, ones_row_f, row_ap, reads, writes):
    return p.step("tensor", lambda t: t.matmul(ps_dst, ones_row_f, row_ap, start=True, stop=True),
                  reads=reads, writes=writes)



def _c(a):
    return np.ascontiguousarray(a)


def _pcol(vec):
    v = np.asarray(vec)
    return _c(v.reshape(-1, 128).T)


def stage_common(inputs):
    g = {}
    x = np.asarray(inputs["L0_attn_x"])[0]
    g["xT"] = _c(x.T)
    g["xT_pad"] = np.pad(g["xT"], ((0, 0), (HALO, 0)))
    for nm in ("Wq", "Wk", "Wv", "Wo"):
        g["l0_" + nm] = _c(np.asarray(inputs[f"L0_attn_{nm}"]).T)
    g["l0_ig"] = _c(np.asarray(inputs["L0_attn_ig_w"]).T)
    g["l0_og"] = _c(np.asarray(inputs["L0_attn_og_w"]).T)
    g["l0_gam"] = _c(np.asarray(inputs["L0_attn_gamma_w"]).T)
    cw = np.asarray(inputs["L0_attn_conv_w"])[:, 0, :]
    g["l0_convw"] = _c(cw.reshape(16, 128, 4).transpose(1, 0, 2).reshape(128, 64))
    for nm, key in (("convb", "L0_attn_conv_b"), ("igb", "L0_attn_ig_b"), ("ogb", "L0_attn_og_b"),
                    ("gng", "L0_attn_gn_g"), ("gnb", "L0_attn_gn_b")):
        g["l0_" + nm] = _pcol(np.asarray(inputs[key]))
    g["l0_gamb"] = _c(np.asarray(inputs["L0_attn_gamma_b"]).reshape(16, 1))
    for nm, key in (("vng", "L0_attn_vn_g"), ("vnb", "L0_attn_vn_b"),
                    ("mng", "L0_attn_mn_g"), ("mnb", "L0_attn_mn_b")):
        g["l0_" + nm] = _c(np.asarray(inputs[key]).reshape(128, 1))
    g["up_l"] = _c(np.asarray(inputs["L0_ffn_W_up"]).T)
    g["gt_l"] = _c(np.asarray(inputs["L0_ffn_W_gate"]).T)
    g["dn_l"] = _c(np.asarray(inputs["L0_ffn_W_down"]).T)
    g["wsug"] = _pcol(np.asarray(inputs["L0_ffn_Wscale_ug"]))
    g["wsdp"] = _pcol(np.asarray(inputs["L0_ffn_Wscale_dp"]))
    g["suug"] = _pcol(np.asarray(inputs["L0_ffn_SU_ug"]))
    g["sudp"] = _pcol(np.asarray(inputs["L0_ffn_SU_dp"]))
    g["h128b"] = _hadamard(128, BF)
    g["l1_Wq"] = _c(np.asarray(inputs["L1_attn_Wq"]))
    for i in range(2):
        g[f"l1_Wk{i}"] = _c(np.asarray(inputs["L1_attn_Wk"])[i])
        g[f"l1_Wv{i}"] = _c(np.asarray(inputs["L1_attn_Wv"])[i])
        g[f"l1_Wb{i}"] = _c(np.asarray(inputs["L1_attn_Wb"])[i])
    g["l1_Wa"] = _c(np.asarray(inputs["L1_attn_Wa"]))
    g["l1_Wb0"] = g["l1_Wb0"]; g["l1_Wb1"] = g["l1_Wb1"]
    def _cpack(cwa):
        C = cwa.shape[0]
        return _c(cwa.reshape(C // 128, 128, 4).transpose(1, 0, 2).reshape(128, -1))
    g["l1_cqw"] = _cpack(np.asarray(inputs["L1_attn_conv_q_w"]))
    g["l1_ckw0"] = _cpack(np.asarray(inputs["L1_attn_conv_k_w"])[0])
    g["l1_ckw1"] = _cpack(np.asarray(inputs["L1_attn_conv_k_w"])[1])
    g["l1_cvw0"] = _cpack(np.asarray(inputs["L1_attn_conv_v_w"])[0])
    g["l1_cvw1"] = _cpack(np.asarray(inputs["L1_attn_conv_v_w"])[1])
    g["l1_alog"] = _c(np.asarray(inputs["L1_attn_A_log"]).reshape(6, 1))
    g["l1_dtb"] = _c(np.asarray(inputs["L1_attn_dt_bias"]).reshape(6, 1))
    g["l1_Wg"] = _c(np.asarray(inputs["L1_attn_Wg"]))
    g["l1_Wo"] = _c(np.asarray(inputs["L1_attn_Wo"]))
    g["l1_onw"] = _c(np.asarray(inputs["L1_attn_o_norm_w"]).reshape(1, 512))
    g["f1_gate"] = _c(np.asarray(inputs["L1_ffn_gate_w"]).T)
    g["f1_w1l"] = _c(np.asarray(inputs["L1_ffn_w1"]).transpose(0, 2, 1))
    g["f1_w3l"] = _c(np.asarray(inputs["L1_ffn_w3"]).transpose(0, 2, 1))
    g["f1_w2l"] = _c(np.asarray(inputs["L1_ffn_w2"]).transpose(0, 2, 1))
    g["f1_ws1l"] = _c(np.asarray(inputs["L1_ffn_ws1"]).T)
    g["f1_ws3l"] = _c(np.asarray(inputs["L1_ffn_ws3"]).T)
    g["f1_ws2l"] = _c(np.asarray(inputs["L1_ffn_ws2"]).T)
    r_ = np.arange(128)
    g["utri_inc"] = (r_[:, None] <= r_[None, :]).astype(np.float32)
    g["utri_strict"] = (r_[:, None] < r_[None, :]).astype(np.float32)
    g["ltri_strict"] = (r_[:, None] > r_[None, :]).astype(np.float32)
    g["ones_col_bf"] = np.ones((128, 1), BF)
    g["ones_row_f"] = np.ones((1, 128), np.float32)
    g["id_bf"] = np.eye(128, dtype=BF)
    g["id_f"] = np.eye(128, dtype=np.float32)
    return g



def emit_ln_c(p, ctxb, src_f32, dst_bf, T, nts, src_is_bf=False):
    xbfB = src_f32 if src_is_bf else dst_bf
    sqs = ctxb["sqs"]
    onesb = ctxb["onesb"]; onesrf = ctxb["onesrf"]
    ps_r1 = ctxb["ps_r1"]; ps_r2 = ctxb["ps_r2"]; ps_m = ctxb["ps_m"]; ps_n = ctxb["ps_n"]
    srow = ctxb["srow"]; srow2 = ctxb["srow2"]; srow3 = ctxb["srow3"]; rrep = ctxb["rrep"]; scr = ctxb["scr"]
    if not src_is_bf:
        for k in range(16):
            p.step1("scalar", "activation", xbfB[:, k * T:(k + 1) * T], src_f32[:, k * T:(k + 1) * T],
                    AF.Copy, reads=[src_f32], writes=[xbfB])
    for (o0, n) in nts:
        mm_group(p, ps_r1.ap[0:1, 0:n], [onesb[:, :]] * 16,
                 [xbfB[:, k * T + o0:k * T + o0 + n] for k in range(16)],
                 reads=[xbfB, onesb], writes=[ps_r1])
        for k in range(16):
            sb = sqs[k % 2]
            p.step1("scalar", "activation", sb[:, 0:n], src_f32[:, k * T + o0:k * T + o0 + n],
                    AF.Square, reads=[src_f32], writes=[sb])
            p.step1("tensor", "matmul", ps_r2[0:1, 0:n], onesb[:, :], sb[:, 0:n],
                    start=(k == 0), stop=(k == 15), reads=[sb, onesb], writes=[ps_r2])
        p.step1("vector", "tensor_scalar_mul", srow[0:1, 0:n], ps_r1[0:1, 0:n], 1.0 / 2048,
                reads=[ps_r1], writes=[srow])
        p.step1("vector", "tensor_scalar_mul", srow2[0:1, 0:n], ps_r2[0:1, 0:n], 1.0 / 2048,
                reads=[ps_r2], writes=[srow2])
        p.step1("vector", "tensor_mul", srow3[0:1, 0:n], srow[0:1, 0:n], srow[0:1, 0:n],
                reads=[srow], writes=[srow3])
        p.step1("vector", "tensor_sub", srow2[0:1, 0:n], srow2[0:1, 0:n], srow3[0:1, 0:n],
                reads=[srow2, srow3], writes=[srow2])
        p.step1("vector", "tensor_scalar_add", srow2[0:1, 0:n], srow2[0:1, 0:n], 1e-5,
                reads=[srow2], writes=[srow2])
        p.step1("scalar", "activation", srow2[0:1, 0:n], srow2[0:1, 0:n], AF.Sqrt,
                reads=[srow2], writes=[srow2])
        p.step1("vector", "reciprocal", srow2[0:1, 0:n], srow2[0:1, 0:n],
                reads=[srow2], writes=[srow2])
        bcast_row(p, ps_m.ap[:, 0:n], onesrf[:, :], srow[0:1, 0:n], reads=[srow, onesrf], writes=[ps_m])
        bcast_row(p, ps_n.ap[:, 0:n], onesrf[:, :], srow2[0:1, 0:n], reads=[srow2, onesrf], writes=[ps_n])
        p.step1("vector", "tensor_copy", rrep[:, 0:n], ps_n[:, 0:n], reads=[ps_n], writes=[rrep])
        scr2 = ctxb["scr2"]
        for k in range(16):
            if src_is_bf:
                p.step1("scalar", "activation", scr2[:, 0:n], src_f32[:, k * T + o0:k * T + o0 + n],
                        AF.Copy, reads=[src_f32], writes=[scr2])
                p.step1("vector", "tensor_sub", scr[:, 0:n], scr2[:, 0:n],
                        ps_m[:, 0:n], reads=[scr2, ps_m], writes=[scr])
            else:
                p.step1("vector", "tensor_sub", scr[:, 0:n], src_f32[:, k * T + o0:k * T + o0 + n],
                        ps_m[:, 0:n], reads=[src_f32, ps_m], writes=[scr])
            p.step1("vector", "tensor_mul", dst_bf[:, k * T + o0:k * T + o0 + n], scr[:, 0:n],
                    rrep[:, 0:n], reads=[scr, rrep], writes=[dst_bf])


def mk_ln_ctx(p, T):
    return {
        "sqs": [p.sbuf("lnsq0", [128, 512], BF16), p.sbuf("lnsq1", [128, 512], BF16)],
        "srow": p.sbuf("srow", [1, 512], F32),
        "srow2": p.sbuf("srow2", [1, 512], F32),
        "srow3": p.sbuf("srow3", [1, 512], F32),
        "rrep": p.sbuf("rrep", [128, 512], F32),
        "scr": p.sbuf("lnscr", [128, 512], F32),
        "scr2": p.sbuf("lnscr2", [128, 512], F32),
        "ps_r1": p.psum("ps_r1", [1, 512], F32),
        "ps_r2": p.psum("ps_r2", [1, 512], F32),
        "ps_m": p.psum("ps_m", [128, 512], F32),
        "ps_n": p.psum("ps_n", [128, 512], F32),
    }



def build_a1():
    p = P("a1")
    XT = p.din("xt", [2048, TLOC], F32)
    WQ = p.din("wq", [2048, 2048], F32)
    WK = p.din("wk", [2048, 2048], F32)
    WV = p.din("wv", [2048, 2048], F32)
    WI = p.din("wi", [2048, 2048], F32)
    WOG = p.din("wog", [2048, 2048], F32)
    WGAM = p.din("wgam", [2048, 16], F32)
    CONVW = p.din("convw", [128, 64], F32)
    CONVB = p.din("convb", [128, 16], F32)
    IGB = p.din("igb", [128, 16], F32)
    OGB = p.din("ogb", [128, 16], F32)
    GAMB = p.din("gamb", [16, 1], F32)
    VNG = p.din("vng", [128, 1], F32)
    VNB = p.din("vnb", [128, 1], F32)
    ONESB = p.din("ones_col_bf", [128, 1], BF16)
    ONESRF = p.din("ones_row_f", [1, 128], F32)

    QO = p.dout("qo", [2048, TOWN], BF16)
    BO = p.dout("bo", [2048, TOWN], BF16)
    OO = p.dout("oo", [2048, TOWN], BF16)
    GO = p.dout("go", [16, TOWN], F32)

    xln = p.sbuf("xln", [128, 16 * TLOC], BF16)
    xc = p.sbuf("xc", [128, 16 * TOWN], BF16)
    lc = mk_ln_ctx(p, TLOC)
    xt = p.sbuf("sb_xt", [128, 16 * TLOC], F32, side="right")
    wbuf = {nm: [p.sbuf(f"w_{nm}{s}", [128, 16 * 128], BF16) for s in range(3 if nm in ("q", "k") else 2)]
            for nm in ("q", "k", "v", "i", "o")}
    wgam = p.sbuf("wgam_b", [128, 16 * 16], BF16)
    convw = p.sbuf("c_convw", [128, 64], F32)
    convb = p.sbuf("c_convb", [128, 16], F32)
    igb = p.sbuf("c_igb", [128, 16], F32)
    ogb = p.sbuf("c_ogb", [128, 16], F32)
    gamb = p.sbuf("c_gamb", [16, 1], F32)
    vng = p.sbuf("c_vng", [128, 1], F32)
    vnb = p.sbuf("c_vnb", [128, 1], F32)
    onesb = p.sbuf("c_onesb", [128, 1], BF16)
    onesrf = p.sbuf("c_onesrf", [1, 128], F32)
    lc["onesb"] = onesb
    lc["onesrf"] = onesrf
    srow3 = lc["srow3"]
    acc = [p.sbuf(f"acc{i}", [128, TOWN], F32) for i in range(2)]
    srow = lc["srow"]; srow2 = lc["srow2"]; rrep = lc["rrep"]; scr = lc["scr"]
    ps_r1 = lc["ps_r1"]; ps_r2 = lc["ps_r2"]; ps_m = lc["ps_m"]; ps_n = lc["ps_n"]
    kbf = p.sbuf("kbf", [128, 512], BF16)
    vbf = p.sbuf("vbf", [128, 512], BF16)
    ibf = p.sbuf("ibf", [128, 512], BF16)
    qst = p.sbuf("qst", [128, 512], BF16)
    bst = p.sbuf("bst", [128, 512], BF16)
    ost = p.sbuf("ost", [128, 512], BF16)
    ps_a = p.psum("ps_a", [128, 512], F32)
    ps_b = p.psum("ps_b", [128, 512], F32)
    ps_g = p.psum("ps_g", [16, 512], F32)

    for i in range(16):
        p.dma("sync", xt[:, i * TLOC:(i + 1) * TLOC], XT[i * 128:(i + 1) * 128, :], writes=[xt])
    for b, srcdr in ((convw, CONVW), (convb, CONVB), (igb, IGB), (ogb, OGB), (gamb, GAMB),
                     (vng, VNG), (vnb, VNB), (onesb, ONESB), (onesrf, ONESRF)):
        p.dma("sync", b[:, :], srcdr, writes=[b])
    p.dma("gpsimd", wgam.ap.rearrange("p (k m) -> p k m", k=16),
          WGAM.rearrange("(k p) m -> p k m", p=128), writes=[wgam])

    emit_ln_c(p, lc, xt, xln, TLOC, [(0, 344), (344, 344), (688, 344)])
    p.free(xt)
    gam = p.sbuf("gam", [16, 512], F32, side="right")

    for kk in range(16):
        eng = "vector"
        a0, a1 = acc[0], acc[1]
        p.step1(eng, "tensor_scalar", a0[:, :], xln[:, kk * TLOC + 5:kk * TLOC + 5 + TOWN],
                convw[:, 4 * kk:4 * kk + 1], None, ALU.mult, reads=[xln, convw], writes=[a0])
        for j in range(3):
            srcacc, dst = (a0, a1) if j % 2 == 0 else (a1, a0)
            p.step1(eng, "scalar_tensor_tensor", dst[:, :],
                    xln[:, kk * TLOC + 6 + j:kk * TLOC + 6 + j + TOWN],
                    convw[:, 4 * kk + j + 1:4 * kk + j + 2], srcacc[:, :], ALU.mult, ALU.add,
                    reads=[xln, convw, srcacc], writes=[dst])
        p.step1("scalar", "activation", xc[:, kk * TOWN:(kk + 1) * TOWN], a1[:, :], AF.Silu,
                bias=convb[:, kk:kk + 1], reads=[a1, convb], writes=[xc])

    def pstats_mu_rstd(eps):
        p.step1("vector", "tensor_scalar_mul", srow[0:1, 0:512], ps_r1[0:1, 0:512], 1.0 / 128,
                reads=[ps_r1], writes=[srow])
        p.step1("vector", "tensor_scalar_mul", srow2[0:1, 0:512], ps_r2[0:1, 0:512], 1.0 / 128,
                reads=[ps_r2], writes=[srow2])
        p.step1("vector", "tensor_mul", srow3[0:1, 0:512], srow[0:1, 0:512], srow[0:1, 0:512],
                reads=[srow], writes=[srow3])
        p.step1("vector", "tensor_sub", srow2[0:1, 0:512], srow2[0:1, 0:512], srow3[0:1, 0:512],
                reads=[srow2, srow3], writes=[srow2])
        p.step1("vector", "tensor_scalar_add", srow2[0:1, 0:512], srow2[0:1, 0:512], eps,
                reads=[srow2], writes=[srow2])
        p.step1("scalar", "activation", srow2[0:1, 0:512], srow2[0:1, 0:512], AF.Sqrt,
                reads=[srow2], writes=[srow2])
        p.step1("vector", "reciprocal", srow2[0:1, 0:512], srow2[0:1, 0:512],
                reads=[srow2], writes=[srow2])

    xln_own = [xln[:, k * TLOC + HALO:k * TLOC + HALO + TOWN] for k in range(16)]
    for h in range(16):
        for nm, W in (("q", WQ), ("k", WK), ("v", WV), ("i", WI), ("o", WOG)):
            slot = wbuf[nm][h % len(wbuf[nm])]
            p.dma("gpsimd", slot.ap.rearrange("p (k m) -> p k m", k=16),
                  W[:, h * 128:(h + 1) * 128].rearrange("(k p) m -> p k m", p=128),
                  writes=[slot])
        for nt in range(2):
            o0 = nt * 512
            osl = slice(o0, o0 + 512)
            mm_group(p, ps_a.ap[:, 0:512], [wbuf["q"][h % 3][:, k * 128:(k + 1) * 128] for k in range(16)],
                     [c[:, osl] for c in xln_own], reads=[wbuf["q"][h % 3], xln], writes=[ps_a])
            p.step1("scalar", "activation", lc["sqs"][0].ap[:, 0:512], ps_a[:, 0:512], AF.Square,
                    reads=[ps_a], writes=[lc["sqs"][0]])
            p.step1("tensor", "matmul", ps_r1[0:1, 0:512], onesb[:, :], lc["sqs"][0].ap[:, 0:512],
                    start=True, stop=True, reads=[lc["sqs"][0], onesb], writes=[ps_r1])
            p.step1("scalar", "activation", srow[0:1, 0:512], ps_r1[0:1, 0:512], AF.Sqrt,
                    reads=[ps_r1], writes=[srow])
            p.step1("vector", "tensor_scalar_max", srow[0:1, 0:512], srow[0:1, 0:512], 1e-12,
                    reads=[srow], writes=[srow])
            p.step1("vector", "reciprocal", srow[0:1, 0:512], srow[0:1, 0:512],
                    reads=[srow], writes=[srow])
            bcast_row(p, ps_m.ap[:, 0:512], onesrf[:, :], srow[0:1, 0:512], reads=[srow, onesrf], writes=[ps_m])
            p.step1("vector", "tensor_copy", rrep[:, 0:512], ps_m[:, 0:512], reads=[ps_m], writes=[rrep])
            p.step1("vector", "tensor_mul", qst[:, 0:512], ps_a[:, 0:512], rrep[:, 0:512],
                    reads=[ps_a, rrep], writes=[qst])
            p.dma("sync", QO[h * 128:(h + 1) * 128, osl], qst[:, 0:512], reads=[qst])
            mm_group(p, ps_b.ap[:, 0:512], [wbuf["k"][h % 3][:, k * 128:(k + 1) * 128] for k in range(16)],
                     [c[:, osl] for c in xln_own], reads=[wbuf["k"][h % 3], xln], writes=[ps_b])
            p.step1("scalar", "activation", lc["sqs"][0].ap[:, 0:512], ps_b[:, 0:512], AF.Square,
                    reads=[ps_b], writes=[lc["sqs"][0]])
            p.step1("tensor", "matmul", ps_r1[0:1, 0:512], onesb[:, :], lc["sqs"][0].ap[:, 0:512],
                    start=True, stop=True, reads=[lc["sqs"][0], onesb], writes=[ps_r1])
            p.step1("scalar", "activation", srow[0:1, 0:512], ps_r1[0:1, 0:512], AF.Sqrt,
                    reads=[ps_r1], writes=[srow])
            p.step1("vector", "tensor_scalar_max", srow[0:1, 0:512], srow[0:1, 0:512], 1e-12,
                    reads=[srow], writes=[srow])
            p.step1("vector", "reciprocal", srow[0:1, 0:512], srow[0:1, 0:512],
                    reads=[srow], writes=[srow])
            bcast_row(p, ps_m.ap[:, 0:512], onesrf[:, :], srow[0:1, 0:512], reads=[srow, onesrf], writes=[ps_m])
            p.step1("vector", "tensor_copy", rrep[:, 0:512], ps_m[:, 0:512], reads=[ps_m], writes=[rrep])
            p.step1("vector", "tensor_mul", kbf[:, 0:512], ps_b[:, 0:512], rrep[:, 0:512],
                    reads=[ps_b, rrep], writes=[kbf])
            mm_group(p, ps_a.ap[:, 0:512], [wbuf["v"][h % 2][:, k * 128:(k + 1) * 128] for k in range(16)],
                     [c[:, osl] for c in xln_own], reads=[wbuf["v"][h % 2], xln], writes=[ps_a])
            p.step1("scalar", "activation", lc["sqs"][1].ap[:, 0:512], ps_a[:, 0:512], AF.Copy,
                    reads=[ps_a], writes=[lc["sqs"][1]])
            p.step1("scalar", "activation", lc["sqs"][0].ap[:, 0:512], ps_a[:, 0:512], AF.Square,
                    reads=[ps_a], writes=[lc["sqs"][0]])
            p.step1("tensor", "matmul", ps_r1[0:1, 0:512], onesb[:, :], lc["sqs"][1].ap[:, 0:512],
                    start=True, stop=True, reads=[lc["sqs"][1], onesb], writes=[ps_r1])
            p.step1("tensor", "matmul", ps_r2[0:1, 0:512], onesb[:, :], lc["sqs"][0].ap[:, 0:512],
                    start=True, stop=True, reads=[lc["sqs"][0], onesb], writes=[ps_r2])
            pstats_mu_rstd(1e-5)
            bcast_row(p, ps_m.ap[:, 0:512], onesrf[:, :], srow[0:1, 0:512], reads=[srow, onesrf], writes=[ps_m])
            bcast_row(p, ps_n.ap[:, 0:512], onesrf[:, :], srow2[0:1, 0:512], reads=[srow2, onesrf], writes=[ps_n])
            p.step1("vector", "tensor_copy", rrep[:, 0:512], ps_n[:, 0:512], reads=[ps_n], writes=[rrep])
            p.step1("vector", "tensor_copy", lc["scr2"].ap[:, 0:512], ps_m[:, 0:512],
                    reads=[ps_m], writes=[lc["scr2"]])
            p.step1("vector", "tensor_sub", scr[:, 0:512], ps_a[:, 0:512], lc["scr2"][:, 0:512],
                    reads=[ps_a, lc["scr2"]], writes=[scr])
            p.step1("vector", "tensor_mul", scr[:, 0:512], scr[:, 0:512], rrep[:, 0:512],
                    reads=[scr, rrep], writes=[scr])
            p.step1("vector", "tensor_scalar", vbf[:, 0:512], scr[:, 0:512], vng[:, :], vnb[:, :],
                    ALU.mult, ALU.add, reads=[scr, vng, vnb], writes=[vbf])
            mm_group(p, ps_b.ap[:, 0:512], [wbuf["i"][h % 2][:, k * 128:(k + 1) * 128] for k in range(16)],
                     [xc[:, k * TOWN + o0:k * TOWN + o0 + 512] for k in range(16)],
                     reads=[wbuf["i"][h % 2], xc], writes=[ps_b])
            p.step1("scalar", "activation", ibf[:, 0:512], ps_b[:, 0:512], AF.Sigmoid,
                    bias=igb[:, h:h + 1], reads=[ps_b, igb], writes=[ibf])
            p.step1("vector", "tensor_mul", scr[:, 0:512], kbf[:, 0:512], vbf[:, 0:512],
                    reads=[kbf, vbf], writes=[scr])
            p.step1("vector", "tensor_mul", bst[:, 0:512], scr[:, 0:512], ibf[:, 0:512],
                    reads=[scr, ibf], writes=[bst])
            p.dma("sync", BO[h * 128:(h + 1) * 128, osl], bst[:, 0:512], reads=[bst])
            mm_group(p, ps_a.ap[:, 0:512], [wbuf["o"][h % 2][:, k * 128:(k + 1) * 128] for k in range(16)],
                     [xc[:, k * TOWN + o0:k * TOWN + o0 + 512] for k in range(16)],
                     reads=[wbuf["o"][h % 2], xc], writes=[ps_a])
            p.step1("scalar", "activation", ost[:, 0:512], ps_a[:, 0:512], AF.Sigmoid,
                    bias=ogb[:, h:h + 1], reads=[ps_a, ogb], writes=[ost])
            p.dma("sync", OO[h * 128:(h + 1) * 128, osl], ost[:, 0:512], reads=[ost])

    for nt in range(2):
        o0 = nt * 512
        mm_group(p, ps_g.ap[0:16, 0:512], [wgam[:, k * 16:(k + 1) * 16] for k in range(16)],
                 [xc[:, k * TOWN + o0:k * TOWN + o0 + 512] for k in range(16)],
                 reads=[wgam, xc], writes=[ps_g])
        p.step1("scalar", "activation", gam[0:16, 0:512], ps_g[0:16, 0:512], AF.Sigmoid,
                bias=gamb[0:16, :], reads=[ps_g, gamb], writes=[gam])
        p.dma("sync", GO[:, o0:o0 + 512], gam[0:16, 0:512], reads=[gam])

    return p.finalize([(s, p.cnt[s]) for s in sorted(p.store_sems)])



def build_a2():
    p = P("a2")
    BO2 = p.din("bo2", [256, 8192], BF16)
    GO2 = p.din("go2", [2, 8192], F32)
    ONESRF = p.din("ones_row_f", [1, 128], F32)
    MO = p.dout("mo", [256, 8192], BF16)

    onesrf = p.sbuf("c_onesrf", [1, 128], F32)
    grow = p.sbuf("grow", [1, 8192], F32)
    bb = p.sbuf("bb", [128, 8192], BF16)
    bf32 = p.sbuf("bf32", [128, 8192], F32)
    grep = p.sbuf("grep", [128, 8192], F32)
    mem = p.sbuf("mem", [128, 8192], BF16)
    ps_bc = p.psum("ps_bc", [128, 512], F32)

    p.dma("sync", onesrf[:, :], ONESRF, writes=[onesrf])
    for j in range(2):
        p.dma("sync", bb[:, :], BO2[j * 128:(j + 1) * 128, :], writes=[bb])
        p.dma("sync", grow[0:1, :], GO2[j:j + 1, :], writes=[grow])
        p.step1("scalar", "activation", bf32[:, :], bb[:, :], AF.Copy, reads=[bb], writes=[bf32])
        for nt in range(16):
            o0 = nt * 512
            bcast_row(p, ps_bc.ap[:, 0:512], onesrf[:, :], grow[0:1, o0:o0 + 512],
                      reads=[grow, onesrf], writes=[ps_bc])
            p.step1("vector", "tensor_copy", grep[:, o0:o0 + 512], ps_bc[:, 0:512],
                    reads=[ps_bc], writes=[grep])
        p.step1("vector", "tensor_tensor_scan", mem[:, :], grep[:, :], bf32[:, :], 0.0,
                ALU.mult, ALU.add, reads=[grep, bf32], writes=[mem])
        p.dma("sync", MO[j * 128:(j + 1) * 128, :], mem[:, :], reads=[mem])
    return p.finalize([(s, p.cnt[s]) for s in sorted(p.store_sems)])



def build_a3c():
    p = P("a3c")
    MO3 = p.din("mo3", [2048, TOWN], BF16)
    QO3 = p.din("qo3", [2048, TOWN], BF16)
    OO3 = p.din("oo3", [2048, TOWN], BF16)
    XT3 = p.din("xt3", [2048, TOWN], F32)
    WOL = p.din("wol", [2048, 2048], F32)
    WUPL = p.din("wupl", [2048, 8192], F32)
    WGTL = p.din("wgtl", [2048, 8192], F32)
    WDNL = p.din("wdnl", [8192, 2048], F32)
    WSUG = p.din("wsug", [128, 128], F32)
    WSDP = p.din("wsdp", [128, 16], F32)
    SUUG = p.din("suug", [128, 16], F32)
    SUDP = p.din("sudp", [128, 64], F32)
    H128B = p.din("h128b", [128, 128], BF16)
    MNG = p.din("mng", [128, 1], F32)
    MNB = p.din("mnb", [128, 1], F32)
    GNG = p.din("gng", [128, 16], F32)
    GNB = p.din("gnb", [128, 16], F32)
    ONESB = p.din("ones_col_bf", [128, 1], BF16)
    ONESRF = p.din("ones_row_f", [1, 128], F32)
    X2O = p.dout("x2o", [2048, TOWN], F32)
    X2LNO = p.dout("x2lno", [2048, TOWN], BF16)
    X1D = p.nc.dram_tensor("x1spill", [2048, TOWN], F32, kind="Internal").ap()

    NT2 = [(0, 512), (512, 512)]
    lc = mk_ln_ctx(p, TOWN)
    onesb = p.sbuf("c_onesb", [128, 1], BF16)
    onesrf = p.sbuf("c_onesrf", [1, 128], F32)
    lc["onesb"] = onesb; lc["onesrf"] = onesrf
    wsug = p.sbuf("c_wsug", [128, 128], F32)
    wsdp = p.sbuf("c_wsdp", [128, 16], F32)
    suug = p.sbuf("c_suug", [128, 16], F32)
    sudp = p.sbuf("c_sudp", [128, 64], F32)
    h128b = p.sbuf("c_h128", [128, 128], BF16)
    mng = p.sbuf("c_mng", [128, 1], F32)
    mnb = p.sbuf("c_mnb", [128, 1], F32)
    gng = p.sbuf("c_gng", [128, 16], F32)
    gnb = p.sbuf("c_gnb", [128, 16], F32)
    x1su = p.sbuf("x1su", [128, 16 * TOWN], BF16)
    x1su_c = p.subdiv(x1su, 16, TOWN)
    srow = lc["srow"]; srow2 = lc["srow2"]; rrep = lc["rrep"]; scr = lc["scr"]; scr2 = lc["scr2"]
    ps_r1 = lc["ps_r1"]; ps_r2 = lc["ps_r2"]; ps_m = lc["ps_m"]; ps_n = lc["ps_n"]
    ps_a = p.psum("ps_a", [128, 512], F32)
    ps_b = p.psum("ps_b", [128, 512], F32)
    sqb = p.sbuf("sqb", [128, 512], BF16)
    t2b = p.sbuf("t2b", [128, 512], BF16)

    for b, srcdr in ((onesb, ONESB), (onesrf, ONESRF), (wsug, WSUG), (wsdp, WSDP),
                     (suug, SUUG), (sudp, SUDP), (h128b, H128B), (mng, MNG), (mnb, MNB),
                     (gng, GNG), (gnb, GNB)):
        p.dma("sync", b.ap[:, :], srcdr, writes=[b])
    p.step1("vector", "tensor_scalar_mul", wsug[:, :], wsug[:, :], 64.0, reads=[wsug], writes=[wsug])
    p.step1("vector", "tensor_scalar_mul", wsdp[:, :], wsdp[:, :], 64.0, reads=[wsdp], writes=[wsdp])
    C2 = 1.0 / (64.0 * float(np.sqrt(8192.0)))
    p.step1("vector", "tensor_scalar_mul", sudp[:, :], sudp[:, :], C2, reads=[sudp], writes=[sudp])

    ofin = p.sbuf("ofin", [128, 16 * TOWN], BF16, side="right")
    mo3 = p.sbuf("mo3b", [128, 16 * TOWN], BF16, side="right")
    qo3 = p.sbuf("qo3b", [128, 16 * TOWN], BF16, side="right")
    oo3 = p.sbuf("oo3b", [128, 16 * TOWN], BF16, side="right")
    for b, srcdr in ((mo3, MO3), (qo3, QO3), (oo3, OO3)):
        p.dma("sync", b.ap.rearrange("p (k t) -> p k t", k=16),
              srcdr.rearrange("(k p) t -> p k t", p=128), writes=[b])

    for h in range(16):
        hs = slice(h * TOWN, (h + 1) * TOWN)
        for (o0, n) in NT2:
            sl = slice(h * TOWN + o0, h * TOWN + o0 + n)
            p.step1("scalar", "activation", lc["sqs"][0].ap[:, 0:n], mo3[:, sl], AF.Square, reads=[mo3], writes=[lc["sqs"][0]])
            p.step1("tensor", "matmul", ps_r1[0:1, 0:n], onesb[:, :], mo3[:, sl],
                    start=True, stop=True, reads=[mo3, onesb], writes=[ps_r1])
            p.step1("tensor", "matmul", ps_r2[0:1, 0:n], onesb[:, :], lc["sqs"][0].ap[:, 0:n],
                    start=True, stop=True, reads=[lc["sqs"][0], onesb], writes=[ps_r2])
            _pstats(p, lc, 1e-5)
            bcast_row(p, ps_m.ap[:, 0:n], onesrf[:, :], srow[0:1, 0:n], reads=[srow, onesrf], writes=[ps_m])
            bcast_row(p, ps_n.ap[:, 0:n], onesrf[:, :], srow2[0:1, 0:n], reads=[srow2, onesrf], writes=[ps_n])
            p.step1("scalar", "activation", scr2[:, 0:n], mo3[:, sl], AF.Copy, reads=[mo3], writes=[scr2])
            p.step1("vector", "tensor_sub", scr[:, 0:n], scr2[:, 0:n], ps_m[:, 0:n],
                    reads=[scr2, ps_m], writes=[scr])
            p.step1("vector", "tensor_copy", rrep[:, 0:n], ps_n[:, 0:n], reads=[ps_n], writes=[rrep])
            p.step1("vector", "tensor_mul", scr[:, 0:n], scr[:, 0:n], rrep[:, 0:n],
                    reads=[scr, rrep], writes=[scr])
            p.step1("vector", "tensor_scalar", scr[:, 0:n], scr[:, 0:n], mng[:, :], mnb[:, :],
                    ALU.mult, ALU.add, reads=[scr, mng, mnb], writes=[scr])
            p.step1("scalar", "activation", scr2[:, 0:n], qo3[:, sl], AF.Copy, reads=[qo3], writes=[scr2])
            p.step1("vector", "tensor_mul", scr[:, 0:n], scr[:, 0:n], scr2[:, 0:n],
                    reads=[scr, scr2], writes=[scr])
            p.step1("scalar", "activation", t2b[:, 0:n], scr[:, 0:n], AF.Copy, reads=[scr], writes=[t2b])
            p.step1("scalar", "activation", lc["sqs"][0].ap[:, 0:n], scr[:, 0:n], AF.Square, reads=[scr], writes=[lc["sqs"][0]])
            p.step1("tensor", "matmul", ps_r1[0:1, 0:n], onesb[:, :], t2b[:, 0:n],
                    start=True, stop=True, reads=[t2b, onesb], writes=[ps_r1])
            p.step1("tensor", "matmul", ps_r2[0:1, 0:n], onesb[:, :], lc["sqs"][0].ap[:, 0:n],
                    start=True, stop=True, reads=[lc["sqs"][0], onesb], writes=[ps_r2])
            _pstats(p, lc, 1e-5)
            bcast_row(p, ps_m.ap[:, 0:n], onesrf[:, :], srow[0:1, 0:n], reads=[srow, onesrf], writes=[ps_m])
            bcast_row(p, ps_n.ap[:, 0:n], onesrf[:, :], srow2[0:1, 0:n], reads=[srow2, onesrf], writes=[ps_n])
            p.step1("vector", "tensor_sub", scr[:, 0:n], scr[:, 0:n], ps_m[:, 0:n],
                    reads=[scr, ps_m], writes=[scr])
            p.step1("vector", "tensor_copy", rrep[:, 0:n], ps_n[:, 0:n], reads=[ps_n], writes=[rrep])
            p.step1("vector", "tensor_mul", scr[:, 0:n], scr[:, 0:n], rrep[:, 0:n],
                    reads=[scr, rrep], writes=[scr])
            p.step1("vector", "tensor_scalar", scr[:, 0:n], scr[:, 0:n], gng[:, h:h + 1], gnb[:, h:h + 1],
                    ALU.mult, ALU.add, reads=[scr, gng, gnb], writes=[scr])
            p.step1("scalar", "activation", scr2[:, 0:n], oo3[:, sl], AF.Copy, reads=[oo3], writes=[scr2])
            p.step1("vector", "tensor_mul", ofin[:, sl], scr[:, 0:n], scr2[:, 0:n],
                    reads=[scr, scr2], writes=[ofin])
    p.free(oo3, qo3, mo3)

    yb = p.sbuf("yb", [128, 16 * TOWN], BF16, side="right")
    wo0 = p.sbuf("wo0", [128, 16 * 128], BF16, side="right")
    wo1 = p.sbuf("wo1", [128, 16 * 128], BF16, side="right")
    ofin_c = p.subdiv(ofin, 16, TOWN)
    pool4 = [ps_a, ps_b, ps_m, ps_n]
    pi = 0
    for mc in range(16):
        wb = (wo0, wo1)[mc % 2]
        p.dma("gpsimd", wb.ap.rearrange("p (k m) -> p k m", k=16),
              WOL[:, mc * 128:(mc + 1) * 128].rearrange("(k p) m -> p k m", p=128), writes=[wb])
        for (o0, n) in NT2:
            psx = pool4[pi % 4]; pi += 1
            mm_group(p, psx.ap[:, 0:n], [wb[:, k * 128:(k + 1) * 128] for k in range(16)],
                     [c[:, o0:o0 + n] for c in ofin_c],
                     reads=[wb] + ofin_c, writes=[psx])
            p.step1("scalar", "activation", yb.ap[:, mc * TOWN + o0:mc * TOWN + o0 + n], psx[:, 0:n],
                    AF.Copy, reads=[psx], writes=[yb])

    if DEBUG_A3C:
        DBGY = p.dout("dbg_y", [2048, TOWN], BF16)
        for k in range(16):
            p.dma("sync", DBGY[k * 128:(k + 1) * 128, :], yb[:, k * TOWN:(k + 1) * TOWN], reads=[yb])
        DBGO = p.dout("dbg_o", [2048, TOWN], BF16)
        p.merge(ofin)
        for k in range(16):
            p.dma("sync", DBGO[k * 128:(k + 1) * 128, :], ofin[:, k * TOWN:(k + 1) * TOWN], reads=[ofin])

    xt3 = p.sbuf("xt3b", [128, 16 * TOWN], F32, side="right")
    rscr = p.sbuf("rscr", [128, TOWN], F32, side="right")
    p.dma("sync", xt3.ap.rearrange("p (k t) -> p k t", k=16),
          XT3.rearrange("(k p) t -> p k t", p=128), writes=[xt3])
    emit_ln_c(p, lc, yb, x1su, TOWN, NT2, src_is_bf=True)
    for k in range(16):
        ks = slice(k * TOWN, (k + 1) * TOWN)
        p.step1("scalar", "activation", rscr[:, :], x1su[:, ks], AF.Copy, reads=[x1su], writes=[rscr])
        p.step1("vector", "tensor_add", xt3[:, ks], xt3[:, ks], rscr[:, :],
                reads=[xt3, rscr], writes=[xt3])
        p.dma("sync", X1D[k * 128:(k + 1) * 128, :], xt3[:, ks], sem="sp_x1", reads=[xt3])
    emit_ln_c(p, lc, xt3, x1su, TOWN, NT2)
    p.free(rscr, xt3, wo1, wo0, yb)
    p.free(ofin)

    p.merge(x1su)

    prod = p.sbuf("prod", [128, 64 * TOWN], BF16, side="right")
    prod_c = p.subdiv(prod, 64, TOWN)
    wus = [p.sbuf(f"wu{i}", [128, 16 * 128], BF16, side="right") for i in range(3)]
    wgs = [p.sbuf(f"wg{i}", [128, 16 * 128], BF16, side="right") for i in range(3)]
    gbuf = p.sbuf("gbuf", [128, 512], BF16, side="right")
    ubuf = p.sbuf("ubuf", [128, 512], BF16, side="right")
    btmp = p.sbuf("btmp", [128, TOWN], BF16, side="right")

    for k in range(16):
        p.step1("vector", "tensor_scalar_mul", x1su_c[k].ap[:, :], x1su_c[k][:, :], suug[:, k:k + 1],
                reads=[x1su_c[k], suug], writes=[x1su_c[k]])
    C1 = 1.0 / (64.0 * float(np.sqrt(2048.0)))
    for k in range(16):
        for (o0, n) in NT2:
            psx = pool4[pi % 4]; pi += 1
            p.step1("tensor", "matmul", psx[:, 0:n], h128b[:, :], x1su_c[k][:, o0:o0 + n],
                    start=True, stop=True, reads=[h128b, x1su_c[k]], writes=[psx])
            p.step1("scalar", "activation", x1su_c[k].ap[:, o0:o0 + n], psx[:, 0:n], AF.Copy,
                    scale=C1, reads=[psx], writes=[x1su_c[k]])
    for st_i, d in enumerate((8, 4, 2, 1)):
        pairs = [(gg, gg + d) for gg in range(16) if (gg // d) % 2 == 0]
        for (ga, gb_) in pairs:
            eng = "vector" if (ga // d) % 4 == 0 else "gpsimd"
            A, Bc = x1su_c[ga], x1su_c[gb_]
            p.step1(eng, "tensor_copy", btmp[:, :], A[:, :], reads=[A], writes=[btmp])
            p.step1(eng, "tensor_add", A.ap[:, :], btmp[:, :], Bc[:, :], reads=[btmp, Bc], writes=[A])
            p.step1(eng, "tensor_sub", Bc.ap[:, :], btmp[:, :], Bc[:, :], reads=[btmp, Bc], writes=[Bc])
    for mc in range(64):
        wu = wus[mc % 3]
        wg = wgs[mc % 3]
        p.dma("gpsimd", wu.ap.rearrange("p (k m) -> p k m", k=16),
              WUPL[:, mc * 128:(mc + 1) * 128].rearrange("(k p) m -> p k m", p=128), writes=[wu])
        p.dma("gpsimd", wg.ap.rearrange("p (k m) -> p k m", k=16),
              WGTL[:, mc * 128:(mc + 1) * 128].rearrange("(k p) m -> p k m", p=128), writes=[wg])
        scol = mc // 64, mc % 64
        for (o0, n) in NT2:
            psx = pool4[pi % 4]; pi += 1
            mm_group(p, psx.ap[:, 0:n], [wg[:, k * 128:(k + 1) * 128] for k in range(16)],
                     [c[:, o0:o0 + n] for c in x1su_c], reads=[wg] + x1su_c, writes=[psx])
            p.step1("scalar", "activation", gbuf[:, 0:n], psx[:, 0:n], AF.Silu,
                    scale=wsug[:, 64 + mc:64 + mc + 1], reads=[psx, wsug], writes=[gbuf])
            psy = pool4[pi % 4]; pi += 1
            mm_group(p, psy.ap[:, 0:n], [wu[:, k * 128:(k + 1) * 128] for k in range(16)],
                     [c[:, o0:o0 + n] for c in x1su_c], reads=[wu] + x1su_c, writes=[psy])
            p.step1("vector", "tensor_scalar_mul", ubuf[:, 0:n], psy[:, 0:n], wsug[:, mc:mc + 1],
                    reads=[psy, wsug], writes=[ubuf])
            p.step1("vector", "scalar_tensor_tensor", prod_c[mc].ap[:, o0:o0 + n], gbuf[:, 0:n],
                    sudp[:, mc:mc + 1], ubuf[:, 0:n], ALU.mult, ALU.mult,
                    reads=[gbuf, sudp, ubuf], writes=[prod_c[mc]])
    for mc in range(64):
        for (o0, n) in NT2:
            psx = pool4[pi % 4]; pi += 1
            p.step1("tensor", "matmul", psx[:, 0:n], h128b[:, :], prod_c[mc][:, o0:o0 + n],
                    start=True, stop=True, reads=[h128b, prod_c[mc]], writes=[psx])
            p.step1("scalar", "activation", prod_c[mc].ap[:, o0:o0 + n], psx[:, 0:n], AF.Copy,
                    reads=[psx], writes=[prod_c[mc]])
    for st_i, d in enumerate((32, 16, 8, 4, 2, 1)):
        pairs = [(gg, gg + d) for gg in range(64) if (gg // d) % 2 == 0]
        for pi, (ga, gb_) in enumerate(pairs):
            eng = "vector" if pi % 2 == 0 else "gpsimd"
            A, Bc = prod_c[ga], prod_c[gb_]
            p.step1(eng, "tensor_copy", btmp[:, :], A[:, :], reads=[A], writes=[btmp])
            p.step1(eng, "tensor_add", A.ap[:, :], btmp[:, :], Bc[:, :], reads=[btmp, Bc], writes=[A])
            p.step1(eng, "tensor_sub", Bc.ap[:, :], btmp[:, :], Bc[:, :], reads=[btmp, Bc], writes=[Bc])
    p.free(btmp, ubuf, gbuf, wgs[2], wgs[1], wgs[0], wus[2], wus[1], wus[0])

    wd0 = p.sbuf("wd0", [128, 64 * 128], BF16, side="right")
    for mc in range(16):
        wd = wd0
        p.dma("gpsimd", wd.ap.rearrange("p (k m) -> p k m", k=64),
              WDNL[:, mc * 128:(mc + 1) * 128].rearrange("(k p) m -> p k m", p=128), writes=[wd])
        for (o0, n) in NT2:
            psx = pool4[pi % 4]; pi += 1
            mm_group(p, psx.ap[:, 0:n], [wd[:, k * 128:(k + 1) * 128] for k in range(64)],
                     [c[:, o0:o0 + n] for c in prod_c], reads=[wd] + prod_c, writes=[psx])
            p.step1("scalar", "activation", x1su_c[mc].ap[:, o0:o0 + n], psx[:, 0:n], AF.Copy,
                    scale=wsdp[:, mc:mc + 1], reads=[psx, wsdp], writes=[x1su_c[mc]])
    p.free(wd0, prod)

    p.merge(x1su)
    x1f = p.sbuf("x1f", [128, 16 * TOWN], F32, side="right")
    rscr = p.sbuf("rscr6", [128, TOWN], F32, side="right")
    p.dma("sync", x1f.ap.rearrange("p (k t) -> p k t", k=16),
          X1D.rearrange("(k p) t -> p k t", p=128), writes=[x1f],
          extra=[("sp_x1", p.cnt.get("sp_x1", 0))])
    zln = p.sbuf("zln", [128, 16 * TOWN], BF16, side="right")
    emit_ln_c(p, lc, x1su, zln, TOWN, NT2, src_is_bf=True)
    for k in range(16):
        ks = slice(k * TOWN, (k + 1) * TOWN)
        p.step1("scalar", "activation", rscr[:, :], zln[:, ks], AF.Copy, reads=[zln], writes=[rscr])
        p.step1("vector", "tensor_add", x1f[:, ks], x1f[:, ks], rscr[:, :], reads=[x1f, rscr], writes=[x1f])
        p.dma("sync", X2O[k * 128:(k + 1) * 128, :], x1f[:, ks], reads=[x1f])
    emit_ln_c(p, lc, x1f, zln, TOWN, NT2)
    for k in range(16):
        p.dma("sync", X2LNO[k * 128:(k + 1) * 128, :], zln[:, k * TOWN:(k + 1) * TOWN], reads=[zln])
    return p.finalize([(s, p.cnt[s]) for s in sorted(p.store_sems)])


def _pstats(p, lc, eps):
    srow = lc["srow"]; srow2 = lc["srow2"]; srow3 = lc["srow3"]
    ps_r1 = lc["ps_r1"]; ps_r2 = lc["ps_r2"]
    p.step1("vector", "tensor_scalar_mul", srow[0:1, 0:512], ps_r1[0:1, 0:512], 1.0 / 128,
            reads=[ps_r1], writes=[srow])
    p.step1("vector", "tensor_scalar_mul", srow2[0:1, 0:512], ps_r2[0:1, 0:512], 1.0 / 128,
            reads=[ps_r2], writes=[srow2])
    p.step1("vector", "tensor_mul", srow3[0:1, 0:512], srow[0:1, 0:512], srow[0:1, 0:512],
            reads=[srow], writes=[srow3])
    p.step1("vector", "tensor_sub", srow2[0:1, 0:512], srow2[0:1, 0:512], srow3[0:1, 0:512],
            reads=[srow2, srow3], writes=[srow2])
    p.step1("vector", "tensor_scalar_add", srow2[0:1, 0:512], srow2[0:1, 0:512], eps,
            reads=[srow2], writes=[srow2])
    p.step1("scalar", "activation", srow2[0:1, 0:512], srow2[0:1, 0:512], AF.Sqrt,
            reads=[srow2], writes=[srow2])
    p.step1("vector", "reciprocal", srow2[0:1, 0:512], srow2[0:1, 0:512],
            reads=[srow2], writes=[srow2])


def _hadamard(n, dtype):
    H = np.array([[1.0]], np.float32)
    while H.shape[0] < n:
        H = np.block([[H, H], [H, -H]])
    return H.astype(dtype)


def stage_a3c(g, M_full, Q_full, O_full, core):
    s = slice(1024 * core, 1024 * (core + 1))
    return {
        "mo3": _c(M_full[:, s]), "qo3": _c(Q_full[:, s]), "oo3": _c(O_full[:, s]),
        "xt3": _c(g["xT"][:, s]),
        "wol": g["l0_Wo"], "wupl": g["up_l"], "wgtl": g["gt_l"], "wdnl": g["dn_l"],
        "wsug": g["wsug"], "wsdp": g["wsdp"], "suug": g["suug"], "sudp": g["sudp"],
        "h128b": g["h128b"], "mng": g["l0_mng"], "mnb": g["l0_mnb"],
        "gng": g["l0_gng"], "gnb": g["l0_gnb"],
        "ones_col_bf": g["ones_col_bf"], "ones_row_f": g["ones_row_f"],
    }



def build_d1():
    p = P("d1")
    HT = p.din("ht", [2048, TLOC], BF16)
    WQ1 = p.din("wq1", [2048, 1536], F32)
    WK10 = p.din("wk10", [2048, 1536], F32)
    WK11 = p.din("wk11", [2048, 1536], F32)
    WV10 = p.din("wv10", [2048, 3072], F32)
    WV11 = p.din("wv11", [2048, 3072], F32)
    WA1 = p.din("wa1", [2048, 6], F32)
    WB10 = p.din("wb10", [2048, 6], F32)
    WB11 = p.din("wb11", [2048, 6], F32)
    CQW = p.din("cqw", [128, 12 * 4], F32)
    CKW0 = p.din("ckw0", [128, 12 * 4], F32)
    CKW1 = p.din("ckw1", [128, 12 * 4], F32)
    CVW0 = p.din("cvw0", [128, 24 * 4], F32)
    CVW1 = p.din("cvw1", [128, 24 * 4], F32)
    ALOG = p.din("alog", [6, 1], F32)
    DTB = p.din("dtb", [6, 1], F32)
    ONESB = p.din("ones_col_bf", [128, 1], BF16)
    ONESRF = p.din("ones_row_f", [1, 128], F32)
    QHO = p.dout("qho", [1536, 2 * TOWN], BF16)
    KHO = p.dout("kho", [1536, 2 * TOWN], BF16)
    VHO = p.dout("vho", [3072, 2 * TOWN], BF16)
    GBO = p.dout("gbo", [12, 2 * TOWN], F32)

    ht = p.sbuf("htb", [128, 16 * TLOC], BF16)
    lc = mk_ln_ctx(p, TLOC)
    onesb = p.sbuf("c_onesb", [128, 1], BF16)
    onesrf = p.sbuf("c_onesrf", [1, 128], F32)
    lc["onesb"] = onesb; lc["onesrf"] = onesrf
    cw = {"q": p.sbuf("c_cqw", [128, 48], F32), "k0": p.sbuf("c_ckw0", [128, 48], F32),
          "k1": p.sbuf("c_ckw1", [128, 48], F32), "v0": p.sbuf("c_cvw0", [128, 96], F32),
          "v1": p.sbuf("c_cvw1", [128, 96], F32)}
    alog = p.sbuf("c_alog", [6, 1], F32)
    dtb = p.sbuf("c_dtb", [6, 1], F32)
    negea = p.sbuf("negea", [6, 1], F32)
    raw = [p.sbuf(f"raw{i}", [128, TLOC], BF16) for i in range(4)]
    cvt = [p.sbuf(f"cvt{i}", [128, TOWN], BF16) for i in range(4)]
    acc = [p.sbuf(f"acc{i}", [128, TOWN], F32) for i in range(4)]
    wsl = [p.sbuf(f"w{i}b", [128, 16 * 128], BF16) for i in range(3)]
    w0 = wsl[0]
    ilv = p.sbuf("ilv", [128, 2 * TOWN], BF16)
    srow = lc["srow"]; srow2 = lc["srow2"]; rrep = lc["rrep"]
    ps_a = p.psum("ps_a", [128, 512], F32)
    ps_r1 = lc["ps_r1"]; ps_m = lc["ps_m"]
    pr6 = p.sbuf("pr6", [6, TOWN], F32)
    sb6 = [p.sbuf(f"sb6_{i}", [6, TOWN], F32) for i in range(3)]
    gb = p.sbuf("gb", [6, 2 * TOWN], F32)
    ps_s = p.psum("ps_s", [6, 512], F32)

    p.dma("sync", ht.ap.rearrange("p (k t) -> p k t", k=16),
          HT.rearrange("(k p) t -> p k t", p=128), writes=[ht])
    for b, srcdr in ((onesb, ONESB), (onesrf, ONESRF), (cw["q"], CQW), (cw["k0"], CKW0),
                     (cw["k1"], CKW1), (cw["v0"], CVW0), (cw["v1"], CVW1),
                     (alog, ALOG), (dtb, DTB)):
        p.dma("sync", b.ap[:, :], srcdr, writes=[b])
    p.step1("scalar", "activation", negea[0:6, :], alog[0:6, :], AF.Exp, reads=[alog], writes=[negea])
    p.step1("vector", "tensor_scalar_mul", negea[0:6, :], negea[0:6, :], -1.0, reads=[negea], writes=[negea])

    NT3 = [(0, 344), (344, 344), (688, 344)]
    NT2 = [(0, 512), (512, 512)]
    ht_c = [ht[:, k * TLOC:(k + 1) * TLOC] for k in range(16)]

    _wsl_ctr = [0]
    def raw_project(W, mc, dst, wslot):
        wb = wsl[_wsl_ctr[0] % 3]; _wsl_ctr[0] += 1
        p.dma("gpsimd", wb.ap.rearrange("p (k m) -> p k m", k=16),
              W[:, mc * 128:(mc + 1) * 128].rearrange("(k p) m -> p k m", p=128), writes=[wb])
        for (o0, n) in NT3:
            mm_group(p, ps_a.ap[:, 0:n], [wb[:, k * 128:(k + 1) * 128] for k in range(16)],
                     [c[:, o0:o0 + n] for c in ht_c], reads=[wb, ht], writes=[ps_a])
            p.step1("scalar", "activation", dst.ap[:, o0:o0 + n], ps_a[:, 0:n],
                    AF.Copy, reads=[ps_a], writes=[dst])

    def conv_silu(wcol, mc, srcb, dst, lane):
        a0, a1 = (acc[0], acc[1]) if lane == 0 else (acc[2], acc[3])
        eng = "vector"
        p.step1(eng, "tensor_scalar", a0[:, :], srcb[:, 5:5 + TOWN],
                wcol[:, 4 * mc:4 * mc + 1], None, ALU.mult, reads=[srcb, wcol], writes=[a0])
        for j in range(3):
            sa, da = (a0, a1) if j % 2 == 0 else (a1, a0)
            p.step1(eng, "scalar_tensor_tensor", da[:, :], srcb[:, 6 + j:6 + j + TOWN],
                    wcol[:, 4 * mc + j + 1:4 * mc + j + 2], sa[:, :], ALU.mult, ALU.add,
                    reads=[srcb, wcol, sa], writes=[da])
        p.step1("scalar", "activation", dst.ap[:, :], a1[:, :], AF.Silu, reads=[a1], writes=[dst])

    def l2n_pair(c0, c1, scale):
        for (o0, n) in NT2:
            for ci, cb in enumerate((c0, c1)):
                p.step1("scalar", "activation", lc["sqs"][ci].ap[:, 0:n], cb[:, o0:o0 + n],
                        AF.Square, reads=[cb], writes=[lc["sqs"][ci]])
                p.step1("tensor", "matmul", ps_r1[0:1, 0:n], onesb[:, :], lc["sqs"][ci][:, 0:n],
                        start=(ci == 0), stop=(ci == 1), reads=[lc["sqs"][ci], onesb], writes=[ps_r1])
            p.step1("vector", "tensor_scalar_add", srow[0:1, 0:n], ps_r1[0:1, 0:n], 1e-6,
                    reads=[ps_r1], writes=[srow])
            p.step1("scalar", "activation", srow[0:1, 0:n], srow[0:1, 0:n], AF.Sqrt,
                    reads=[srow], writes=[srow])
            p.step1("vector", "reciprocal", srow[0:1, 0:n], srow[0:1, 0:n], reads=[srow], writes=[srow])
            if scale != 1.0:
                p.step1("vector", "tensor_scalar_mul", srow[0:1, 0:n], srow[0:1, 0:n], scale,
                        reads=[srow], writes=[srow])
            bcast_row(p, ps_m.ap[:, 0:n], onesrf[:, :], srow[0:1, 0:n],
                      reads=[srow, onesrf], writes=[ps_m])
            p.step1("vector", "tensor_copy", rrep[:, 0:n], ps_m[:, 0:n], reads=[ps_m], writes=[rrep])
            for cb in (c0, c1):
                p.step1("vector", "tensor_mul", cb.ap[:, o0:o0 + n], cb[:, o0:o0 + n], rrep[:, 0:n],
                        reads=[cb, rrep], writes=[cb])

    for h in range(6):
        for ci, mc in enumerate((2 * h, 2 * h + 1)):
            raw_project(WQ1, mc, raw[ci], ci)
            conv_silu(cw["q"], mc, raw[ci], cvt[ci], ci % 2)
        l2n_pair(cvt[0], cvt[1], float(256.0 ** -0.5))
        for ci, mc in enumerate((2 * h, 2 * h + 1)):
            p.step1("vector", "memset", ilv.ap[:, :], 0.0, reads=[], writes=[ilv])
            p.step1("vector", "tensor_copy", ilv.ap[:, 1::2], cvt[ci][:, :], reads=[cvt[ci]], writes=[ilv])
            p.dma("sync", QHO[mc * 128:(mc + 1) * 128, :], ilv[:, :], reads=[ilv])
    for h in range(6):
        for ci, mc in enumerate((2 * h, 2 * h + 1)):
            raw_project(WK10, mc, raw[ci], ci)
            conv_silu(cw["k0"], mc, raw[ci], cvt[ci], ci % 2)
            raw_project(WK11, mc, raw[2 + ci], ci)
            conv_silu(cw["k1"], mc, raw[2 + ci], cvt[2 + ci], ci % 2)
        l2n_pair(cvt[0], cvt[1], 1.0)
        l2n_pair(cvt[2], cvt[3], 1.0)
        for ci, mc in enumerate((2 * h, 2 * h + 1)):
            p.step1("vector", "tensor_copy", ilv.ap[:, 0::2], cvt[ci][:, :], reads=[cvt[ci]], writes=[ilv])
            p.step1("gpsimd", "tensor_copy", ilv.ap[:, 1::2], cvt[2 + ci][:, :], reads=[cvt[2 + ci]], writes=[ilv])
            p.dma("sync", KHO[mc * 128:(mc + 1) * 128, :], ilv[:, :], reads=[ilv])
    for mc in range(24):
        raw_project(WV10, mc, raw[0], 0)
        conv_silu(cw["v0"], mc, raw[0], cvt[0], 0)
        raw_project(WV11, mc, raw[1], 1)
        conv_silu(cw["v1"], mc, raw[1], cvt[1], 1)
        p.step1("vector", "tensor_copy", ilv.ap[:, 0::2], cvt[0][:, :], reads=[cvt[0]], writes=[ilv])
        p.step1("gpsimd", "tensor_copy", ilv.ap[:, 1::2], cvt[1][:, :], reads=[cvt[1]], writes=[ilv])
        p.dma("sync", VHO[mc * 128:(mc + 1) * 128, :], ilv[:, :], reads=[ilv])

    ht_own = [ht[:, k * TLOC + HALO:k * TLOC + HALO + TOWN] for k in range(16)]
    for wi, (W, dst, bias) in enumerate(((WA1, pr6, dtb), (WB10, sb6[0], None), (WB11, sb6[1], None))):
        p.dma("gpsimd", w0.ap.rearrange("p (k m) -> p k m", k=16)[:, :, 0:6],
              W.rearrange("(k p) m -> p k m", p=128), writes=[w0])
        for (o0, n) in NT2:
            mm_group(p, ps_s.ap[0:6, 0:n], [w0[:, k * 128:k * 128 + 6] for k in range(16)],
                     [c[:, o0:o0 + n] for c in ht_own], reads=[w0, ht], writes=[ps_s])
            if wi == 0:
                p.step1("vector", "tensor_scalar_add", dst.ap[0:6, o0:o0 + n], ps_s[0:6, 0:n],
                        dtb[0:6, :], reads=[ps_s, dtb], writes=[dst])
            else:
                p.step1("scalar", "activation", dst.ap[0:6, o0:o0 + n], ps_s[0:6, 0:n], AF.Sigmoid,
                        reads=[ps_s], writes=[dst])
    p.step1("scalar", "activation", sb6[2].ap[0:6, :], pr6[0:6, :], AF.Abs, reads=[pr6], writes=[sb6[2]])
    p.step1("scalar", "activation", sb6[2].ap[0:6, :], sb6[2][0:6, :], AF.Exp, scale=-1.0,
            reads=[sb6[2]], writes=[sb6[2]])
    p.step1("scalar", "activation", sb6[2].ap[0:6, :], sb6[2][0:6, :], AF.Ln, bias=1.0,
            reads=[sb6[2]], writes=[sb6[2]])
    p.step1("vector", "tensor_scalar_max", pr6.ap[0:6, :], pr6[0:6, :], 0.0, reads=[pr6], writes=[pr6])
    p.step1("vector", "tensor_add", pr6.ap[0:6, :], pr6[0:6, :], sb6[2][0:6, :],
            reads=[pr6, sb6[2]], writes=[pr6])
    p.step1("vector", "tensor_scalar_mul", pr6.ap[0:6, :], pr6[0:6, :], negea[0:6, :],
            reads=[pr6, negea], writes=[pr6])
    p.step1("vector", "memset", gb.ap[0:6, :], 0.0, reads=[], writes=[gb])
    p.step1("vector", "tensor_copy", gb.ap[0:6, 0::2], pr6[0:6, :], reads=[pr6], writes=[gb])
    p.dma("sync", GBO[0:6, :], gb[0:6, :], reads=[gb])
    p.step1("vector", "tensor_copy", gb.ap[0:6, 0::2], sb6[0][0:6, :], reads=[sb6[0], gb], writes=[gb])
    p.step1("vector", "tensor_copy", gb.ap[0:6, 1::2], sb6[1][0:6, :], reads=[sb6[1]], writes=[gb])
    p.dma("sync", GBO[6:12, :], gb[0:6, :], reads=[gb])
    return p.finalize([(s, p.cnt[s]) for s in sorted(p.store_sems)])


def stage_d1(g, X2LN_pad, core):
    s = slice(1024 * core, 1024 * core + TLOC)
    return {
        "ht": _c(X2LN_pad[:, s]),
        "wq1": g["l1_Wq"], "wk10": g["l1_Wk0"], "wk11": g["l1_Wk1"],
        "wv10": g["l1_Wv0"], "wv11": g["l1_Wv1"],
        "wa1": g["l1_Wa"], "wb10": g["l1_Wb0"], "wb11": g["l1_Wb1"],
        "cqw": g["l1_cqw"], "ckw0": g["l1_ckw0"], "ckw1": g["l1_ckw1"],
        "cvw0": g["l1_cvw0"], "cvw1": g["l1_cvw1"],
        "alog": g["l1_alog"], "dtb": g["l1_dtb"],
        "ones_col_bf": g["ones_col_bf"], "ones_row_f": g["ones_row_f"],
    }



def build_d2(nseg=16):
    p = P("d2")
    QH2 = p.din("qh2", [256, 16384], BF16)
    KH2 = p.din("kh2", [256, 16384], BF16)
    VH2 = p.din("vh2", [512, 16384], BF16)
    GR = p.din("gr", [1, 16384], F32)
    BR = p.din("br", [1, 16384], F32)
    IDB = p.din("id_bf", [128, 128], BF16)
    IDF = p.din("id_f", [128, 128], F32)
    ID64B = p.din("id128b", [128, 128], BF16)
    UTI = p.din("utri_inc", [128, 128], F32)
    UTS = p.din("utri_strict", [128, 128], F32)
    LTS = p.din("ltri_strict", [128, 128], F32)
    ONESRF = p.din("ones_row_f", [1, 128], F32)
    OO2 = p.dout("oo2", [16384, 512], BF16)
    SCR_D = p.nc.dram_tensor("rowscr", [16, 128], F32, kind="Internal").ap()

    idb = p.sbuf("c_idb", [128, 128], BF16)
    idf = p.sbuf("c_idf", [128, 128], F32)
    id128 = p.sbuf("c_id128", [128, 128], BF16)
    uti = p.sbuf("c_uti", [128, 128], F32)
    uts = p.sbuf("c_uts", [128, 128], F32)
    lts = p.sbuf("c_lts", [128, 128], F32)
    onesrf = p.sbuf("c_onesrf", [1, 128], F32)
    ones8 = p.sbuf("ones8", [8, 128], F32)
    kseg = p.sbuf("kseg", [128, 2 * 1024], BF16)
    qseg = p.sbuf("qseg", [128, 2 * 1024], BF16)
    vseg = p.sbuf("vseg", [128, 4 * 1024], BF16)
    gseg = p.sbuf("gseg", [8, 128], F32)
    bseg = p.sbuf("bseg", [8, 128], F32)
    cumg = p.sbuf("cumg", [8, 128], F32)
    cumgt = p.sbuf("cumgt", [128, 8], F32)
    betat = p.sbuf("betat", [128, 8], F32)
    growc = p.sbuf("growc", [1, 1024], F32)
    brwc = p.sbuf("brwc", [1, 1024], F32)
    CB = []
    for par in range(2):
        CB.append({
            "lam": p.sbuf(f"lam{par}", [128, 1], F32),
            "lamend": p.sbuf(f"lamend{par}", [128, 1], F32),
            "lesc": p.sbuf(f"lesc{par}", [1, 1], F32),
            "e1": p.sbuf(f"e1_{par}", [128, 128], F32),
            "e2": p.sbuf(f"e2_{par}", [128, 128], F32),
            "s64a": p.sbuf(f"s64a{par}", [128, 128], F32),
            "s64b": p.sbuf(f"s64b{par}", [128, 128], F32),
            "wpb": p.sbuf(f"wpb{par}", [128, 128], BF16),
            "abf": [p.sbuf(f"abf{par}_{i}", [128, 128], BF16) for i in range(2)],
            "bbf": [p.sbuf(f"bbf{par}_{i}", [128, 128], BF16) for i in range(2)],
            "ptb": [p.sbuf(f"ptb{par}_{i}", [128, 128], BF16) for i in range(2)],
            "vtb": p.sbuf(f"vtb{par}", [128, 512], F32),
            "rbf": p.sbuf(f"rbf{par}", [128, 512], BF16),
            "ubf": p.sbuf(f"ubf{par}", [128, 512], BF16),
            "obf": p.sbuf(f"obf{par}", [128, 512], BF16),
            "ktb": p.sbuf(f"ktb{par}", [128, 256], BF16),
            "scro": p.sbuf(f"scro{par}", [128, 512], F32),
        })
    sf32 = [p.sbuf(f"sf32_{i}", [128, 512], F32) for i in range(2)]
    sbfb = [p.sbuf(f"sbf_{i}", [128, 512], BF16) for i in range(2)]

    ps_p1 = p.psum("ps_p1", [128, 512], F32)
    ps_p2 = p.psum("ps_p2", [128, 512], F32)
    ps_r = p.psum("ps_r", [128, 512], F32)
    ps_u = p.psum("ps_u", [128, 512], F32)
    ps_o = p.psum("ps_o", [128, 512], F32)
    ps_t = p.psum("ps_t", [128, 768], BF16)
    ps_s1 = p.psum("ps_s1", [128, 512], F32)
    ps_s2 = p.psum("ps_s2", [128, 512], F32)

    for b, srcdr in ((idb, IDB), (idf, IDF), (id128, ID64B), (uti, UTI), (uts, UTS),
                     (lts, LTS), (onesrf, ONESRF)):
        p.dma("sync", b.ap[:, :], srcdr, writes=[b])
    p.step1("vector", "memset", ones8.ap[:, :], 1.0, reads=[], writes=[ones8])
    for i in range(2):
        p.step1("vector", "memset", sf32[i].ap[:, :], 0.0, reads=[], writes=[sf32[i]])
        p.step1("vector", "memset", sbfb[i].ap[:, :], 0.0, reads=[], writes=[sbfb[i]])

    for seg in range(nseg):
        ssl = slice(seg * 1024, (seg + 1) * 1024)
        p.dma("sync", kseg.ap.rearrange("p (c t) -> p c t", c=2),
              KH2[:, ssl].rearrange("(c p) t -> p c t", p=128), writes=[kseg])
        p.dma("sync", qseg.ap.rearrange("p (c t) -> p c t", c=2),
              QH2[:, ssl].rearrange("(c p) t -> p c t", p=128), writes=[qseg])
        p.dma("sync", vseg.ap.rearrange("p (c t) -> p c t", c=4),
              VH2[:, ssl].rearrange("(c p) t -> p c t", p=128), writes=[vseg])
        p.dma("sync", gseg.ap[0:8, :], GR[0:1, ssl].rearrange("a (c t) -> (a c) t", c=8), writes=[gseg])
        p.dma("sync", bseg.ap[0:8, :], BR[0:1, ssl].rearrange("a (c t) -> (a c) t", c=8), writes=[bseg])
        p.step1("vector", "tensor_tensor_scan", cumg.ap[0:8, :], ones8[0:8, :], gseg[0:8, :],
                0.0, ALU.mult, ALU.add, reads=[ones8, gseg], writes=[cumg])
        p.step1("tensor", "transpose", ps_p1[:, 0:8], cumg[0:8, :], idf[0:8, 0:8],
                reads=[cumg, idf], writes=[ps_p1])
        p.step1("vector", "tensor_copy", cumgt.ap[:, 0:8], ps_p1[:, 0:8], reads=[ps_p1], writes=[cumgt])
        p.step1("tensor", "transpose", ps_p1[:, 8:16], bseg[0:8, :], idf[0:8, 0:8],
                reads=[bseg, idf], writes=[ps_p1])
        p.step1("vector", "tensor_copy", betat.ap[:, 0:8], ps_p1[:, 8:16], reads=[ps_p1], writes=[betat])
        p.dma("sync", SCR_D[0:8, :], cumg[0:8, :], sem="d_rows", reads=[cumg])
        p.dma("sync", SCR_D[8:16, :], bseg[0:8, :], sem="d_rows", reads=[bseg])
        p.dma("sync", growc.ap[0:1, :], SCR_D[0:8, :].rearrange("c t -> (c t)"),
              writes=[growc], extra=[("d_rows", 0)])
        p.dma("sync", brwc.ap[0:1, :], SCR_D[8:16, :].rearrange("c t -> (c t)"),
              writes=[brwc], extra=[("d_rows", 0)])

        for c8 in range(8):
            cb_ = CB[(seg * 8 + c8) % 2]
            lam = cb_["lam"]; lamend = cb_["lamend"]; lesc = cb_["lesc"]
            e1 = cb_["e1"]; e2 = cb_["e2"]; s64a = cb_["s64a"]; s64b = cb_["s64b"]
            wpb = cb_["wpb"]; abf = cb_["abf"]; bbf = cb_["bbf"]; ptb = cb_["ptb"]
            vtb = cb_["vtb"]; rbf = cb_["rbf"]; ubf = cb_["ubf"]; obf = cb_["obf"]
            ktb = cb_["ktb"]; scro = cb_["scro"]
            csl = slice(c8 * 128, (c8 + 1) * 128)
            kc = [kseg[:, 1024 * i + c8 * 128:1024 * i + (c8 + 1) * 128] for i in range(2)]
            qc = [qseg[:, 1024 * i + c8 * 128:1024 * i + (c8 + 1) * 128] for i in range(2)]
            gcol = cumgt[:, c8:c8 + 1]
            bcol = betat[:, c8:c8 + 1]
            for i in range(2):
                p.step1("tensor", "matmul", ps_p1[:, 0:128], kc[i], kc[i],
                        start=(i == 0), stop=(i == 1), reads=[kseg], writes=[ps_p1])
            for i in range(2):
                p.step1("tensor", "matmul", ps_p1[:, 128:256], kc[i], qc[i],
                        start=(i == 0), stop=(i == 1), reads=[kseg, qseg], writes=[ps_p1])
            p.step1("tensor", "matmul", ps_p1[:, 256:384], onesrf[0:1, :], growc[0:1, csl],
                    start=True, stop=True, reads=[onesrf, growc], writes=[ps_p1])
            p.step1("tensor", "matmul", ps_p1[:, 384:512], onesrf[0:1, :], brwc[0:1, csl],
                    start=True, stop=True, reads=[onesrf, brwc], writes=[ps_p1])
            p.step1("vector", "tensor_scalar", e1.ap[:, :], ps_p1[:, 256:384], gcol, None,
                    ALU.subtract, reads=[ps_p1, cumgt], writes=[e1])
            p.step1("vector", "tensor_scalar_min", e1.ap[:, :], e1[:, :], 0.0, reads=[e1], writes=[e1])
            p.step1("scalar", "activation", e1.ap[:, :], e1[:, :], AF.Exp, reads=[e1], writes=[e1])
            p.step1("vector", "tensor_scalar", e2.ap[:, :], ps_p1[:, 256:384], -1.0, gcol,
                    ALU.mult, ALU.add, reads=[ps_p1, cumgt], writes=[e2])
            p.step1("vector", "tensor_scalar_min", e2.ap[:, :], e2[:, :], 0.0, reads=[e2], writes=[e2])
            p.step1("scalar", "activation", e2.ap[:, :], e2[:, :], AF.Exp, reads=[e2], writes=[e2])
            p.step1("vector", "tensor_mul", s64a.ap[:, :], e1[:, :], uti[:, :],
                    reads=[e1, uti], writes=[s64a])
            p.step1("vector", "tensor_mul", wpb.ap[:, :], s64a[:, :], ps_p1[:, 128:256],
                    reads=[s64a, ps_p1], writes=[wpb])
            p.step1("vector", "tensor_mul", s64a.ap[:, :], e1[:, :], uts[:, :],
                    reads=[e1, uts], writes=[s64a])
            p.step1("vector", "tensor_mul", s64a.ap[:, :], s64a[:, :], ps_p1[:, 384:512],
                    reads=[s64a, ps_p1], writes=[s64a])
            p.step1("vector", "tensor_mul", bbf[0].ap[:, :], s64a[:, :], ps_p1[:, 0:128],
                    reads=[s64a, ps_p1], writes=[bbf[0]])
            p.step1("vector", "tensor_mul", s64b.ap[:, :], e2[:, :], lts[:, :],
                    reads=[e2, lts], writes=[s64b])
            p.step1("vector", "tensor_scalar_mul", s64b.ap[:, :], s64b[:, :], bcol,
                    reads=[s64b, betat], writes=[s64b])
            p.step1("vector", "tensor_mul", abf[0].ap[:, :], s64b[:, :], ps_p1[:, 0:128],
                    reads=[s64b, ps_p1], writes=[abf[0]])
            p.step1("vector", "tensor_sub", ptb[0].ap[:, :], id128[:, :], bbf[0][:, :],
                    reads=[id128, bbf[0]], writes=[ptb[0]])
            cura, curb, curpt = 0, 0, 0
            for st_i in range(6):
                na, nb, npt = 1 - cura, 1 - curb, 1 - curpt
                p.step1("tensor", "matmul", ps_p2[:, 0:128], bbf[curb][:, :], abf[cura][:, :],
                        start=True, stop=True, reads=[bbf[curb], abf[cura]], writes=[ps_p2])
                p.step1("scalar", "activation", abf[na].ap[:, :], ps_p2[:, 0:128], AF.Copy,
                        reads=[ps_p2], writes=[abf[na]])
                if st_i < 5:
                    p.step1("tensor", "matmul", ps_p2[:, 128:256], abf[cura][:, :], bbf[curb][:, :],
                            start=True, stop=True, reads=[abf[cura], bbf[curb]], writes=[ps_p2])
                    p.step1("scalar", "activation", bbf[nb].ap[:, :], ps_p2[:, 128:256], AF.Copy,
                            reads=[ps_p2], writes=[bbf[nb]])
                p.step1("tensor", "matmul", ps_p2[:, 256:384], abf[na][:, :], ptb[curpt][:, :],
                        start=True, stop=True, reads=[abf[na], ptb[curpt]], writes=[ps_p2])
                p.step1("vector", "tensor_add", ptb[npt].ap[:, :], ps_p2[:, 256:384], ptb[curpt][:, :],
                        reads=[ps_p2, ptb[curpt]], writes=[ptb[npt]])
                cura, curb, curpt = na, nb, npt
            PT = ptb[curpt]
            p.step1("scalar", "activation", lam.ap[:, 0:1], gcol, AF.Exp, reads=[cumgt], writes=[lam])
            for i in range(2):
                p.step1("tensor", "matmul", ps_r[:, 0:512], kc[i], sbfb[i][:, :],
                        start=(i == 0), stop=(i == 1), reads=[kseg, sbfb[i]], writes=[ps_r])
            for i in range(4):
                p.step1("tensor", "transpose", ps_t[:, i * 128:(i + 1) * 128],
                        vseg[:, 1024 * i + c8 * 128:1024 * i + (c8 + 1) * 128], idb[:, :],
                        reads=[vseg, idb], writes=[ps_t])
            p.step1("vector", "tensor_copy", vtb.ap[:, :], ps_t[:, 0:512], reads=[ps_t], writes=[vtb])
            p.step1("vector", "tensor_scalar", scro.ap[:, :], ps_r[:, 0:512], lam[:, 0:1], None,
                    ALU.mult, reads=[ps_r, lam], writes=[scro])
            p.step1("vector", "tensor_sub", scro.ap[:, :], vtb[:, :], scro[:, :],
                    reads=[vtb, scro], writes=[scro])
            p.step1("vector", "tensor_scalar", rbf.ap[:, :], scro[:, :], bcol, None,
                    ALU.mult, reads=[scro, betat], writes=[rbf])
            p.step1("tensor", "matmul", ps_u[:, 0:512], PT[:, :], rbf[:, :],
                    start=True, stop=True, reads=[PT, rbf], writes=[ps_u])
            p.step1("scalar", "activation", ubf.ap[:, :], ps_u[:, 0:512], AF.Copy,
                    reads=[ps_u], writes=[ubf])
            p.step1("tensor", "matmul", ps_o[:, 0:512], wpb[:, :], ubf[:, :],
                    start=True, stop=True, reads=[wpb, ubf], writes=[ps_o])
            for i in range(2):
                p.step1("tensor", "matmul", ps_r[:, 0:512], qc[i], sbfb[i][:, :],
                        start=(i == 0), stop=(i == 1), reads=[qseg, sbfb[i]], writes=[ps_r])
            p.step1("vector", "tensor_scalar", scro.ap[:, :], ps_r[:, 0:512], lam[:, 0:1], None,
                    ALU.mult, reads=[ps_r, lam], writes=[scro])
            p.step1("vector", "tensor_add", obf.ap[:, :], scro[:, :], ps_o[:, 0:512],
                    reads=[scro, ps_o], writes=[obf])
            cg = 8 * seg + c8
            p.dma("sync", OO2[128 * cg:128 * (cg + 1), :], obf[:, :], reads=[obf])
            for i in range(2):
                p.step1("tensor", "transpose", ps_t[:, 512 + i * 128:512 + (i + 1) * 128],
                        kc[i], idb[:, :], reads=[kseg, idb], writes=[ps_t])
            p.step1("vector", "tensor_copy", ktb.ap[:, :], ps_t[:, 512:768], reads=[ps_t], writes=[ktb])
            p.step1("vector", "tensor_scalar", ktb.ap[:, :], ktb[:, :], e1[:, 127:128], None,
                    ALU.mult, reads=[ktb, e1], writes=[ktb])
            p.step1("scalar", "activation", lesc.ap[0:1, 0:1], growc[0:1, 128 * c8 + 127:128 * c8 + 128],
                    AF.Exp, reads=[growc], writes=[lesc])
            p.step1("tensor", "matmul", ps_p2[:, 384:385], onesrf[0:1, :], lesc[0:1, 0:1],
                    start=True, stop=True, reads=[onesrf, lesc], writes=[ps_p2])
            p.step1("vector", "tensor_copy", lamend.ap[:, 0:1], ps_p2[:, 384:385],
                    reads=[ps_p2], writes=[lamend])
            for i in range(2):
                pss = (ps_s1, ps_s2)[i]
                p.step1("tensor", "matmul", pss[:, 0:512], ktb[:, i * 128:(i + 1) * 128], ubf[:, :],
                        start=True, stop=True, reads=[ktb, ubf], writes=[pss])
                p.step1("vector", "scalar_tensor_tensor", sf32[i].ap[:, :], sf32[i][:, :],
                        lamend[:, 0:1], pss[:, 0:512], ALU.mult, ALU.add,
                        reads=[sf32[i], lamend, pss], writes=[sf32[i]])
                p.step1("scalar", "activation", sbfb[i].ap[:, :], sf32[i][:, :], AF.Copy,
                        reads=[sf32[i]], writes=[sbfb[i]])
    return p.finalize([(s, p.cnt[s]) for s in sorted(p.store_sems)])


def stage_d2(g, QH, KH, VH, GB, core):
    h = core if core < 6 else 0
    return {
        "qh2": _c(QH[256 * h:256 * (h + 1), :]),
        "kh2": _c(KH[256 * h:256 * (h + 1), :]),
        "vh2": _c(VH[512 * h:512 * (h + 1), :]),
        "gr": _c(GB[h:h + 1, :]),
        "br": _c(GB[6 + h:7 + h, :]),
        "id_bf": g["id_bf"], "id_f": g["id_f"], "id128b": g["id_bf"],
        "utri_inc": g["utri_inc"], "utri_strict": g["utri_strict"], "ltri_strict": g["ltri_strict"],
        "ones_row_f": g["ones_row_f"],
    }



def build_d3e():
    p = P("d3e")
    OH = p.din("oh", [TOWN, 3072], BF16)
    X2T = p.din("x2t", [2048, TOWN], F32)
    X2LNT = p.din("x2lnt", [2048, TOWN], BF16)
    WG1 = p.din("wg1", [2048, 3072], F32)
    WO1 = p.din("wo1", [3072, 2048], F32)
    ONW = p.din("onw", [1, 512], F32)
    GATEW = p.din("gatew", [2048, 8], F32)
    W1L = p.din("w1l", [8, 2048, 1024], F32)
    W3L = p.din("w3l", [8, 2048, 1024], F32)
    W2L = p.din("w2l", [8, 1024, 2048], F32)
    WS1L = p.din("ws1l", [2048, 1024], F32)
    WS3L = p.din("ws3l", [2048, 1024], F32)
    WS2L = p.din("ws2l", [1024, 2048], F32)
    ONESB = p.din("ones_col_bf", [128, 1], BF16)
    ONESRF = p.din("ones_row_f", [1, 128], F32)
    IDB = p.din("id_bf", [128, 128], BF16)
    X4O = p.dout("x4o", [2048, TOWN], F32)
    CROW_D = p.nc.dram_tensor("crow_d", [8, TOWN], F32, kind="Internal").ap()

    NT2 = [(0, 512), (512, 512)]
    lc = mk_ln_ctx(p, TOWN)
    onesb = p.sbuf("c_onesb", [128, 1], BF16)
    onesrf = p.sbuf("c_onesrf", [1, 128], F32)
    lc["onesb"] = onesb; lc["onesrf"] = onesrf
    idb = p.sbuf("c_idb", [128, 128], BF16)
    onwrep = p.sbuf("onwrep", [128, 512], F32)
    onwrow = p.sbuf("onwrow", [1, 512], F32)

    yacc = p.sbuf("yacc", [128, 16 * TOWN], F32)
    srow = lc["srow"]; srow2 = lc["srow2"]; rrep = lc["rrep"]; scr = lc["scr"]; scr2 = lc["scr2"]
    ps_r1 = lc["ps_r1"]; ps_m = lc["ps_m"]; ps_n = lc["ps_n"]
    ps_a = p.psum("ps_a", [128, 512], F32)
    ps_b = p.psum("ps_b", [128, 512], F32)
    ps_t = p.psum("ps_t", [128, 1024], BF16)

    X3D = p.nc.dram_tensor("x3spill", [2048, TOWN], F32, kind="Internal").ap()
    for b, srcdr in ((onesb, ONESB), (onesrf, ONESRF), (idb, IDB), (onwrow, ONW)):
        p.dma("sync", b.ap[:, :], srcdr, writes=[b])
    x3ln = p.sbuf("x3ln", [128, 16 * TOWN], BF16, side="right")

    bcast_row(p, ps_m.ap[:, 0:512], onesrf[:, :], onwrow[0:1, :], reads=[onwrow, onesrf], writes=[ps_m])
    p.step1("vector", "tensor_copy", onwrep.ap[:, :], ps_m[:, 0:512], reads=[ps_m], writes=[onwrep])

    x3ln_r = x3ln
    x2lnf = p.sbuf("x2lnf", [128, 16 * TOWN], BF16, side="right")
    p.dma("sync", x2lnf.ap.rearrange("p (k t) -> p k t", k=16),
          X2LNT.rearrange("(k p) t -> p k t", p=128), writes=[x2lnf])
    ggall = p.sbuf("ggall", [128, 4 * 3072], BF16, side="right")
    wgb = p.sbuf("wgb", [128, 16 * 512], BF16, side="right")
    otile = p.sbuf("otile", [128, 3072], BF16, side="right")
    oft = p.sbuf("oft", [128, 24 * 128], BF16, side="right")
    wol = [p.sbuf("wol0", [128, 24 * 128], BF16, side="right")] * 2
    rsq = p.sbuf("rsq", [128, 1], F32, side="right")
    pool4 = [ps_a, ps_b, ps_m, ps_n]
    pi = 0
    for half in range(2):
        tts = range(4 * half, 4 * half + 4)
        for ns in range(6):
            p.dma("gpsimd", wgb.ap.rearrange("p (k m) -> p k m", k=16),
                  WG1[:, ns * 512:(ns + 1) * 512].rearrange("(k p) m -> p k m", p=128), writes=[wgb])
            for tt in tts:
                ps = pool4[pi % 4]; pi += 1
                mm_group(p, ps.ap[:, 0:512],
                         [x2lnf[:, k * TOWN + tt * 128:k * TOWN + (tt + 1) * 128] for k in range(16)],
                         [wgb[:, k * 512:(k + 1) * 512] for k in range(16)],
                         reads=[x2lnf, wgb], writes=[ps])
                p.step1("scalar", "activation",
                        ggall.ap[:, (tt - 4 * half) * 3072 + ns * 512:(tt - 4 * half) * 3072 + (ns + 1) * 512],
                        ps[:, 0:512], AF.Silu, reads=[ps], writes=[ggall])
        for tt in tts:
            tsl = slice(tt * 128, (tt + 1) * 128)
            g0 = (tt - 4 * half) * 3072
            p.dma("sync", otile.ap[:, :], OH[tsl, :], writes=[otile])
            for h in range(6):
                hsl = slice(h * 512, (h + 1) * 512)
                p.step1("scalar", "activation", lc["sqs"][0].ap[:, 0:512], otile[:, hsl], AF.Square,
                        reads=[otile], writes=[lc["sqs"][0]])
                p.step1("vector", "tensor_reduce", rsq.ap[:, 0:1], lc["sqs"][0][:, 0:512],
                        mybir.AxisListType.X, ALU.add, reads=[lc["sqs"][0]], writes=[rsq])
                p.step1("vector", "tensor_scalar", rsq.ap[:, 0:1], rsq[:, 0:1], 1.0 / 512, 1e-5,
                        ALU.mult, ALU.add, reads=[rsq], writes=[rsq])
                p.step1("scalar", "activation", rsq.ap[:, 0:1], rsq[:, 0:1], AF.Sqrt,
                        reads=[rsq], writes=[rsq])
                p.step1("vector", "reciprocal", rsq.ap[:, 0:1], rsq[:, 0:1], reads=[rsq], writes=[rsq])
                p.step1("vector", "tensor_scalar", scr.ap[:, 0:512], otile[:, hsl], rsq[:, 0:1], None,
                        ALU.mult, reads=[otile, rsq], writes=[scr])
                p.step1("vector", "tensor_mul", scr.ap[:, 0:512], scr[:, 0:512], onwrep[:, :],
                        reads=[scr, onwrep], writes=[scr])
                p.step1("vector", "tensor_mul", otile.ap[:, hsl], scr[:, 0:512],
                        ggall[:, g0 + h * 512:g0 + (h + 1) * 512],
                        reads=[scr, ggall], writes=[otile])
            for v in range(24):
                p.step1("tensor", "transpose", ps_t[:, (v % 8) * 128:(v % 8 + 1) * 128],
                        otile[:, v * 128:(v + 1) * 128], idb[:, :], reads=[otile, idb], writes=[ps_t])
                p.step1("scalar", "activation", oft.ap[:, v * 128:(v + 1) * 128],
                        ps_t[:, (v % 8) * 128:(v % 8 + 1) * 128], AF.Copy, reads=[ps_t], writes=[oft])
            for mc in range(16):
                wb = wol[mc % 2]
                p.dma("gpsimd", wb.ap.rearrange("p (k m) -> p k m", k=24)[:, 0:24, :],
                      WO1[:, mc * 128:(mc + 1) * 128].rearrange("(k p) m -> p k m", p=128), writes=[wb])
                ps = pool4[pi % 4]; pi += 1
                mm_group(p, ps.ap[:, 0:128], [wb[:, k * 128:(k + 1) * 128] for k in range(24)],
                         [oft[:, k * 128:(k + 1) * 128] for k in range(24)],
                         reads=[wb, oft], writes=[ps])
                p.step1("vector", "tensor_copy", yacc.ap[:, mc * TOWN + tt * 128:mc * TOWN + (tt + 1) * 128],
                        ps[:, 0:128], reads=[ps], writes=[yacc])
    p.free(rsq)
    p.free(wol[0], oft, otile, wgb, ggall, x2lnf)

    x2 = p.sbuf("x2b", [128, 16 * TOWN], F32, side="right")
    p.dma("sync", x2.ap.rearrange("p (k t) -> p k t", k=16),
          X2T.rearrange("(k p) t -> p k t", p=128), writes=[x2])
    emit_ln_c(p, lc, yacc, x3ln, TOWN, NT2)
    for k in range(16):
        for (o0, n) in NT2:
            p.step1("scalar", "activation", scr2.ap[:, 0:n], x3ln[:, k * TOWN + o0:k * TOWN + o0 + n],
                    AF.Copy, reads=[x3ln], writes=[scr2])
            p.step1("vector", "tensor_add", x2.ap[:, k * TOWN + o0:k * TOWN + o0 + n],
                    x2[:, k * TOWN + o0:k * TOWN + o0 + n], scr2[:, 0:n], reads=[x2, scr2], writes=[x2])
        p.dma("sync", X3D[k * 128:(k + 1) * 128, :], x2[:, k * TOWN:(k + 1) * TOWN], sem="sp_x3", reads=[x2])
    emit_ln_c(p, lc, x2, x3ln, TOWN, NT2)
    p.free(x2)

    gw = p.sbuf("gw", [128, 8], BF16, side="right")
    sc8 = p.sbuf("sc8", [128, 8], F32, side="right")
    m1c = p.sbuf("m1c", [128, 1], F32, side="right")
    combt = p.sbuf("combt", [8, TOWN], F32, side="right")
    x3ln_c = p.subdiv(x3ln, 16, TOWN)
    for tt in range(8):
        tsl = slice(tt * 128, (tt + 1) * 128)
        for k in range(16):
            if tt == 0 and k == 0:
                pass
            p.dma("gpsimd", gw.ap[:, :], GATEW[k * 128:(k + 1) * 128, :], writes=[gw])
            p.step1("tensor", "matmul", ps_a[:, 0:8], x3ln_c[k][:, tsl], gw[:, :],
                    start=(k == 0), stop=(k == 15), reads=[x3ln_c[k], gw], writes=[ps_a])
        p.step1("vector", "reduce_max", m1c.ap[:, 0:1], ps_a[:, 0:8], mybir.AxisListType.X,
                reads=[ps_a], writes=[m1c])
        p.step1("vector", "tensor_scalar", sc8.ap[:, :], ps_a[:, 0:8], m1c[:, 0:1], None,
                ALU.subtract, reads=[ps_a, m1c], writes=[sc8])
        p.step1("scalar", "activation", sc8.ap[:, :], sc8[:, :], AF.Exp, reads=[sc8], writes=[sc8])
        p.step1("vector", "tensor_reduce", m1c.ap[:, 0:1], sc8[:, :], mybir.AxisListType.X, ALU.add,
                reads=[sc8], writes=[m1c])
        p.step1("vector", "reciprocal", m1c.ap[:, 0:1], m1c[:, 0:1], reads=[m1c], writes=[m1c])
        p.step1("vector", "tensor_scalar", sc8.ap[:, :], sc8[:, :], m1c[:, 0:1], None, ALU.mult,
                reads=[sc8, m1c], writes=[sc8])
        p.step1("vector", "reduce_max", m1c.ap[:, 0:1], sc8[:, :], mybir.AxisListType.X,
                reads=[sc8], writes=[m1c])
        p.step1("vector", "tensor_scalar", scr.ap[:, 0:8], sc8[:, :], m1c[:, 0:1], None,
                ALU.is_ge, reads=[sc8, m1c], writes=[scr])
        p.step1("vector", "scalar_tensor_tensor", scr.ap[:, 8:16], scr[:, 0:8], -1e30,
                sc8[:, :], ALU.mult, ALU.add, reads=[scr, sc8], writes=[scr])
        p.step1("vector", "reduce_max", m1c.ap[:, 0:1], scr[:, 8:16], mybir.AxisListType.X,
                reads=[scr], writes=[m1c])
        p.step1("vector", "tensor_scalar", scr.ap[:, 0:8], sc8[:, :], m1c[:, 0:1], None,
                ALU.is_ge, reads=[sc8, m1c], writes=[scr])
        p.step1("vector", "tensor_mul", sc8.ap[:, :], sc8[:, :], scr[:, 0:8],
                reads=[sc8, scr], writes=[sc8])
        p.step1("scalar", "activation", lc["sqs"][0].ap[:, 0:8], sc8[:, :], AF.Copy,
                reads=[sc8], writes=[lc["sqs"][0]])
        p.step1("tensor", "transpose", ps_t[0:8, 0:128], lc["sqs"][0][:, 0:8], idb[:, :],
                reads=[lc["sqs"][0], idb], writes=[ps_t])
        p.step1("vector", "tensor_copy", combt.ap[0:8, tsl], ps_t[0:8, 0:128],
                reads=[ps_t], writes=[combt])
    p.dma("sync", CROW_D, combt[0:8, :], sem="d_crow", reads=[combt])
    p.free(combt, m1c, sc8, gw)
    crow = p.sbuf("crow", [1, TOWN], F32, side="right")

    prodb = p.sbuf("prodb", [128, 8 * TOWN], BF16, side="right")
    prod_c = p.subdiv(prodb, 8, TOWN)
    w1b = [p.sbuf(f"w1b{i}", [128, 16 * 128], BF16, side="right") for i in range(3)]
    w3b = [p.sbuf(f"w3b{i}", [128, 16 * 128], BF16, side="right") for i in range(3)]
    w2b = [p.sbuf(f"w2b{i}", [128, 8 * 128], BF16, side="right") for i in range(3)]
    gb1 = p.sbuf("gb1", [128, 512], BF16, side="right")
    crep = p.sbuf("crep", [128, TOWN], F32, side="right")
    p.step1("vector", "memset", yacc.ap[:, :], 0.0, reads=[yacc], writes=[yacc])
    for e in range(9):
        if e < 8:
            p.dma("sync", crow.ap[0:1, :], CROW_D[e:e + 1, :], writes=[crow], extra=[("d_crow", 0)])
            for (o0, n) in NT2:
                bcast_row(p, ps_m.ap[:, 0:n], onesrf[:, :], crow[0:1, o0:o0 + n],
                          reads=[crow, onesrf], writes=[ps_m])
                p.step1("vector", "tensor_copy", crep.ap[:, o0:o0 + n], ps_m[:, 0:n],
                        reads=[ps_m], writes=[crep])
        else:
            p.step1("vector", "memset", crep.ap[:, :], 1.0, reads=[], writes=[crep])
        for mc in range(8):
            wa = w1b[(e * 8 + mc) % 3]; wc = w3b[(e * 8 + mc) % 3]
            srcW1 = W1L[e] if e < 8 else WS1L
            srcW3 = W3L[e] if e < 8 else WS3L
            p.dma("gpsimd", wa.ap.rearrange("p (k m) -> p k m", k=16),
                  srcW1[:, mc * 128:(mc + 1) * 128].rearrange("(k p) m -> p k m", p=128), writes=[wa])
            p.dma("gpsimd", wc.ap.rearrange("p (k m) -> p k m", k=16),
                  srcW3[:, mc * 128:(mc + 1) * 128].rearrange("(k p) m -> p k m", p=128), writes=[wc])
            for (o0, n) in NT2:
                psx = pool4[pi % 4]; pi += 1
                mm_group(p, psx.ap[:, 0:n], [wa[:, k * 128:(k + 1) * 128] for k in range(16)],
                         [c[:, o0:o0 + n] for c in x3ln_c], reads=[wa] + x3ln_c, writes=[psx])
                p.step1("scalar", "activation", gb1.ap[:, 0:n], psx[:, 0:n], AF.Silu,
                        reads=[psx], writes=[gb1])
                psy = pool4[pi % 4]; pi += 1
                mm_group(p, psy.ap[:, 0:n], [wc[:, k * 128:(k + 1) * 128] for k in range(16)],
                         [c[:, o0:o0 + n] for c in x3ln_c], reads=[wc] + x3ln_c, writes=[psy])
                p.step1("vector", "tensor_mul", scr.ap[:, 0:n], gb1[:, 0:n], psy[:, 0:n],
                        reads=[gb1, psy], writes=[scr])
                p.step1("vector", "tensor_mul", prod_c[mc].ap[:, o0:o0 + n], scr[:, 0:n],
                        crep[:, o0:o0 + n], reads=[scr, crep], writes=[prod_c[mc]])
        for mc in range(16):
            wd = w2b[(e * 16 + mc) % 3]
            srcW2 = W2L[e] if e < 8 else WS2L
            p.dma("gpsimd", wd.ap.rearrange("p (k m) -> p k m", k=8),
                  srcW2[:, mc * 128:(mc + 1) * 128].rearrange("(k p) m -> p k m", p=128), writes=[wd])
            for (o0, n) in NT2:
                psx = pool4[pi % 4]; pi += 1
                mm_group(p, psx.ap[:, 0:n], [wd[:, k * 128:(k + 1) * 128] for k in range(8)],
                         [c[:, o0:o0 + n] for c in prod_c], reads=[wd] + prod_c, writes=[psx])
                p.step1("vector", "tensor_add", yacc.ap[:, mc * TOWN + o0:mc * TOWN + o0 + n],
                        yacc[:, mc * TOWN + o0:mc * TOWN + o0 + n], psx[:, 0:n],
                        reads=[yacc, psx], writes=[yacc])
    p.free(crep, gb1, *w2b[::-1], *w3b[::-1], *w1b[::-1], prodb)

    x3f = p.sbuf("x3f", [128, 16 * TOWN], F32, side="right")
    rscr2 = p.sbuf("rscr2", [128, TOWN], F32, side="right")
    p.dma("sync", x3f.ap.rearrange("p (k t) -> p k t", k=16),
          X3D.rearrange("(k p) t -> p k t", p=128), writes=[x3f],
          extra=[("sp_x3", 0)])
    emit_ln_c(p, lc, yacc, x3ln, TOWN, NT2)
    for k in range(16):
        ks = slice(k * TOWN, (k + 1) * TOWN)
        p.step1("scalar", "activation", rscr2.ap[:, :], x3ln[:, ks], AF.Copy, reads=[x3ln], writes=[rscr2])
        p.step1("vector", "tensor_add", x3f.ap[:, ks], x3f[:, ks], rscr2[:, :], reads=[x3f, rscr2], writes=[x3f])
        p.dma("sync", X4O[k * 128:(k + 1) * 128, :], x3f[:, ks], reads=[x3f])
    return p.finalize([(s, p.cnt[s]) for s in sorted(p.store_sems)])


def stage_d3e(g, O_full, X2, X2LN, core):
    s = slice(1024 * core, 1024 * (core + 1))
    return {
        "oh": _c(O_full[s, :]), "x2t": _c(X2[:, s]), "x2lnt": _c(X2LN[:, s]),
        "wg1": g["l1_Wg"], "wo1": g["l1_Wo"], "onw": g["l1_onw"],
        "gatew": g["f1_gate"], "w1l": g["f1_w1l"], "w3l": g["f1_w3l"], "w2l": g["f1_w2l"],
        "ws1l": g["f1_ws1l"], "ws3l": g["f1_ws3l"], "ws2l": g["f1_ws2l"],
        "ones_col_bf": g["ones_col_bf"], "ones_row_f": g["ones_row_f"], "id_bf": g["id_bf"],
    }


def stage_a2(B_full, G_full, g, core):
    return {
        "bo2": _c(B_full[256 * core:256 * (core + 1), :]),
        "go2": _c(G_full[2 * core:2 * (core + 1), :]),
        "ones_row_f": g["ones_row_f"],
    }


def stage_a1(g, core):
    s = slice(1024 * core, 1024 * core + TLOC)
    return {
        "xt": _c(g["xT_pad"][:, s]),
        "wq": g["l0_Wq"], "wk": g["l0_Wk"], "wv": g["l0_Wv"], "wi": g["l0_ig"],
        "wog": g["l0_og"], "wgam": g["l0_gam"],
        "convw": g["l0_convw"], "convb": g["l0_convb"], "igb": g["l0_igb"],
        "ogb": g["l0_ogb"], "gamb": g["l0_gamb"], "vng": g["l0_vng"], "vnb": g["l0_vnb"],
        "ones_col_bf": g["ones_col_bf"], "ones_row_f": g["ones_row_f"],
    }


_BUILT = {}


def _get(name, builder):
    if name not in _BUILT:
        _BUILT[name] = builder()
    return _BUILT[name]


def kernel(**inputs):
    g = stage_common(inputs)
    cores = list(range(NCORES))

    nc = _get("a1", build_a1)
    res = run_bass_kernel_spmd(nc, [stage_a1(g, c) for c in cores], core_ids=cores)
    Q_full = np.concatenate([res.results[c]["qo"] for c in cores], axis=1)
    B_full = np.concatenate([res.results[c]["bo"] for c in cores], axis=1)
    O_full = np.concatenate([res.results[c]["oo"] for c in cores], axis=1)
    G_full = np.concatenate([res.results[c]["go"] for c in cores], axis=1)

    nc = _get("a2", build_a2)
    res = run_bass_kernel_spmd(nc, [stage_a2(B_full, G_full, g, c) for c in cores], core_ids=cores)
    M_full = np.concatenate([res.results[c]["mo"] for c in cores], axis=0)

    nc = _get("a3c", build_a3c)
    res = run_bass_kernel_spmd(nc, [stage_a3c(g, M_full, Q_full, O_full, c) for c in cores],
                               core_ids=cores)
    X2 = np.concatenate([res.results[c]["x2o"] for c in cores], axis=1)
    X2LN = np.concatenate([res.results[c]["x2lno"] for c in cores], axis=1)

    X2LN_pad = np.pad(X2LN, ((0, 0), (HALO, 0)))
    nc = _get("d1", build_d1)
    res = run_bass_kernel_spmd(nc, [stage_d1(g, X2LN_pad, c) for c in cores], core_ids=cores)
    QH = np.concatenate([res.results[c]["qho"] for c in cores], axis=1)
    KH = np.concatenate([res.results[c]["kho"] for c in cores], axis=1)
    VH = np.concatenate([res.results[c]["vho"] for c in cores], axis=1)
    GB = np.concatenate([res.results[c]["gbo"] for c in cores], axis=1)

    nc = _get("d2", lambda: build_d2(16))
    res = run_bass_kernel_spmd(nc, [stage_d2(g, QH, KH, VH, GB, c) for c in cores], core_ids=cores)
    O_tm = np.concatenate([np.asarray(res.results[h]["oo2"])[1::2] for h in range(6)], axis=1)

    nc = _get("d3e", build_d3e)
    res = run_bass_kernel_spmd(nc, [stage_d3e(g, O_tm, X2, X2LN, c) for c in cores], core_ids=cores)
    X4 = np.concatenate([res.results[c]["x4o"] for c in cores], axis=1)
    return np.ascontiguousarray(X4.T).reshape(1, 8192, 2048).astype(np.float32)


# version 07
# speedup vs baseline: 1.0378x; 1.0215x over previous
import numpy as np
import ml_dtypes
from contextlib import ExitStack

import concourse.bass as bass
import concourse.mybir as mybir
from concourse.bass_utils import run_bass_kernel_spmd

F32 = mybir.dt.float32
BF16 = mybir.dt.bfloat16
AF = mybir.ActivationFunctionType
ALU = mybir.AluOpType
BF = ml_dtypes.bfloat16

NCORES = 8
DEBUG_A3C = False
TOWN = 1024
HALO = 8
TLOC = TOWN + HALO
ENGS = ("tensor", "vector", "scalar", "gpsimd", "sync")


class Buf:
    __slots__ = ("ap", "name", "ws", "rs", "children")

    def __init__(self, ap, name):
        self.ap = ap
        self.name = name
        self.ws = []
        self.rs = []
        self.children = []

    def __getitem__(self, sl):
        return self.ap[sl]


class P:
    def __init__(self, name):
        self.nc = bass.Bass("TRN2", target_bir_lowering=False)
        self.name = name
        self.ctx = ExitStack()
        self.streams = {e: [] for e in ENGS}
        self.sems = {}
        self.cnt = {}
        self.out_names = []
        self.store_sems = set()
        self._open_sbufs = {}
        self._freed_markers = []

    def sem(self, name):
        if name not in self.sems:
            self.sems[name] = self.ctx.enter_context(self.nc.semaphore(name))
            self.cnt[name] = 0
        return self.sems[name]

    def sbuf(self, name, shape, dtype, side=None):
        cm = self.nc.sbuf_tensor(name, shape, dtype, side=side)
        b = Buf(cm.__enter__(), name)
        b.rs = list(self._freed_markers)
        self._open_sbufs.setdefault(side or "left", []).append((b, cm))
        return b

    def free(self, *bufs):
        group = set(id(b) for b in bufs)
        for b in bufs:
            for bb in [b] + b.children:
                self._freed_markers.extend(bb.ws)
                self._freed_markers.extend(bb.rs)
        for side in list(self._open_sbufs):
            st = self._open_sbufs[side]
            while st and id(st[-1][0]) in group:
                b, cm = st.pop()
                cm.__exit__(None, None, None)
                group.discard(id(b))
        assert not group, f"free(): non-LIFO free requested: {[b.name for b in bufs if id(b) in group]}"

    def psum(self, name, shape, dtype=F32):
        return Buf(self.ctx.enter_context(self.nc.psum_tensor(name, shape, dtype)), name)

    def subdiv(self, buf, n, w):
        cs = []
        for i in range(n):
            c = Buf(buf.ap[:, i * w:(i + 1) * w], f"{buf.name}.{i}")
            c.ws = list(buf.ws)
            c.rs = list(buf.rs)
            cs.append(c)
        buf.children.extend(cs)
        return cs

    def merge(self, buf):
        ws, rs = list(buf.ws), list(buf.rs)
        for c in buf.children:
            ws.extend(c.ws)
            rs.extend(c.rs)
        for b in [buf] + buf.children:
            b.ws = list(ws)
            b.rs = list(rs)

    def din(self, name, shape, dtype):
        return self.nc.dram_tensor(name, shape, dtype, kind="ExternalInput").ap()

    def dout(self, name, shape, dtype):
        self.out_names.append(name)
        return self.nc.dram_tensor(name, shape, dtype, kind="ExternalOutput").ap()

    def _resolve(self, eng, reads, writes, extra):
        waits = {}
        drain = False
        me = "m_" + eng
        def add(mk):
            nonlocal drain
            if mk is None:
                return
            sn, v = mk
            if sn == me:
                drain = True
            elif sn.startswith("m_"):
                waits[sn] = max(waits.get(sn, 0), v)
            else:
                waits[sn] = max(waits.get(sn, 0), self.cnt[sn])
        for b in reads:
            for w in b.ws:
                add(w)
        for b in writes:
            for w in b.ws:
                add(w)
            for r in b.rs:
                add(r)
        for mk in extra:
            add(mk)
        return sorted(waits.items()), drain

    def _update(self, marker, reads, writes):
        for b in reads:
            b.rs.append(marker)
        for b in writes:
            b.ws = [marker]
            b.rs = []

    def step(self, eng, fn, reads=(), writes=(), extra=(), n=1):
        waits, drain = self._resolve(eng, reads, writes, extra)
        m = "m_" + eng
        self.sem(m)
        self.cnt[m] += 1
        val = self.cnt[m]
        def cl(engine, fn=fn, waits=waits, drain=drain, m=m):
            for sn, v in waits:
                engine.wait_ge(self.sems[sn], v)
            if drain and eng != "tensor":
                engine.drain()
            ins = fn(engine)
            ins.then_inc(self.sems[m], 1)
        self.streams[eng].append(cl)
        marker = (m, val)
        self._update(marker, reads, writes)
        return marker

    def step1(self, eng, meth, *args, reads=(), writes=(), extra=(), **kwargs):
        def fn(engine, meth=meth, args=args, kwargs=kwargs):
            return getattr(engine, meth)(*args, **kwargs)
        return self.step(eng, fn, reads=reads, writes=writes, extra=extra)

    def dma(self, eng, out, in_, sem=None, reads=(), writes=(), extra=()):
        if sem is None:
            if writes:
                sem = "d_" + writes[0].name
            else:
                sem = "st_" + reads[0].name
                self.store_sems.add(sem)
        waits, _ = self._resolve("__dma__", reads, writes, extra)
        self.sem(sem)
        self.cnt[sem] += 16
        val = self.cnt[sem]
        def cl(engine, out=out, in_=in_, waits=waits, sem=sem):
            for sn, v in waits:
                engine.wait_ge(self.sems[sn], v)
            engine.dma_start(out=out, in_=in_).then_inc(self.sems[sem], 16)
        self.streams[eng].append(cl)
        marker = (sem, val)
        self._update(marker, reads, writes)
        return marker

    def finalize(self, final_waits):
        def cl(engine, wl=tuple(final_waits)):
            for sn, v in wl:
                engine.wait_ge(self.sems[sn], v)
        self.streams["sync"].append(cl)
        block = self.ctx.enter_context(self.nc.Block())
        for e in ENGS:
            fns = self.streams[e]
            if not fns:
                continue
            getattr(block, e)(lambda engine, fns=fns: [fn(engine) for fn in fns])
        for side in list(self._open_sbufs):
            st = self._open_sbufs[side]
            while st:
                _, cm = st.pop()
                cm.__exit__(None, None, None)
        self.ctx.close()
        return self.nc



def mm_group(p, ps, lhsT_slices, rhs_slices, reads, writes, extra=()):
    nk = len(lhsT_slices)
    def fn(t):
        ins = None
        for k in range(nk):
            ins = t.matmul(ps, lhsT_slices[k], rhs_slices[k], start=(k == 0), stop=(k == nk - 1))
        return ins
    return p.step("tensor", fn, reads=reads, writes=writes, extra=extra, n=nk)


def part_reduce(p, ps_row, ones_bf, src_bf_ap, reads, writes):
    return p.step("tensor", lambda t: t.matmul(ps_row, ones_bf, src_bf_ap, start=True, stop=True),
                  reads=reads, writes=writes)


def bcast_row(p, ps_dst, ones_row_f, row_ap, reads, writes):
    return p.step("tensor", lambda t: t.matmul(ps_dst, ones_row_f, row_ap, start=True, stop=True),
                  reads=reads, writes=writes)



def _c(a):
    return np.ascontiguousarray(a)


def _pcol(vec):
    v = np.asarray(vec)
    return _c(v.reshape(-1, 128).T)


def stage_common(inputs):
    g = {}
    x = np.asarray(inputs["L0_attn_x"])[0]
    g["xT"] = _c(x.T)
    g["xT_pad"] = np.pad(g["xT"], ((0, 0), (HALO, 0)))
    for nm in ("Wq", "Wk", "Wv", "Wo"):
        g["l0_" + nm] = _c(np.asarray(inputs[f"L0_attn_{nm}"]).T)
    g["l0_ig"] = _c(np.asarray(inputs["L0_attn_ig_w"]).T)
    g["l0_og"] = _c(np.asarray(inputs["L0_attn_og_w"]).T)
    g["l0_gam"] = _c(np.asarray(inputs["L0_attn_gamma_w"]).T)
    cw = np.asarray(inputs["L0_attn_conv_w"])[:, 0, :]
    g["l0_convw"] = _c(cw.reshape(16, 128, 4).transpose(1, 0, 2).reshape(128, 64))
    for nm, key in (("convb", "L0_attn_conv_b"), ("igb", "L0_attn_ig_b"), ("ogb", "L0_attn_og_b"),
                    ("gng", "L0_attn_gn_g"), ("gnb", "L0_attn_gn_b")):
        g["l0_" + nm] = _pcol(np.asarray(inputs[key]))
    g["l0_gamb"] = _c(np.asarray(inputs["L0_attn_gamma_b"]).reshape(16, 1))
    for nm, key in (("vng", "L0_attn_vn_g"), ("vnb", "L0_attn_vn_b"),
                    ("mng", "L0_attn_mn_g"), ("mnb", "L0_attn_mn_b")):
        g["l0_" + nm] = _c(np.asarray(inputs[key]).reshape(128, 1))
    g["up_l"] = _c(np.asarray(inputs["L0_ffn_W_up"]).T)
    g["gt_l"] = _c(np.asarray(inputs["L0_ffn_W_gate"]).T)
    g["dn_l"] = _c(np.asarray(inputs["L0_ffn_W_down"]).T)
    g["wsug"] = _pcol(np.asarray(inputs["L0_ffn_Wscale_ug"]))
    g["wsdp"] = _pcol(np.asarray(inputs["L0_ffn_Wscale_dp"]))
    g["suug"] = _pcol(np.asarray(inputs["L0_ffn_SU_ug"]))
    g["sudp"] = _pcol(np.asarray(inputs["L0_ffn_SU_dp"]))
    g["h128b"] = _hadamard(128, BF)
    g["l1_Wq"] = _c(np.asarray(inputs["L1_attn_Wq"]))
    for i in range(2):
        g[f"l1_Wk{i}"] = _c(np.asarray(inputs["L1_attn_Wk"])[i])
        g[f"l1_Wv{i}"] = _c(np.asarray(inputs["L1_attn_Wv"])[i])
        g[f"l1_Wb{i}"] = _c(np.asarray(inputs["L1_attn_Wb"])[i])
    g["l1_Wa"] = _c(np.asarray(inputs["L1_attn_Wa"]))
    g["l1_Wb0"] = g["l1_Wb0"]; g["l1_Wb1"] = g["l1_Wb1"]
    def _cpack(cwa):
        C = cwa.shape[0]
        return _c(cwa.reshape(C // 128, 128, 4).transpose(1, 0, 2).reshape(128, -1))
    g["l1_cqw"] = _cpack(np.asarray(inputs["L1_attn_conv_q_w"]))
    g["l1_ckw0"] = _cpack(np.asarray(inputs["L1_attn_conv_k_w"])[0])
    g["l1_ckw1"] = _cpack(np.asarray(inputs["L1_attn_conv_k_w"])[1])
    g["l1_cvw0"] = _cpack(np.asarray(inputs["L1_attn_conv_v_w"])[0])
    g["l1_cvw1"] = _cpack(np.asarray(inputs["L1_attn_conv_v_w"])[1])
    g["l1_alog"] = _c(np.asarray(inputs["L1_attn_A_log"]).reshape(6, 1))
    g["l1_dtb"] = _c(np.asarray(inputs["L1_attn_dt_bias"]).reshape(6, 1))
    g["l1_Wg"] = _c(np.asarray(inputs["L1_attn_Wg"]))
    g["l1_Wo"] = _c(np.asarray(inputs["L1_attn_Wo"]))
    g["l1_onw"] = _c(np.asarray(inputs["L1_attn_o_norm_w"]).reshape(1, 512))
    g["f1_gate"] = _c(np.asarray(inputs["L1_ffn_gate_w"]).T)
    g["f1_w1l"] = _c(np.asarray(inputs["L1_ffn_w1"]).transpose(0, 2, 1))
    g["f1_w3l"] = _c(np.asarray(inputs["L1_ffn_w3"]).transpose(0, 2, 1))
    g["f1_w2l"] = _c(np.asarray(inputs["L1_ffn_w2"]).transpose(0, 2, 1))
    g["f1_ws1l"] = _c(np.asarray(inputs["L1_ffn_ws1"]).T)
    g["f1_ws3l"] = _c(np.asarray(inputs["L1_ffn_ws3"]).T)
    g["f1_ws2l"] = _c(np.asarray(inputs["L1_ffn_ws2"]).T)
    r_ = np.arange(128)
    g["utri_inc"] = (r_[:, None] <= r_[None, :]).astype(np.float32)
    g["utri_strict"] = (r_[:, None] < r_[None, :]).astype(np.float32)
    g["ltri_strict"] = (r_[:, None] > r_[None, :]).astype(np.float32)
    g["ones_col_bf"] = np.ones((128, 1), BF)
    g["ones_row_f"] = np.ones((1, 128), np.float32)
    g["id_bf"] = np.eye(128, dtype=BF)
    g["id_f"] = np.eye(128, dtype=np.float32)
    return g



def emit_ln_c(p, ctxb, src_f32, dst_bf, T, nts, src_is_bf=False):
    xbfB = src_f32 if src_is_bf else dst_bf
    sqs = ctxb["sqs"]
    onesb = ctxb["onesb"]; onesrf = ctxb["onesrf"]
    ps_r1 = ctxb["ps_r1"]; ps_r2 = ctxb["ps_r2"]; ps_m = ctxb["ps_m"]; ps_n = ctxb["ps_n"]
    srow = ctxb["srow"]; srow2 = ctxb["srow2"]; srow3 = ctxb["srow3"]; rrep = ctxb["rrep"]; scr = ctxb["scr"]
    if not src_is_bf:
        for k in range(16):
            p.step1("scalar", "activation", xbfB[:, k * T:(k + 1) * T], src_f32[:, k * T:(k + 1) * T],
                    AF.Copy, reads=[src_f32], writes=[xbfB])
    for (o0, n) in nts:
        mm_group(p, ps_r1.ap[0:1, 0:n], [onesb[:, :]] * 16,
                 [xbfB[:, k * T + o0:k * T + o0 + n] for k in range(16)],
                 reads=[xbfB, onesb], writes=[ps_r1])
        for k in range(16):
            sb = sqs[k % 2]
            p.step1("scalar", "activation", sb[:, 0:n], src_f32[:, k * T + o0:k * T + o0 + n],
                    AF.Square, reads=[src_f32], writes=[sb])
            p.step1("tensor", "matmul", ps_r2[0:1, 0:n], onesb[:, :], sb[:, 0:n],
                    start=(k == 0), stop=(k == 15), reads=[sb, onesb], writes=[ps_r2])
        p.step1("vector", "tensor_scalar_mul", srow[0:1, 0:n], ps_r1[0:1, 0:n], 1.0 / 2048,
                reads=[ps_r1], writes=[srow])
        p.step1("vector", "tensor_scalar_mul", srow2[0:1, 0:n], ps_r2[0:1, 0:n], 1.0 / 2048,
                reads=[ps_r2], writes=[srow2])
        p.step1("vector", "tensor_mul", srow3[0:1, 0:n], srow[0:1, 0:n], srow[0:1, 0:n],
                reads=[srow], writes=[srow3])
        p.step1("vector", "tensor_sub", srow2[0:1, 0:n], srow2[0:1, 0:n], srow3[0:1, 0:n],
                reads=[srow2, srow3], writes=[srow2])
        p.step1("vector", "tensor_scalar_add", srow2[0:1, 0:n], srow2[0:1, 0:n], 1e-5,
                reads=[srow2], writes=[srow2])
        p.step1("scalar", "activation", srow2[0:1, 0:n], srow2[0:1, 0:n], AF.Sqrt,
                reads=[srow2], writes=[srow2])
        p.step1("vector", "reciprocal", srow2[0:1, 0:n], srow2[0:1, 0:n],
                reads=[srow2], writes=[srow2])
        bcast_row(p, ps_m.ap[:, 0:n], onesrf[:, :], srow[0:1, 0:n], reads=[srow, onesrf], writes=[ps_m])
        bcast_row(p, ps_n.ap[:, 0:n], onesrf[:, :], srow2[0:1, 0:n], reads=[srow2, onesrf], writes=[ps_n])
        p.step1("vector", "tensor_copy", rrep[:, 0:n], ps_n[:, 0:n], reads=[ps_n], writes=[rrep])
        scr2 = ctxb["scr2"]
        for k in range(16):
            if src_is_bf:
                p.step1("scalar", "activation", scr2[:, 0:n], src_f32[:, k * T + o0:k * T + o0 + n],
                        AF.Copy, reads=[src_f32], writes=[scr2])
                p.step1("vector", "tensor_sub", scr[:, 0:n], scr2[:, 0:n],
                        ps_m[:, 0:n], reads=[scr2, ps_m], writes=[scr])
            else:
                p.step1("vector", "tensor_sub", scr[:, 0:n], src_f32[:, k * T + o0:k * T + o0 + n],
                        ps_m[:, 0:n], reads=[src_f32, ps_m], writes=[scr])
            p.step1("vector", "tensor_mul", dst_bf[:, k * T + o0:k * T + o0 + n], scr[:, 0:n],
                    rrep[:, 0:n], reads=[scr, rrep], writes=[dst_bf])


def mk_ln_ctx(p, T):
    return {
        "sqs": [p.sbuf("lnsq0", [128, 512], BF16), p.sbuf("lnsq1", [128, 512], BF16)],
        "srow": p.sbuf("srow", [1, 512], F32),
        "srow2": p.sbuf("srow2", [1, 512], F32),
        "srow3": p.sbuf("srow3", [1, 512], F32),
        "rrep": p.sbuf("rrep", [128, 512], F32),
        "scr": p.sbuf("lnscr", [128, 512], F32),
        "scr2": p.sbuf("lnscr2", [128, 512], F32),
        "ps_r1": p.psum("ps_r1", [1, 512], F32),
        "ps_r2": p.psum("ps_r2", [1, 512], F32),
        "ps_m": p.psum("ps_m", [128, 512], F32),
        "ps_n": p.psum("ps_n", [128, 512], F32),
    }



def build_a1():
    p = P("a1")
    XT = p.din("xt", [2048, TLOC], F32)
    WQ = p.din("wq", [2048, 2048], F32)
    WK = p.din("wk", [2048, 2048], F32)
    WV = p.din("wv", [2048, 2048], F32)
    WI = p.din("wi", [2048, 2048], F32)
    WOG = p.din("wog", [2048, 2048], F32)
    WGAM = p.din("wgam", [2048, 16], F32)
    CONVW = p.din("convw", [128, 64], F32)
    CONVB = p.din("convb", [128, 16], F32)
    IGB = p.din("igb", [128, 16], F32)
    OGB = p.din("ogb", [128, 16], F32)
    GAMB = p.din("gamb", [16, 1], F32)
    VNG = p.din("vng", [128, 1], F32)
    VNB = p.din("vnb", [128, 1], F32)
    ONESB = p.din("ones_col_bf", [128, 1], BF16)
    ONESRF = p.din("ones_row_f", [1, 128], F32)

    QO = p.dout("qo", [2048, TOWN], BF16)
    BO = p.dout("bo", [2048, TOWN], BF16)
    OO = p.dout("oo", [2048, TOWN], BF16)
    GO = p.dout("go", [16, TOWN], F32)

    xln = p.sbuf("xln", [128, 16 * TLOC], BF16)
    xc = p.sbuf("xc", [128, 16 * TOWN], BF16)
    lc = mk_ln_ctx(p, TLOC)
    xt = p.sbuf("sb_xt", [128, 16 * TLOC], F32, side="right")
    wbuf = {nm: [p.sbuf(f"w_{nm}{s}", [128, 16 * 128], BF16) for s in range(3 if nm in ("q", "k") else 2)]
            for nm in ("q", "k", "v", "i", "o")}
    wgam = p.sbuf("wgam_b", [128, 16 * 16], BF16)
    convw = p.sbuf("c_convw", [128, 64], F32)
    convb = p.sbuf("c_convb", [128, 16], F32)
    igb = p.sbuf("c_igb", [128, 16], F32)
    ogb = p.sbuf("c_ogb", [128, 16], F32)
    gamb = p.sbuf("c_gamb", [16, 1], F32)
    vng = p.sbuf("c_vng", [128, 1], F32)
    vnb = p.sbuf("c_vnb", [128, 1], F32)
    onesb = p.sbuf("c_onesb", [128, 1], BF16)
    onesrf = p.sbuf("c_onesrf", [1, 128], F32)
    lc["onesb"] = onesb
    lc["onesrf"] = onesrf
    srow3 = lc["srow3"]
    acc = [p.sbuf(f"acc{i}", [128, TOWN], F32) for i in range(2)]
    srow = lc["srow"]; srow2 = lc["srow2"]; rrep = lc["rrep"]; scr = lc["scr"]
    ps_r1 = lc["ps_r1"]; ps_r2 = lc["ps_r2"]; ps_m = lc["ps_m"]; ps_n = lc["ps_n"]
    kbf = p.sbuf("kbf", [128, 512], BF16)
    vbf = p.sbuf("vbf", [128, 512], BF16)
    ibf = p.sbuf("ibf", [128, 512], BF16)
    qst = p.sbuf("qst", [128, 512], BF16)
    bst = p.sbuf("bst", [128, 512], BF16)
    ost = p.sbuf("ost", [128, 512], BF16)
    ps_a = p.psum("ps_a", [128, 512], F32)
    ps_b = p.psum("ps_b", [128, 512], F32)
    ps_g = p.psum("ps_g", [16, 512], F32)

    for i in range(16):
        p.dma("sync", xt[:, i * TLOC:(i + 1) * TLOC], XT[i * 128:(i + 1) * 128, :], writes=[xt])
    for b, srcdr in ((convw, CONVW), (convb, CONVB), (igb, IGB), (ogb, OGB), (gamb, GAMB),
                     (vng, VNG), (vnb, VNB), (onesb, ONESB), (onesrf, ONESRF)):
        p.dma("sync", b[:, :], srcdr, writes=[b])
    p.dma("gpsimd", wgam.ap.rearrange("p (k m) -> p k m", k=16),
          WGAM.rearrange("(k p) m -> p k m", p=128), writes=[wgam])

    emit_ln_c(p, lc, xt, xln, TLOC, [(0, 344), (344, 344), (688, 344)])
    p.free(xt)
    gam = p.sbuf("gam", [16, 512], F32, side="right")

    for kk in range(16):
        eng = "vector"
        a0, a1 = acc[0], acc[1]
        p.step1(eng, "tensor_scalar", a0[:, :], xln[:, kk * TLOC + 5:kk * TLOC + 5 + TOWN],
                convw[:, 4 * kk:4 * kk + 1], None, ALU.mult, reads=[xln, convw], writes=[a0])
        for j in range(3):
            srcacc, dst = (a0, a1) if j % 2 == 0 else (a1, a0)
            p.step1(eng, "scalar_tensor_tensor", dst[:, :],
                    xln[:, kk * TLOC + 6 + j:kk * TLOC + 6 + j + TOWN],
                    convw[:, 4 * kk + j + 1:4 * kk + j + 2], srcacc[:, :], ALU.mult, ALU.add,
                    reads=[xln, convw, srcacc], writes=[dst])
        p.step1("scalar", "activation", xc[:, kk * TOWN:(kk + 1) * TOWN], a1[:, :], AF.Silu,
                bias=convb[:, kk:kk + 1], reads=[a1, convb], writes=[xc])

    def pstats_mu_rstd(eps):
        p.step1("vector", "tensor_scalar_mul", srow[0:1, 0:512], ps_r1[0:1, 0:512], 1.0 / 128,
                reads=[ps_r1], writes=[srow])
        p.step1("vector", "tensor_scalar_mul", srow2[0:1, 0:512], ps_r2[0:1, 0:512], 1.0 / 128,
                reads=[ps_r2], writes=[srow2])
        p.step1("vector", "tensor_mul", srow3[0:1, 0:512], srow[0:1, 0:512], srow[0:1, 0:512],
                reads=[srow], writes=[srow3])
        p.step1("vector", "tensor_sub", srow2[0:1, 0:512], srow2[0:1, 0:512], srow3[0:1, 0:512],
                reads=[srow2, srow3], writes=[srow2])
        p.step1("vector", "tensor_scalar_add", srow2[0:1, 0:512], srow2[0:1, 0:512], eps,
                reads=[srow2], writes=[srow2])
        p.step1("scalar", "activation", srow2[0:1, 0:512], srow2[0:1, 0:512], AF.Sqrt,
                reads=[srow2], writes=[srow2])
        p.step1("vector", "reciprocal", srow2[0:1, 0:512], srow2[0:1, 0:512],
                reads=[srow2], writes=[srow2])

    xln_own = [xln[:, k * TLOC + HALO:k * TLOC + HALO + TOWN] for k in range(16)]
    for h in range(16):
        for nm, W in (("q", WQ), ("k", WK), ("v", WV), ("i", WI), ("o", WOG)):
            slot = wbuf[nm][h % len(wbuf[nm])]
            p.dma("gpsimd", slot.ap.rearrange("p (k m) -> p k m", k=16),
                  W[:, h * 128:(h + 1) * 128].rearrange("(k p) m -> p k m", p=128),
                  writes=[slot])
        for nt in range(2):
            o0 = nt * 512
            osl = slice(o0, o0 + 512)
            mm_group(p, ps_a.ap[:, 0:512], [wbuf["q"][h % 3][:, k * 128:(k + 1) * 128] for k in range(16)],
                     [c[:, osl] for c in xln_own], reads=[wbuf["q"][h % 3], xln], writes=[ps_a])
            p.step1("scalar", "activation", lc["sqs"][0].ap[:, 0:512], ps_a[:, 0:512], AF.Square,
                    reads=[ps_a], writes=[lc["sqs"][0]])
            p.step1("tensor", "matmul", ps_r1[0:1, 0:512], onesb[:, :], lc["sqs"][0].ap[:, 0:512],
                    start=True, stop=True, reads=[lc["sqs"][0], onesb], writes=[ps_r1])
            p.step1("scalar", "activation", srow[0:1, 0:512], ps_r1[0:1, 0:512], AF.Sqrt,
                    reads=[ps_r1], writes=[srow])
            p.step1("vector", "tensor_scalar_max", srow[0:1, 0:512], srow[0:1, 0:512], 1e-12,
                    reads=[srow], writes=[srow])
            p.step1("vector", "reciprocal", srow[0:1, 0:512], srow[0:1, 0:512],
                    reads=[srow], writes=[srow])
            bcast_row(p, ps_m.ap[:, 0:512], onesrf[:, :], srow[0:1, 0:512], reads=[srow, onesrf], writes=[ps_m])
            p.step1("vector", "tensor_copy", rrep[:, 0:512], ps_m[:, 0:512], reads=[ps_m], writes=[rrep])
            p.step1("vector", "tensor_mul", qst[:, 0:512], ps_a[:, 0:512], rrep[:, 0:512],
                    reads=[ps_a, rrep], writes=[qst])
            p.dma("sync", QO[h * 128:(h + 1) * 128, osl], qst[:, 0:512], reads=[qst])
            mm_group(p, ps_b.ap[:, 0:512], [wbuf["k"][h % 3][:, k * 128:(k + 1) * 128] for k in range(16)],
                     [c[:, osl] for c in xln_own], reads=[wbuf["k"][h % 3], xln], writes=[ps_b])
            p.step1("scalar", "activation", lc["sqs"][0].ap[:, 0:512], ps_b[:, 0:512], AF.Square,
                    reads=[ps_b], writes=[lc["sqs"][0]])
            p.step1("tensor", "matmul", ps_r1[0:1, 0:512], onesb[:, :], lc["sqs"][0].ap[:, 0:512],
                    start=True, stop=True, reads=[lc["sqs"][0], onesb], writes=[ps_r1])
            p.step1("scalar", "activation", srow[0:1, 0:512], ps_r1[0:1, 0:512], AF.Sqrt,
                    reads=[ps_r1], writes=[srow])
            p.step1("vector", "tensor_scalar_max", srow[0:1, 0:512], srow[0:1, 0:512], 1e-12,
                    reads=[srow], writes=[srow])
            p.step1("vector", "reciprocal", srow[0:1, 0:512], srow[0:1, 0:512],
                    reads=[srow], writes=[srow])
            bcast_row(p, ps_m.ap[:, 0:512], onesrf[:, :], srow[0:1, 0:512], reads=[srow, onesrf], writes=[ps_m])
            p.step1("vector", "tensor_copy", rrep[:, 0:512], ps_m[:, 0:512], reads=[ps_m], writes=[rrep])
            p.step1("vector", "tensor_mul", kbf[:, 0:512], ps_b[:, 0:512], rrep[:, 0:512],
                    reads=[ps_b, rrep], writes=[kbf])
            mm_group(p, ps_a.ap[:, 0:512], [wbuf["v"][h % 2][:, k * 128:(k + 1) * 128] for k in range(16)],
                     [c[:, osl] for c in xln_own], reads=[wbuf["v"][h % 2], xln], writes=[ps_a])
            p.step1("scalar", "activation", lc["sqs"][1].ap[:, 0:512], ps_a[:, 0:512], AF.Copy,
                    reads=[ps_a], writes=[lc["sqs"][1]])
            p.step1("scalar", "activation", lc["sqs"][0].ap[:, 0:512], ps_a[:, 0:512], AF.Square,
                    reads=[ps_a], writes=[lc["sqs"][0]])
            p.step1("tensor", "matmul", ps_r1[0:1, 0:512], onesb[:, :], lc["sqs"][1].ap[:, 0:512],
                    start=True, stop=True, reads=[lc["sqs"][1], onesb], writes=[ps_r1])
            p.step1("tensor", "matmul", ps_r2[0:1, 0:512], onesb[:, :], lc["sqs"][0].ap[:, 0:512],
                    start=True, stop=True, reads=[lc["sqs"][0], onesb], writes=[ps_r2])
            pstats_mu_rstd(1e-5)
            bcast_row(p, ps_m.ap[:, 0:512], onesrf[:, :], srow[0:1, 0:512], reads=[srow, onesrf], writes=[ps_m])
            bcast_row(p, ps_n.ap[:, 0:512], onesrf[:, :], srow2[0:1, 0:512], reads=[srow2, onesrf], writes=[ps_n])
            p.step1("vector", "tensor_copy", rrep[:, 0:512], ps_n[:, 0:512], reads=[ps_n], writes=[rrep])
            p.step1("vector", "tensor_copy", lc["scr2"].ap[:, 0:512], ps_m[:, 0:512],
                    reads=[ps_m], writes=[lc["scr2"]])
            p.step1("vector", "tensor_sub", scr[:, 0:512], ps_a[:, 0:512], lc["scr2"][:, 0:512],
                    reads=[ps_a, lc["scr2"]], writes=[scr])
            p.step1("vector", "tensor_mul", scr[:, 0:512], scr[:, 0:512], rrep[:, 0:512],
                    reads=[scr, rrep], writes=[scr])
            p.step1("vector", "tensor_scalar", vbf[:, 0:512], scr[:, 0:512], vng[:, :], vnb[:, :],
                    ALU.mult, ALU.add, reads=[scr, vng, vnb], writes=[vbf])
            mm_group(p, ps_b.ap[:, 0:512], [wbuf["i"][h % 2][:, k * 128:(k + 1) * 128] for k in range(16)],
                     [xc[:, k * TOWN + o0:k * TOWN + o0 + 512] for k in range(16)],
                     reads=[wbuf["i"][h % 2], xc], writes=[ps_b])
            p.step1("scalar", "activation", ibf[:, 0:512], ps_b[:, 0:512], AF.Sigmoid,
                    bias=igb[:, h:h + 1], reads=[ps_b, igb], writes=[ibf])
            p.step1("vector", "tensor_mul", scr[:, 0:512], kbf[:, 0:512], vbf[:, 0:512],
                    reads=[kbf, vbf], writes=[scr])
            p.step1("vector", "tensor_mul", bst[:, 0:512], scr[:, 0:512], ibf[:, 0:512],
                    reads=[scr, ibf], writes=[bst])
            p.dma("sync", BO[h * 128:(h + 1) * 128, osl], bst[:, 0:512], reads=[bst])
            mm_group(p, ps_a.ap[:, 0:512], [wbuf["o"][h % 2][:, k * 128:(k + 1) * 128] for k in range(16)],
                     [xc[:, k * TOWN + o0:k * TOWN + o0 + 512] for k in range(16)],
                     reads=[wbuf["o"][h % 2], xc], writes=[ps_a])
            p.step1("scalar", "activation", ost[:, 0:512], ps_a[:, 0:512], AF.Sigmoid,
                    bias=ogb[:, h:h + 1], reads=[ps_a, ogb], writes=[ost])
            p.dma("sync", OO[h * 128:(h + 1) * 128, osl], ost[:, 0:512], reads=[ost])

    for nt in range(2):
        o0 = nt * 512
        mm_group(p, ps_g.ap[0:16, 0:512], [wgam[:, k * 16:(k + 1) * 16] for k in range(16)],
                 [xc[:, k * TOWN + o0:k * TOWN + o0 + 512] for k in range(16)],
                 reads=[wgam, xc], writes=[ps_g])
        p.step1("scalar", "activation", gam[0:16, 0:512], ps_g[0:16, 0:512], AF.Sigmoid,
                bias=gamb[0:16, :], reads=[ps_g, gamb], writes=[gam])
        p.dma("sync", GO[:, o0:o0 + 512], gam[0:16, 0:512], reads=[gam])

    return p.finalize([(s, p.cnt[s]) for s in sorted(p.store_sems)])



def build_a2():
    p = P("a2")
    BO2 = p.din("bo2", [256, 8192], BF16)
    GO2 = p.din("go2", [2, 8192], F32)
    ONESRF = p.din("ones_row_f", [1, 128], F32)
    MO = p.dout("mo", [256, 8192], BF16)

    onesrf = p.sbuf("c_onesrf", [1, 128], F32)
    grow = p.sbuf("grow", [1, 8192], F32)
    bb = p.sbuf("bb", [128, 8192], BF16)
    bf32 = p.sbuf("bf32", [128, 8192], F32)
    grep = p.sbuf("grep", [128, 8192], F32)
    mem = p.sbuf("mem", [128, 8192], BF16)
    ps_bc = p.psum("ps_bc", [128, 512], F32)

    p.dma("sync", onesrf[:, :], ONESRF, writes=[onesrf])
    for j in range(2):
        p.dma("sync", bb[:, :], BO2[j * 128:(j + 1) * 128, :], writes=[bb])
        p.dma("sync", grow[0:1, :], GO2[j:j + 1, :], writes=[grow])
        p.step1("scalar", "activation", bf32[:, :], bb[:, :], AF.Copy, reads=[bb], writes=[bf32])
        for nt in range(16):
            o0 = nt * 512
            bcast_row(p, ps_bc.ap[:, 0:512], onesrf[:, :], grow[0:1, o0:o0 + 512],
                      reads=[grow, onesrf], writes=[ps_bc])
            p.step1("vector", "tensor_copy", grep[:, o0:o0 + 512], ps_bc[:, 0:512],
                    reads=[ps_bc], writes=[grep])
        p.step1("vector", "tensor_tensor_scan", mem[:, :], grep[:, :], bf32[:, :], 0.0,
                ALU.mult, ALU.add, reads=[grep, bf32], writes=[mem])
        p.dma("sync", MO[j * 128:(j + 1) * 128, :], mem[:, :], reads=[mem])
    return p.finalize([(s, p.cnt[s]) for s in sorted(p.store_sems)])



def build_a3c():
    p = P("a3c")
    MO3 = p.din("mo3", [2048, TOWN], BF16)
    QO3 = p.din("qo3", [2048, TOWN], BF16)
    OO3 = p.din("oo3", [2048, TOWN], BF16)
    XT3 = p.din("xt3", [2048, TOWN], F32)
    WOL = p.din("wol", [2048, 2048], F32)
    WUPL = p.din("wupl", [2048, 8192], F32)
    WGTL = p.din("wgtl", [2048, 8192], F32)
    WDNL = p.din("wdnl", [8192, 2048], F32)
    WSUG = p.din("wsug", [128, 128], F32)
    WSDP = p.din("wsdp", [128, 16], F32)
    SUUG = p.din("suug", [128, 16], F32)
    SUDP = p.din("sudp", [128, 64], F32)
    H128B = p.din("h128b", [128, 128], BF16)
    MNG = p.din("mng", [128, 1], F32)
    MNB = p.din("mnb", [128, 1], F32)
    GNG = p.din("gng", [128, 16], F32)
    GNB = p.din("gnb", [128, 16], F32)
    ONESB = p.din("ones_col_bf", [128, 1], BF16)
    ONESRF = p.din("ones_row_f", [1, 128], F32)
    X2O = p.dout("x2o", [2048, TOWN], F32)
    X2LNO = p.dout("x2lno", [2048, TOWN], BF16)
    X1D = p.nc.dram_tensor("x1spill", [2048, TOWN], F32, kind="Internal").ap()

    NT2 = [(0, 512), (512, 512)]
    lc = mk_ln_ctx(p, TOWN)
    onesb = p.sbuf("c_onesb", [128, 1], BF16)
    onesrf = p.sbuf("c_onesrf", [1, 128], F32)
    lc["onesb"] = onesb; lc["onesrf"] = onesrf
    wsug = p.sbuf("c_wsug", [128, 128], F32)
    wsdp = p.sbuf("c_wsdp", [128, 16], F32)
    suug = p.sbuf("c_suug", [128, 16], F32)
    sudp = p.sbuf("c_sudp", [128, 64], F32)
    h128b = p.sbuf("c_h128", [128, 128], BF16)
    mng = p.sbuf("c_mng", [128, 1], F32)
    mnb = p.sbuf("c_mnb", [128, 1], F32)
    gng = p.sbuf("c_gng", [128, 16], F32)
    gnb = p.sbuf("c_gnb", [128, 16], F32)
    x1su = p.sbuf("x1su", [128, 16 * TOWN], BF16)
    x1su_c = p.subdiv(x1su, 16, TOWN)
    srow = lc["srow"]; srow2 = lc["srow2"]; rrep = lc["rrep"]; scr = lc["scr"]; scr2 = lc["scr2"]
    ps_r1 = lc["ps_r1"]; ps_r2 = lc["ps_r2"]; ps_m = lc["ps_m"]; ps_n = lc["ps_n"]
    ps_a = p.psum("ps_a", [128, 512], F32)
    ps_b = p.psum("ps_b", [128, 512], F32)
    sqb = p.sbuf("sqb", [128, 512], BF16)
    t2b = p.sbuf("t2b", [128, 512], BF16)

    for b, srcdr in ((onesb, ONESB), (onesrf, ONESRF), (wsug, WSUG), (wsdp, WSDP),
                     (suug, SUUG), (sudp, SUDP), (h128b, H128B), (mng, MNG), (mnb, MNB),
                     (gng, GNG), (gnb, GNB)):
        p.dma("sync", b.ap[:, :], srcdr, writes=[b])
    p.step1("vector", "tensor_scalar_mul", wsug[:, :], wsug[:, :], 64.0, reads=[wsug], writes=[wsug])
    p.step1("vector", "tensor_scalar_mul", wsdp[:, :], wsdp[:, :], 64.0, reads=[wsdp], writes=[wsdp])
    C2 = 1.0 / (64.0 * float(np.sqrt(8192.0)))
    p.step1("vector", "tensor_scalar_mul", sudp[:, :], sudp[:, :], C2, reads=[sudp], writes=[sudp])

    ofin = p.sbuf("ofin", [128, 16 * TOWN], BF16, side="right")
    mo3 = p.sbuf("mo3b", [128, 16 * TOWN], BF16, side="right")
    qo3 = p.sbuf("qo3b", [128, 16 * TOWN], BF16, side="right")
    oo3 = p.sbuf("oo3b", [128, 16 * TOWN], BF16, side="right")
    for b, srcdr in ((mo3, MO3), (qo3, QO3), (oo3, OO3)):
        p.dma("sync", b.ap.rearrange("p (k t) -> p k t", k=16),
              srcdr.rearrange("(k p) t -> p k t", p=128), writes=[b])

    ps_m2 = p.psum("ps_m2", [128, 512], F32)
    ps_n2 = p.psum("ps_n2", [128, 512], F32)
    EP = []
    for par in range(2):
        EP.append({
            "sq": p.sbuf(f"ep_sq{par}", [128, 512], BF16),
            "t2": p.sbuf(f"ep_t2{par}", [128, 512], BF16),
            "s1": p.sbuf(f"ep_s1{par}", [128, 512], F32),
            "s2": p.sbuf(f"ep_s2{par}", [128, 512], F32),
            "rr": p.sbuf(f"ep_rr{par}", [128, 512], F32),
            "rmu": p.sbuf(f"ep_rmu{par}", [1, 512], F32),
            "rrs": p.sbuf(f"ep_rrs{par}", [1, 512], F32),
            "rm3": p.sbuf(f"ep_rm3{par}", [1, 512], F32),
            "pm": (ps_m, ps_m2)[par],
            "pn": (ps_n, ps_n2)[par],
        })

    def ep_stats(E, n, eps):
        p.step1("vector", "tensor_scalar_mul", E["rmu"].ap[0:1, 0:n], ps_r1[0:1, 0:n], 1.0 / 128,
                reads=[ps_r1], writes=[E["rmu"]])
        p.step1("vector", "tensor_scalar_mul", E["rrs"].ap[0:1, 0:n], ps_r2[0:1, 0:n], 1.0 / 128,
                reads=[ps_r2], writes=[E["rrs"]])
        p.step1("vector", "tensor_mul", E["rm3"].ap[0:1, 0:n], E["rmu"][0:1, 0:n], E["rmu"][0:1, 0:n],
                reads=[E["rmu"]], writes=[E["rm3"]])
        p.step1("vector", "tensor_sub", E["rrs"].ap[0:1, 0:n], E["rrs"][0:1, 0:n], E["rm3"][0:1, 0:n],
                reads=[E["rrs"], E["rm3"]], writes=[E["rrs"]])
        p.step1("vector", "tensor_scalar_add", E["rrs"].ap[0:1, 0:n], E["rrs"][0:1, 0:n], eps,
                reads=[E["rrs"]], writes=[E["rrs"]])
        p.step1("scalar", "activation", E["rrs"].ap[0:1, 0:n], E["rrs"][0:1, 0:n], AF.Sqrt,
                reads=[E["rrs"]], writes=[E["rrs"]])
        p.step1("vector", "reciprocal", E["rrs"].ap[0:1, 0:n], E["rrs"][0:1, 0:n],
                reads=[E["rrs"]], writes=[E["rrs"]])
        bcast_row(p, E["pm"].ap[:, 0:n], onesrf[:, :], E["rmu"][0:1, 0:n],
                  reads=[E["rmu"], onesrf], writes=[E["pm"]])
        bcast_row(p, E["pn"].ap[:, 0:n], onesrf[:, :], E["rrs"][0:1, 0:n],
                  reads=[E["rrs"], onesrf], writes=[E["pn"]])

    for h in range(16):
        for nti, (o0, n) in enumerate(NT2):
            E = EP[(h * 2 + nti) % 2]
            sl = slice(h * TOWN + o0, h * TOWN + o0 + n)
            p.step1("scalar", "activation", E["sq"].ap[:, 0:n], mo3[:, sl], AF.Square,
                    reads=[mo3], writes=[E["sq"]])
            p.step1("tensor", "matmul", ps_r1[0:1, 0:n], onesb[:, :], mo3[:, sl],
                    start=True, stop=True, reads=[mo3, onesb], writes=[ps_r1])
            p.step1("tensor", "matmul", ps_r2[0:1, 0:n], onesb[:, :], E["sq"][:, 0:n],
                    start=True, stop=True, reads=[E["sq"], onesb], writes=[ps_r2])
            ep_stats(E, n, 1e-5)
            p.step1("scalar", "activation", E["s2"].ap[:, 0:n], mo3[:, sl], AF.Copy,
                    reads=[mo3], writes=[E["s2"]])
            p.step1("vector", "tensor_sub", E["s1"].ap[:, 0:n], E["s2"][:, 0:n], E["pm"][:, 0:n],
                    reads=[E["s2"], E["pm"]], writes=[E["s1"]])
            p.step1("vector", "tensor_copy", E["rr"].ap[:, 0:n], E["pn"][:, 0:n],
                    reads=[E["pn"]], writes=[E["rr"]])
            p.step1("vector", "tensor_mul", E["s1"].ap[:, 0:n], E["s1"][:, 0:n], E["rr"][:, 0:n],
                    reads=[E["s1"], E["rr"]], writes=[E["s1"]])
            p.step1("vector", "tensor_scalar", E["s1"].ap[:, 0:n], E["s1"][:, 0:n], mng[:, :], mnb[:, :],
                    ALU.mult, ALU.add, reads=[E["s1"], mng, mnb], writes=[E["s1"]])
            p.step1("scalar", "activation", E["s2"].ap[:, 0:n], qo3[:, sl], AF.Copy,
                    reads=[qo3], writes=[E["s2"]])
            p.step1("vector", "tensor_mul", E["s1"].ap[:, 0:n], E["s1"][:, 0:n], E["s2"][:, 0:n],
                    reads=[E["s1"], E["s2"]], writes=[E["s1"]])
            p.step1("scalar", "activation", E["t2"].ap[:, 0:n], E["s1"][:, 0:n], AF.Copy,
                    reads=[E["s1"]], writes=[E["t2"]])
            p.step1("scalar", "activation", E["sq"].ap[:, 0:n], E["s1"][:, 0:n], AF.Square,
                    reads=[E["s1"]], writes=[E["sq"]])
            p.step1("tensor", "matmul", ps_r1[0:1, 0:n], onesb[:, :], E["t2"][:, 0:n],
                    start=True, stop=True, reads=[E["t2"], onesb], writes=[ps_r1])
            p.step1("tensor", "matmul", ps_r2[0:1, 0:n], onesb[:, :], E["sq"][:, 0:n],
                    start=True, stop=True, reads=[E["sq"], onesb], writes=[ps_r2])
            ep_stats(E, n, 1e-5)
            p.step1("vector", "tensor_sub", E["s1"].ap[:, 0:n], E["s1"][:, 0:n], E["pm"][:, 0:n],
                    reads=[E["s1"], E["pm"]], writes=[E["s1"]])
            p.step1("vector", "tensor_copy", E["rr"].ap[:, 0:n], E["pn"][:, 0:n],
                    reads=[E["pn"]], writes=[E["rr"]])
            p.step1("vector", "tensor_mul", E["s1"].ap[:, 0:n], E["s1"][:, 0:n], E["rr"][:, 0:n],
                    reads=[E["s1"], E["rr"]], writes=[E["s1"]])
            p.step1("vector", "tensor_scalar", E["s1"].ap[:, 0:n], E["s1"][:, 0:n], gng[:, h:h + 1], gnb[:, h:h + 1],
                    ALU.mult, ALU.add, reads=[E["s1"], gng, gnb], writes=[E["s1"]])
            p.step1("scalar", "activation", E["s2"].ap[:, 0:n], oo3[:, sl], AF.Copy,
                    reads=[oo3], writes=[E["s2"]])
            p.step1("vector", "tensor_mul", ofin[:, sl], E["s1"][:, 0:n], E["s2"][:, 0:n],
                    reads=[E["s1"], E["s2"]], writes=[ofin])
    p.free(*[E[k] for E in EP[::-1] for k in ("rm3", "rrs", "rmu", "rr", "s2", "s1", "t2", "sq")])
    p.free(oo3, qo3, mo3)

    yb = p.sbuf("yb", [128, 16 * TOWN], BF16, side="right")
    wo0 = p.sbuf("wo0", [128, 16 * 128], BF16, side="right")
    wo1 = p.sbuf("wo1", [128, 16 * 128], BF16, side="right")
    ofin_c = p.subdiv(ofin, 16, TOWN)
    pool4 = [ps_a, ps_b, ps_m, ps_n]
    pi = 0
    for mc in range(16):
        wb = (wo0, wo1)[mc % 2]
        p.dma("gpsimd", wb.ap.rearrange("p (k m) -> p k m", k=16),
              WOL[:, mc * 128:(mc + 1) * 128].rearrange("(k p) m -> p k m", p=128), writes=[wb])
        for (o0, n) in NT2:
            psx = pool4[pi % 4]; pi += 1
            mm_group(p, psx.ap[:, 0:n], [wb[:, k * 128:(k + 1) * 128] for k in range(16)],
                     [c[:, o0:o0 + n] for c in ofin_c],
                     reads=[wb] + ofin_c, writes=[psx])
            p.step1("scalar", "activation", yb.ap[:, mc * TOWN + o0:mc * TOWN + o0 + n], psx[:, 0:n],
                    AF.Copy, reads=[psx], writes=[yb])

    if DEBUG_A3C:
        DBGY = p.dout("dbg_y", [2048, TOWN], BF16)
        for k in range(16):
            p.dma("sync", DBGY[k * 128:(k + 1) * 128, :], yb[:, k * TOWN:(k + 1) * TOWN], reads=[yb])
        DBGO = p.dout("dbg_o", [2048, TOWN], BF16)
        p.merge(ofin)
        for k in range(16):
            p.dma("sync", DBGO[k * 128:(k + 1) * 128, :], ofin[:, k * TOWN:(k + 1) * TOWN], reads=[ofin])

    xt3 = p.sbuf("xt3b", [128, 16 * TOWN], F32, side="right")
    rscr = p.sbuf("rscr", [128, TOWN], F32, side="right")
    p.dma("sync", xt3.ap.rearrange("p (k t) -> p k t", k=16),
          XT3.rearrange("(k p) t -> p k t", p=128), writes=[xt3])
    emit_ln_c(p, lc, yb, x1su, TOWN, NT2, src_is_bf=True)
    for k in range(16):
        ks = slice(k * TOWN, (k + 1) * TOWN)
        p.step1("scalar", "activation", rscr[:, :], x1su[:, ks], AF.Copy, reads=[x1su], writes=[rscr])
        p.step1("vector", "tensor_add", xt3[:, ks], xt3[:, ks], rscr[:, :],
                reads=[xt3, rscr], writes=[xt3])
        p.dma("sync", X1D[k * 128:(k + 1) * 128, :], xt3[:, ks], sem="sp_x1", reads=[xt3])
    emit_ln_c(p, lc, xt3, x1su, TOWN, NT2)
    p.free(rscr, xt3, wo1, wo0, yb)
    p.free(ofin)

    p.merge(x1su)

    prod = p.sbuf("prod", [128, 64 * TOWN], BF16, side="right")
    prod_c = p.subdiv(prod, 64, TOWN)
    wus = [p.sbuf(f"wu{i}", [128, 16 * 128], BF16, side="right") for i in range(3)]
    wgs = [p.sbuf(f"wg{i}", [128, 16 * 128], BF16, side="right") for i in range(3)]
    gbuf = p.sbuf("gbuf", [128, 512], BF16, side="right")
    ubuf = p.sbuf("ubuf", [128, 512], BF16, side="right")
    btmp = p.sbuf("btmp", [128, TOWN], BF16, side="right")
    btmp2 = p.sbuf("btmp2", [128, TOWN], BF16, side="right")

    for k in range(16):
        p.step1("vector", "tensor_scalar_mul", x1su_c[k].ap[:, :], x1su_c[k][:, :], suug[:, k:k + 1],
                reads=[x1su_c[k], suug], writes=[x1su_c[k]])
    C1 = 1.0 / (64.0 * float(np.sqrt(2048.0)))
    for k in range(16):
        for (o0, n) in NT2:
            psx = pool4[pi % 4]; pi += 1
            p.step1("tensor", "matmul", psx[:, 0:n], h128b[:, :], x1su_c[k][:, o0:o0 + n],
                    start=True, stop=True, reads=[h128b, x1su_c[k]], writes=[psx])
            p.step1("scalar", "activation", x1su_c[k].ap[:, o0:o0 + n], psx[:, 0:n], AF.Copy,
                    scale=C1, reads=[psx], writes=[x1su_c[k]])
    def bfly(chunks, dlist, tmps):
        nt_ = len(tmps)
        for d in dlist:
            pairs = [(gg, gg + d) for gg in range(len(chunks)) if (gg // d) % 2 == 0]
            for pi_, (ga, gb_) in enumerate(pairs):
                eng = ("vector", "vector", "gpsimd")[pi_ % 3]
                tmp = tmps[0] if eng == "vector" else tmps[1]
                A, Bc = chunks[ga], chunks[gb_]
                p.step1(eng, "tensor_copy", tmp.ap[:, :], A[:, :], reads=[A], writes=[tmp])
                p.step1(eng, "tensor_add", A.ap[:, :], tmp[:, :], Bc[:, :], reads=[tmp, Bc], writes=[A])
                p.step1(eng, "tensor_sub", Bc.ap[:, :], tmp[:, :], Bc[:, :], reads=[tmp, Bc], writes=[Bc])

    bfly(x1su_c, (8, 4, 2, 1), [btmp, btmp2])
    for mc in range(64):
        wu = wus[mc % 3]
        wg = wgs[mc % 3]
        p.dma("gpsimd", wu.ap.rearrange("p (k m) -> p k m", k=16),
              WUPL[:, mc * 128:(mc + 1) * 128].rearrange("(k p) m -> p k m", p=128), writes=[wu])
        p.dma("gpsimd", wg.ap.rearrange("p (k m) -> p k m", k=16),
              WGTL[:, mc * 128:(mc + 1) * 128].rearrange("(k p) m -> p k m", p=128), writes=[wg])
        scol = mc // 64, mc % 64
        for (o0, n) in NT2:
            psx = pool4[pi % 4]; pi += 1
            mm_group(p, psx.ap[:, 0:n], [wg[:, k * 128:(k + 1) * 128] for k in range(16)],
                     [c[:, o0:o0 + n] for c in x1su_c], reads=[wg] + x1su_c, writes=[psx])
            p.step1("scalar", "activation", gbuf[:, 0:n], psx[:, 0:n], AF.Silu,
                    scale=wsug[:, 64 + mc:64 + mc + 1], reads=[psx, wsug], writes=[gbuf])
            psy = pool4[pi % 4]; pi += 1
            mm_group(p, psy.ap[:, 0:n], [wu[:, k * 128:(k + 1) * 128] for k in range(16)],
                     [c[:, o0:o0 + n] for c in x1su_c], reads=[wu] + x1su_c, writes=[psy])
            p.step1("vector", "tensor_scalar_mul", ubuf[:, 0:n], psy[:, 0:n], wsug[:, mc:mc + 1],
                    reads=[psy, wsug], writes=[ubuf])
            p.step1("vector", "scalar_tensor_tensor", prod_c[mc].ap[:, o0:o0 + n], gbuf[:, 0:n],
                    sudp[:, mc:mc + 1], ubuf[:, 0:n], ALU.mult, ALU.mult,
                    reads=[gbuf, sudp, ubuf], writes=[prod_c[mc]])
    for mc in range(64):
        for (o0, n) in NT2:
            psx = pool4[pi % 4]; pi += 1
            p.step1("tensor", "matmul", psx[:, 0:n], h128b[:, :], prod_c[mc][:, o0:o0 + n],
                    start=True, stop=True, reads=[h128b, prod_c[mc]], writes=[psx])
            p.step1("scalar", "activation", prod_c[mc].ap[:, o0:o0 + n], psx[:, 0:n], AF.Copy,
                    reads=[psx], writes=[prod_c[mc]])
    bfly(prod_c, (32, 16, 8, 4, 2, 1), [btmp, btmp2])
    p.free(btmp2, btmp, ubuf, gbuf, wgs[2], wgs[1], wgs[0], wus[2], wus[1], wus[0])

    wd0 = p.sbuf("wd0h", [128, 32 * 128], BF16, side="right")
    wd1 = p.sbuf("wd1h", [128, 32 * 128], BF16, side="right")
    for mc in range(16):
        p.dma("gpsimd", wd0.ap.rearrange("p (k m) -> p k m", k=32),
              WDNL[0:4096, mc * 128:(mc + 1) * 128].rearrange("(k p) m -> p k m", p=128), writes=[wd0])
        p.dma("gpsimd", wd1.ap.rearrange("p (k m) -> p k m", k=32),
              WDNL[4096:8192, mc * 128:(mc + 1) * 128].rearrange("(k p) m -> p k m", p=128), writes=[wd1])
        for (o0, n) in NT2:
            psx = pool4[pi % 4]; pi += 1
            def fn(t, wd0=wd0, wd1=wd1, psx=psx, o0=o0, n=n):
                ins = None
                for k in range(64):
                    wd = wd0 if k < 32 else wd1
                    ins = t.matmul(psx.ap[:, 0:n], wd[:, (k % 32) * 128:(k % 32 + 1) * 128],
                                   prod_c[k].ap[:, o0:o0 + n], start=(k == 0), stop=(k == 63))
                return ins
            p.step("tensor", fn, reads=[wd0, wd1] + prod_c, writes=[psx])
            p.step1("scalar", "activation", x1su_c[mc].ap[:, o0:o0 + n], psx[:, 0:n], AF.Copy,
                    scale=wsdp[:, mc:mc + 1], reads=[psx, wsdp], writes=[x1su_c[mc]])
    p.free(wd1, wd0, prod)

    p.merge(x1su)
    x1f = p.sbuf("x1f", [128, 16 * TOWN], F32, side="right")
    rscr = p.sbuf("rscr6", [128, TOWN], F32, side="right")
    p.dma("sync", x1f.ap.rearrange("p (k t) -> p k t", k=16),
          X1D.rearrange("(k p) t -> p k t", p=128), writes=[x1f],
          extra=[("sp_x1", p.cnt.get("sp_x1", 0))])
    zln = p.sbuf("zln", [128, 16 * TOWN], BF16, side="right")
    emit_ln_c(p, lc, x1su, zln, TOWN, NT2, src_is_bf=True)
    for k in range(16):
        ks = slice(k * TOWN, (k + 1) * TOWN)
        p.step1("scalar", "activation", rscr[:, :], zln[:, ks], AF.Copy, reads=[zln], writes=[rscr])
        p.step1("vector", "tensor_add", x1f[:, ks], x1f[:, ks], rscr[:, :], reads=[x1f, rscr], writes=[x1f])
        p.dma("sync", X2O[k * 128:(k + 1) * 128, :], x1f[:, ks], reads=[x1f])
    emit_ln_c(p, lc, x1f, zln, TOWN, NT2)
    for k in range(16):
        p.dma("sync", X2LNO[k * 128:(k + 1) * 128, :], zln[:, k * TOWN:(k + 1) * TOWN], reads=[zln])
    return p.finalize([(s, p.cnt[s]) for s in sorted(p.store_sems)])


def _pstats(p, lc, eps):
    srow = lc["srow"]; srow2 = lc["srow2"]; srow3 = lc["srow3"]
    ps_r1 = lc["ps_r1"]; ps_r2 = lc["ps_r2"]
    p.step1("vector", "tensor_scalar_mul", srow[0:1, 0:512], ps_r1[0:1, 0:512], 1.0 / 128,
            reads=[ps_r1], writes=[srow])
    p.step1("vector", "tensor_scalar_mul", srow2[0:1, 0:512], ps_r2[0:1, 0:512], 1.0 / 128,
            reads=[ps_r2], writes=[srow2])
    p.step1("vector", "tensor_mul", srow3[0:1, 0:512], srow[0:1, 0:512], srow[0:1, 0:512],
            reads=[srow], writes=[srow3])
    p.step1("vector", "tensor_sub", srow2[0:1, 0:512], srow2[0:1, 0:512], srow3[0:1, 0:512],
            reads=[srow2, srow3], writes=[srow2])
    p.step1("vector", "tensor_scalar_add", srow2[0:1, 0:512], srow2[0:1, 0:512], eps,
            reads=[srow2], writes=[srow2])
    p.step1("scalar", "activation", srow2[0:1, 0:512], srow2[0:1, 0:512], AF.Sqrt,
            reads=[srow2], writes=[srow2])
    p.step1("vector", "reciprocal", srow2[0:1, 0:512], srow2[0:1, 0:512],
            reads=[srow2], writes=[srow2])


def _hadamard(n, dtype):
    H = np.array([[1.0]], np.float32)
    while H.shape[0] < n:
        H = np.block([[H, H], [H, -H]])
    return H.astype(dtype)


def stage_a3c(g, M_full, Q_full, O_full, core):
    s = slice(1024 * core, 1024 * (core + 1))
    return {
        "mo3": _c(M_full[:, s]), "qo3": _c(Q_full[:, s]), "oo3": _c(O_full[:, s]),
        "xt3": _c(g["xT"][:, s]),
        "wol": g["l0_Wo"], "wupl": g["up_l"], "wgtl": g["gt_l"], "wdnl": g["dn_l"],
        "wsug": g["wsug"], "wsdp": g["wsdp"], "suug": g["suug"], "sudp": g["sudp"],
        "h128b": g["h128b"], "mng": g["l0_mng"], "mnb": g["l0_mnb"],
        "gng": g["l0_gng"], "gnb": g["l0_gnb"],
        "ones_col_bf": g["ones_col_bf"], "ones_row_f": g["ones_row_f"],
    }



def build_d1():
    p = P("d1")
    HT = p.din("ht", [2048, TLOC], BF16)
    WQ1 = p.din("wq1", [2048, 1536], F32)
    WK10 = p.din("wk10", [2048, 1536], F32)
    WK11 = p.din("wk11", [2048, 1536], F32)
    WV10 = p.din("wv10", [2048, 3072], F32)
    WV11 = p.din("wv11", [2048, 3072], F32)
    WA1 = p.din("wa1", [2048, 6], F32)
    WB10 = p.din("wb10", [2048, 6], F32)
    WB11 = p.din("wb11", [2048, 6], F32)
    CQW = p.din("cqw", [128, 12 * 4], F32)
    CKW0 = p.din("ckw0", [128, 12 * 4], F32)
    CKW1 = p.din("ckw1", [128, 12 * 4], F32)
    CVW0 = p.din("cvw0", [128, 24 * 4], F32)
    CVW1 = p.din("cvw1", [128, 24 * 4], F32)
    ALOG = p.din("alog", [6, 1], F32)
    DTB = p.din("dtb", [6, 1], F32)
    ONESB = p.din("ones_col_bf", [128, 1], BF16)
    ONESRF = p.din("ones_row_f", [1, 128], F32)
    QHO = p.dout("qho", [1536, 2 * TOWN], BF16)
    KHO = p.dout("kho", [1536, 2 * TOWN], BF16)
    VHO = p.dout("vho", [3072, 2 * TOWN], BF16)
    GBO = p.dout("gbo", [12, 2 * TOWN], F32)

    ht = p.sbuf("htb", [128, 16 * TLOC], BF16)
    lc = mk_ln_ctx(p, TLOC)
    onesb = p.sbuf("c_onesb", [128, 1], BF16)
    onesrf = p.sbuf("c_onesrf", [1, 128], F32)
    lc["onesb"] = onesb; lc["onesrf"] = onesrf
    cw = {"q": p.sbuf("c_cqw", [128, 48], F32), "k0": p.sbuf("c_ckw0", [128, 48], F32),
          "k1": p.sbuf("c_ckw1", [128, 48], F32), "v0": p.sbuf("c_cvw0", [128, 96], F32),
          "v1": p.sbuf("c_cvw1", [128, 96], F32)}
    alog = p.sbuf("c_alog", [6, 1], F32)
    dtb = p.sbuf("c_dtb", [6, 1], F32)
    negea = p.sbuf("negea", [6, 1], F32)
    raw = [p.sbuf(f"raw{i}", [128, TLOC], BF16) for i in range(4)]
    cvt = [p.sbuf(f"cvt{i}", [128, TOWN], BF16) for i in range(4)]
    acc = [p.sbuf(f"acc{i}", [128, TOWN], F32) for i in range(4)]
    wsl = [p.sbuf(f"w{i}b", [128, 16 * 128], BF16) for i in range(3)]
    w0 = wsl[0]
    ilv = p.sbuf("ilv", [128, 2 * TOWN], BF16)
    srow = lc["srow"]; srow2 = lc["srow2"]; rrep = lc["rrep"]
    ps_a = p.psum("ps_a", [128, 512], F32)
    ps_r1 = lc["ps_r1"]; ps_m = lc["ps_m"]
    pr6 = p.sbuf("pr6", [6, TOWN], F32)
    sb6 = [p.sbuf(f"sb6_{i}", [6, TOWN], F32) for i in range(3)]
    gb = p.sbuf("gb", [6, 2 * TOWN], F32)
    ps_s = p.psum("ps_s", [6, 512], F32)

    p.dma("sync", ht.ap.rearrange("p (k t) -> p k t", k=16),
          HT.rearrange("(k p) t -> p k t", p=128), writes=[ht])
    for b, srcdr in ((onesb, ONESB), (onesrf, ONESRF), (cw["q"], CQW), (cw["k0"], CKW0),
                     (cw["k1"], CKW1), (cw["v0"], CVW0), (cw["v1"], CVW1),
                     (alog, ALOG), (dtb, DTB)):
        p.dma("sync", b.ap[:, :], srcdr, writes=[b])
    p.step1("scalar", "activation", negea[0:6, :], alog[0:6, :], AF.Exp, reads=[alog], writes=[negea])
    p.step1("vector", "tensor_scalar_mul", negea[0:6, :], negea[0:6, :], -1.0, reads=[negea], writes=[negea])

    NT3 = [(0, 344), (344, 344), (688, 344)]
    NT2 = [(0, 512), (512, 512)]
    ht_c = [ht[:, k * TLOC:(k + 1) * TLOC] for k in range(16)]

    _wsl_ctr = [0]
    def raw_project(W, mc, dst, wslot):
        wb = wsl[_wsl_ctr[0] % 3]; _wsl_ctr[0] += 1
        p.dma("gpsimd", wb.ap.rearrange("p (k m) -> p k m", k=16),
              W[:, mc * 128:(mc + 1) * 128].rearrange("(k p) m -> p k m", p=128), writes=[wb])
        for (o0, n) in NT3:
            mm_group(p, ps_a.ap[:, 0:n], [wb[:, k * 128:(k + 1) * 128] for k in range(16)],
                     [c[:, o0:o0 + n] for c in ht_c], reads=[wb, ht], writes=[ps_a])
            p.step1("scalar", "activation", dst.ap[:, o0:o0 + n], ps_a[:, 0:n],
                    AF.Copy, reads=[ps_a], writes=[dst])

    def conv_silu(wcol, mc, srcb, dst, lane):
        a0, a1 = (acc[0], acc[1]) if lane == 0 else (acc[2], acc[3])
        eng = "vector"
        p.step1(eng, "tensor_scalar", a0[:, :], srcb[:, 5:5 + TOWN],
                wcol[:, 4 * mc:4 * mc + 1], None, ALU.mult, reads=[srcb, wcol], writes=[a0])
        for j in range(3):
            sa, da = (a0, a1) if j % 2 == 0 else (a1, a0)
            p.step1(eng, "scalar_tensor_tensor", da[:, :], srcb[:, 6 + j:6 + j + TOWN],
                    wcol[:, 4 * mc + j + 1:4 * mc + j + 2], sa[:, :], ALU.mult, ALU.add,
                    reads=[srcb, wcol, sa], writes=[da])
        p.step1("scalar", "activation", dst.ap[:, :], a1[:, :], AF.Silu, reads=[a1], writes=[dst])

    def l2n_pair(c0, c1, scale):
        for (o0, n) in NT2:
            for ci, cb in enumerate((c0, c1)):
                p.step1("scalar", "activation", lc["sqs"][ci].ap[:, 0:n], cb[:, o0:o0 + n],
                        AF.Square, reads=[cb], writes=[lc["sqs"][ci]])
                p.step1("tensor", "matmul", ps_r1[0:1, 0:n], onesb[:, :], lc["sqs"][ci][:, 0:n],
                        start=(ci == 0), stop=(ci == 1), reads=[lc["sqs"][ci], onesb], writes=[ps_r1])
            p.step1("vector", "tensor_scalar_add", srow[0:1, 0:n], ps_r1[0:1, 0:n], 1e-6,
                    reads=[ps_r1], writes=[srow])
            p.step1("scalar", "activation", srow[0:1, 0:n], srow[0:1, 0:n], AF.Sqrt,
                    reads=[srow], writes=[srow])
            p.step1("vector", "reciprocal", srow[0:1, 0:n], srow[0:1, 0:n], reads=[srow], writes=[srow])
            if scale != 1.0:
                p.step1("vector", "tensor_scalar_mul", srow[0:1, 0:n], srow[0:1, 0:n], scale,
                        reads=[srow], writes=[srow])
            bcast_row(p, ps_m.ap[:, 0:n], onesrf[:, :], srow[0:1, 0:n],
                      reads=[srow, onesrf], writes=[ps_m])
            p.step1("vector", "tensor_copy", rrep[:, 0:n], ps_m[:, 0:n], reads=[ps_m], writes=[rrep])
            for cb in (c0, c1):
                p.step1("vector", "tensor_mul", cb.ap[:, o0:o0 + n], cb[:, o0:o0 + n], rrep[:, 0:n],
                        reads=[cb, rrep], writes=[cb])

    for h in range(6):
        for ci, mc in enumerate((2 * h, 2 * h + 1)):
            raw_project(WQ1, mc, raw[ci], ci)
            conv_silu(cw["q"], mc, raw[ci], cvt[ci], ci % 2)
        l2n_pair(cvt[0], cvt[1], float(256.0 ** -0.5))
        for ci, mc in enumerate((2 * h, 2 * h + 1)):
            p.step1("vector", "memset", ilv.ap[:, :], 0.0, reads=[], writes=[ilv])
            p.step1("vector", "tensor_copy", ilv.ap[:, 1::2], cvt[ci][:, :], reads=[cvt[ci]], writes=[ilv])
            p.dma("sync", QHO[mc * 128:(mc + 1) * 128, :], ilv[:, :], reads=[ilv])
    for h in range(6):
        for ci, mc in enumerate((2 * h, 2 * h + 1)):
            raw_project(WK10, mc, raw[ci], ci)
            conv_silu(cw["k0"], mc, raw[ci], cvt[ci], ci % 2)
            raw_project(WK11, mc, raw[2 + ci], ci)
            conv_silu(cw["k1"], mc, raw[2 + ci], cvt[2 + ci], ci % 2)
        l2n_pair(cvt[0], cvt[1], 1.0)
        l2n_pair(cvt[2], cvt[3], 1.0)
        for ci, mc in enumerate((2 * h, 2 * h + 1)):
            p.step1("vector", "tensor_copy", ilv.ap[:, 0::2], cvt[ci][:, :], reads=[cvt[ci]], writes=[ilv])
            p.step1("gpsimd", "tensor_copy", ilv.ap[:, 1::2], cvt[2 + ci][:, :], reads=[cvt[2 + ci]], writes=[ilv])
            p.dma("sync", KHO[mc * 128:(mc + 1) * 128, :], ilv[:, :], reads=[ilv])
    for mc in range(24):
        raw_project(WV10, mc, raw[0], 0)
        conv_silu(cw["v0"], mc, raw[0], cvt[0], 0)
        raw_project(WV11, mc, raw[1], 1)
        conv_silu(cw["v1"], mc, raw[1], cvt[1], 1)
        p.step1("vector", "tensor_copy", ilv.ap[:, 0::2], cvt[0][:, :], reads=[cvt[0]], writes=[ilv])
        p.step1("gpsimd", "tensor_copy", ilv.ap[:, 1::2], cvt[1][:, :], reads=[cvt[1]], writes=[ilv])
        p.dma("sync", VHO[mc * 128:(mc + 1) * 128, :], ilv[:, :], reads=[ilv])

    ht_own = [ht[:, k * TLOC + HALO:k * TLOC + HALO + TOWN] for k in range(16)]
    for wi, (W, dst, bias) in enumerate(((WA1, pr6, dtb), (WB10, sb6[0], None), (WB11, sb6[1], None))):
        p.dma("gpsimd", w0.ap.rearrange("p (k m) -> p k m", k=16)[:, :, 0:6],
              W.rearrange("(k p) m -> p k m", p=128), writes=[w0])
        for (o0, n) in NT2:
            mm_group(p, ps_s.ap[0:6, 0:n], [w0[:, k * 128:k * 128 + 6] for k in range(16)],
                     [c[:, o0:o0 + n] for c in ht_own], reads=[w0, ht], writes=[ps_s])
            if wi == 0:
                p.step1("vector", "tensor_scalar_add", dst.ap[0:6, o0:o0 + n], ps_s[0:6, 0:n],
                        dtb[0:6, :], reads=[ps_s, dtb], writes=[dst])
            else:
                p.step1("scalar", "activation", dst.ap[0:6, o0:o0 + n], ps_s[0:6, 0:n], AF.Sigmoid,
                        reads=[ps_s], writes=[dst])
    p.step1("scalar", "activation", sb6[2].ap[0:6, :], pr6[0:6, :], AF.Abs, reads=[pr6], writes=[sb6[2]])
    p.step1("scalar", "activation", sb6[2].ap[0:6, :], sb6[2][0:6, :], AF.Exp, scale=-1.0,
            reads=[sb6[2]], writes=[sb6[2]])
    p.step1("scalar", "activation", sb6[2].ap[0:6, :], sb6[2][0:6, :], AF.Ln, bias=1.0,
            reads=[sb6[2]], writes=[sb6[2]])
    p.step1("vector", "tensor_scalar_max", pr6.ap[0:6, :], pr6[0:6, :], 0.0, reads=[pr6], writes=[pr6])
    p.step1("vector", "tensor_add", pr6.ap[0:6, :], pr6[0:6, :], sb6[2][0:6, :],
            reads=[pr6, sb6[2]], writes=[pr6])
    p.step1("vector", "tensor_scalar_mul", pr6.ap[0:6, :], pr6[0:6, :], negea[0:6, :],
            reads=[pr6, negea], writes=[pr6])
    p.step1("vector", "memset", gb.ap[0:6, :], 0.0, reads=[], writes=[gb])
    p.step1("vector", "tensor_copy", gb.ap[0:6, 0::2], pr6[0:6, :], reads=[pr6], writes=[gb])
    p.dma("sync", GBO[0:6, :], gb[0:6, :], reads=[gb])
    p.step1("vector", "tensor_copy", gb.ap[0:6, 0::2], sb6[0][0:6, :], reads=[sb6[0], gb], writes=[gb])
    p.step1("vector", "tensor_copy", gb.ap[0:6, 1::2], sb6[1][0:6, :], reads=[sb6[1]], writes=[gb])
    p.dma("sync", GBO[6:12, :], gb[0:6, :], reads=[gb])
    return p.finalize([(s, p.cnt[s]) for s in sorted(p.store_sems)])


def stage_d1(g, X2LN_pad, core):
    s = slice(1024 * core, 1024 * core + TLOC)
    return {
        "ht": _c(X2LN_pad[:, s]),
        "wq1": g["l1_Wq"], "wk10": g["l1_Wk0"], "wk11": g["l1_Wk1"],
        "wv10": g["l1_Wv0"], "wv11": g["l1_Wv1"],
        "wa1": g["l1_Wa"], "wb10": g["l1_Wb0"], "wb11": g["l1_Wb1"],
        "cqw": g["l1_cqw"], "ckw0": g["l1_ckw0"], "ckw1": g["l1_ckw1"],
        "cvw0": g["l1_cvw0"], "cvw1": g["l1_cvw1"],
        "alog": g["l1_alog"], "dtb": g["l1_dtb"],
        "ones_col_bf": g["ones_col_bf"], "ones_row_f": g["ones_row_f"],
    }



def build_d2(nseg=16):
    p = P("d2")
    QH2 = p.din("qh2", [256, 16384], BF16)
    KH2 = p.din("kh2", [256, 16384], BF16)
    VH2 = p.din("vh2", [512, 16384], BF16)
    GR = p.din("gr", [1, 16384], F32)
    BR = p.din("br", [1, 16384], F32)
    IDB = p.din("id_bf", [128, 128], BF16)
    IDF = p.din("id_f", [128, 128], F32)
    ID64B = p.din("id128b", [128, 128], BF16)
    UTI = p.din("utri_inc", [128, 128], F32)
    UTS = p.din("utri_strict", [128, 128], F32)
    LTS = p.din("ltri_strict", [128, 128], F32)
    ONESRF = p.din("ones_row_f", [1, 128], F32)
    OO2 = p.dout("oo2", [16384, 512], BF16)
    SCR_D = p.nc.dram_tensor("rowscr", [16, 128], F32, kind="Internal").ap()

    idb = p.sbuf("c_idb", [128, 128], BF16)
    idf = p.sbuf("c_idf", [128, 128], F32)
    id128 = p.sbuf("c_id128", [128, 128], BF16)
    uti = p.sbuf("c_uti", [128, 128], F32)
    uts = p.sbuf("c_uts", [128, 128], F32)
    lts = p.sbuf("c_lts", [128, 128], F32)
    onesrf = p.sbuf("c_onesrf", [1, 128], F32)
    ones8 = p.sbuf("ones8", [8, 128], F32)
    kseg = p.sbuf("kseg", [128, 2 * 1024], BF16)
    qseg = p.sbuf("qseg", [128, 2 * 1024], BF16)
    vseg = p.sbuf("vseg", [128, 4 * 1024], BF16)
    gseg = p.sbuf("gseg", [8, 128], F32)
    bseg = p.sbuf("bseg", [8, 128], F32)
    cumg = p.sbuf("cumg", [8, 128], F32)
    cumgt = p.sbuf("cumgt", [128, 8], F32)
    betat = p.sbuf("betat", [128, 8], F32)
    growc = p.sbuf("growc", [1, 1024], F32)
    brwc = p.sbuf("brwc", [1, 1024], F32)
    CB = []
    for par in range(2):
        CB.append({
            "lam": p.sbuf(f"lam{par}", [128, 1], F32),
            "lamend": p.sbuf(f"lamend{par}", [128, 1], F32),
            "lesc": p.sbuf(f"lesc{par}", [1, 1], F32),
            "e1": p.sbuf(f"e1_{par}", [128, 128], F32),
            "e2": p.sbuf(f"e2_{par}", [128, 128], F32),
            "s64a": p.sbuf(f"s64a{par}", [128, 128], F32),
            "s64b": p.sbuf(f"s64b{par}", [128, 128], F32),
            "wpb": p.sbuf(f"wpb{par}", [128, 128], BF16),
            "abf": [p.sbuf(f"abf{par}_{i}", [128, 128], BF16) for i in range(2)],
            "bbf": [p.sbuf(f"bbf{par}_{i}", [128, 128], BF16) for i in range(2)],
            "ptb": [p.sbuf(f"ptb{par}_{i}", [128, 128], BF16) for i in range(2)],
            "vtb": p.sbuf(f"vtb{par}", [128, 512], F32),
            "rbf": p.sbuf(f"rbf{par}", [128, 512], BF16),
            "ubf": p.sbuf(f"ubf{par}", [128, 512], BF16),
            "obf": p.sbuf(f"obf{par}", [128, 512], BF16),
            "ktb": p.sbuf(f"ktb{par}", [128, 256], BF16),
            "scro": p.sbuf(f"scro{par}", [128, 512], F32),
        })
    sf32 = [p.sbuf(f"sf32_{i}", [128, 512], F32) for i in range(2)]
    sbfb = [p.sbuf(f"sbf_{i}", [128, 512], BF16) for i in range(2)]

    ps_p1 = p.psum("ps_p1", [128, 512], F32)
    ps_p2 = p.psum("ps_p2", [128, 512], F32)
    ps_r = p.psum("ps_r", [128, 512], F32)
    ps_u = p.psum("ps_u", [128, 512], F32)
    ps_o = p.psum("ps_o", [128, 512], F32)
    ps_t = p.psum("ps_t", [128, 768], BF16)
    ps_s1 = p.psum("ps_s1", [128, 512], F32)
    ps_s2 = p.psum("ps_s2", [128, 512], F32)

    for b, srcdr in ((idb, IDB), (idf, IDF), (id128, ID64B), (uti, UTI), (uts, UTS),
                     (lts, LTS), (onesrf, ONESRF)):
        p.dma("sync", b.ap[:, :], srcdr, writes=[b])
    p.step1("vector", "memset", ones8.ap[:, :], 1.0, reads=[], writes=[ones8])
    for i in range(2):
        p.step1("vector", "memset", sf32[i].ap[:, :], 0.0, reads=[], writes=[sf32[i]])
        p.step1("vector", "memset", sbfb[i].ap[:, :], 0.0, reads=[], writes=[sbfb[i]])

    for seg in range(nseg):
        ssl = slice(seg * 1024, (seg + 1) * 1024)
        p.dma("sync", kseg.ap.rearrange("p (c t) -> p c t", c=2),
              KH2[:, ssl].rearrange("(c p) t -> p c t", p=128), writes=[kseg])
        p.dma("sync", qseg.ap.rearrange("p (c t) -> p c t", c=2),
              QH2[:, ssl].rearrange("(c p) t -> p c t", p=128), writes=[qseg])
        p.dma("sync", vseg.ap.rearrange("p (c t) -> p c t", c=4),
              VH2[:, ssl].rearrange("(c p) t -> p c t", p=128), writes=[vseg])
        p.dma("sync", gseg.ap[0:8, :], GR[0:1, ssl].rearrange("a (c t) -> (a c) t", c=8), writes=[gseg])
        p.dma("sync", bseg.ap[0:8, :], BR[0:1, ssl].rearrange("a (c t) -> (a c) t", c=8), writes=[bseg])
        p.step1("vector", "tensor_tensor_scan", cumg.ap[0:8, :], ones8[0:8, :], gseg[0:8, :],
                0.0, ALU.mult, ALU.add, reads=[ones8, gseg], writes=[cumg])
        p.step1("tensor", "transpose", ps_p1[:, 0:8], cumg[0:8, :], idf[0:8, 0:8],
                reads=[cumg, idf], writes=[ps_p1])
        p.step1("vector", "tensor_copy", cumgt.ap[:, 0:8], ps_p1[:, 0:8], reads=[ps_p1], writes=[cumgt])
        p.step1("tensor", "transpose", ps_p1[:, 8:16], bseg[0:8, :], idf[0:8, 0:8],
                reads=[bseg, idf], writes=[ps_p1])
        p.step1("vector", "tensor_copy", betat.ap[:, 0:8], ps_p1[:, 8:16], reads=[ps_p1], writes=[betat])
        p.dma("sync", SCR_D[0:8, :], cumg[0:8, :], sem="d_rows", reads=[cumg])
        p.dma("sync", SCR_D[8:16, :], bseg[0:8, :], sem="d_rows", reads=[bseg])
        p.dma("sync", growc.ap[0:1, :], SCR_D[0:8, :].rearrange("c t -> (c t)"),
              writes=[growc], extra=[("d_rows", 0)])
        p.dma("sync", brwc.ap[0:1, :], SCR_D[8:16, :].rearrange("c t -> (c t)"),
              writes=[brwc], extra=[("d_rows", 0)])

        for c8 in range(8):
            cb_ = CB[(seg * 8 + c8) % 2]
            lam = cb_["lam"]; lamend = cb_["lamend"]; lesc = cb_["lesc"]
            e1 = cb_["e1"]; e2 = cb_["e2"]; s64a = cb_["s64a"]; s64b = cb_["s64b"]
            wpb = cb_["wpb"]; abf = cb_["abf"]; bbf = cb_["bbf"]; ptb = cb_["ptb"]
            vtb = cb_["vtb"]; rbf = cb_["rbf"]; ubf = cb_["ubf"]; obf = cb_["obf"]
            ktb = cb_["ktb"]; scro = cb_["scro"]
            csl = slice(c8 * 128, (c8 + 1) * 128)
            kc = [kseg[:, 1024 * i + c8 * 128:1024 * i + (c8 + 1) * 128] for i in range(2)]
            qc = [qseg[:, 1024 * i + c8 * 128:1024 * i + (c8 + 1) * 128] for i in range(2)]
            gcol = cumgt[:, c8:c8 + 1]
            bcol = betat[:, c8:c8 + 1]
            for i in range(2):
                p.step1("tensor", "matmul", ps_p1[:, 0:128], kc[i], kc[i],
                        start=(i == 0), stop=(i == 1), reads=[kseg], writes=[ps_p1])
            for i in range(2):
                p.step1("tensor", "matmul", ps_p1[:, 128:256], kc[i], qc[i],
                        start=(i == 0), stop=(i == 1), reads=[kseg, qseg], writes=[ps_p1])
            p.step1("tensor", "matmul", ps_p1[:, 256:384], onesrf[0:1, :], growc[0:1, csl],
                    start=True, stop=True, reads=[onesrf, growc], writes=[ps_p1])
            p.step1("tensor", "matmul", ps_p1[:, 384:512], onesrf[0:1, :], brwc[0:1, csl],
                    start=True, stop=True, reads=[onesrf, brwc], writes=[ps_p1])
            p.step1("vector", "tensor_scalar", e1.ap[:, :], ps_p1[:, 256:384], gcol, None,
                    ALU.subtract, reads=[ps_p1, cumgt], writes=[e1])
            p.step1("vector", "tensor_scalar_min", e1.ap[:, :], e1[:, :], 0.0, reads=[e1], writes=[e1])
            p.step1("scalar", "activation", e1.ap[:, :], e1[:, :], AF.Exp, reads=[e1], writes=[e1])
            p.step1("vector", "tensor_scalar", e2.ap[:, :], ps_p1[:, 256:384], -1.0, gcol,
                    ALU.mult, ALU.add, reads=[ps_p1, cumgt], writes=[e2])
            p.step1("vector", "tensor_scalar_min", e2.ap[:, :], e2[:, :], 0.0, reads=[e2], writes=[e2])
            p.step1("scalar", "activation", e2.ap[:, :], e2[:, :], AF.Exp, reads=[e2], writes=[e2])
            p.step1("vector", "tensor_mul", s64a.ap[:, :], e1[:, :], uti[:, :],
                    reads=[e1, uti], writes=[s64a])
            p.step1("vector", "tensor_mul", wpb.ap[:, :], s64a[:, :], ps_p1[:, 128:256],
                    reads=[s64a, ps_p1], writes=[wpb])
            p.step1("vector", "tensor_mul", s64a.ap[:, :], e1[:, :], uts[:, :],
                    reads=[e1, uts], writes=[s64a])
            p.step1("vector", "tensor_mul", s64a.ap[:, :], s64a[:, :], ps_p1[:, 384:512],
                    reads=[s64a, ps_p1], writes=[s64a])
            p.step1("vector", "tensor_mul", bbf[0].ap[:, :], s64a[:, :], ps_p1[:, 0:128],
                    reads=[s64a, ps_p1], writes=[bbf[0]])
            p.step1("vector", "tensor_mul", s64b.ap[:, :], e2[:, :], lts[:, :],
                    reads=[e2, lts], writes=[s64b])
            p.step1("vector", "tensor_scalar_mul", s64b.ap[:, :], s64b[:, :], bcol,
                    reads=[s64b, betat], writes=[s64b])
            p.step1("vector", "tensor_mul", abf[0].ap[:, :], s64b[:, :], ps_p1[:, 0:128],
                    reads=[s64b, ps_p1], writes=[abf[0]])
            p.step1("vector", "tensor_sub", ptb[0].ap[:, :], id128[:, :], bbf[0][:, :],
                    reads=[id128, bbf[0]], writes=[ptb[0]])
            cura, curb, curpt = 0, 0, 0
            for st_i in range(6):
                na, nb, npt = 1 - cura, 1 - curb, 1 - curpt
                p.step1("tensor", "matmul", ps_p2[:, 0:128], bbf[curb][:, :], abf[cura][:, :],
                        start=True, stop=True, reads=[bbf[curb], abf[cura]], writes=[ps_p2])
                p.step1("scalar", "activation", abf[na].ap[:, :], ps_p2[:, 0:128], AF.Copy,
                        reads=[ps_p2], writes=[abf[na]])
                if st_i < 5:
                    p.step1("tensor", "matmul", ps_p2[:, 128:256], abf[cura][:, :], bbf[curb][:, :],
                            start=True, stop=True, reads=[abf[cura], bbf[curb]], writes=[ps_p2])
                    p.step1("scalar", "activation", bbf[nb].ap[:, :], ps_p2[:, 128:256], AF.Copy,
                            reads=[ps_p2], writes=[bbf[nb]])
                p.step1("tensor", "matmul", ps_p2[:, 256:384], abf[na][:, :], ptb[curpt][:, :],
                        start=True, stop=True, reads=[abf[na], ptb[curpt]], writes=[ps_p2])
                p.step1("vector", "tensor_add", ptb[npt].ap[:, :], ps_p2[:, 256:384], ptb[curpt][:, :],
                        reads=[ps_p2, ptb[curpt]], writes=[ptb[npt]])
                cura, curb, curpt = na, nb, npt
            PT = ptb[curpt]
            p.step1("scalar", "activation", lam.ap[:, 0:1], gcol, AF.Exp, reads=[cumgt], writes=[lam])
            for i in range(2):
                p.step1("tensor", "matmul", ps_r[:, 0:512], kc[i], sbfb[i][:, :],
                        start=(i == 0), stop=(i == 1), reads=[kseg, sbfb[i]], writes=[ps_r])
            for i in range(4):
                p.step1("tensor", "transpose", ps_t[:, i * 128:(i + 1) * 128],
                        vseg[:, 1024 * i + c8 * 128:1024 * i + (c8 + 1) * 128], idb[:, :],
                        reads=[vseg, idb], writes=[ps_t])
            p.step1("vector", "tensor_copy", vtb.ap[:, :], ps_t[:, 0:512], reads=[ps_t], writes=[vtb])
            p.step1("vector", "tensor_scalar", scro.ap[:, :], ps_r[:, 0:512], lam[:, 0:1], None,
                    ALU.mult, reads=[ps_r, lam], writes=[scro])
            p.step1("vector", "tensor_sub", scro.ap[:, :], vtb[:, :], scro[:, :],
                    reads=[vtb, scro], writes=[scro])
            p.step1("vector", "tensor_scalar", rbf.ap[:, :], scro[:, :], bcol, None,
                    ALU.mult, reads=[scro, betat], writes=[rbf])
            p.step1("tensor", "matmul", ps_u[:, 0:512], PT[:, :], rbf[:, :],
                    start=True, stop=True, reads=[PT, rbf], writes=[ps_u])
            p.step1("scalar", "activation", ubf.ap[:, :], ps_u[:, 0:512], AF.Copy,
                    reads=[ps_u], writes=[ubf])
            p.step1("tensor", "matmul", ps_o[:, 0:512], wpb[:, :], ubf[:, :],
                    start=True, stop=True, reads=[wpb, ubf], writes=[ps_o])
            for i in range(2):
                p.step1("tensor", "matmul", ps_r[:, 0:512], qc[i], sbfb[i][:, :],
                        start=(i == 0), stop=(i == 1), reads=[qseg, sbfb[i]], writes=[ps_r])
            p.step1("vector", "tensor_scalar", scro.ap[:, :], ps_r[:, 0:512], lam[:, 0:1], None,
                    ALU.mult, reads=[ps_r, lam], writes=[scro])
            p.step1("vector", "tensor_add", obf.ap[:, :], scro[:, :], ps_o[:, 0:512],
                    reads=[scro, ps_o], writes=[obf])
            cg = 8 * seg + c8
            p.dma("sync", OO2[128 * cg:128 * (cg + 1), :], obf[:, :], reads=[obf])
            for i in range(2):
                p.step1("tensor", "transpose", ps_t[:, 512 + i * 128:512 + (i + 1) * 128],
                        kc[i], idb[:, :], reads=[kseg, idb], writes=[ps_t])
            p.step1("vector", "tensor_copy", ktb.ap[:, :], ps_t[:, 512:768], reads=[ps_t], writes=[ktb])
            p.step1("vector", "tensor_scalar", ktb.ap[:, :], ktb[:, :], e1[:, 127:128], None,
                    ALU.mult, reads=[ktb, e1], writes=[ktb])
            p.step1("scalar", "activation", lesc.ap[0:1, 0:1], growc[0:1, 128 * c8 + 127:128 * c8 + 128],
                    AF.Exp, reads=[growc], writes=[lesc])
            p.step1("tensor", "matmul", ps_p2[:, 384:385], onesrf[0:1, :], lesc[0:1, 0:1],
                    start=True, stop=True, reads=[onesrf, lesc], writes=[ps_p2])
            p.step1("vector", "tensor_copy", lamend.ap[:, 0:1], ps_p2[:, 384:385],
                    reads=[ps_p2], writes=[lamend])
            for i in range(2):
                pss = (ps_s1, ps_s2)[i]
                p.step1("tensor", "matmul", pss[:, 0:512], ktb[:, i * 128:(i + 1) * 128], ubf[:, :],
                        start=True, stop=True, reads=[ktb, ubf], writes=[pss])
                p.step1("vector", "scalar_tensor_tensor", sf32[i].ap[:, :], sf32[i][:, :],
                        lamend[:, 0:1], pss[:, 0:512], ALU.mult, ALU.add,
                        reads=[sf32[i], lamend, pss], writes=[sf32[i]])
                p.step1("scalar", "activation", sbfb[i].ap[:, :], sf32[i][:, :], AF.Copy,
                        reads=[sf32[i]], writes=[sbfb[i]])
    return p.finalize([(s, p.cnt[s]) for s in sorted(p.store_sems)])


def stage_d2(g, QH, KH, VH, GB, core):
    h = core if core < 6 else 0
    return {
        "qh2": _c(QH[256 * h:256 * (h + 1), :]),
        "kh2": _c(KH[256 * h:256 * (h + 1), :]),
        "vh2": _c(VH[512 * h:512 * (h + 1), :]),
        "gr": _c(GB[h:h + 1, :]),
        "br": _c(GB[6 + h:7 + h, :]),
        "id_bf": g["id_bf"], "id_f": g["id_f"], "id128b": g["id_bf"],
        "utri_inc": g["utri_inc"], "utri_strict": g["utri_strict"], "ltri_strict": g["ltri_strict"],
        "ones_row_f": g["ones_row_f"],
    }



def build_d3e():
    p = P("d3e")
    OH = p.din("oh", [TOWN, 3072], BF16)
    X2T = p.din("x2t", [2048, TOWN], F32)
    X2LNT = p.din("x2lnt", [2048, TOWN], BF16)
    WG1 = p.din("wg1", [2048, 3072], F32)
    WO1 = p.din("wo1", [3072, 2048], F32)
    ONW = p.din("onw", [1, 512], F32)
    GATEW = p.din("gatew", [2048, 8], F32)
    W1L = p.din("w1l", [8, 2048, 1024], F32)
    W3L = p.din("w3l", [8, 2048, 1024], F32)
    W2L = p.din("w2l", [8, 1024, 2048], F32)
    WS1L = p.din("ws1l", [2048, 1024], F32)
    WS3L = p.din("ws3l", [2048, 1024], F32)
    WS2L = p.din("ws2l", [1024, 2048], F32)
    ONESB = p.din("ones_col_bf", [128, 1], BF16)
    ONESRF = p.din("ones_row_f", [1, 128], F32)
    IDB = p.din("id_bf", [128, 128], BF16)
    X4O = p.dout("x4o", [2048, TOWN], F32)
    CROW_D = p.nc.dram_tensor("crow_d", [8, TOWN], F32, kind="Internal").ap()

    NT2 = [(0, 512), (512, 512)]
    lc = mk_ln_ctx(p, TOWN)
    onesb = p.sbuf("c_onesb", [128, 1], BF16)
    onesrf = p.sbuf("c_onesrf", [1, 128], F32)
    lc["onesb"] = onesb; lc["onesrf"] = onesrf
    idb = p.sbuf("c_idb", [128, 128], BF16)
    onwrep = p.sbuf("onwrep", [128, 512], F32)
    onwrow = p.sbuf("onwrow", [1, 512], F32)

    yacc = p.sbuf("yacc", [128, 16 * TOWN], F32)
    srow = lc["srow"]; srow2 = lc["srow2"]; rrep = lc["rrep"]; scr = lc["scr"]; scr2 = lc["scr2"]
    ps_r1 = lc["ps_r1"]; ps_m = lc["ps_m"]; ps_n = lc["ps_n"]
    ps_a = p.psum("ps_a", [128, 512], F32)
    ps_b = p.psum("ps_b", [128, 512], F32)
    ps_t = p.psum("ps_t", [128, 1024], BF16)

    X3D = p.nc.dram_tensor("x3spill", [2048, TOWN], F32, kind="Internal").ap()
    for b, srcdr in ((onesb, ONESB), (onesrf, ONESRF), (idb, IDB), (onwrow, ONW)):
        p.dma("sync", b.ap[:, :], srcdr, writes=[b])
    x3ln = p.sbuf("x3ln", [128, 16 * TOWN], BF16, side="right")

    bcast_row(p, ps_m.ap[:, 0:512], onesrf[:, :], onwrow[0:1, :], reads=[onwrow, onesrf], writes=[ps_m])
    p.step1("vector", "tensor_copy", onwrep.ap[:, :], ps_m[:, 0:512], reads=[ps_m], writes=[onwrep])

    x3ln_r = x3ln
    x2lnf = p.sbuf("x2lnf", [128, 16 * TOWN], BF16, side="right")
    p.dma("sync", x2lnf.ap.rearrange("p (k t) -> p k t", k=16),
          X2LNT.rearrange("(k p) t -> p k t", p=128), writes=[x2lnf])
    ggall = p.sbuf("ggall", [128, 4 * 3072], BF16, side="right")
    wgb = p.sbuf("wgb", [128, 16 * 512], BF16, side="right")
    otile = p.sbuf("otile", [128, 3072], BF16, side="right")
    oft = p.sbuf("oft", [128, 24 * 128], BF16, side="right")
    wol = [p.sbuf("wol0", [128, 24 * 128], BF16, side="right")] * 2
    rsq = p.sbuf("rsq", [128, 1], F32, side="right")
    pool4 = [ps_a, ps_b, ps_m, ps_n]
    pi = 0
    for half in range(2):
        tts = range(4 * half, 4 * half + 4)
        for ns in range(6):
            p.dma("gpsimd", wgb.ap.rearrange("p (k m) -> p k m", k=16),
                  WG1[:, ns * 512:(ns + 1) * 512].rearrange("(k p) m -> p k m", p=128), writes=[wgb])
            for tt in tts:
                ps = pool4[pi % 4]; pi += 1
                mm_group(p, ps.ap[:, 0:512],
                         [x2lnf[:, k * TOWN + tt * 128:k * TOWN + (tt + 1) * 128] for k in range(16)],
                         [wgb[:, k * 512:(k + 1) * 512] for k in range(16)],
                         reads=[x2lnf, wgb], writes=[ps])
                p.step1("scalar", "activation",
                        ggall.ap[:, (tt - 4 * half) * 3072 + ns * 512:(tt - 4 * half) * 3072 + (ns + 1) * 512],
                        ps[:, 0:512], AF.Silu, reads=[ps], writes=[ggall])
        for tt in tts:
            tsl = slice(tt * 128, (tt + 1) * 128)
            g0 = (tt - 4 * half) * 3072
            p.dma("sync", otile.ap[:, :], OH[tsl, :], writes=[otile])
            for h in range(6):
                hsl = slice(h * 512, (h + 1) * 512)
                p.step1("scalar", "activation", lc["sqs"][0].ap[:, 0:512], otile[:, hsl], AF.Square,
                        reads=[otile], writes=[lc["sqs"][0]])
                p.step1("vector", "tensor_reduce", rsq.ap[:, 0:1], lc["sqs"][0][:, 0:512],
                        mybir.AxisListType.X, ALU.add, reads=[lc["sqs"][0]], writes=[rsq])
                p.step1("vector", "tensor_scalar", rsq.ap[:, 0:1], rsq[:, 0:1], 1.0 / 512, 1e-5,
                        ALU.mult, ALU.add, reads=[rsq], writes=[rsq])
                p.step1("scalar", "activation", rsq.ap[:, 0:1], rsq[:, 0:1], AF.Sqrt,
                        reads=[rsq], writes=[rsq])
                p.step1("vector", "reciprocal", rsq.ap[:, 0:1], rsq[:, 0:1], reads=[rsq], writes=[rsq])
                p.step1("vector", "tensor_scalar", scr.ap[:, 0:512], otile[:, hsl], rsq[:, 0:1], None,
                        ALU.mult, reads=[otile, rsq], writes=[scr])
                p.step1("vector", "tensor_mul", scr.ap[:, 0:512], scr[:, 0:512], onwrep[:, :],
                        reads=[scr, onwrep], writes=[scr])
                p.step1("vector", "tensor_mul", otile.ap[:, hsl], scr[:, 0:512],
                        ggall[:, g0 + h * 512:g0 + (h + 1) * 512],
                        reads=[scr, ggall], writes=[otile])
            for v in range(24):
                p.step1("tensor", "transpose", ps_t[:, (v % 8) * 128:(v % 8 + 1) * 128],
                        otile[:, v * 128:(v + 1) * 128], idb[:, :], reads=[otile, idb], writes=[ps_t])
                p.step1("scalar", "activation", oft.ap[:, v * 128:(v + 1) * 128],
                        ps_t[:, (v % 8) * 128:(v % 8 + 1) * 128], AF.Copy, reads=[ps_t], writes=[oft])
            for mc in range(16):
                wb = wol[mc % 2]
                p.dma("gpsimd", wb.ap.rearrange("p (k m) -> p k m", k=24)[:, 0:24, :],
                      WO1[:, mc * 128:(mc + 1) * 128].rearrange("(k p) m -> p k m", p=128), writes=[wb])
                ps = pool4[pi % 4]; pi += 1
                mm_group(p, ps.ap[:, 0:128], [wb[:, k * 128:(k + 1) * 128] for k in range(24)],
                         [oft[:, k * 128:(k + 1) * 128] for k in range(24)],
                         reads=[wb, oft], writes=[ps])
                p.step1("vector", "tensor_copy", yacc.ap[:, mc * TOWN + tt * 128:mc * TOWN + (tt + 1) * 128],
                        ps[:, 0:128], reads=[ps], writes=[yacc])
    p.free(rsq)
    p.free(wol[0], oft, otile, wgb, ggall, x2lnf)

    x2 = p.sbuf("x2b", [128, 16 * TOWN], F32, side="right")
    p.dma("sync", x2.ap.rearrange("p (k t) -> p k t", k=16),
          X2T.rearrange("(k p) t -> p k t", p=128), writes=[x2])
    emit_ln_c(p, lc, yacc, x3ln, TOWN, NT2)
    for k in range(16):
        for (o0, n) in NT2:
            p.step1("scalar", "activation", scr2.ap[:, 0:n], x3ln[:, k * TOWN + o0:k * TOWN + o0 + n],
                    AF.Copy, reads=[x3ln], writes=[scr2])
            p.step1("vector", "tensor_add", x2.ap[:, k * TOWN + o0:k * TOWN + o0 + n],
                    x2[:, k * TOWN + o0:k * TOWN + o0 + n], scr2[:, 0:n], reads=[x2, scr2], writes=[x2])
        p.dma("sync", X3D[k * 128:(k + 1) * 128, :], x2[:, k * TOWN:(k + 1) * TOWN], sem="sp_x3", reads=[x2])
    emit_ln_c(p, lc, x2, x3ln, TOWN, NT2)
    p.free(x2)

    gw = p.sbuf("gw", [128, 8], BF16, side="right")
    sc8 = p.sbuf("sc8", [128, 8], F32, side="right")
    m1c = p.sbuf("m1c", [128, 1], F32, side="right")
    combt = p.sbuf("combt", [8, TOWN], F32, side="right")
    x3ln_c = p.subdiv(x3ln, 16, TOWN)
    for tt in range(8):
        tsl = slice(tt * 128, (tt + 1) * 128)
        for k in range(16):
            if tt == 0 and k == 0:
                pass
            p.dma("gpsimd", gw.ap[:, :], GATEW[k * 128:(k + 1) * 128, :], writes=[gw])
            p.step1("tensor", "matmul", ps_a[:, 0:8], x3ln_c[k][:, tsl], gw[:, :],
                    start=(k == 0), stop=(k == 15), reads=[x3ln_c[k], gw], writes=[ps_a])
        p.step1("vector", "reduce_max", m1c.ap[:, 0:1], ps_a[:, 0:8], mybir.AxisListType.X,
                reads=[ps_a], writes=[m1c])
        p.step1("vector", "tensor_scalar", sc8.ap[:, :], ps_a[:, 0:8], m1c[:, 0:1], None,
                ALU.subtract, reads=[ps_a, m1c], writes=[sc8])
        p.step1("scalar", "activation", sc8.ap[:, :], sc8[:, :], AF.Exp, reads=[sc8], writes=[sc8])
        p.step1("vector", "tensor_reduce", m1c.ap[:, 0:1], sc8[:, :], mybir.AxisListType.X, ALU.add,
                reads=[sc8], writes=[m1c])
        p.step1("vector", "reciprocal", m1c.ap[:, 0:1], m1c[:, 0:1], reads=[m1c], writes=[m1c])
        p.step1("vector", "tensor_scalar", sc8.ap[:, :], sc8[:, :], m1c[:, 0:1], None, ALU.mult,
                reads=[sc8, m1c], writes=[sc8])
        p.step1("vector", "reduce_max", m1c.ap[:, 0:1], sc8[:, :], mybir.AxisListType.X,
                reads=[sc8], writes=[m1c])
        p.step1("vector", "tensor_scalar", scr.ap[:, 0:8], sc8[:, :], m1c[:, 0:1], None,
                ALU.is_ge, reads=[sc8, m1c], writes=[scr])
        p.step1("vector", "scalar_tensor_tensor", scr.ap[:, 8:16], scr[:, 0:8], -1e30,
                sc8[:, :], ALU.mult, ALU.add, reads=[scr, sc8], writes=[scr])
        p.step1("vector", "reduce_max", m1c.ap[:, 0:1], scr[:, 8:16], mybir.AxisListType.X,
                reads=[scr], writes=[m1c])
        p.step1("vector", "tensor_scalar", scr.ap[:, 0:8], sc8[:, :], m1c[:, 0:1], None,
                ALU.is_ge, reads=[sc8, m1c], writes=[scr])
        p.step1("vector", "tensor_mul", sc8.ap[:, :], sc8[:, :], scr[:, 0:8],
                reads=[sc8, scr], writes=[sc8])
        p.step1("scalar", "activation", lc["sqs"][0].ap[:, 0:8], sc8[:, :], AF.Copy,
                reads=[sc8], writes=[lc["sqs"][0]])
        p.step1("tensor", "transpose", ps_t[0:8, 0:128], lc["sqs"][0][:, 0:8], idb[:, :],
                reads=[lc["sqs"][0], idb], writes=[ps_t])
        p.step1("vector", "tensor_copy", combt.ap[0:8, tsl], ps_t[0:8, 0:128],
                reads=[ps_t], writes=[combt])
    p.dma("sync", CROW_D, combt[0:8, :], sem="d_crow", reads=[combt])
    p.free(combt, m1c, sc8, gw)
    crow = p.sbuf("crow", [1, TOWN], F32, side="right")

    prodb = p.sbuf("prodb", [128, 8 * TOWN], BF16, side="right")
    prod_c = p.subdiv(prodb, 8, TOWN)
    w1b = [p.sbuf(f"w1b{i}", [128, 16 * 128], BF16, side="right") for i in range(3)]
    w3b = [p.sbuf(f"w3b{i}", [128, 16 * 128], BF16, side="right") for i in range(3)]
    w2b = [p.sbuf(f"w2b{i}", [128, 8 * 128], BF16, side="right") for i in range(3)]
    gb1 = p.sbuf("gb1", [128, 512], BF16, side="right")
    crep = p.sbuf("crep", [128, TOWN], F32, side="right")
    p.step1("vector", "memset", yacc.ap[:, :], 0.0, reads=[yacc], writes=[yacc])
    for e in range(9):
        if e < 8:
            p.dma("sync", crow.ap[0:1, :], CROW_D[e:e + 1, :], writes=[crow], extra=[("d_crow", 0)])
            for (o0, n) in NT2:
                bcast_row(p, ps_m.ap[:, 0:n], onesrf[:, :], crow[0:1, o0:o0 + n],
                          reads=[crow, onesrf], writes=[ps_m])
                p.step1("vector", "tensor_copy", crep.ap[:, o0:o0 + n], ps_m[:, 0:n],
                        reads=[ps_m], writes=[crep])
        else:
            p.step1("vector", "memset", crep.ap[:, :], 1.0, reads=[], writes=[crep])
        for mc in range(8):
            wa = w1b[(e * 8 + mc) % 3]; wc = w3b[(e * 8 + mc) % 3]
            srcW1 = W1L[e] if e < 8 else WS1L
            srcW3 = W3L[e] if e < 8 else WS3L
            p.dma("gpsimd", wa.ap.rearrange("p (k m) -> p k m", k=16),
                  srcW1[:, mc * 128:(mc + 1) * 128].rearrange("(k p) m -> p k m", p=128), writes=[wa])
            p.dma("gpsimd", wc.ap.rearrange("p (k m) -> p k m", k=16),
                  srcW3[:, mc * 128:(mc + 1) * 128].rearrange("(k p) m -> p k m", p=128), writes=[wc])
            for (o0, n) in NT2:
                psx = pool4[pi % 4]; pi += 1
                mm_group(p, psx.ap[:, 0:n], [wa[:, k * 128:(k + 1) * 128] for k in range(16)],
                         [c[:, o0:o0 + n] for c in x3ln_c], reads=[wa] + x3ln_c, writes=[psx])
                p.step1("scalar", "activation", gb1.ap[:, 0:n], psx[:, 0:n], AF.Silu,
                        reads=[psx], writes=[gb1])
                psy = pool4[pi % 4]; pi += 1
                mm_group(p, psy.ap[:, 0:n], [wc[:, k * 128:(k + 1) * 128] for k in range(16)],
                         [c[:, o0:o0 + n] for c in x3ln_c], reads=[wc] + x3ln_c, writes=[psy])
                p.step1("vector", "tensor_mul", scr.ap[:, 0:n], gb1[:, 0:n], psy[:, 0:n],
                        reads=[gb1, psy], writes=[scr])
                p.step1("vector", "tensor_mul", prod_c[mc].ap[:, o0:o0 + n], scr[:, 0:n],
                        crep[:, o0:o0 + n], reads=[scr, crep], writes=[prod_c[mc]])
        for mc in range(16):
            wd = w2b[(e * 16 + mc) % 3]
            srcW2 = W2L[e] if e < 8 else WS2L
            p.dma("gpsimd", wd.ap.rearrange("p (k m) -> p k m", k=8),
                  srcW2[:, mc * 128:(mc + 1) * 128].rearrange("(k p) m -> p k m", p=128), writes=[wd])
            for (o0, n) in NT2:
                psx = pool4[pi % 4]; pi += 1
                mm_group(p, psx.ap[:, 0:n], [wd[:, k * 128:(k + 1) * 128] for k in range(8)],
                         [c[:, o0:o0 + n] for c in prod_c], reads=[wd] + prod_c, writes=[psx])
                p.step1("vector", "tensor_add", yacc.ap[:, mc * TOWN + o0:mc * TOWN + o0 + n],
                        yacc[:, mc * TOWN + o0:mc * TOWN + o0 + n], psx[:, 0:n],
                        reads=[yacc, psx], writes=[yacc])
    p.free(crep, gb1, *w2b[::-1], *w3b[::-1], *w1b[::-1], prodb)

    x3f = p.sbuf("x3f", [128, 16 * TOWN], F32, side="right")
    rscr2 = p.sbuf("rscr2", [128, TOWN], F32, side="right")
    p.dma("sync", x3f.ap.rearrange("p (k t) -> p k t", k=16),
          X3D.rearrange("(k p) t -> p k t", p=128), writes=[x3f],
          extra=[("sp_x3", 0)])
    emit_ln_c(p, lc, yacc, x3ln, TOWN, NT2)
    for k in range(16):
        ks = slice(k * TOWN, (k + 1) * TOWN)
        p.step1("scalar", "activation", rscr2.ap[:, :], x3ln[:, ks], AF.Copy, reads=[x3ln], writes=[rscr2])
        p.step1("vector", "tensor_add", x3f.ap[:, ks], x3f[:, ks], rscr2[:, :], reads=[x3f, rscr2], writes=[x3f])
        p.dma("sync", X4O[k * 128:(k + 1) * 128, :], x3f[:, ks], reads=[x3f])
    return p.finalize([(s, p.cnt[s]) for s in sorted(p.store_sems)])


def stage_d3e(g, O_full, X2, X2LN, core):
    s = slice(1024 * core, 1024 * (core + 1))
    return {
        "oh": _c(O_full[s, :]), "x2t": _c(X2[:, s]), "x2lnt": _c(X2LN[:, s]),
        "wg1": g["l1_Wg"], "wo1": g["l1_Wo"], "onw": g["l1_onw"],
        "gatew": g["f1_gate"], "w1l": g["f1_w1l"], "w3l": g["f1_w3l"], "w2l": g["f1_w2l"],
        "ws1l": g["f1_ws1l"], "ws3l": g["f1_ws3l"], "ws2l": g["f1_ws2l"],
        "ones_col_bf": g["ones_col_bf"], "ones_row_f": g["ones_row_f"], "id_bf": g["id_bf"],
    }


def stage_a2(B_full, G_full, g, core):
    return {
        "bo2": _c(B_full[256 * core:256 * (core + 1), :]),
        "go2": _c(G_full[2 * core:2 * (core + 1), :]),
        "ones_row_f": g["ones_row_f"],
    }


def stage_a1(g, core):
    s = slice(1024 * core, 1024 * core + TLOC)
    return {
        "xt": _c(g["xT_pad"][:, s]),
        "wq": g["l0_Wq"], "wk": g["l0_Wk"], "wv": g["l0_Wv"], "wi": g["l0_ig"],
        "wog": g["l0_og"], "wgam": g["l0_gam"],
        "convw": g["l0_convw"], "convb": g["l0_convb"], "igb": g["l0_igb"],
        "ogb": g["l0_ogb"], "gamb": g["l0_gamb"], "vng": g["l0_vng"], "vnb": g["l0_vnb"],
        "ones_col_bf": g["ones_col_bf"], "ones_row_f": g["ones_row_f"],
    }


_BUILT = {}


def _get(name, builder):
    if name not in _BUILT:
        _BUILT[name] = builder()
    return _BUILT[name]


def kernel(**inputs):
    g = stage_common(inputs)
    cores = list(range(NCORES))

    nc = _get("a1", build_a1)
    res = run_bass_kernel_spmd(nc, [stage_a1(g, c) for c in cores], core_ids=cores)
    Q_full = np.concatenate([res.results[c]["qo"] for c in cores], axis=1)
    B_full = np.concatenate([res.results[c]["bo"] for c in cores], axis=1)
    O_full = np.concatenate([res.results[c]["oo"] for c in cores], axis=1)
    G_full = np.concatenate([res.results[c]["go"] for c in cores], axis=1)

    nc = _get("a2", build_a2)
    res = run_bass_kernel_spmd(nc, [stage_a2(B_full, G_full, g, c) for c in cores], core_ids=cores)
    M_full = np.concatenate([res.results[c]["mo"] for c in cores], axis=0)

    nc = _get("a3c", build_a3c)
    res = run_bass_kernel_spmd(nc, [stage_a3c(g, M_full, Q_full, O_full, c) for c in cores],
                               core_ids=cores)
    X2 = np.concatenate([res.results[c]["x2o"] for c in cores], axis=1)
    X2LN = np.concatenate([res.results[c]["x2lno"] for c in cores], axis=1)

    X2LN_pad = np.pad(X2LN, ((0, 0), (HALO, 0)))
    nc = _get("d1", build_d1)
    res = run_bass_kernel_spmd(nc, [stage_d1(g, X2LN_pad, c) for c in cores], core_ids=cores)
    QH = np.concatenate([res.results[c]["qho"] for c in cores], axis=1)
    KH = np.concatenate([res.results[c]["kho"] for c in cores], axis=1)
    VH = np.concatenate([res.results[c]["vho"] for c in cores], axis=1)
    GB = np.concatenate([res.results[c]["gbo"] for c in cores], axis=1)

    nc = _get("d2", lambda: build_d2(16))
    res = run_bass_kernel_spmd(nc, [stage_d2(g, QH, KH, VH, GB, c) for c in cores], core_ids=cores)
    O_tm = np.concatenate([np.asarray(res.results[h]["oo2"])[1::2] for h in range(6)], axis=1)

    nc = _get("d3e", build_d3e)
    res = run_bass_kernel_spmd(nc, [stage_d3e(g, O_tm, X2, X2LN, c) for c in cores], core_ids=cores)
    X4 = np.concatenate([res.results[c]["x4o"] for c in cores], axis=1)
    return np.ascontiguousarray(X4.T).reshape(1, 8192, 2048).astype(np.float32)


# version 08
# speedup vs baseline: 1.0488x; 1.0106x over previous
import numpy as np
import ml_dtypes
from contextlib import ExitStack

import concourse.bass as bass
import concourse.mybir as mybir
from concourse.bass_utils import run_bass_kernel_spmd

F32 = mybir.dt.float32
BF16 = mybir.dt.bfloat16
AF = mybir.ActivationFunctionType
ALU = mybir.AluOpType
BF = ml_dtypes.bfloat16

NCORES = 8
DEBUG_A3C = False
TOWN = 1024
HALO = 8
TLOC = TOWN + HALO
ENGS = ("tensor", "vector", "scalar", "gpsimd", "sync")


class Buf:
    __slots__ = ("ap", "name", "ws", "rs", "children")

    def __init__(self, ap, name):
        self.ap = ap
        self.name = name
        self.ws = []
        self.rs = []
        self.children = []

    def __getitem__(self, sl):
        return self.ap[sl]


class P:
    def __init__(self, name):
        self.nc = bass.Bass("TRN2", target_bir_lowering=False)
        self.name = name
        self.ctx = ExitStack()
        self.streams = {e: [] for e in ENGS}
        self.sems = {}
        self.cnt = {}
        self.out_names = []
        self.store_sems = set()
        self._open_sbufs = {}
        self._freed_markers = []

    def sem(self, name):
        if name not in self.sems:
            self.sems[name] = self.ctx.enter_context(self.nc.semaphore(name))
            self.cnt[name] = 0
        return self.sems[name]

    def sbuf(self, name, shape, dtype, side=None):
        cm = self.nc.sbuf_tensor(name, shape, dtype, side=side)
        b = Buf(cm.__enter__(), name)
        b.rs = list(self._freed_markers)
        self._open_sbufs.setdefault(side or "left", []).append((b, cm))
        return b

    def free(self, *bufs):
        group = set(id(b) for b in bufs)
        for b in bufs:
            for bb in [b] + b.children:
                self._freed_markers.extend(bb.ws)
                self._freed_markers.extend(bb.rs)
        for side in list(self._open_sbufs):
            st = self._open_sbufs[side]
            while st and id(st[-1][0]) in group:
                b, cm = st.pop()
                cm.__exit__(None, None, None)
                group.discard(id(b))
        assert not group, f"free(): non-LIFO free requested: {[b.name for b in bufs if id(b) in group]}"

    def psum(self, name, shape, dtype=F32):
        return Buf(self.ctx.enter_context(self.nc.psum_tensor(name, shape, dtype)), name)

    def subdiv(self, buf, n, w):
        cs = []
        for i in range(n):
            c = Buf(buf.ap[:, i * w:(i + 1) * w], f"{buf.name}.{i}")
            c.ws = list(buf.ws)
            c.rs = list(buf.rs)
            cs.append(c)
        buf.children.extend(cs)
        return cs

    def merge(self, buf):
        ws, rs = list(buf.ws), list(buf.rs)
        for c in buf.children:
            ws.extend(c.ws)
            rs.extend(c.rs)
        for b in [buf] + buf.children:
            b.ws = list(ws)
            b.rs = list(rs)

    def din(self, name, shape, dtype):
        return self.nc.dram_tensor(name, shape, dtype, kind="ExternalInput").ap()

    def dout(self, name, shape, dtype):
        self.out_names.append(name)
        return self.nc.dram_tensor(name, shape, dtype, kind="ExternalOutput").ap()

    def _resolve(self, eng, reads, writes, extra):
        waits = {}
        drain = False
        me = "m_" + eng
        def add(mk):
            nonlocal drain
            if mk is None:
                return
            sn, v = mk
            if sn == me:
                drain = True
            elif sn.startswith("m_"):
                waits[sn] = max(waits.get(sn, 0), v)
            else:
                waits[sn] = max(waits.get(sn, 0), self.cnt[sn])
        for b in reads:
            for w in b.ws:
                add(w)
        for b in writes:
            for w in b.ws:
                add(w)
            for r in b.rs:
                add(r)
        for mk in extra:
            add(mk)
        return sorted(waits.items()), drain

    def _update(self, marker, reads, writes):
        for b in reads:
            b.rs.append(marker)
        for b in writes:
            b.ws = [marker]
            b.rs = []

    def step(self, eng, fn, reads=(), writes=(), extra=(), n=1):
        waits, drain = self._resolve(eng, reads, writes, extra)
        m = "m_" + eng
        self.sem(m)
        self.cnt[m] += 1
        val = self.cnt[m]
        def cl(engine, fn=fn, waits=waits, drain=drain, m=m):
            for sn, v in waits:
                engine.wait_ge(self.sems[sn], v)
            if drain and eng != "tensor":
                engine.drain()
            ins = fn(engine)
            ins.then_inc(self.sems[m], 1)
        self.streams[eng].append(cl)
        marker = (m, val)
        self._update(marker, reads, writes)
        return marker

    def step1(self, eng, meth, *args, reads=(), writes=(), extra=(), **kwargs):
        def fn(engine, meth=meth, args=args, kwargs=kwargs):
            return getattr(engine, meth)(*args, **kwargs)
        return self.step(eng, fn, reads=reads, writes=writes, extra=extra)

    def dma(self, eng, out, in_, sem=None, reads=(), writes=(), extra=()):
        if sem is None:
            if writes:
                sem = "d_" + writes[0].name
            else:
                sem = "st_" + reads[0].name
                self.store_sems.add(sem)
        waits, _ = self._resolve("__dma__", reads, writes, extra)
        self.sem(sem)
        self.cnt[sem] += 16
        val = self.cnt[sem]
        def cl(engine, out=out, in_=in_, waits=waits, sem=sem):
            for sn, v in waits:
                engine.wait_ge(self.sems[sn], v)
            engine.dma_start(out=out, in_=in_).then_inc(self.sems[sem], 16)
        self.streams[eng].append(cl)
        marker = (sem, val)
        self._update(marker, reads, writes)
        return marker

    def finalize(self, final_waits):
        def cl(engine, wl=tuple(final_waits)):
            for sn, v in wl:
                engine.wait_ge(self.sems[sn], v)
        self.streams["sync"].append(cl)
        block = self.ctx.enter_context(self.nc.Block())
        for e in ENGS:
            fns = self.streams[e]
            if not fns:
                continue
            getattr(block, e)(lambda engine, fns=fns: [fn(engine) for fn in fns])
        for side in list(self._open_sbufs):
            st = self._open_sbufs[side]
            while st:
                _, cm = st.pop()
                cm.__exit__(None, None, None)
        self.ctx.close()
        return self.nc



def mm_group(p, ps, lhsT_slices, rhs_slices, reads, writes, extra=()):
    nk = len(lhsT_slices)
    def fn(t):
        ins = None
        for k in range(nk):
            ins = t.matmul(ps, lhsT_slices[k], rhs_slices[k], start=(k == 0), stop=(k == nk - 1))
        return ins
    return p.step("tensor", fn, reads=reads, writes=writes, extra=extra, n=nk)


def part_reduce(p, ps_row, ones_bf, src_bf_ap, reads, writes):
    return p.step("tensor", lambda t: t.matmul(ps_row, ones_bf, src_bf_ap, start=True, stop=True),
                  reads=reads, writes=writes)


def bcast_row(p, ps_dst, ones_row_f, row_ap, reads, writes):
    return p.step("tensor", lambda t: t.matmul(ps_dst, ones_row_f, row_ap, start=True, stop=True),
                  reads=reads, writes=writes)



def _c(a):
    return np.ascontiguousarray(a)


def _pcol(vec):
    v = np.asarray(vec)
    return _c(v.reshape(-1, 128).T)


def stage_common(inputs):
    g = {}
    x = np.asarray(inputs["L0_attn_x"])[0]
    g["xT"] = _c(x.T)
    g["xT_pad"] = np.pad(g["xT"], ((0, 0), (HALO, 0)))
    for nm in ("Wq", "Wk", "Wv", "Wo"):
        g["l0_" + nm] = _c(np.asarray(inputs[f"L0_attn_{nm}"]).T)
    g["l0_ig"] = _c(np.asarray(inputs["L0_attn_ig_w"]).T)
    g["l0_og"] = _c(np.asarray(inputs["L0_attn_og_w"]).T)
    g["l0_gam"] = _c(np.asarray(inputs["L0_attn_gamma_w"]).T)
    cw = np.asarray(inputs["L0_attn_conv_w"])[:, 0, :]
    g["l0_convw"] = _c(cw.reshape(16, 128, 4).transpose(1, 0, 2).reshape(128, 64))
    for nm, key in (("convb", "L0_attn_conv_b"), ("igb", "L0_attn_ig_b"), ("ogb", "L0_attn_og_b"),
                    ("gng", "L0_attn_gn_g"), ("gnb", "L0_attn_gn_b")):
        g["l0_" + nm] = _pcol(np.asarray(inputs[key]))
    g["l0_gamb"] = _c(np.asarray(inputs["L0_attn_gamma_b"]).reshape(16, 1))
    for nm, key in (("vng", "L0_attn_vn_g"), ("vnb", "L0_attn_vn_b"),
                    ("mng", "L0_attn_mn_g"), ("mnb", "L0_attn_mn_b")):
        g["l0_" + nm] = _c(np.asarray(inputs[key]).reshape(128, 1))
    g["up_l"] = _c(np.asarray(inputs["L0_ffn_W_up"]).T)
    g["gt_l"] = _c(np.asarray(inputs["L0_ffn_W_gate"]).T)
    g["dn_l"] = _c(np.asarray(inputs["L0_ffn_W_down"]).T)
    g["wsug"] = _pcol(np.asarray(inputs["L0_ffn_Wscale_ug"]))
    g["wsdp"] = _pcol(np.asarray(inputs["L0_ffn_Wscale_dp"]))
    g["suug"] = _pcol(np.asarray(inputs["L0_ffn_SU_ug"]))
    g["sudp"] = _pcol(np.asarray(inputs["L0_ffn_SU_dp"]))
    g["h128b"] = _hadamard(128, BF)
    g["l1_Wq"] = _c(np.asarray(inputs["L1_attn_Wq"]))
    for i in range(2):
        g[f"l1_Wk{i}"] = _c(np.asarray(inputs["L1_attn_Wk"])[i])
        g[f"l1_Wv{i}"] = _c(np.asarray(inputs["L1_attn_Wv"])[i])
        g[f"l1_Wb{i}"] = _c(np.asarray(inputs["L1_attn_Wb"])[i])
    g["l1_Wa"] = _c(np.asarray(inputs["L1_attn_Wa"]))
    g["l1_Wb0"] = g["l1_Wb0"]; g["l1_Wb1"] = g["l1_Wb1"]
    def _cpack(cwa):
        C = cwa.shape[0]
        return _c(cwa.reshape(C // 128, 128, 4).transpose(1, 0, 2).reshape(128, -1))
    g["l1_cqw"] = _cpack(np.asarray(inputs["L1_attn_conv_q_w"]))
    g["l1_ckw0"] = _cpack(np.asarray(inputs["L1_attn_conv_k_w"])[0])
    g["l1_ckw1"] = _cpack(np.asarray(inputs["L1_attn_conv_k_w"])[1])
    g["l1_cvw0"] = _cpack(np.asarray(inputs["L1_attn_conv_v_w"])[0])
    g["l1_cvw1"] = _cpack(np.asarray(inputs["L1_attn_conv_v_w"])[1])
    g["l1_alog"] = _c(np.asarray(inputs["L1_attn_A_log"]).reshape(6, 1))
    g["l1_dtb"] = _c(np.asarray(inputs["L1_attn_dt_bias"]).reshape(6, 1))
    g["l1_Wg"] = _c(np.asarray(inputs["L1_attn_Wg"]))
    g["l1_Wo"] = _c(np.asarray(inputs["L1_attn_Wo"]))
    g["l1_onw"] = _c(np.asarray(inputs["L1_attn_o_norm_w"]).reshape(1, 512))
    g["f1_gate"] = _c(np.asarray(inputs["L1_ffn_gate_w"]).T)
    g["f1_w1l"] = _c(np.asarray(inputs["L1_ffn_w1"]).transpose(0, 2, 1))
    g["f1_w3l"] = _c(np.asarray(inputs["L1_ffn_w3"]).transpose(0, 2, 1))
    g["f1_w2l"] = _c(np.asarray(inputs["L1_ffn_w2"]).transpose(0, 2, 1))
    g["f1_ws1l"] = _c(np.asarray(inputs["L1_ffn_ws1"]).T)
    g["f1_ws3l"] = _c(np.asarray(inputs["L1_ffn_ws3"]).T)
    g["f1_ws2l"] = _c(np.asarray(inputs["L1_ffn_ws2"]).T)
    r_ = np.arange(128)
    g["utri_inc"] = (r_[:, None] <= r_[None, :]).astype(np.float32)
    g["utri_strict"] = (r_[:, None] < r_[None, :]).astype(np.float32)
    g["ltri_strict"] = (r_[:, None] > r_[None, :]).astype(np.float32)
    g["ones_col_bf"] = np.ones((128, 1), BF)
    g["ones_row_f"] = np.ones((1, 128), np.float32)
    g["id_bf"] = np.eye(128, dtype=BF)
    g["id_f"] = np.eye(128, dtype=np.float32)
    return g



def emit_ln_c(p, ctxb, src_f32, dst_bf, T, nts, src_is_bf=False):
    xbfB = src_f32 if src_is_bf else dst_bf
    sqs = ctxb["sqs"]
    onesb = ctxb["onesb"]; onesrf = ctxb["onesrf"]
    ps_r1 = ctxb["ps_r1"]; ps_r2 = ctxb["ps_r2"]; ps_m = ctxb["ps_m"]; ps_n = ctxb["ps_n"]
    srow = ctxb["srow"]; srow2 = ctxb["srow2"]; srow3 = ctxb["srow3"]; rrep = ctxb["rrep"]; scr = ctxb["scr"]
    if not src_is_bf:
        for k in range(16):
            p.step1("scalar", "activation", xbfB[:, k * T:(k + 1) * T], src_f32[:, k * T:(k + 1) * T],
                    AF.Copy, reads=[src_f32], writes=[xbfB])
    for (o0, n) in nts:
        mm_group(p, ps_r1.ap[0:1, 0:n], [onesb[:, :]] * 16,
                 [xbfB[:, k * T + o0:k * T + o0 + n] for k in range(16)],
                 reads=[xbfB, onesb], writes=[ps_r1])
        for k in range(16):
            sb = sqs[k % 2]
            p.step1("scalar", "activation", sb[:, 0:n], src_f32[:, k * T + o0:k * T + o0 + n],
                    AF.Square, reads=[src_f32], writes=[sb])
            p.step1("tensor", "matmul", ps_r2[0:1, 0:n], onesb[:, :], sb[:, 0:n],
                    start=(k == 0), stop=(k == 15), reads=[sb, onesb], writes=[ps_r2])
        p.step1("vector", "tensor_scalar_mul", srow[0:1, 0:n], ps_r1[0:1, 0:n], 1.0 / 2048,
                reads=[ps_r1], writes=[srow])
        p.step1("vector", "tensor_scalar_mul", srow2[0:1, 0:n], ps_r2[0:1, 0:n], 1.0 / 2048,
                reads=[ps_r2], writes=[srow2])
        p.step1("vector", "tensor_mul", srow3[0:1, 0:n], srow[0:1, 0:n], srow[0:1, 0:n],
                reads=[srow], writes=[srow3])
        p.step1("vector", "tensor_sub", srow2[0:1, 0:n], srow2[0:1, 0:n], srow3[0:1, 0:n],
                reads=[srow2, srow3], writes=[srow2])
        p.step1("vector", "tensor_scalar_add", srow2[0:1, 0:n], srow2[0:1, 0:n], 1e-5,
                reads=[srow2], writes=[srow2])
        p.step1("scalar", "activation", srow2[0:1, 0:n], srow2[0:1, 0:n], AF.Sqrt,
                reads=[srow2], writes=[srow2])
        p.step1("vector", "reciprocal", srow2[0:1, 0:n], srow2[0:1, 0:n],
                reads=[srow2], writes=[srow2])
        bcast_row(p, ps_m.ap[:, 0:n], onesrf[:, :], srow[0:1, 0:n], reads=[srow, onesrf], writes=[ps_m])
        bcast_row(p, ps_n.ap[:, 0:n], onesrf[:, :], srow2[0:1, 0:n], reads=[srow2, onesrf], writes=[ps_n])
        p.step1("vector", "tensor_copy", rrep[:, 0:n], ps_n[:, 0:n], reads=[ps_n], writes=[rrep])
        scr2 = ctxb["scr2"]
        for k in range(16):
            if src_is_bf:
                p.step1("scalar", "activation", scr2[:, 0:n], src_f32[:, k * T + o0:k * T + o0 + n],
                        AF.Copy, reads=[src_f32], writes=[scr2])
                p.step1("vector", "tensor_sub", scr[:, 0:n], scr2[:, 0:n],
                        ps_m[:, 0:n], reads=[scr2, ps_m], writes=[scr])
            else:
                p.step1("vector", "tensor_sub", scr[:, 0:n], src_f32[:, k * T + o0:k * T + o0 + n],
                        ps_m[:, 0:n], reads=[src_f32, ps_m], writes=[scr])
            p.step1("vector", "tensor_mul", dst_bf[:, k * T + o0:k * T + o0 + n], scr[:, 0:n],
                    rrep[:, 0:n], reads=[scr, rrep], writes=[dst_bf])


def mk_ln_ctx(p, T):
    return {
        "sqs": [p.sbuf("lnsq0", [128, 512], BF16), p.sbuf("lnsq1", [128, 512], BF16)],
        "srow": p.sbuf("srow", [1, 512], F32),
        "srow2": p.sbuf("srow2", [1, 512], F32),
        "srow3": p.sbuf("srow3", [1, 512], F32),
        "rrep": p.sbuf("rrep", [128, 512], F32),
        "scr": p.sbuf("lnscr", [128, 512], F32),
        "scr2": p.sbuf("lnscr2", [128, 512], F32),
        "ps_r1": p.psum("ps_r1", [1, 512], F32),
        "ps_r2": p.psum("ps_r2", [1, 512], F32),
        "ps_m": p.psum("ps_m", [128, 512], F32),
        "ps_n": p.psum("ps_n", [128, 512], F32),
    }



def build_a1():
    p = P("a1")
    XT = p.din("xt", [2048, TLOC], F32)
    WQ = p.din("wq", [2048, 2048], F32)
    WK = p.din("wk", [2048, 2048], F32)
    WV = p.din("wv", [2048, 2048], F32)
    WI = p.din("wi", [2048, 2048], F32)
    WOG = p.din("wog", [2048, 2048], F32)
    WGAM = p.din("wgam", [2048, 16], F32)
    CONVW = p.din("convw", [128, 64], F32)
    CONVB = p.din("convb", [128, 16], F32)
    IGB = p.din("igb", [128, 16], F32)
    OGB = p.din("ogb", [128, 16], F32)
    GAMB = p.din("gamb", [16, 1], F32)
    VNG = p.din("vng", [128, 1], F32)
    VNB = p.din("vnb", [128, 1], F32)
    ONESB = p.din("ones_col_bf", [128, 1], BF16)
    ONESRF = p.din("ones_row_f", [1, 128], F32)

    QO = p.dout("qo", [2048, TOWN], BF16)
    BO = p.dout("bo", [2048, TOWN], BF16)
    OO = p.dout("oo", [2048, TOWN], BF16)
    GO = p.dout("go", [16, TOWN], F32)

    xln = p.sbuf("xln", [128, 16 * TLOC], BF16)
    xc = p.sbuf("xc", [128, 16 * TOWN], BF16)
    lc = mk_ln_ctx(p, TLOC)
    xt = p.sbuf("sb_xt", [128, 16 * TLOC], F32, side="right")
    wbuf = {nm: [p.sbuf(f"w_{nm}{s}", [128, 16 * 128], BF16) for s in range(3 if nm in ("q", "k") else 2)]
            for nm in ("q", "k", "v", "i", "o")}
    wgam = p.sbuf("wgam_b", [128, 16 * 16], BF16)
    convw = p.sbuf("c_convw", [128, 64], F32)
    convb = p.sbuf("c_convb", [128, 16], F32)
    igb = p.sbuf("c_igb", [128, 16], F32)
    ogb = p.sbuf("c_ogb", [128, 16], F32)
    gamb = p.sbuf("c_gamb", [16, 1], F32)
    vng = p.sbuf("c_vng", [128, 1], F32)
    vnb = p.sbuf("c_vnb", [128, 1], F32)
    onesb = p.sbuf("c_onesb", [128, 1], BF16)
    onesrf = p.sbuf("c_onesrf", [1, 128], F32)
    lc["onesb"] = onesb
    lc["onesrf"] = onesrf
    srow3 = lc["srow3"]
    acc = [p.sbuf(f"acc{i}", [128, TOWN], F32) for i in range(2)]
    srow = lc["srow"]; srow2 = lc["srow2"]; rrep = lc["rrep"]; scr = lc["scr"]
    ps_r1 = lc["ps_r1"]; ps_r2 = lc["ps_r2"]; ps_m = lc["ps_m"]; ps_n = lc["ps_n"]
    kbf = p.sbuf("kbf", [128, 512], BF16)
    vbf = p.sbuf("vbf", [128, 512], BF16)
    ibf = p.sbuf("ibf", [128, 512], BF16)
    qst = p.sbuf("qst", [128, 512], BF16)
    bst = p.sbuf("bst", [128, 512], BF16)
    ost = p.sbuf("ost", [128, 512], BF16)
    ps_a = p.psum("ps_a", [128, 512], F32)
    ps_b = p.psum("ps_b", [128, 512], F32)
    ps_g = p.psum("ps_g", [16, 512], F32)

    for i in range(16):
        p.dma("sync", xt[:, i * TLOC:(i + 1) * TLOC], XT[i * 128:(i + 1) * 128, :], writes=[xt])
    for b, srcdr in ((convw, CONVW), (convb, CONVB), (igb, IGB), (ogb, OGB), (gamb, GAMB),
                     (vng, VNG), (vnb, VNB), (onesb, ONESB), (onesrf, ONESRF)):
        p.dma("sync", b[:, :], srcdr, writes=[b])
    p.dma("gpsimd", wgam.ap.rearrange("p (k m) -> p k m", k=16),
          WGAM.rearrange("(k p) m -> p k m", p=128), writes=[wgam])

    emit_ln_c(p, lc, xt, xln, TLOC, [(0, 344), (344, 344), (688, 344)])
    p.free(xt)
    gam = p.sbuf("gam", [16, 512], F32, side="right")

    for kk in range(16):
        eng = "vector"
        a0, a1 = acc[0], acc[1]
        p.step1(eng, "tensor_scalar", a0[:, :], xln[:, kk * TLOC + 5:kk * TLOC + 5 + TOWN],
                convw[:, 4 * kk:4 * kk + 1], None, ALU.mult, reads=[xln, convw], writes=[a0])
        for j in range(3):
            srcacc, dst = (a0, a1) if j % 2 == 0 else (a1, a0)
            p.step1(eng, "scalar_tensor_tensor", dst[:, :],
                    xln[:, kk * TLOC + 6 + j:kk * TLOC + 6 + j + TOWN],
                    convw[:, 4 * kk + j + 1:4 * kk + j + 2], srcacc[:, :], ALU.mult, ALU.add,
                    reads=[xln, convw, srcacc], writes=[dst])
        p.step1("scalar", "activation", xc[:, kk * TOWN:(kk + 1) * TOWN], a1[:, :], AF.Silu,
                bias=convb[:, kk:kk + 1], reads=[a1, convb], writes=[xc])

    def pstats_mu_rstd(eps):
        p.step1("vector", "tensor_scalar_mul", srow[0:1, 0:512], ps_r1[0:1, 0:512], 1.0 / 128,
                reads=[ps_r1], writes=[srow])
        p.step1("vector", "tensor_scalar_mul", srow2[0:1, 0:512], ps_r2[0:1, 0:512], 1.0 / 128,
                reads=[ps_r2], writes=[srow2])
        p.step1("vector", "tensor_mul", srow3[0:1, 0:512], srow[0:1, 0:512], srow[0:1, 0:512],
                reads=[srow], writes=[srow3])
        p.step1("vector", "tensor_sub", srow2[0:1, 0:512], srow2[0:1, 0:512], srow3[0:1, 0:512],
                reads=[srow2, srow3], writes=[srow2])
        p.step1("vector", "tensor_scalar_add", srow2[0:1, 0:512], srow2[0:1, 0:512], eps,
                reads=[srow2], writes=[srow2])
        p.step1("scalar", "activation", srow2[0:1, 0:512], srow2[0:1, 0:512], AF.Sqrt,
                reads=[srow2], writes=[srow2])
        p.step1("vector", "reciprocal", srow2[0:1, 0:512], srow2[0:1, 0:512],
                reads=[srow2], writes=[srow2])

    xln_own = [xln[:, k * TLOC + HALO:k * TLOC + HALO + TOWN] for k in range(16)]
    for h in range(16):
        for nm, W in (("q", WQ), ("k", WK), ("v", WV), ("i", WI), ("o", WOG)):
            slot = wbuf[nm][h % len(wbuf[nm])]
            p.dma("gpsimd", slot.ap.rearrange("p (k m) -> p k m", k=16),
                  W[:, h * 128:(h + 1) * 128].rearrange("(k p) m -> p k m", p=128),
                  writes=[slot])
        for nt in range(2):
            o0 = nt * 512
            osl = slice(o0, o0 + 512)
            mm_group(p, ps_a.ap[:, 0:512], [wbuf["q"][h % 3][:, k * 128:(k + 1) * 128] for k in range(16)],
                     [c[:, osl] for c in xln_own], reads=[wbuf["q"][h % 3], xln], writes=[ps_a])
            p.step1("scalar", "activation", lc["sqs"][0].ap[:, 0:512], ps_a[:, 0:512], AF.Square,
                    reads=[ps_a], writes=[lc["sqs"][0]])
            p.step1("tensor", "matmul", ps_r1[0:1, 0:512], onesb[:, :], lc["sqs"][0].ap[:, 0:512],
                    start=True, stop=True, reads=[lc["sqs"][0], onesb], writes=[ps_r1])
            p.step1("scalar", "activation", srow[0:1, 0:512], ps_r1[0:1, 0:512], AF.Sqrt,
                    reads=[ps_r1], writes=[srow])
            p.step1("vector", "tensor_scalar_max", srow[0:1, 0:512], srow[0:1, 0:512], 1e-12,
                    reads=[srow], writes=[srow])
            p.step1("vector", "reciprocal", srow[0:1, 0:512], srow[0:1, 0:512],
                    reads=[srow], writes=[srow])
            bcast_row(p, ps_m.ap[:, 0:512], onesrf[:, :], srow[0:1, 0:512], reads=[srow, onesrf], writes=[ps_m])
            p.step1("vector", "tensor_copy", rrep[:, 0:512], ps_m[:, 0:512], reads=[ps_m], writes=[rrep])
            p.step1("vector", "tensor_mul", qst[:, 0:512], ps_a[:, 0:512], rrep[:, 0:512],
                    reads=[ps_a, rrep], writes=[qst])
            p.dma("sync", QO[h * 128:(h + 1) * 128, osl], qst[:, 0:512], reads=[qst])
            mm_group(p, ps_b.ap[:, 0:512], [wbuf["k"][h % 3][:, k * 128:(k + 1) * 128] for k in range(16)],
                     [c[:, osl] for c in xln_own], reads=[wbuf["k"][h % 3], xln], writes=[ps_b])
            p.step1("scalar", "activation", lc["sqs"][0].ap[:, 0:512], ps_b[:, 0:512], AF.Square,
                    reads=[ps_b], writes=[lc["sqs"][0]])
            p.step1("tensor", "matmul", ps_r1[0:1, 0:512], onesb[:, :], lc["sqs"][0].ap[:, 0:512],
                    start=True, stop=True, reads=[lc["sqs"][0], onesb], writes=[ps_r1])
            p.step1("scalar", "activation", srow[0:1, 0:512], ps_r1[0:1, 0:512], AF.Sqrt,
                    reads=[ps_r1], writes=[srow])
            p.step1("vector", "tensor_scalar_max", srow[0:1, 0:512], srow[0:1, 0:512], 1e-12,
                    reads=[srow], writes=[srow])
            p.step1("vector", "reciprocal", srow[0:1, 0:512], srow[0:1, 0:512],
                    reads=[srow], writes=[srow])
            bcast_row(p, ps_m.ap[:, 0:512], onesrf[:, :], srow[0:1, 0:512], reads=[srow, onesrf], writes=[ps_m])
            p.step1("vector", "tensor_copy", rrep[:, 0:512], ps_m[:, 0:512], reads=[ps_m], writes=[rrep])
            p.step1("vector", "tensor_mul", kbf[:, 0:512], ps_b[:, 0:512], rrep[:, 0:512],
                    reads=[ps_b, rrep], writes=[kbf])
            mm_group(p, ps_a.ap[:, 0:512], [wbuf["v"][h % 2][:, k * 128:(k + 1) * 128] for k in range(16)],
                     [c[:, osl] for c in xln_own], reads=[wbuf["v"][h % 2], xln], writes=[ps_a])
            p.step1("scalar", "activation", lc["sqs"][1].ap[:, 0:512], ps_a[:, 0:512], AF.Copy,
                    reads=[ps_a], writes=[lc["sqs"][1]])
            p.step1("scalar", "activation", lc["sqs"][0].ap[:, 0:512], ps_a[:, 0:512], AF.Square,
                    reads=[ps_a], writes=[lc["sqs"][0]])
            p.step1("tensor", "matmul", ps_r1[0:1, 0:512], onesb[:, :], lc["sqs"][1].ap[:, 0:512],
                    start=True, stop=True, reads=[lc["sqs"][1], onesb], writes=[ps_r1])
            p.step1("tensor", "matmul", ps_r2[0:1, 0:512], onesb[:, :], lc["sqs"][0].ap[:, 0:512],
                    start=True, stop=True, reads=[lc["sqs"][0], onesb], writes=[ps_r2])
            pstats_mu_rstd(1e-5)
            bcast_row(p, ps_m.ap[:, 0:512], onesrf[:, :], srow[0:1, 0:512], reads=[srow, onesrf], writes=[ps_m])
            bcast_row(p, ps_n.ap[:, 0:512], onesrf[:, :], srow2[0:1, 0:512], reads=[srow2, onesrf], writes=[ps_n])
            p.step1("vector", "tensor_copy", rrep[:, 0:512], ps_n[:, 0:512], reads=[ps_n], writes=[rrep])
            p.step1("vector", "tensor_copy", lc["scr2"].ap[:, 0:512], ps_m[:, 0:512],
                    reads=[ps_m], writes=[lc["scr2"]])
            p.step1("vector", "tensor_sub", scr[:, 0:512], ps_a[:, 0:512], lc["scr2"][:, 0:512],
                    reads=[ps_a, lc["scr2"]], writes=[scr])
            p.step1("vector", "tensor_mul", scr[:, 0:512], scr[:, 0:512], rrep[:, 0:512],
                    reads=[scr, rrep], writes=[scr])
            p.step1("vector", "tensor_scalar", vbf[:, 0:512], scr[:, 0:512], vng[:, :], vnb[:, :],
                    ALU.mult, ALU.add, reads=[scr, vng, vnb], writes=[vbf])
            mm_group(p, ps_b.ap[:, 0:512], [wbuf["i"][h % 2][:, k * 128:(k + 1) * 128] for k in range(16)],
                     [xc[:, k * TOWN + o0:k * TOWN + o0 + 512] for k in range(16)],
                     reads=[wbuf["i"][h % 2], xc], writes=[ps_b])
            p.step1("scalar", "activation", ibf[:, 0:512], ps_b[:, 0:512], AF.Sigmoid,
                    bias=igb[:, h:h + 1], reads=[ps_b, igb], writes=[ibf])
            p.step1("vector", "tensor_mul", scr[:, 0:512], kbf[:, 0:512], vbf[:, 0:512],
                    reads=[kbf, vbf], writes=[scr])
            p.step1("vector", "tensor_mul", bst[:, 0:512], scr[:, 0:512], ibf[:, 0:512],
                    reads=[scr, ibf], writes=[bst])
            p.dma("sync", BO[h * 128:(h + 1) * 128, osl], bst[:, 0:512], reads=[bst])
            mm_group(p, ps_a.ap[:, 0:512], [wbuf["o"][h % 2][:, k * 128:(k + 1) * 128] for k in range(16)],
                     [xc[:, k * TOWN + o0:k * TOWN + o0 + 512] for k in range(16)],
                     reads=[wbuf["o"][h % 2], xc], writes=[ps_a])
            p.step1("scalar", "activation", ost[:, 0:512], ps_a[:, 0:512], AF.Sigmoid,
                    bias=ogb[:, h:h + 1], reads=[ps_a, ogb], writes=[ost])
            p.dma("sync", OO[h * 128:(h + 1) * 128, osl], ost[:, 0:512], reads=[ost])

    for nt in range(2):
        o0 = nt * 512
        mm_group(p, ps_g.ap[0:16, 0:512], [wgam[:, k * 16:(k + 1) * 16] for k in range(16)],
                 [xc[:, k * TOWN + o0:k * TOWN + o0 + 512] for k in range(16)],
                 reads=[wgam, xc], writes=[ps_g])
        p.step1("scalar", "activation", gam[0:16, 0:512], ps_g[0:16, 0:512], AF.Sigmoid,
                bias=gamb[0:16, :], reads=[ps_g, gamb], writes=[gam])
        p.dma("sync", GO[:, o0:o0 + 512], gam[0:16, 0:512], reads=[gam])

    return p.finalize([(s, p.cnt[s]) for s in sorted(p.store_sems)])



def build_a2():
    p = P("a2")
    BO2 = p.din("bo2", [256, 8192], BF16)
    GO2 = p.din("go2", [2, 8192], F32)
    ONESRF = p.din("ones_row_f", [1, 128], F32)
    MO = p.dout("mo", [256, 8192], BF16)

    onesrf = p.sbuf("c_onesrf", [1, 128], F32)
    grow = p.sbuf("grow", [1, 8192], F32)
    bb = p.sbuf("bb", [128, 8192], BF16)
    bf32 = p.sbuf("bf32", [128, 8192], F32)
    grep = p.sbuf("grep", [128, 8192], F32)
    mem = p.sbuf("mem", [128, 8192], BF16)
    ps_bc = p.psum("ps_bc", [128, 512], F32)

    p.dma("sync", onesrf[:, :], ONESRF, writes=[onesrf])
    for j in range(2):
        p.dma("sync", bb[:, :], BO2[j * 128:(j + 1) * 128, :], writes=[bb])
        p.dma("sync", grow[0:1, :], GO2[j:j + 1, :], writes=[grow])
        p.step1("scalar", "activation", bf32[:, :], bb[:, :], AF.Copy, reads=[bb], writes=[bf32])
        for nt in range(16):
            o0 = nt * 512
            bcast_row(p, ps_bc.ap[:, 0:512], onesrf[:, :], grow[0:1, o0:o0 + 512],
                      reads=[grow, onesrf], writes=[ps_bc])
            p.step1("vector", "tensor_copy", grep[:, o0:o0 + 512], ps_bc[:, 0:512],
                    reads=[ps_bc], writes=[grep])
        p.step1("vector", "tensor_tensor_scan", mem[:, :], grep[:, :], bf32[:, :], 0.0,
                ALU.mult, ALU.add, reads=[grep, bf32], writes=[mem])
        p.dma("sync", MO[j * 128:(j + 1) * 128, :], mem[:, :], reads=[mem])
    return p.finalize([(s, p.cnt[s]) for s in sorted(p.store_sems)])



def build_a3c():
    p = P("a3c")
    MO3 = p.din("mo3", [2048, TOWN], BF16)
    QO3 = p.din("qo3", [2048, TOWN], BF16)
    OO3 = p.din("oo3", [2048, TOWN], BF16)
    XT3 = p.din("xt3", [2048, TOWN], F32)
    WOL = p.din("wol", [2048, 2048], F32)
    WUPL = p.din("wupl", [2048, 8192], F32)
    WGTL = p.din("wgtl", [2048, 8192], F32)
    WDNL = p.din("wdnl", [8192, 2048], F32)
    WSUG = p.din("wsug", [128, 128], F32)
    WSDP = p.din("wsdp", [128, 16], F32)
    SUUG = p.din("suug", [128, 16], F32)
    SUDP = p.din("sudp", [128, 64], F32)
    H128B = p.din("h128b", [128, 128], BF16)
    MNG = p.din("mng", [128, 1], F32)
    MNB = p.din("mnb", [128, 1], F32)
    GNG = p.din("gng", [128, 16], F32)
    GNB = p.din("gnb", [128, 16], F32)
    ONESB = p.din("ones_col_bf", [128, 1], BF16)
    ONESRF = p.din("ones_row_f", [1, 128], F32)
    X2O = p.dout("x2o", [2048, TOWN], F32)
    X2LNO = p.dout("x2lno", [2048, TOWN], BF16)
    X1D = p.nc.dram_tensor("x1spill", [2048, TOWN], F32, kind="Internal").ap()

    NT2 = [(0, 512), (512, 512)]
    lc = mk_ln_ctx(p, TOWN)
    onesb = p.sbuf("c_onesb", [128, 1], BF16)
    onesrf = p.sbuf("c_onesrf", [1, 128], F32)
    lc["onesb"] = onesb; lc["onesrf"] = onesrf
    wsug = p.sbuf("c_wsug", [128, 128], F32)
    wsdp = p.sbuf("c_wsdp", [128, 16], F32)
    suug = p.sbuf("c_suug", [128, 16], F32)
    sudp = p.sbuf("c_sudp", [128, 64], F32)
    h128b = p.sbuf("c_h128", [128, 128], BF16)
    mng = p.sbuf("c_mng", [128, 1], F32)
    mnb = p.sbuf("c_mnb", [128, 1], F32)
    gng = p.sbuf("c_gng", [128, 16], F32)
    gnb = p.sbuf("c_gnb", [128, 16], F32)
    x1su = p.sbuf("x1su", [128, 16 * TOWN], BF16)
    x1su_c = p.subdiv(x1su, 16, TOWN)
    srow = lc["srow"]; srow2 = lc["srow2"]; rrep = lc["rrep"]; scr = lc["scr"]; scr2 = lc["scr2"]
    ps_r1 = lc["ps_r1"]; ps_r2 = lc["ps_r2"]; ps_m = lc["ps_m"]; ps_n = lc["ps_n"]
    ps_a = p.psum("ps_a", [128, 512], F32)
    ps_b = p.psum("ps_b", [128, 512], F32)
    sqb = p.sbuf("sqb", [128, 512], BF16)
    t2b = p.sbuf("t2b", [128, 512], BF16)

    for b, srcdr in ((onesb, ONESB), (onesrf, ONESRF), (wsug, WSUG), (wsdp, WSDP),
                     (suug, SUUG), (sudp, SUDP), (h128b, H128B), (mng, MNG), (mnb, MNB),
                     (gng, GNG), (gnb, GNB)):
        p.dma("sync", b.ap[:, :], srcdr, writes=[b])
    p.step1("vector", "tensor_scalar_mul", wsug[:, :], wsug[:, :], 64.0, reads=[wsug], writes=[wsug])
    p.step1("vector", "tensor_scalar_mul", wsdp[:, :], wsdp[:, :], 64.0, reads=[wsdp], writes=[wsdp])
    C2 = 1.0 / (64.0 * float(np.sqrt(8192.0)))
    p.step1("vector", "tensor_scalar_mul", sudp[:, :], sudp[:, :], C2, reads=[sudp], writes=[sudp])

    ofin = p.sbuf("ofin", [128, 16 * TOWN], BF16, side="right")
    mo3 = p.sbuf("mo3b", [128, 16 * TOWN], BF16, side="right")
    qo3 = p.sbuf("qo3b", [128, 16 * TOWN], BF16, side="right")
    oo3 = p.sbuf("oo3b", [128, 16 * TOWN], BF16, side="right")
    for b, srcdr in ((mo3, MO3), (qo3, QO3), (oo3, OO3)):
        p.dma("sync", b.ap.rearrange("p (k t) -> p k t", k=16),
              srcdr.rearrange("(k p) t -> p k t", p=128), writes=[b])

    ps_m2 = p.psum("ps_m2", [128, 512], F32)
    ps_n2 = p.psum("ps_n2", [128, 512], F32)
    EP = []
    for par in range(2):
        EP.append({
            "sq": p.sbuf(f"ep_sq{par}", [128, 512], BF16),
            "t2": p.sbuf(f"ep_t2{par}", [128, 512], BF16),
            "s1": p.sbuf(f"ep_s1{par}", [128, 512], F32),
            "s2": p.sbuf(f"ep_s2{par}", [128, 512], F32),
            "rr": p.sbuf(f"ep_rr{par}", [128, 512], F32),
            "rmu": p.sbuf(f"ep_rmu{par}", [1, 512], F32),
            "rrs": p.sbuf(f"ep_rrs{par}", [1, 512], F32),
            "rm3": p.sbuf(f"ep_rm3{par}", [1, 512], F32),
            "pm": (ps_m, ps_m2)[par],
            "pn": (ps_n, ps_n2)[par],
        })

    def ep_stats(E, n, eps):
        p.step1("vector", "tensor_scalar_mul", E["rmu"].ap[0:1, 0:n], ps_r1[0:1, 0:n], 1.0 / 128,
                reads=[ps_r1], writes=[E["rmu"]])
        p.step1("vector", "tensor_scalar_mul", E["rrs"].ap[0:1, 0:n], ps_r2[0:1, 0:n], 1.0 / 128,
                reads=[ps_r2], writes=[E["rrs"]])
        p.step1("vector", "tensor_mul", E["rm3"].ap[0:1, 0:n], E["rmu"][0:1, 0:n], E["rmu"][0:1, 0:n],
                reads=[E["rmu"]], writes=[E["rm3"]])
        p.step1("vector", "tensor_sub", E["rrs"].ap[0:1, 0:n], E["rrs"][0:1, 0:n], E["rm3"][0:1, 0:n],
                reads=[E["rrs"], E["rm3"]], writes=[E["rrs"]])
        p.step1("vector", "tensor_scalar_add", E["rrs"].ap[0:1, 0:n], E["rrs"][0:1, 0:n], eps,
                reads=[E["rrs"]], writes=[E["rrs"]])
        p.step1("scalar", "activation", E["rrs"].ap[0:1, 0:n], E["rrs"][0:1, 0:n], AF.Sqrt,
                reads=[E["rrs"]], writes=[E["rrs"]])
        p.step1("vector", "reciprocal", E["rrs"].ap[0:1, 0:n], E["rrs"][0:1, 0:n],
                reads=[E["rrs"]], writes=[E["rrs"]])
        bcast_row(p, E["pm"].ap[:, 0:n], onesrf[:, :], E["rmu"][0:1, 0:n],
                  reads=[E["rmu"], onesrf], writes=[E["pm"]])
        bcast_row(p, E["pn"].ap[:, 0:n], onesrf[:, :], E["rrs"][0:1, 0:n],
                  reads=[E["rrs"], onesrf], writes=[E["pn"]])

    for h in range(16):
        for nti, (o0, n) in enumerate(NT2):
            E = EP[(h * 2 + nti) % 2]
            sl = slice(h * TOWN + o0, h * TOWN + o0 + n)
            p.step1("scalar", "activation", E["sq"].ap[:, 0:n], mo3[:, sl], AF.Square,
                    reads=[mo3], writes=[E["sq"]])
            p.step1("tensor", "matmul", ps_r1[0:1, 0:n], onesb[:, :], mo3[:, sl],
                    start=True, stop=True, reads=[mo3, onesb], writes=[ps_r1])
            p.step1("tensor", "matmul", ps_r2[0:1, 0:n], onesb[:, :], E["sq"][:, 0:n],
                    start=True, stop=True, reads=[E["sq"], onesb], writes=[ps_r2])
            ep_stats(E, n, 1e-5)
            p.step1("scalar", "activation", E["s2"].ap[:, 0:n], mo3[:, sl], AF.Copy,
                    reads=[mo3], writes=[E["s2"]])
            p.step1("vector", "tensor_sub", E["s1"].ap[:, 0:n], E["s2"][:, 0:n], E["pm"][:, 0:n],
                    reads=[E["s2"], E["pm"]], writes=[E["s1"]])
            p.step1("vector", "tensor_copy", E["rr"].ap[:, 0:n], E["pn"][:, 0:n],
                    reads=[E["pn"]], writes=[E["rr"]])
            p.step1("vector", "tensor_mul", E["s1"].ap[:, 0:n], E["s1"][:, 0:n], E["rr"][:, 0:n],
                    reads=[E["s1"], E["rr"]], writes=[E["s1"]])
            p.step1("vector", "tensor_scalar", E["s1"].ap[:, 0:n], E["s1"][:, 0:n], mng[:, :], mnb[:, :],
                    ALU.mult, ALU.add, reads=[E["s1"], mng, mnb], writes=[E["s1"]])
            p.step1("scalar", "activation", E["s2"].ap[:, 0:n], qo3[:, sl], AF.Copy,
                    reads=[qo3], writes=[E["s2"]])
            p.step1("vector", "tensor_mul", E["s1"].ap[:, 0:n], E["s1"][:, 0:n], E["s2"][:, 0:n],
                    reads=[E["s1"], E["s2"]], writes=[E["s1"]])
            p.step1("scalar", "activation", E["t2"].ap[:, 0:n], E["s1"][:, 0:n], AF.Copy,
                    reads=[E["s1"]], writes=[E["t2"]])
            p.step1("scalar", "activation", E["sq"].ap[:, 0:n], E["s1"][:, 0:n], AF.Square,
                    reads=[E["s1"]], writes=[E["sq"]])
            p.step1("tensor", "matmul", ps_r1[0:1, 0:n], onesb[:, :], E["t2"][:, 0:n],
                    start=True, stop=True, reads=[E["t2"], onesb], writes=[ps_r1])
            p.step1("tensor", "matmul", ps_r2[0:1, 0:n], onesb[:, :], E["sq"][:, 0:n],
                    start=True, stop=True, reads=[E["sq"], onesb], writes=[ps_r2])
            ep_stats(E, n, 1e-5)
            p.step1("vector", "tensor_sub", E["s1"].ap[:, 0:n], E["s1"][:, 0:n], E["pm"][:, 0:n],
                    reads=[E["s1"], E["pm"]], writes=[E["s1"]])
            p.step1("vector", "tensor_copy", E["rr"].ap[:, 0:n], E["pn"][:, 0:n],
                    reads=[E["pn"]], writes=[E["rr"]])
            p.step1("vector", "tensor_mul", E["s1"].ap[:, 0:n], E["s1"][:, 0:n], E["rr"][:, 0:n],
                    reads=[E["s1"], E["rr"]], writes=[E["s1"]])
            p.step1("vector", "tensor_scalar", E["s1"].ap[:, 0:n], E["s1"][:, 0:n], gng[:, h:h + 1], gnb[:, h:h + 1],
                    ALU.mult, ALU.add, reads=[E["s1"], gng, gnb], writes=[E["s1"]])
            p.step1("scalar", "activation", E["s2"].ap[:, 0:n], oo3[:, sl], AF.Copy,
                    reads=[oo3], writes=[E["s2"]])
            p.step1("vector", "tensor_mul", ofin[:, sl], E["s1"][:, 0:n], E["s2"][:, 0:n],
                    reads=[E["s1"], E["s2"]], writes=[ofin])
    p.free(*[E[k] for E in EP[::-1] for k in ("rm3", "rrs", "rmu", "rr", "s2", "s1", "t2", "sq")])
    p.free(oo3, qo3, mo3)

    yb = p.sbuf("yb", [128, 16 * TOWN], BF16, side="right")
    wo0 = p.sbuf("wo0", [128, 16 * 128], BF16, side="right")
    wo1 = p.sbuf("wo1", [128, 16 * 128], BF16, side="right")
    ofin_c = p.subdiv(ofin, 16, TOWN)
    pool4 = [ps_a, ps_b, ps_m, ps_n]
    pi = 0
    for mc in range(16):
        wb = (wo0, wo1)[mc % 2]
        p.dma("gpsimd", wb.ap.rearrange("p (k m) -> p k m", k=16),
              WOL[:, mc * 128:(mc + 1) * 128].rearrange("(k p) m -> p k m", p=128), writes=[wb])
        for (o0, n) in NT2:
            psx = pool4[pi % 4]; pi += 1
            mm_group(p, psx.ap[:, 0:n], [wb[:, k * 128:(k + 1) * 128] for k in range(16)],
                     [c[:, o0:o0 + n] for c in ofin_c],
                     reads=[wb] + ofin_c, writes=[psx])
            p.step1("scalar", "activation", yb.ap[:, mc * TOWN + o0:mc * TOWN + o0 + n], psx[:, 0:n],
                    AF.Copy, reads=[psx], writes=[yb])

    if DEBUG_A3C:
        DBGY = p.dout("dbg_y", [2048, TOWN], BF16)
        for k in range(16):
            p.dma("sync", DBGY[k * 128:(k + 1) * 128, :], yb[:, k * TOWN:(k + 1) * TOWN], reads=[yb])
        DBGO = p.dout("dbg_o", [2048, TOWN], BF16)
        p.merge(ofin)
        for k in range(16):
            p.dma("sync", DBGO[k * 128:(k + 1) * 128, :], ofin[:, k * TOWN:(k + 1) * TOWN], reads=[ofin])

    xt3 = p.sbuf("xt3b", [128, 16 * TOWN], F32, side="right")
    rscr = p.sbuf("rscr", [128, TOWN], F32, side="right")
    p.dma("sync", xt3.ap.rearrange("p (k t) -> p k t", k=16),
          XT3.rearrange("(k p) t -> p k t", p=128), writes=[xt3])
    emit_ln_c(p, lc, yb, x1su, TOWN, NT2, src_is_bf=True)
    for k in range(16):
        ks = slice(k * TOWN, (k + 1) * TOWN)
        p.step1("scalar", "activation", rscr[:, :], x1su[:, ks], AF.Copy, reads=[x1su], writes=[rscr])
        p.step1("vector", "tensor_add", xt3[:, ks], xt3[:, ks], rscr[:, :],
                reads=[xt3, rscr], writes=[xt3])
        p.dma("sync", X1D[k * 128:(k + 1) * 128, :], xt3[:, ks], sem="sp_x1", reads=[xt3])
    emit_ln_c(p, lc, xt3, x1su, TOWN, NT2)
    p.free(rscr, xt3, wo1, wo0, yb)
    p.free(ofin)

    p.merge(x1su)

    prod = p.sbuf("prod", [128, 64 * TOWN], BF16, side="right")
    prod_c = p.subdiv(prod, 64, TOWN)
    wus = [p.sbuf(f"wu{i}", [128, 16 * 128], BF16, side="right") for i in range(3)]
    wgs = [p.sbuf(f"wg{i}", [128, 16 * 128], BF16, side="right") for i in range(3)]
    gbuf = p.sbuf("gbuf", [128, 512], BF16, side="right")
    ubuf = p.sbuf("ubuf", [128, 512], BF16, side="right")
    btmp = p.sbuf("btmp", [128, TOWN], BF16, side="right")
    btmp2 = p.sbuf("btmp2", [128, TOWN], BF16, side="right")

    for k in range(16):
        p.step1("vector", "tensor_scalar_mul", x1su_c[k].ap[:, :], x1su_c[k][:, :], suug[:, k:k + 1],
                reads=[x1su_c[k], suug], writes=[x1su_c[k]])
    C1 = 1.0 / (64.0 * float(np.sqrt(2048.0)))
    for k in range(16):
        for (o0, n) in NT2:
            psx = pool4[pi % 4]; pi += 1
            p.step1("tensor", "matmul", psx[:, 0:n], h128b[:, :], x1su_c[k][:, o0:o0 + n],
                    start=True, stop=True, reads=[h128b, x1su_c[k]], writes=[psx])
            p.step1("scalar", "activation", x1su_c[k].ap[:, o0:o0 + n], psx[:, 0:n], AF.Copy,
                    scale=C1, reads=[psx], writes=[x1su_c[k]])
    def bfly(chunks, dlist, spares):
        cur = list(chunks)
        sp = {"vector": spares[0], "gpsimd": spares[1]}
        for d in dlist:
            pairs = [(gg, gg + d) for gg in range(len(cur)) if (gg // d) % 2 == 0]
            for pi_, (ga, gb_) in enumerate(pairs):
                eng = ("vector", "vector", "gpsimd")[pi_ % 3]
                A, Bc = cur[ga], cur[gb_]
                S = sp[eng]
                p.step1(eng, "tensor_add", S.ap[:, :], A[:, :], Bc[:, :], reads=[A, Bc], writes=[S])
                p.step1(eng, "tensor_sub", A.ap[:, :], A[:, :], Bc[:, :], reads=[A, Bc], writes=[A])
                cur[ga], cur[gb_], sp[eng] = S, A, Bc
        return cur, [sp["vector"], sp["gpsimd"]]

    def land_canonical(cur, canon, free_tiles):
        cur = list(cur)
        free = set(id(t) for t in free_tiles)
        byid = {id(t): t for t in cur + canon + list(free_tiles)}
        live = set(id(t) for t in cur)
        pending = set(i for i in range(len(cur)) if id(cur[i]) != id(canon[i]))
        ci = 0
        while pending:
            moved = False
            for i in sorted(pending):
                tgt = canon[i]
                if id(tgt) not in live:
                    src_b = cur[i]
                    eng = ("vector", "vector", "gpsimd")[ci % 3]; ci += 1
                    p.step1(eng, "tensor_copy", tgt.ap[:, :], src_b[:, :], reads=[src_b], writes=[tgt])
                    live.discard(id(src_b)); free.add(id(src_b))
                    live.add(id(tgt)); free.discard(id(tgt))
                    cur[i] = tgt; pending.discard(i); moved = True
                    break
            if not moved:
                i = sorted(pending)[0]
                f = byid[sorted(free)[0]]
                src_b = cur[i]
                eng = ("vector", "vector", "gpsimd")[ci % 3]; ci += 1
                p.step1(eng, "tensor_copy", f.ap[:, :], src_b[:, :], reads=[src_b], writes=[f])
                live.discard(id(src_b)); free.add(id(src_b))
                live.add(id(f)); free.discard(id(f))
                cur[i] = f
        return cur

    x1su_orig = list(x1su_c)
    x1su_c, spares2 = bfly(x1su_c, (8, 4, 2, 1), [btmp, btmp2])
    for mc in range(64):
        wu = wus[mc % 3]
        wg = wgs[mc % 3]
        p.dma("gpsimd", wu.ap.rearrange("p (k m) -> p k m", k=16),
              WUPL[:, mc * 128:(mc + 1) * 128].rearrange("(k p) m -> p k m", p=128), writes=[wu])
        p.dma("gpsimd", wg.ap.rearrange("p (k m) -> p k m", k=16),
              WGTL[:, mc * 128:(mc + 1) * 128].rearrange("(k p) m -> p k m", p=128), writes=[wg])
        scol = mc // 64, mc % 64
        for (o0, n) in NT2:
            psx = pool4[pi % 4]; pi += 1
            mm_group(p, psx.ap[:, 0:n], [wg[:, k * 128:(k + 1) * 128] for k in range(16)],
                     [c[:, o0:o0 + n] for c in x1su_c], reads=[wg] + x1su_c, writes=[psx])
            p.step1("scalar", "activation", gbuf[:, 0:n], psx[:, 0:n], AF.Silu,
                    scale=wsug[:, 64 + mc:64 + mc + 1], reads=[psx, wsug], writes=[gbuf])
            psy = pool4[pi % 4]; pi += 1
            mm_group(p, psy.ap[:, 0:n], [wu[:, k * 128:(k + 1) * 128] for k in range(16)],
                     [c[:, o0:o0 + n] for c in x1su_c], reads=[wu] + x1su_c, writes=[psy])
            p.step1("vector", "tensor_scalar_mul", ubuf[:, 0:n], psy[:, 0:n], wsug[:, mc:mc + 1],
                    reads=[psy, wsug], writes=[ubuf])
            p.step1("vector", "scalar_tensor_tensor", prod_c[mc].ap[:, o0:o0 + n], gbuf[:, 0:n],
                    sudp[:, mc:mc + 1], ubuf[:, 0:n], ALU.mult, ALU.mult,
                    reads=[gbuf, sudp, ubuf], writes=[prod_c[mc]])
    for mc in range(64):
        for (o0, n) in NT2:
            psx = pool4[pi % 4]; pi += 1
            p.step1("tensor", "matmul", psx[:, 0:n], h128b[:, :], prod_c[mc][:, o0:o0 + n],
                    start=True, stop=True, reads=[h128b, prod_c[mc]], writes=[psx])
            p.step1("scalar", "activation", prod_c[mc].ap[:, o0:o0 + n], psx[:, 0:n], AF.Copy,
                    reads=[psx], writes=[prod_c[mc]])
    prod_canon = list(prod_c)
    prod_c, spares3 = bfly(prod_c, (32, 16, 8, 4, 2, 1), spares2)
    prod_c = land_canonical(prod_c, prod_canon, spares3)
    p.free(btmp2, btmp, ubuf, gbuf, wgs[2], wgs[1], wgs[0], wus[2], wus[1], wus[0])

    wd0 = p.sbuf("wd0h", [128, 32 * 128], BF16, side="right")
    wd1 = p.sbuf("wd1h", [128, 32 * 128], BF16, side="right")
    for mc in range(16):
        p.dma("gpsimd", wd0.ap.rearrange("p (k m) -> p k m", k=32),
              WDNL[0:4096, mc * 128:(mc + 1) * 128].rearrange("(k p) m -> p k m", p=128), writes=[wd0])
        p.dma("gpsimd", wd1.ap.rearrange("p (k m) -> p k m", k=32),
              WDNL[4096:8192, mc * 128:(mc + 1) * 128].rearrange("(k p) m -> p k m", p=128), writes=[wd1])
        for (o0, n) in NT2:
            psx = pool4[pi % 4]; pi += 1
            def fn(t, wd0=wd0, wd1=wd1, psx=psx, o0=o0, n=n):
                ins = None
                for k in range(64):
                    wd = wd0 if k < 32 else wd1
                    ins = t.matmul(psx.ap[:, 0:n], wd[:, (k % 32) * 128:(k % 32 + 1) * 128],
                                   prod_c[k].ap[:, o0:o0 + n], start=(k == 0), stop=(k == 63))
                return ins
            p.step("tensor", fn, reads=[wd0, wd1] + prod_c, writes=[psx])
            p.step1("scalar", "activation", x1su_orig[mc].ap[:, o0:o0 + n], psx[:, 0:n], AF.Copy,
                    scale=wsdp[:, mc:mc + 1], reads=[psx, wsdp], writes=[x1su_orig[mc]])
    p.free(wd1, wd0, prod)

    p.merge(x1su)
    x1f = p.sbuf("x1f", [128, 16 * TOWN], F32, side="right")
    rscr = p.sbuf("rscr6", [128, TOWN], F32, side="right")
    p.dma("sync", x1f.ap.rearrange("p (k t) -> p k t", k=16),
          X1D.rearrange("(k p) t -> p k t", p=128), writes=[x1f],
          extra=[("sp_x1", p.cnt.get("sp_x1", 0))])
    zln = p.sbuf("zln", [128, 16 * TOWN], BF16, side="right")
    emit_ln_c(p, lc, x1su, zln, TOWN, NT2, src_is_bf=True)
    for k in range(16):
        ks = slice(k * TOWN, (k + 1) * TOWN)
        p.step1("scalar", "activation", rscr[:, :], zln[:, ks], AF.Copy, reads=[zln], writes=[rscr])
        p.step1("vector", "tensor_add", x1f[:, ks], x1f[:, ks], rscr[:, :], reads=[x1f, rscr], writes=[x1f])
        p.dma("sync", X2O[k * 128:(k + 1) * 128, :], x1f[:, ks], reads=[x1f])
    emit_ln_c(p, lc, x1f, zln, TOWN, NT2)
    for k in range(16):
        p.dma("sync", X2LNO[k * 128:(k + 1) * 128, :], zln[:, k * TOWN:(k + 1) * TOWN], reads=[zln])
    return p.finalize([(s, p.cnt[s]) for s in sorted(p.store_sems)])


def _pstats(p, lc, eps):
    srow = lc["srow"]; srow2 = lc["srow2"]; srow3 = lc["srow3"]
    ps_r1 = lc["ps_r1"]; ps_r2 = lc["ps_r2"]
    p.step1("vector", "tensor_scalar_mul", srow[0:1, 0:512], ps_r1[0:1, 0:512], 1.0 / 128,
            reads=[ps_r1], writes=[srow])
    p.step1("vector", "tensor_scalar_mul", srow2[0:1, 0:512], ps_r2[0:1, 0:512], 1.0 / 128,
            reads=[ps_r2], writes=[srow2])
    p.step1("vector", "tensor_mul", srow3[0:1, 0:512], srow[0:1, 0:512], srow[0:1, 0:512],
            reads=[srow], writes=[srow3])
    p.step1("vector", "tensor_sub", srow2[0:1, 0:512], srow2[0:1, 0:512], srow3[0:1, 0:512],
            reads=[srow2, srow3], writes=[srow2])
    p.step1("vector", "tensor_scalar_add", srow2[0:1, 0:512], srow2[0:1, 0:512], eps,
            reads=[srow2], writes=[srow2])
    p.step1("scalar", "activation", srow2[0:1, 0:512], srow2[0:1, 0:512], AF.Sqrt,
            reads=[srow2], writes=[srow2])
    p.step1("vector", "reciprocal", srow2[0:1, 0:512], srow2[0:1, 0:512],
            reads=[srow2], writes=[srow2])


def _hadamard(n, dtype):
    H = np.array([[1.0]], np.float32)
    while H.shape[0] < n:
        H = np.block([[H, H], [H, -H]])
    return H.astype(dtype)


def stage_a3c(g, M_full, Q_full, O_full, core):
    s = slice(1024 * core, 1024 * (core + 1))
    return {
        "mo3": _c(M_full[:, s]), "qo3": _c(Q_full[:, s]), "oo3": _c(O_full[:, s]),
        "xt3": _c(g["xT"][:, s]),
        "wol": g["l0_Wo"], "wupl": g["up_l"], "wgtl": g["gt_l"], "wdnl": g["dn_l"],
        "wsug": g["wsug"], "wsdp": g["wsdp"], "suug": g["suug"], "sudp": g["sudp"],
        "h128b": g["h128b"], "mng": g["l0_mng"], "mnb": g["l0_mnb"],
        "gng": g["l0_gng"], "gnb": g["l0_gnb"],
        "ones_col_bf": g["ones_col_bf"], "ones_row_f": g["ones_row_f"],
    }



def build_d1():
    p = P("d1")
    HT = p.din("ht", [2048, TLOC], BF16)
    WQ1 = p.din("wq1", [2048, 1536], F32)
    WK10 = p.din("wk10", [2048, 1536], F32)
    WK11 = p.din("wk11", [2048, 1536], F32)
    WV10 = p.din("wv10", [2048, 3072], F32)
    WV11 = p.din("wv11", [2048, 3072], F32)
    WA1 = p.din("wa1", [2048, 6], F32)
    WB10 = p.din("wb10", [2048, 6], F32)
    WB11 = p.din("wb11", [2048, 6], F32)
    CQW = p.din("cqw", [128, 12 * 4], F32)
    CKW0 = p.din("ckw0", [128, 12 * 4], F32)
    CKW1 = p.din("ckw1", [128, 12 * 4], F32)
    CVW0 = p.din("cvw0", [128, 24 * 4], F32)
    CVW1 = p.din("cvw1", [128, 24 * 4], F32)
    ALOG = p.din("alog", [6, 1], F32)
    DTB = p.din("dtb", [6, 1], F32)
    ONESB = p.din("ones_col_bf", [128, 1], BF16)
    ONESRF = p.din("ones_row_f", [1, 128], F32)
    QHO = p.dout("qho", [1536, 2 * TOWN], BF16)
    KHO = p.dout("kho", [1536, 2 * TOWN], BF16)
    VHO = p.dout("vho", [3072, 2 * TOWN], BF16)
    GBO = p.dout("gbo", [12, 2 * TOWN], F32)

    ht = p.sbuf("htb", [128, 16 * TLOC], BF16)
    lc = mk_ln_ctx(p, TLOC)
    onesb = p.sbuf("c_onesb", [128, 1], BF16)
    onesrf = p.sbuf("c_onesrf", [1, 128], F32)
    lc["onesb"] = onesb; lc["onesrf"] = onesrf
    cw = {"q": p.sbuf("c_cqw", [128, 48], F32), "k0": p.sbuf("c_ckw0", [128, 48], F32),
          "k1": p.sbuf("c_ckw1", [128, 48], F32), "v0": p.sbuf("c_cvw0", [128, 96], F32),
          "v1": p.sbuf("c_cvw1", [128, 96], F32)}
    alog = p.sbuf("c_alog", [6, 1], F32)
    dtb = p.sbuf("c_dtb", [6, 1], F32)
    negea = p.sbuf("negea", [6, 1], F32)
    raw = [p.sbuf(f"raw{i}", [128, TLOC], BF16) for i in range(4)]
    cvt = [p.sbuf(f"cvt{i}", [128, TOWN], BF16) for i in range(4)]
    acc = [p.sbuf(f"acc{i}", [128, TOWN], F32) for i in range(4)]
    wsl = [p.sbuf(f"w{i}b", [128, 16 * 128], BF16) for i in range(3)]
    w0 = wsl[0]
    ilv = p.sbuf("ilv", [128, 2 * TOWN], BF16)
    srow = lc["srow"]; srow2 = lc["srow2"]; rrep = lc["rrep"]
    ps_a = p.psum("ps_a", [128, 512], F32)
    ps_r1 = lc["ps_r1"]; ps_m = lc["ps_m"]
    pr6 = p.sbuf("pr6", [6, TOWN], F32)
    sb6 = [p.sbuf(f"sb6_{i}", [6, TOWN], F32) for i in range(3)]
    gb = p.sbuf("gb", [6, 2 * TOWN], F32)
    ps_s = p.psum("ps_s", [6, 512], F32)

    p.dma("sync", ht.ap.rearrange("p (k t) -> p k t", k=16),
          HT.rearrange("(k p) t -> p k t", p=128), writes=[ht])
    for b, srcdr in ((onesb, ONESB), (onesrf, ONESRF), (cw["q"], CQW), (cw["k0"], CKW0),
                     (cw["k1"], CKW1), (cw["v0"], CVW0), (cw["v1"], CVW1),
                     (alog, ALOG), (dtb, DTB)):
        p.dma("sync", b.ap[:, :], srcdr, writes=[b])
    p.step1("scalar", "activation", negea[0:6, :], alog[0:6, :], AF.Exp, reads=[alog], writes=[negea])
    p.step1("vector", "tensor_scalar_mul", negea[0:6, :], negea[0:6, :], -1.0, reads=[negea], writes=[negea])

    NT3 = [(0, 344), (344, 344), (688, 344)]
    NT2 = [(0, 512), (512, 512)]
    ht_c = [ht[:, k * TLOC:(k + 1) * TLOC] for k in range(16)]

    _wsl_ctr = [0]
    def raw_project(W, mc, dst, wslot):
        wb = wsl[_wsl_ctr[0] % 3]; _wsl_ctr[0] += 1
        p.dma("gpsimd", wb.ap.rearrange("p (k m) -> p k m", k=16),
              W[:, mc * 128:(mc + 1) * 128].rearrange("(k p) m -> p k m", p=128), writes=[wb])
        for (o0, n) in NT3:
            mm_group(p, ps_a.ap[:, 0:n], [wb[:, k * 128:(k + 1) * 128] for k in range(16)],
                     [c[:, o0:o0 + n] for c in ht_c], reads=[wb, ht], writes=[ps_a])
            p.step1("scalar", "activation", dst.ap[:, o0:o0 + n], ps_a[:, 0:n],
                    AF.Copy, reads=[ps_a], writes=[dst])

    def conv_silu(wcol, mc, srcb, dst, lane):
        a0, a1 = (acc[0], acc[1]) if lane == 0 else (acc[2], acc[3])
        eng = "vector"
        p.step1(eng, "tensor_scalar", a0[:, :], srcb[:, 5:5 + TOWN],
                wcol[:, 4 * mc:4 * mc + 1], None, ALU.mult, reads=[srcb, wcol], writes=[a0])
        for j in range(3):
            sa, da = (a0, a1) if j % 2 == 0 else (a1, a0)
            p.step1(eng, "scalar_tensor_tensor", da[:, :], srcb[:, 6 + j:6 + j + TOWN],
                    wcol[:, 4 * mc + j + 1:4 * mc + j + 2], sa[:, :], ALU.mult, ALU.add,
                    reads=[srcb, wcol, sa], writes=[da])
        p.step1("scalar", "activation", dst.ap[:, :], a1[:, :], AF.Silu, reads=[a1], writes=[dst])

    def l2n_pair(c0, c1, scale):
        for (o0, n) in NT2:
            for ci, cb in enumerate((c0, c1)):
                p.step1("scalar", "activation", lc["sqs"][ci].ap[:, 0:n], cb[:, o0:o0 + n],
                        AF.Square, reads=[cb], writes=[lc["sqs"][ci]])
                p.step1("tensor", "matmul", ps_r1[0:1, 0:n], onesb[:, :], lc["sqs"][ci][:, 0:n],
                        start=(ci == 0), stop=(ci == 1), reads=[lc["sqs"][ci], onesb], writes=[ps_r1])
            p.step1("vector", "tensor_scalar_add", srow[0:1, 0:n], ps_r1[0:1, 0:n], 1e-6,
                    reads=[ps_r1], writes=[srow])
            p.step1("scalar", "activation", srow[0:1, 0:n], srow[0:1, 0:n], AF.Sqrt,
                    reads=[srow], writes=[srow])
            p.step1("vector", "reciprocal", srow[0:1, 0:n], srow[0:1, 0:n], reads=[srow], writes=[srow])
            if scale != 1.0:
                p.step1("vector", "tensor_scalar_mul", srow[0:1, 0:n], srow[0:1, 0:n], scale,
                        reads=[srow], writes=[srow])
            bcast_row(p, ps_m.ap[:, 0:n], onesrf[:, :], srow[0:1, 0:n],
                      reads=[srow, onesrf], writes=[ps_m])
            p.step1("vector", "tensor_copy", rrep[:, 0:n], ps_m[:, 0:n], reads=[ps_m], writes=[rrep])
            for cb in (c0, c1):
                p.step1("vector", "tensor_mul", cb.ap[:, o0:o0 + n], cb[:, o0:o0 + n], rrep[:, 0:n],
                        reads=[cb, rrep], writes=[cb])

    for h in range(6):
        for ci, mc in enumerate((2 * h, 2 * h + 1)):
            raw_project(WQ1, mc, raw[ci], ci)
            conv_silu(cw["q"], mc, raw[ci], cvt[ci], ci % 2)
        l2n_pair(cvt[0], cvt[1], float(256.0 ** -0.5))
        for ci, mc in enumerate((2 * h, 2 * h + 1)):
            p.step1("vector", "memset", ilv.ap[:, :], 0.0, reads=[], writes=[ilv])
            p.step1("vector", "tensor_copy", ilv.ap[:, 1::2], cvt[ci][:, :], reads=[cvt[ci]], writes=[ilv])
            p.dma("sync", QHO[mc * 128:(mc + 1) * 128, :], ilv[:, :], reads=[ilv])
    for h in range(6):
        for ci, mc in enumerate((2 * h, 2 * h + 1)):
            raw_project(WK10, mc, raw[ci], ci)
            conv_silu(cw["k0"], mc, raw[ci], cvt[ci], ci % 2)
            raw_project(WK11, mc, raw[2 + ci], ci)
            conv_silu(cw["k1"], mc, raw[2 + ci], cvt[2 + ci], ci % 2)
        l2n_pair(cvt[0], cvt[1], 1.0)
        l2n_pair(cvt[2], cvt[3], 1.0)
        for ci, mc in enumerate((2 * h, 2 * h + 1)):
            p.step1("vector", "tensor_copy", ilv.ap[:, 0::2], cvt[ci][:, :], reads=[cvt[ci]], writes=[ilv])
            p.step1("gpsimd", "tensor_copy", ilv.ap[:, 1::2], cvt[2 + ci][:, :], reads=[cvt[2 + ci]], writes=[ilv])
            p.dma("sync", KHO[mc * 128:(mc + 1) * 128, :], ilv[:, :], reads=[ilv])
    for mc in range(24):
        raw_project(WV10, mc, raw[0], 0)
        conv_silu(cw["v0"], mc, raw[0], cvt[0], 0)
        raw_project(WV11, mc, raw[1], 1)
        conv_silu(cw["v1"], mc, raw[1], cvt[1], 1)
        p.step1("vector", "tensor_copy", ilv.ap[:, 0::2], cvt[0][:, :], reads=[cvt[0]], writes=[ilv])
        p.step1("gpsimd", "tensor_copy", ilv.ap[:, 1::2], cvt[1][:, :], reads=[cvt[1]], writes=[ilv])
        p.dma("sync", VHO[mc * 128:(mc + 1) * 128, :], ilv[:, :], reads=[ilv])

    ht_own = [ht[:, k * TLOC + HALO:k * TLOC + HALO + TOWN] for k in range(16)]
    for wi, (W, dst, bias) in enumerate(((WA1, pr6, dtb), (WB10, sb6[0], None), (WB11, sb6[1], None))):
        p.dma("gpsimd", w0.ap.rearrange("p (k m) -> p k m", k=16)[:, :, 0:6],
              W.rearrange("(k p) m -> p k m", p=128), writes=[w0])
        for (o0, n) in NT2:
            mm_group(p, ps_s.ap[0:6, 0:n], [w0[:, k * 128:k * 128 + 6] for k in range(16)],
                     [c[:, o0:o0 + n] for c in ht_own], reads=[w0, ht], writes=[ps_s])
            if wi == 0:
                p.step1("vector", "tensor_scalar_add", dst.ap[0:6, o0:o0 + n], ps_s[0:6, 0:n],
                        dtb[0:6, :], reads=[ps_s, dtb], writes=[dst])
            else:
                p.step1("scalar", "activation", dst.ap[0:6, o0:o0 + n], ps_s[0:6, 0:n], AF.Sigmoid,
                        reads=[ps_s], writes=[dst])
    p.step1("scalar", "activation", sb6[2].ap[0:6, :], pr6[0:6, :], AF.Abs, reads=[pr6], writes=[sb6[2]])
    p.step1("scalar", "activation", sb6[2].ap[0:6, :], sb6[2][0:6, :], AF.Exp, scale=-1.0,
            reads=[sb6[2]], writes=[sb6[2]])
    p.step1("scalar", "activation", sb6[2].ap[0:6, :], sb6[2][0:6, :], AF.Ln, bias=1.0,
            reads=[sb6[2]], writes=[sb6[2]])
    p.step1("vector", "tensor_scalar_max", pr6.ap[0:6, :], pr6[0:6, :], 0.0, reads=[pr6], writes=[pr6])
    p.step1("vector", "tensor_add", pr6.ap[0:6, :], pr6[0:6, :], sb6[2][0:6, :],
            reads=[pr6, sb6[2]], writes=[pr6])
    p.step1("vector", "tensor_scalar_mul", pr6.ap[0:6, :], pr6[0:6, :], negea[0:6, :],
            reads=[pr6, negea], writes=[pr6])
    p.step1("vector", "memset", gb.ap[0:6, :], 0.0, reads=[], writes=[gb])
    p.step1("vector", "tensor_copy", gb.ap[0:6, 0::2], pr6[0:6, :], reads=[pr6], writes=[gb])
    p.dma("sync", GBO[0:6, :], gb[0:6, :], reads=[gb])
    p.step1("vector", "tensor_copy", gb.ap[0:6, 0::2], sb6[0][0:6, :], reads=[sb6[0], gb], writes=[gb])
    p.step1("vector", "tensor_copy", gb.ap[0:6, 1::2], sb6[1][0:6, :], reads=[sb6[1]], writes=[gb])
    p.dma("sync", GBO[6:12, :], gb[0:6, :], reads=[gb])
    return p.finalize([(s, p.cnt[s]) for s in sorted(p.store_sems)])


def stage_d1(g, X2LN_pad, core):
    s = slice(1024 * core, 1024 * core + TLOC)
    return {
        "ht": _c(X2LN_pad[:, s]),
        "wq1": g["l1_Wq"], "wk10": g["l1_Wk0"], "wk11": g["l1_Wk1"],
        "wv10": g["l1_Wv0"], "wv11": g["l1_Wv1"],
        "wa1": g["l1_Wa"], "wb10": g["l1_Wb0"], "wb11": g["l1_Wb1"],
        "cqw": g["l1_cqw"], "ckw0": g["l1_ckw0"], "ckw1": g["l1_ckw1"],
        "cvw0": g["l1_cvw0"], "cvw1": g["l1_cvw1"],
        "alog": g["l1_alog"], "dtb": g["l1_dtb"],
        "ones_col_bf": g["ones_col_bf"], "ones_row_f": g["ones_row_f"],
    }



def build_d2(nseg=16):
    p = P("d2")
    QH2 = p.din("qh2", [256, 16384], BF16)
    KH2 = p.din("kh2", [256, 16384], BF16)
    VH2 = p.din("vh2", [512, 16384], BF16)
    GR = p.din("gr", [1, 16384], F32)
    BR = p.din("br", [1, 16384], F32)
    IDB = p.din("id_bf", [128, 128], BF16)
    IDF = p.din("id_f", [128, 128], F32)
    ID64B = p.din("id128b", [128, 128], BF16)
    UTI = p.din("utri_inc", [128, 128], F32)
    UTS = p.din("utri_strict", [128, 128], F32)
    LTS = p.din("ltri_strict", [128, 128], F32)
    ONESRF = p.din("ones_row_f", [1, 128], F32)
    OO2 = p.dout("oo2", [16384, 512], BF16)
    SCR_D = p.nc.dram_tensor("rowscr", [16, 128], F32, kind="Internal").ap()

    idb = p.sbuf("c_idb", [128, 128], BF16)
    idf = p.sbuf("c_idf", [128, 128], F32)
    id128 = p.sbuf("c_id128", [128, 128], BF16)
    uti = p.sbuf("c_uti", [128, 128], F32)
    uts = p.sbuf("c_uts", [128, 128], F32)
    lts = p.sbuf("c_lts", [128, 128], F32)
    onesrf = p.sbuf("c_onesrf", [1, 128], F32)
    ones8 = p.sbuf("ones8", [8, 128], F32)
    kseg = p.sbuf("kseg", [128, 2 * 1024], BF16)
    qseg = p.sbuf("qseg", [128, 2 * 1024], BF16)
    vseg = p.sbuf("vseg", [128, 4 * 1024], BF16)
    gseg = p.sbuf("gseg", [8, 128], F32)
    bseg = p.sbuf("bseg", [8, 128], F32)
    cumg = p.sbuf("cumg", [8, 128], F32)
    cumgt = p.sbuf("cumgt", [128, 8], F32)
    betat = p.sbuf("betat", [128, 8], F32)
    growc = p.sbuf("growc", [1, 1024], F32)
    brwc = p.sbuf("brwc", [1, 1024], F32)
    CB = []
    for par in range(2):
        CB.append({
            "lam": p.sbuf(f"lam{par}", [128, 1], F32),
            "lamend": p.sbuf(f"lamend{par}", [128, 1], F32),
            "lesc": p.sbuf(f"lesc{par}", [1, 1], F32),
            "e1": p.sbuf(f"e1_{par}", [128, 128], F32),
            "e2": p.sbuf(f"e2_{par}", [128, 128], F32),
            "s64a": p.sbuf(f"s64a{par}", [128, 128], F32),
            "s64b": p.sbuf(f"s64b{par}", [128, 128], F32),
            "wpb": p.sbuf(f"wpb{par}", [128, 128], BF16),
            "abf": [p.sbuf(f"abf{par}_{i}", [128, 128], BF16) for i in range(2)],
            "bbf": [p.sbuf(f"bbf{par}_{i}", [128, 128], BF16) for i in range(2)],
            "ptb": [p.sbuf(f"ptb{par}_{i}", [128, 128], BF16) for i in range(2)],
            "vtb": p.sbuf(f"vtb{par}", [128, 512], F32),
            "rbf": p.sbuf(f"rbf{par}", [128, 512], BF16),
            "ubf": p.sbuf(f"ubf{par}", [128, 512], BF16),
            "obf": p.sbuf(f"obf{par}", [128, 512], BF16),
            "ktb": p.sbuf(f"ktb{par}", [128, 256], BF16),
            "scro": p.sbuf(f"scro{par}", [128, 512], F32),
        })
    sf32 = [p.sbuf(f"sf32_{i}", [128, 512], F32) for i in range(2)]
    sbfb = [p.sbuf(f"sbf_{i}", [128, 512], BF16) for i in range(2)]

    ps_p1 = p.psum("ps_p1", [128, 512], F32)
    ps_p2 = p.psum("ps_p2", [128, 512], F32)
    ps_r = p.psum("ps_r", [128, 512], F32)
    ps_u = p.psum("ps_u", [128, 512], F32)
    ps_o = p.psum("ps_o", [128, 512], F32)
    ps_t = p.psum("ps_t", [128, 768], BF16)
    ps_s1 = p.psum("ps_s1", [128, 512], F32)
    ps_s2 = p.psum("ps_s2", [128, 512], F32)

    for b, srcdr in ((idb, IDB), (idf, IDF), (id128, ID64B), (uti, UTI), (uts, UTS),
                     (lts, LTS), (onesrf, ONESRF)):
        p.dma("sync", b.ap[:, :], srcdr, writes=[b])
    p.step1("vector", "memset", ones8.ap[:, :], 1.0, reads=[], writes=[ones8])
    for i in range(2):
        p.step1("vector", "memset", sf32[i].ap[:, :], 0.0, reads=[], writes=[sf32[i]])
        p.step1("vector", "memset", sbfb[i].ap[:, :], 0.0, reads=[], writes=[sbfb[i]])

    for seg in range(nseg):
        ssl = slice(seg * 1024, (seg + 1) * 1024)
        p.dma("sync", kseg.ap.rearrange("p (c t) -> p c t", c=2),
              KH2[:, ssl].rearrange("(c p) t -> p c t", p=128), writes=[kseg])
        p.dma("sync", qseg.ap.rearrange("p (c t) -> p c t", c=2),
              QH2[:, ssl].rearrange("(c p) t -> p c t", p=128), writes=[qseg])
        p.dma("sync", vseg.ap.rearrange("p (c t) -> p c t", c=4),
              VH2[:, ssl].rearrange("(c p) t -> p c t", p=128), writes=[vseg])
        p.dma("sync", gseg.ap[0:8, :], GR[0:1, ssl].rearrange("a (c t) -> (a c) t", c=8), writes=[gseg])
        p.dma("sync", bseg.ap[0:8, :], BR[0:1, ssl].rearrange("a (c t) -> (a c) t", c=8), writes=[bseg])
        p.step1("vector", "tensor_tensor_scan", cumg.ap[0:8, :], ones8[0:8, :], gseg[0:8, :],
                0.0, ALU.mult, ALU.add, reads=[ones8, gseg], writes=[cumg])
        p.step1("tensor", "transpose", ps_p1[:, 0:8], cumg[0:8, :], idf[0:8, 0:8],
                reads=[cumg, idf], writes=[ps_p1])
        p.step1("vector", "tensor_copy", cumgt.ap[:, 0:8], ps_p1[:, 0:8], reads=[ps_p1], writes=[cumgt])
        p.step1("tensor", "transpose", ps_p1[:, 8:16], bseg[0:8, :], idf[0:8, 0:8],
                reads=[bseg, idf], writes=[ps_p1])
        p.step1("vector", "tensor_copy", betat.ap[:, 0:8], ps_p1[:, 8:16], reads=[ps_p1], writes=[betat])
        p.dma("sync", SCR_D[0:8, :], cumg[0:8, :], sem="d_rows", reads=[cumg])
        p.dma("sync", SCR_D[8:16, :], bseg[0:8, :], sem="d_rows", reads=[bseg])
        p.dma("sync", growc.ap[0:1, :], SCR_D[0:8, :].rearrange("c t -> (c t)"),
              writes=[growc], extra=[("d_rows", 0)])
        p.dma("sync", brwc.ap[0:1, :], SCR_D[8:16, :].rearrange("c t -> (c t)"),
              writes=[brwc], extra=[("d_rows", 0)])

        for c8 in range(8):
            cb_ = CB[(seg * 8 + c8) % 2]
            lam = cb_["lam"]; lamend = cb_["lamend"]; lesc = cb_["lesc"]
            e1 = cb_["e1"]; e2 = cb_["e2"]; s64a = cb_["s64a"]; s64b = cb_["s64b"]
            wpb = cb_["wpb"]; abf = cb_["abf"]; bbf = cb_["bbf"]; ptb = cb_["ptb"]
            vtb = cb_["vtb"]; rbf = cb_["rbf"]; ubf = cb_["ubf"]; obf = cb_["obf"]
            ktb = cb_["ktb"]; scro = cb_["scro"]
            csl = slice(c8 * 128, (c8 + 1) * 128)
            kc = [kseg[:, 1024 * i + c8 * 128:1024 * i + (c8 + 1) * 128] for i in range(2)]
            qc = [qseg[:, 1024 * i + c8 * 128:1024 * i + (c8 + 1) * 128] for i in range(2)]
            gcol = cumgt[:, c8:c8 + 1]
            bcol = betat[:, c8:c8 + 1]
            for i in range(2):
                p.step1("tensor", "matmul", ps_p1[:, 0:128], kc[i], kc[i],
                        start=(i == 0), stop=(i == 1), reads=[kseg], writes=[ps_p1])
            for i in range(2):
                p.step1("tensor", "matmul", ps_p1[:, 128:256], kc[i], qc[i],
                        start=(i == 0), stop=(i == 1), reads=[kseg, qseg], writes=[ps_p1])
            p.step1("tensor", "matmul", ps_p1[:, 256:384], onesrf[0:1, :], growc[0:1, csl],
                    start=True, stop=True, reads=[onesrf, growc], writes=[ps_p1])
            p.step1("tensor", "matmul", ps_p1[:, 384:512], onesrf[0:1, :], brwc[0:1, csl],
                    start=True, stop=True, reads=[onesrf, brwc], writes=[ps_p1])
            p.step1("vector", "tensor_scalar", e1.ap[:, :], ps_p1[:, 256:384], gcol, None,
                    ALU.subtract, reads=[ps_p1, cumgt], writes=[e1])
            p.step1("vector", "tensor_scalar_min", e1.ap[:, :], e1[:, :], 0.0, reads=[e1], writes=[e1])
            p.step1("scalar", "activation", e1.ap[:, :], e1[:, :], AF.Exp, reads=[e1], writes=[e1])
            p.step1("vector", "tensor_scalar", e2.ap[:, :], ps_p1[:, 256:384], -1.0, gcol,
                    ALU.mult, ALU.add, reads=[ps_p1, cumgt], writes=[e2])
            p.step1("vector", "tensor_scalar_min", e2.ap[:, :], e2[:, :], 0.0, reads=[e2], writes=[e2])
            p.step1("scalar", "activation", e2.ap[:, :], e2[:, :], AF.Exp, reads=[e2], writes=[e2])
            p.step1("vector", "tensor_mul", s64a.ap[:, :], e1[:, :], uti[:, :],
                    reads=[e1, uti], writes=[s64a])
            p.step1("vector", "tensor_mul", wpb.ap[:, :], s64a[:, :], ps_p1[:, 128:256],
                    reads=[s64a, ps_p1], writes=[wpb])
            p.step1("vector", "tensor_mul", s64a.ap[:, :], e1[:, :], uts[:, :],
                    reads=[e1, uts], writes=[s64a])
            p.step1("vector", "tensor_mul", s64a.ap[:, :], s64a[:, :], ps_p1[:, 384:512],
                    reads=[s64a, ps_p1], writes=[s64a])
            p.step1("vector", "tensor_mul", bbf[0].ap[:, :], s64a[:, :], ps_p1[:, 0:128],
                    reads=[s64a, ps_p1], writes=[bbf[0]])
            p.step1("vector", "tensor_mul", s64b.ap[:, :], e2[:, :], lts[:, :],
                    reads=[e2, lts], writes=[s64b])
            p.step1("vector", "tensor_scalar_mul", s64b.ap[:, :], s64b[:, :], bcol,
                    reads=[s64b, betat], writes=[s64b])
            p.step1("vector", "tensor_mul", abf[0].ap[:, :], s64b[:, :], ps_p1[:, 0:128],
                    reads=[s64b, ps_p1], writes=[abf[0]])
            p.step1("vector", "tensor_sub", ptb[0].ap[:, :], id128[:, :], bbf[0][:, :],
                    reads=[id128, bbf[0]], writes=[ptb[0]])
            cura, curb, curpt = 0, 0, 0
            for st_i in range(6):
                na, nb, npt = 1 - cura, 1 - curb, 1 - curpt
                p.step1("tensor", "matmul", ps_p2[:, 0:128], bbf[curb][:, :], abf[cura][:, :],
                        start=True, stop=True, reads=[bbf[curb], abf[cura]], writes=[ps_p2])
                p.step1("scalar", "activation", abf[na].ap[:, :], ps_p2[:, 0:128], AF.Copy,
                        reads=[ps_p2], writes=[abf[na]])
                if st_i < 5:
                    p.step1("tensor", "matmul", ps_p2[:, 128:256], abf[cura][:, :], bbf[curb][:, :],
                            start=True, stop=True, reads=[abf[cura], bbf[curb]], writes=[ps_p2])
                    p.step1("scalar", "activation", bbf[nb].ap[:, :], ps_p2[:, 128:256], AF.Copy,
                            reads=[ps_p2], writes=[bbf[nb]])
                p.step1("tensor", "matmul", ps_p2[:, 256:384], abf[na][:, :], ptb[curpt][:, :],
                        start=True, stop=True, reads=[abf[na], ptb[curpt]], writes=[ps_p2])
                p.step1("vector", "tensor_add", ptb[npt].ap[:, :], ps_p2[:, 256:384], ptb[curpt][:, :],
                        reads=[ps_p2, ptb[curpt]], writes=[ptb[npt]])
                cura, curb, curpt = na, nb, npt
            PT = ptb[curpt]
            p.step1("scalar", "activation", lam.ap[:, 0:1], gcol, AF.Exp, reads=[cumgt], writes=[lam])
            for i in range(2):
                p.step1("tensor", "matmul", ps_r[:, 0:512], kc[i], sbfb[i][:, :],
                        start=(i == 0), stop=(i == 1), reads=[kseg, sbfb[i]], writes=[ps_r])
            for i in range(4):
                p.step1("tensor", "transpose", ps_t[:, i * 128:(i + 1) * 128],
                        vseg[:, 1024 * i + c8 * 128:1024 * i + (c8 + 1) * 128], idb[:, :],
                        reads=[vseg, idb], writes=[ps_t])
            p.step1("vector", "tensor_copy", vtb.ap[:, :], ps_t[:, 0:512], reads=[ps_t], writes=[vtb])
            p.step1("vector", "tensor_scalar", scro.ap[:, :], ps_r[:, 0:512], lam[:, 0:1], None,
                    ALU.mult, reads=[ps_r, lam], writes=[scro])
            p.step1("vector", "tensor_sub", scro.ap[:, :], vtb[:, :], scro[:, :],
                    reads=[vtb, scro], writes=[scro])
            p.step1("vector", "tensor_scalar", rbf.ap[:, :], scro[:, :], bcol, None,
                    ALU.mult, reads=[scro, betat], writes=[rbf])
            p.step1("tensor", "matmul", ps_u[:, 0:512], PT[:, :], rbf[:, :],
                    start=True, stop=True, reads=[PT, rbf], writes=[ps_u])
            p.step1("scalar", "activation", ubf.ap[:, :], ps_u[:, 0:512], AF.Copy,
                    reads=[ps_u], writes=[ubf])
            p.step1("tensor", "matmul", ps_o[:, 0:512], wpb[:, :], ubf[:, :],
                    start=True, stop=True, reads=[wpb, ubf], writes=[ps_o])
            for i in range(2):
                p.step1("tensor", "matmul", ps_r[:, 0:512], qc[i], sbfb[i][:, :],
                        start=(i == 0), stop=(i == 1), reads=[qseg, sbfb[i]], writes=[ps_r])
            p.step1("vector", "tensor_scalar", scro.ap[:, :], ps_r[:, 0:512], lam[:, 0:1], None,
                    ALU.mult, reads=[ps_r, lam], writes=[scro])
            p.step1("vector", "tensor_add", obf.ap[:, :], scro[:, :], ps_o[:, 0:512],
                    reads=[scro, ps_o], writes=[obf])
            cg = 8 * seg + c8
            p.dma("sync", OO2[128 * cg:128 * (cg + 1), :], obf[:, :], reads=[obf])
            for i in range(2):
                p.step1("tensor", "transpose", ps_t[:, 512 + i * 128:512 + (i + 1) * 128],
                        kc[i], idb[:, :], reads=[kseg, idb], writes=[ps_t])
            p.step1("vector", "tensor_copy", ktb.ap[:, :], ps_t[:, 512:768], reads=[ps_t], writes=[ktb])
            p.step1("vector", "tensor_scalar", ktb.ap[:, :], ktb[:, :], e1[:, 127:128], None,
                    ALU.mult, reads=[ktb, e1], writes=[ktb])
            p.step1("scalar", "activation", lesc.ap[0:1, 0:1], growc[0:1, 128 * c8 + 127:128 * c8 + 128],
                    AF.Exp, reads=[growc], writes=[lesc])
            p.step1("tensor", "matmul", ps_p2[:, 384:385], onesrf[0:1, :], lesc[0:1, 0:1],
                    start=True, stop=True, reads=[onesrf, lesc], writes=[ps_p2])
            p.step1("vector", "tensor_copy", lamend.ap[:, 0:1], ps_p2[:, 384:385],
                    reads=[ps_p2], writes=[lamend])
            for i in range(2):
                pss = (ps_s1, ps_s2)[i]
                p.step1("tensor", "matmul", pss[:, 0:512], ktb[:, i * 128:(i + 1) * 128], ubf[:, :],
                        start=True, stop=True, reads=[ktb, ubf], writes=[pss])
                p.step1("vector", "scalar_tensor_tensor", sf32[i].ap[:, :], sf32[i][:, :],
                        lamend[:, 0:1], pss[:, 0:512], ALU.mult, ALU.add,
                        reads=[sf32[i], lamend, pss], writes=[sf32[i]])
                p.step1("scalar", "activation", sbfb[i].ap[:, :], sf32[i][:, :], AF.Copy,
                        reads=[sf32[i]], writes=[sbfb[i]])
    return p.finalize([(s, p.cnt[s]) for s in sorted(p.store_sems)])


def stage_d2(g, QH, KH, VH, GB, core):
    h = core if core < 6 else 0
    return {
        "qh2": _c(QH[256 * h:256 * (h + 1), :]),
        "kh2": _c(KH[256 * h:256 * (h + 1), :]),
        "vh2": _c(VH[512 * h:512 * (h + 1), :]),
        "gr": _c(GB[h:h + 1, :]),
        "br": _c(GB[6 + h:7 + h, :]),
        "id_bf": g["id_bf"], "id_f": g["id_f"], "id128b": g["id_bf"],
        "utri_inc": g["utri_inc"], "utri_strict": g["utri_strict"], "ltri_strict": g["ltri_strict"],
        "ones_row_f": g["ones_row_f"],
    }



def build_d3e():
    p = P("d3e")
    OH = p.din("oh", [TOWN, 3072], BF16)
    X2T = p.din("x2t", [2048, TOWN], F32)
    X2LNT = p.din("x2lnt", [2048, TOWN], BF16)
    WG1 = p.din("wg1", [2048, 3072], F32)
    WO1 = p.din("wo1", [3072, 2048], F32)
    ONW = p.din("onw", [1, 512], F32)
    GATEW = p.din("gatew", [2048, 8], F32)
    W1L = p.din("w1l", [8, 2048, 1024], F32)
    W3L = p.din("w3l", [8, 2048, 1024], F32)
    W2L = p.din("w2l", [8, 1024, 2048], F32)
    WS1L = p.din("ws1l", [2048, 1024], F32)
    WS3L = p.din("ws3l", [2048, 1024], F32)
    WS2L = p.din("ws2l", [1024, 2048], F32)
    ONESB = p.din("ones_col_bf", [128, 1], BF16)
    ONESRF = p.din("ones_row_f", [1, 128], F32)
    IDB = p.din("id_bf", [128, 128], BF16)
    X4O = p.dout("x4o", [2048, TOWN], F32)
    CROW_D = p.nc.dram_tensor("crow_d", [8, TOWN], F32, kind="Internal").ap()

    NT2 = [(0, 512), (512, 512)]
    lc = mk_ln_ctx(p, TOWN)
    onesb = p.sbuf("c_onesb", [128, 1], BF16)
    onesrf = p.sbuf("c_onesrf", [1, 128], F32)
    lc["onesb"] = onesb; lc["onesrf"] = onesrf
    idb = p.sbuf("c_idb", [128, 128], BF16)
    onwrep = p.sbuf("onwrep", [128, 512], F32)
    onwrow = p.sbuf("onwrow", [1, 512], F32)

    yacc = p.sbuf("yacc", [128, 16 * TOWN], F32)
    srow = lc["srow"]; srow2 = lc["srow2"]; rrep = lc["rrep"]; scr = lc["scr"]; scr2 = lc["scr2"]
    ps_r1 = lc["ps_r1"]; ps_m = lc["ps_m"]; ps_n = lc["ps_n"]
    ps_a = p.psum("ps_a", [128, 512], F32)
    ps_b = p.psum("ps_b", [128, 512], F32)
    ps_t = p.psum("ps_t", [128, 1024], BF16)

    X3D = p.nc.dram_tensor("x3spill", [2048, TOWN], F32, kind="Internal").ap()
    for b, srcdr in ((onesb, ONESB), (onesrf, ONESRF), (idb, IDB), (onwrow, ONW)):
        p.dma("sync", b.ap[:, :], srcdr, writes=[b])
    x3ln = p.sbuf("x3ln", [128, 16 * TOWN], BF16, side="right")

    bcast_row(p, ps_m.ap[:, 0:512], onesrf[:, :], onwrow[0:1, :], reads=[onwrow, onesrf], writes=[ps_m])
    p.step1("vector", "tensor_copy", onwrep.ap[:, :], ps_m[:, 0:512], reads=[ps_m], writes=[onwrep])

    x3ln_r = x3ln
    x2lnf = p.sbuf("x2lnf", [128, 16 * TOWN], BF16, side="right")
    p.dma("sync", x2lnf.ap.rearrange("p (k t) -> p k t", k=16),
          X2LNT.rearrange("(k p) t -> p k t", p=128), writes=[x2lnf])
    ggall = p.sbuf("ggall", [128, 4 * 3072], BF16, side="right")
    wgb = p.sbuf("wgb", [128, 16 * 512], BF16, side="right")
    otile = p.sbuf("otile", [128, 3072], BF16, side="right")
    oft = p.sbuf("oft", [128, 24 * 128], BF16, side="right")
    wol = [p.sbuf("wol0", [128, 24 * 128], BF16, side="right")] * 2
    rsq = p.sbuf("rsq", [128, 1], F32, side="right")
    pool4 = [ps_a, ps_b, ps_m, ps_n]
    pi = 0
    for half in range(2):
        tts = range(4 * half, 4 * half + 4)
        for ns in range(6):
            p.dma("gpsimd", wgb.ap.rearrange("p (k m) -> p k m", k=16),
                  WG1[:, ns * 512:(ns + 1) * 512].rearrange("(k p) m -> p k m", p=128), writes=[wgb])
            for tt in tts:
                ps = pool4[pi % 4]; pi += 1
                mm_group(p, ps.ap[:, 0:512],
                         [x2lnf[:, k * TOWN + tt * 128:k * TOWN + (tt + 1) * 128] for k in range(16)],
                         [wgb[:, k * 512:(k + 1) * 512] for k in range(16)],
                         reads=[x2lnf, wgb], writes=[ps])
                p.step1("scalar", "activation",
                        ggall.ap[:, (tt - 4 * half) * 3072 + ns * 512:(tt - 4 * half) * 3072 + (ns + 1) * 512],
                        ps[:, 0:512], AF.Silu, reads=[ps], writes=[ggall])
        for tt in tts:
            tsl = slice(tt * 128, (tt + 1) * 128)
            g0 = (tt - 4 * half) * 3072
            p.dma("sync", otile.ap[:, :], OH[tsl, :], writes=[otile])
            for h in range(6):
                hsl = slice(h * 512, (h + 1) * 512)
                p.step1("scalar", "activation", lc["sqs"][0].ap[:, 0:512], otile[:, hsl], AF.Square,
                        reads=[otile], writes=[lc["sqs"][0]])
                p.step1("vector", "tensor_reduce", rsq.ap[:, 0:1], lc["sqs"][0][:, 0:512],
                        mybir.AxisListType.X, ALU.add, reads=[lc["sqs"][0]], writes=[rsq])
                p.step1("vector", "tensor_scalar", rsq.ap[:, 0:1], rsq[:, 0:1], 1.0 / 512, 1e-5,
                        ALU.mult, ALU.add, reads=[rsq], writes=[rsq])
                p.step1("scalar", "activation", rsq.ap[:, 0:1], rsq[:, 0:1], AF.Sqrt,
                        reads=[rsq], writes=[rsq])
                p.step1("vector", "reciprocal", rsq.ap[:, 0:1], rsq[:, 0:1], reads=[rsq], writes=[rsq])
                p.step1("vector", "tensor_scalar", scr.ap[:, 0:512], otile[:, hsl], rsq[:, 0:1], None,
                        ALU.mult, reads=[otile, rsq], writes=[scr])
                p.step1("vector", "tensor_mul", scr.ap[:, 0:512], scr[:, 0:512], onwrep[:, :],
                        reads=[scr, onwrep], writes=[scr])
                p.step1("vector", "tensor_mul", otile.ap[:, hsl], scr[:, 0:512],
                        ggall[:, g0 + h * 512:g0 + (h + 1) * 512],
                        reads=[scr, ggall], writes=[otile])
            for v in range(24):
                p.step1("tensor", "transpose", ps_t[:, (v % 8) * 128:(v % 8 + 1) * 128],
                        otile[:, v * 128:(v + 1) * 128], idb[:, :], reads=[otile, idb], writes=[ps_t])
                p.step1("scalar", "activation", oft.ap[:, v * 128:(v + 1) * 128],
                        ps_t[:, (v % 8) * 128:(v % 8 + 1) * 128], AF.Copy, reads=[ps_t], writes=[oft])
            for mc in range(16):
                wb = wol[mc % 2]
                p.dma("gpsimd", wb.ap.rearrange("p (k m) -> p k m", k=24)[:, 0:24, :],
                      WO1[:, mc * 128:(mc + 1) * 128].rearrange("(k p) m -> p k m", p=128), writes=[wb])
                ps = pool4[pi % 4]; pi += 1
                mm_group(p, ps.ap[:, 0:128], [wb[:, k * 128:(k + 1) * 128] for k in range(24)],
                         [oft[:, k * 128:(k + 1) * 128] for k in range(24)],
                         reads=[wb, oft], writes=[ps])
                p.step1("vector", "tensor_copy", yacc.ap[:, mc * TOWN + tt * 128:mc * TOWN + (tt + 1) * 128],
                        ps[:, 0:128], reads=[ps], writes=[yacc])
    p.free(rsq)
    p.free(wol[0], oft, otile, wgb, ggall, x2lnf)

    x2 = p.sbuf("x2b", [128, 16 * TOWN], F32, side="right")
    p.dma("sync", x2.ap.rearrange("p (k t) -> p k t", k=16),
          X2T.rearrange("(k p) t -> p k t", p=128), writes=[x2])
    emit_ln_c(p, lc, yacc, x3ln, TOWN, NT2)
    for k in range(16):
        for (o0, n) in NT2:
            p.step1("scalar", "activation", scr2.ap[:, 0:n], x3ln[:, k * TOWN + o0:k * TOWN + o0 + n],
                    AF.Copy, reads=[x3ln], writes=[scr2])
            p.step1("vector", "tensor_add", x2.ap[:, k * TOWN + o0:k * TOWN + o0 + n],
                    x2[:, k * TOWN + o0:k * TOWN + o0 + n], scr2[:, 0:n], reads=[x2, scr2], writes=[x2])
        p.dma("sync", X3D[k * 128:(k + 1) * 128, :], x2[:, k * TOWN:(k + 1) * TOWN], sem="sp_x3", reads=[x2])
    emit_ln_c(p, lc, x2, x3ln, TOWN, NT2)
    p.free(x2)

    gw = p.sbuf("gw", [128, 8], BF16, side="right")
    sc8 = p.sbuf("sc8", [128, 8], F32, side="right")
    m1c = p.sbuf("m1c", [128, 1], F32, side="right")
    combt = p.sbuf("combt", [8, TOWN], F32, side="right")
    x3ln_c = p.subdiv(x3ln, 16, TOWN)
    for tt in range(8):
        tsl = slice(tt * 128, (tt + 1) * 128)
        for k in range(16):
            if tt == 0 and k == 0:
                pass
            p.dma("gpsimd", gw.ap[:, :], GATEW[k * 128:(k + 1) * 128, :], writes=[gw])
            p.step1("tensor", "matmul", ps_a[:, 0:8], x3ln_c[k][:, tsl], gw[:, :],
                    start=(k == 0), stop=(k == 15), reads=[x3ln_c[k], gw], writes=[ps_a])
        p.step1("vector", "reduce_max", m1c.ap[:, 0:1], ps_a[:, 0:8], mybir.AxisListType.X,
                reads=[ps_a], writes=[m1c])
        p.step1("vector", "tensor_scalar", sc8.ap[:, :], ps_a[:, 0:8], m1c[:, 0:1], None,
                ALU.subtract, reads=[ps_a, m1c], writes=[sc8])
        p.step1("scalar", "activation", sc8.ap[:, :], sc8[:, :], AF.Exp, reads=[sc8], writes=[sc8])
        p.step1("vector", "tensor_reduce", m1c.ap[:, 0:1], sc8[:, :], mybir.AxisListType.X, ALU.add,
                reads=[sc8], writes=[m1c])
        p.step1("vector", "reciprocal", m1c.ap[:, 0:1], m1c[:, 0:1], reads=[m1c], writes=[m1c])
        p.step1("vector", "tensor_scalar", sc8.ap[:, :], sc8[:, :], m1c[:, 0:1], None, ALU.mult,
                reads=[sc8, m1c], writes=[sc8])
        p.step1("vector", "reduce_max", m1c.ap[:, 0:1], sc8[:, :], mybir.AxisListType.X,
                reads=[sc8], writes=[m1c])
        p.step1("vector", "tensor_scalar", scr.ap[:, 0:8], sc8[:, :], m1c[:, 0:1], None,
                ALU.is_ge, reads=[sc8, m1c], writes=[scr])
        p.step1("vector", "scalar_tensor_tensor", scr.ap[:, 8:16], scr[:, 0:8], -1e30,
                sc8[:, :], ALU.mult, ALU.add, reads=[scr, sc8], writes=[scr])
        p.step1("vector", "reduce_max", m1c.ap[:, 0:1], scr[:, 8:16], mybir.AxisListType.X,
                reads=[scr], writes=[m1c])
        p.step1("vector", "tensor_scalar", scr.ap[:, 0:8], sc8[:, :], m1c[:, 0:1], None,
                ALU.is_ge, reads=[sc8, m1c], writes=[scr])
        p.step1("vector", "tensor_mul", sc8.ap[:, :], sc8[:, :], scr[:, 0:8],
                reads=[sc8, scr], writes=[sc8])
        p.step1("scalar", "activation", lc["sqs"][0].ap[:, 0:8], sc8[:, :], AF.Copy,
                reads=[sc8], writes=[lc["sqs"][0]])
        p.step1("tensor", "transpose", ps_t[0:8, 0:128], lc["sqs"][0][:, 0:8], idb[:, :],
                reads=[lc["sqs"][0], idb], writes=[ps_t])
        p.step1("vector", "tensor_copy", combt.ap[0:8, tsl], ps_t[0:8, 0:128],
                reads=[ps_t], writes=[combt])
    p.dma("sync", CROW_D, combt[0:8, :], sem="d_crow", reads=[combt])
    p.free(combt, m1c, sc8, gw)
    crow = p.sbuf("crow", [1, TOWN], F32, side="right")

    prodb = p.sbuf("prodb", [128, 8 * TOWN], BF16, side="right")
    prod_c = p.subdiv(prodb, 8, TOWN)
    w1b = [p.sbuf(f"w1b{i}", [128, 16 * 128], BF16, side="right") for i in range(3)]
    w3b = [p.sbuf(f"w3b{i}", [128, 16 * 128], BF16, side="right") for i in range(3)]
    w2b = [p.sbuf(f"w2b{i}", [128, 8 * 128], BF16, side="right") for i in range(3)]
    gb1 = p.sbuf("gb1", [128, 512], BF16, side="right")
    crep = p.sbuf("crep", [128, TOWN], F32, side="right")
    p.step1("vector", "memset", yacc.ap[:, :], 0.0, reads=[yacc], writes=[yacc])
    for e in range(9):
        if e < 8:
            p.dma("sync", crow.ap[0:1, :], CROW_D[e:e + 1, :], writes=[crow], extra=[("d_crow", 0)])
            for (o0, n) in NT2:
                bcast_row(p, ps_m.ap[:, 0:n], onesrf[:, :], crow[0:1, o0:o0 + n],
                          reads=[crow, onesrf], writes=[ps_m])
                p.step1("vector", "tensor_copy", crep.ap[:, o0:o0 + n], ps_m[:, 0:n],
                        reads=[ps_m], writes=[crep])
        else:
            p.step1("vector", "memset", crep.ap[:, :], 1.0, reads=[], writes=[crep])
        for mc in range(8):
            wa = w1b[(e * 8 + mc) % 3]; wc = w3b[(e * 8 + mc) % 3]
            srcW1 = W1L[e] if e < 8 else WS1L
            srcW3 = W3L[e] if e < 8 else WS3L
            p.dma("gpsimd", wa.ap.rearrange("p (k m) -> p k m", k=16),
                  srcW1[:, mc * 128:(mc + 1) * 128].rearrange("(k p) m -> p k m", p=128), writes=[wa])
            p.dma("gpsimd", wc.ap.rearrange("p (k m) -> p k m", k=16),
                  srcW3[:, mc * 128:(mc + 1) * 128].rearrange("(k p) m -> p k m", p=128), writes=[wc])
            for (o0, n) in NT2:
                psx = pool4[pi % 4]; pi += 1
                mm_group(p, psx.ap[:, 0:n], [wa[:, k * 128:(k + 1) * 128] for k in range(16)],
                         [c[:, o0:o0 + n] for c in x3ln_c], reads=[wa] + x3ln_c, writes=[psx])
                p.step1("scalar", "activation", gb1.ap[:, 0:n], psx[:, 0:n], AF.Silu,
                        reads=[psx], writes=[gb1])
                psy = pool4[pi % 4]; pi += 1
                mm_group(p, psy.ap[:, 0:n], [wc[:, k * 128:(k + 1) * 128] for k in range(16)],
                         [c[:, o0:o0 + n] for c in x3ln_c], reads=[wc] + x3ln_c, writes=[psy])
                p.step1("vector", "tensor_mul", scr.ap[:, 0:n], gb1[:, 0:n], psy[:, 0:n],
                        reads=[gb1, psy], writes=[scr])
                p.step1("vector", "tensor_mul", prod_c[mc].ap[:, o0:o0 + n], scr[:, 0:n],
                        crep[:, o0:o0 + n], reads=[scr, crep], writes=[prod_c[mc]])
        for mc in range(16):
            wd = w2b[(e * 16 + mc) % 3]
            srcW2 = W2L[e] if e < 8 else WS2L
            p.dma("gpsimd", wd.ap.rearrange("p (k m) -> p k m", k=8),
                  srcW2[:, mc * 128:(mc + 1) * 128].rearrange("(k p) m -> p k m", p=128), writes=[wd])
            for (o0, n) in NT2:
                psx = pool4[pi % 4]; pi += 1
                mm_group(p, psx.ap[:, 0:n], [wd[:, k * 128:(k + 1) * 128] for k in range(8)],
                         [c[:, o0:o0 + n] for c in prod_c], reads=[wd] + prod_c, writes=[psx])
                p.step1("vector", "tensor_add", yacc.ap[:, mc * TOWN + o0:mc * TOWN + o0 + n],
                        yacc[:, mc * TOWN + o0:mc * TOWN + o0 + n], psx[:, 0:n],
                        reads=[yacc, psx], writes=[yacc])
    p.free(crep, gb1, *w2b[::-1], *w3b[::-1], *w1b[::-1], prodb)

    x3f = p.sbuf("x3f", [128, 16 * TOWN], F32, side="right")
    rscr2 = p.sbuf("rscr2", [128, TOWN], F32, side="right")
    p.dma("sync", x3f.ap.rearrange("p (k t) -> p k t", k=16),
          X3D.rearrange("(k p) t -> p k t", p=128), writes=[x3f],
          extra=[("sp_x3", 0)])
    emit_ln_c(p, lc, yacc, x3ln, TOWN, NT2)
    for k in range(16):
        ks = slice(k * TOWN, (k + 1) * TOWN)
        p.step1("scalar", "activation", rscr2.ap[:, :], x3ln[:, ks], AF.Copy, reads=[x3ln], writes=[rscr2])
        p.step1("vector", "tensor_add", x3f.ap[:, ks], x3f[:, ks], rscr2[:, :], reads=[x3f, rscr2], writes=[x3f])
        p.dma("sync", X4O[k * 128:(k + 1) * 128, :], x3f[:, ks], reads=[x3f])
    return p.finalize([(s, p.cnt[s]) for s in sorted(p.store_sems)])


def stage_d3e(g, O_full, X2, X2LN, core):
    s = slice(1024 * core, 1024 * (core + 1))
    return {
        "oh": _c(O_full[s, :]), "x2t": _c(X2[:, s]), "x2lnt": _c(X2LN[:, s]),
        "wg1": g["l1_Wg"], "wo1": g["l1_Wo"], "onw": g["l1_onw"],
        "gatew": g["f1_gate"], "w1l": g["f1_w1l"], "w3l": g["f1_w3l"], "w2l": g["f1_w2l"],
        "ws1l": g["f1_ws1l"], "ws3l": g["f1_ws3l"], "ws2l": g["f1_ws2l"],
        "ones_col_bf": g["ones_col_bf"], "ones_row_f": g["ones_row_f"], "id_bf": g["id_bf"],
    }


def stage_a2(B_full, G_full, g, core):
    return {
        "bo2": _c(B_full[256 * core:256 * (core + 1), :]),
        "go2": _c(G_full[2 * core:2 * (core + 1), :]),
        "ones_row_f": g["ones_row_f"],
    }


def stage_a1(g, core):
    s = slice(1024 * core, 1024 * core + TLOC)
    return {
        "xt": _c(g["xT_pad"][:, s]),
        "wq": g["l0_Wq"], "wk": g["l0_Wk"], "wv": g["l0_Wv"], "wi": g["l0_ig"],
        "wog": g["l0_og"], "wgam": g["l0_gam"],
        "convw": g["l0_convw"], "convb": g["l0_convb"], "igb": g["l0_igb"],
        "ogb": g["l0_ogb"], "gamb": g["l0_gamb"], "vng": g["l0_vng"], "vnb": g["l0_vnb"],
        "ones_col_bf": g["ones_col_bf"], "ones_row_f": g["ones_row_f"],
    }


_BUILT = {}


def _get(name, builder):
    if name not in _BUILT:
        _BUILT[name] = builder()
    return _BUILT[name]


def kernel(**inputs):
    g = stage_common(inputs)
    cores = list(range(NCORES))

    nc = _get("a1", build_a1)
    res = run_bass_kernel_spmd(nc, [stage_a1(g, c) for c in cores], core_ids=cores)
    Q_full = np.concatenate([res.results[c]["qo"] for c in cores], axis=1)
    B_full = np.concatenate([res.results[c]["bo"] for c in cores], axis=1)
    O_full = np.concatenate([res.results[c]["oo"] for c in cores], axis=1)
    G_full = np.concatenate([res.results[c]["go"] for c in cores], axis=1)

    nc = _get("a2", build_a2)
    res = run_bass_kernel_spmd(nc, [stage_a2(B_full, G_full, g, c) for c in cores], core_ids=cores)
    M_full = np.concatenate([res.results[c]["mo"] for c in cores], axis=0)

    nc = _get("a3c", build_a3c)
    res = run_bass_kernel_spmd(nc, [stage_a3c(g, M_full, Q_full, O_full, c) for c in cores],
                               core_ids=cores)
    X2 = np.concatenate([res.results[c]["x2o"] for c in cores], axis=1)
    X2LN = np.concatenate([res.results[c]["x2lno"] for c in cores], axis=1)

    X2LN_pad = np.pad(X2LN, ((0, 0), (HALO, 0)))
    nc = _get("d1", build_d1)
    res = run_bass_kernel_spmd(nc, [stage_d1(g, X2LN_pad, c) for c in cores], core_ids=cores)
    QH = np.concatenate([res.results[c]["qho"] for c in cores], axis=1)
    KH = np.concatenate([res.results[c]["kho"] for c in cores], axis=1)
    VH = np.concatenate([res.results[c]["vho"] for c in cores], axis=1)
    GB = np.concatenate([res.results[c]["gbo"] for c in cores], axis=1)

    nc = _get("d2", lambda: build_d2(16))
    res = run_bass_kernel_spmd(nc, [stage_d2(g, QH, KH, VH, GB, c) for c in cores], core_ids=cores)
    O_tm = np.concatenate([np.asarray(res.results[h]["oo2"])[1::2] for h in range(6)], axis=1)

    nc = _get("d3e", build_d3e)
    res = run_bass_kernel_spmd(nc, [stage_d3e(g, O_tm, X2, X2LN, c) for c in cores], core_ids=cores)
    X4 = np.concatenate([res.results[c]["x4o"] for c in cores], axis=1)
    return np.ascontiguousarray(X4.T).reshape(1, 8192, 2048).astype(np.float32)
